# version 01
# baseline (speedup 1.0000x reference)
import numpy as np
import ml_dtypes
from contextlib import ExitStack

import concourse.bass as bass
import concourse.mybir as mybir
from concourse.bass_utils import run_bass_kernel_spmd
from concourse.alu_op_type import AluOpType as OP

F32 = mybir.dt.float32
BF16 = mybir.dt.bfloat16
I32 = mybir.dt.int32
AF = mybir.ActivationFunctionType
AX = mybir.AxisListType
BF = ml_dtypes.bfloat16

NC = 8
B, N, D = 2, 8192, 1024
H, DH = 16, 64
I_FFN = 2816


def np_identity_bf16(n=128):
    return np.eye(n, dtype=BF)


class Ctr:
    def __init__(self):
        self.v = 0

    def inc(self, n=1):
        self.v += n
        return self.v


def emit_fp4_quant(nc, x_sb, out_sb, scr, P, W, cmp, c_cmp, done_sem=None):
    v, sc = nc.vector, nc.scalar
    nb = W // 16
    amax, s_t, r_t = scr["amax"], scr["s"], scr["r"]
    u, au, sg = scr["u"], scr["au"], scr["sg"]
    t1, t2, ti = scr["t1"], scr["t2"], scr["ti"]
    x3 = x_sb[0:P, 0:W].rearrange("p (a b) -> p a b", b=16)
    v.tensor_reduce(out=amax[0:P, 0:nb], in_=x3, axis=AX.X, op=OP.max,
                    apply_absolute_value=True).then_inc(cmp, 1)
    v.wait_ge(cmp, c_cmp.inc())
    v.tensor_scalar(out=s_t[0:P, 0:nb], in0=amax[0:P, 0:nb], scalar1=1.0 / 6.0,
                    scalar2=1e-12, op0=OP.mult, op1=OP.max).then_inc(cmp, 1)
    v.wait_ge(cmp, c_cmp.inc())
    v.reciprocal(out=r_t[0:P, 0:nb], in_=s_t[0:P, 0:nb]).then_inc(cmp, 1)
    v.wait_ge(cmp, c_cmp.inc())
    rb = r_t[0:P, 0:nb].unsqueeze(2).broadcast_to([P, nb, 16])
    v.tensor_tensor(out=u[0:P, 0:W], in0=x3, in1=rb, op=OP.mult).then_inc(cmp, 1)
    c_cmp.inc()
    sc.wait_ge(cmp, c_cmp.v)
    sc.activation(out=au[0:P, 0:W], in_=u[0:P, 0:W], func=AF.Abs)
    sc.activation(out=sg[0:P, 0:W], in_=u[0:P, 0:W], func=AF.Sign).then_inc(cmp, 1)
    c_cmp.inc()
    v.wait_ge(cmp, c_cmp.v)
    MAGIC = float(2 ** 22)
    v.tensor_scalar(out=t1[0:P, 0:W], in0=au[0:P, 0:W], scalar1=MAGIC, scalar2=-MAGIC,
                    op0=OP.add, op1=OP.add)
    v.tensor_scalar(out=t1[0:P, 0:W], in0=t1[0:P, 0:W], scalar1=2.0, scalar2=None,
                    op0=OP.min)
    v.scalar_tensor_tensor(out=t2[0:P, 0:W], in0=au[0:P, 0:W], scalar=2.5,
                           in1=t1[0:P, 0:W], op0=OP.is_gt, op1=OP.add)
    v.scalar_tensor_tensor(out=t1[0:P, 0:W], in0=au[0:P, 0:W], scalar=3.5,
                           in1=t2[0:P, 0:W], op0=OP.is_gt, op1=OP.add)
    v.scalar_tensor_tensor(out=t2[0:P, 0:W], in0=au[0:P, 0:W], scalar=5.0,
                           in1=t1[0:P, 0:W], op0=OP.is_gt, op1=OP.add)
    v.scalar_tensor_tensor(out=t1[0:P, 0:W], in0=au[0:P, 0:W], scalar=5.0,
                           in1=t2[0:P, 0:W], op0=OP.is_gt, op1=OP.add)
    v.tensor_tensor(out=t2[0:P, 0:W], in0=t1[0:P, 0:W], in1=sg[0:P, 0:W], op=OP.mult)
    sb = s_t[0:P, 0:nb].unsqueeze(2).broadcast_to([P, nb, 16])
    inst = v.tensor_tensor(out=out_sb[0:P, 0:W].rearrange("p (a b) -> p a b", b=16),
                           in0=t2[0:P, 0:W].rearrange("p (a b) -> p a b", b=16),
                           in1=sb, op=OP.mult)
    if done_sem is not None:
        inst.then_inc(done_sem, 1)


def mk_quant_scratch(nc, ctx, P, W, tag):
    E = ctx.enter_context
    nb = W // 16
    return {
        "amax": E(nc.sbuf_tensor(f"{tag}_amax", [P, nb], F32)),
        "s": E(nc.sbuf_tensor(f"{tag}_s", [P, nb], F32)),
        "r": E(nc.sbuf_tensor(f"{tag}_r", [P, nb], F32)),
        "u": E(nc.sbuf_tensor(f"{tag}_u", [P, W], F32)),
        "au": E(nc.sbuf_tensor(f"{tag}_au", [P, W], F32)),
        "sg": E(nc.sbuf_tensor(f"{tag}_sg", [P, W], F32)),
        "t1": E(nc.sbuf_tensor(f"{tag}_t1", [P, W], F32)),
        "t2": E(nc.sbuf_tensor(f"{tag}_t2", [P, W], F32)),
        "ti": E(nc.sbuf_tensor(f"{tag}_ti", [P, W], I32)),
    }


def build_a0():
    nc = bass.Bass("TRN2", target_bir_lowering=False)
    wg = nc.dram_tensor("wg", [352, 1024], F32, kind="ExternalInput").ap()
    wu = nc.dram_tensor("wu", [352, 1024], F32, kind="ExternalInput").ap()
    wd = nc.dram_tensor("wd", [128, I_FFN], F32, kind="ExternalInput").ap()
    w1 = nc.dram_tensor("w1", [1024, 4096], F32, kind="ExternalInput").ap()
    w2 = nc.dram_tensor("w2", [4096, 1024], F32, kind="ExternalInput").ap()
    ident_in = nc.dram_tensor("ident", [128, 128], BF16, kind="ExternalInput").ap()
    wgqT = nc.dram_tensor("wgqT", [1024, 352], BF16, kind="ExternalOutput").ap()
    wuqT = nc.dram_tensor("wuqT", [1024, 352], BF16, kind="ExternalOutput").ap()
    wdqT = nc.dram_tensor("wdqT", [I_FFN, 128], BF16, kind="ExternalOutput").ap()
    w1b = nc.dram_tensor("w1b", [1024, 4096], BF16, kind="ExternalOutput").ap()
    w2b = nc.dram_tensor("w2b", [4096, 1024], BF16, kind="ExternalOutput").ap()

    tiles = []
    for src, dst, rts, W in ((wg, wgqT, (128, 128, 96), 1024),
                             (wu, wuqT, (128, 128, 96), 1024),
                             (wd, wdqT, (128,), I_FFN)):
        r0 = 0
        for P in rts:
            tiles.append((src, r0, P, W, dst))
            r0 += P

    with ExitStack() as ctx:
        E = ctx.enter_context
        ident = E(nc.sbuf_tensor("identt", [128, 128], BF16))
        xin = [E(nc.sbuf_tensor(f"xin{b}", [128, I_FFN], F32)) for b in range(2)]
        qo = [E(nc.sbuf_tensor(f"qo{b}", [128, I_FFN], BF16)) for b in range(2)]
        scr = mk_quant_scratch(nc, ctx, 128, I_FFN, "qs")
        tpo = [E(nc.sbuf_tensor(f"tpo{b}", [128, 128], BF16)) for b in range(2)]
        cvt = [E(nc.sbuf_tensor(f"cvt{b}", [128, 4096], BF16)) for b in range(2)]
        ps = [E(nc.psum_tensor(f"ps{b}", [128, 512], F32)) for b in range(2)]
        ld = E(nc.semaphore("ld"))
        cmp = E(nc.semaphore("cmp"))
        qd = E(nc.semaphore("qd"))
        pe = E(nc.semaphore("pe"))
        ev = E(nc.semaphore("ev"))
        st = E(nc.semaphore("st"))
        ld2 = E(nc.semaphore("ld2"))
        st2 = E(nc.semaphore("st2"))
        block = E(nc.Block())

        nsl_of = [t[3] // 128 for t in tiles]
        slice_base = np.cumsum([0] + nsl_of).tolist()

        @block.sync
        def _(sync):
            v, t, sc, g = nc.vector, nc.tensor, nc.scalar, nc.gpsimd
            sync.dma_start(out=ident[:], in_=ident_in).then_inc(ld, 16)
            for i, (src, r0, P, W, dst) in enumerate(tiles):
                if i >= 2:
                    sync.wait_ge(qd, i - 1)
                sync.dma_start(out=xin[i % 2][0:P, 0:W],
                               in_=src[r0:r0 + P, :]).then_inc(ld, 16)
            c_cmp = Ctr()
            for i, (src, r0, P, W, dst) in enumerate(tiles):
                v.wait_ge(ld, 16 + 16 * (i + 1))
                if i >= 1:
                    v.wait_ge(pe, slice_base[i])
                emit_fp4_quant(nc, xin[i % 2], qo[i % 2], scr, P, W, cmp, c_cmp,
                               done_sem=qd)
                t.wait_ge(qd, i + 1)
                nsl = W // 128
                for sl in range(nsl):
                    gsl = slice_base[i] + sl
                    if gsl >= 2:
                        t.wait_ge(ev, gsl - 1)
                    t.matmul(out=ps[gsl % 2][0:128, 0:P],
                             lhsT=qo[i % 2][0:P, sl * 128:(sl + 1) * 128],
                             rhs=ident[0:P, 0:P],
                             start=True, stop=True).then_inc(pe, 1)
                    sc.wait_ge(pe, gsl + 1)
                    if gsl >= 2:
                        sc.wait_ge(st, 16 * (gsl - 1))
                    sc.copy(out=tpo[gsl % 2][0:128, 0:P],
                            in_=ps[gsl % 2][0:128, 0:P]).then_inc(ev, 1)
                    g.wait_ge(ev, gsl + 1)
                    g.dma_start(out=dst[sl * 128:(sl + 1) * 128, r0:r0 + P],
                                in_=tpo[gsl % 2][0:128, 0:P]).then_inc(st, 16)

            nconv = 0
            for src, dst, ntile, Wc in ((w1, w1b, 8, 4096), (w2, w2b, 32, 1024)):
                for i in range(ntile):
                    bidx = nconv % 2
                    if nconv >= 2:
                        g.wait_ge(st2, 16 * (nconv - 1))
                    g.dma_start(out=cvt[bidx][:, 0:Wc],
                                in_=src[i * 128:(i + 1) * 128, :]).then_inc(ld2, 16)
                    g.wait_ge(ld2, 16 * (nconv + 1))
                    g.dma_start(out=dst[i * 128:(i + 1) * 128, :],
                                in_=cvt[bidx][:, 0:Wc]).then_inc(st2, 16)
                    nconv += 1
            g.wait_ge(st2, 16 * nconv)
            g.wait_ge(st, 16 * slice_base[-1])

    return nc


def run_a0(inputs, core_ids=None):
    ident = np_identity_bf16()
    in_maps = []
    for c in range(NC):
        in_maps.append({
            "wg": np.ascontiguousarray(inputs["L0_ffn_w_gate"][c * 352:(c + 1) * 352]),
            "wu": np.ascontiguousarray(inputs["L0_ffn_w_up"][c * 352:(c + 1) * 352]),
            "wd": np.ascontiguousarray(inputs["L0_ffn_w_down"][c * 128:(c + 1) * 128]),
            "w1": np.ascontiguousarray(inputs["L1_ffn_W1"][c]),
            "w2": np.ascontiguousarray(inputs["L1_ffn_W2"][c]),
            "ident": ident,
        })
    nc = build_a0()
    res = run_bass_kernel_spmd(nc, in_maps, core_ids=list(range(NC)))
    wgqT = np.concatenate([r["wgqT"] for r in res.results], axis=1)
    wuqT = np.concatenate([r["wuqT"] for r in res.results], axis=1)
    wdqT = np.concatenate([r["wdqT"] for r in res.results], axis=1)
    w1b = np.stack([r["w1b"] for r in res.results], axis=0)
    w2b = np.stack([r["w2b"] for r in res.results], axis=0)
    return {"wgqT": wgqT, "wuqT": wuqT, "wdqT": wdqT, "w1b": w1b, "w2b": w2b}


def kernel(**inputs):
    a0 = run_a0(inputs)
    a = run_a(inputs, a0)
    o = run_b(inputs, a)
    cc = run_c(inputs, a, o)
    attn = run_d(inputs, cc)
    parts, wtok, cvs = run_e1(inputs, cc, a0)
    return run_e2(inputs, cc, attn, parts, wtok, cvs)


NT = 16
TOK = 2048


def build_a():
    nc = bass.Bass("TRN2", target_bir_lowering=False)
    x_in = nc.dram_tensor("x", [TOK, D], F32, kind="ExternalInput").ap()
    Wq_in = nc.dram_tensor("Wq", [D, D], F32, kind="ExternalInput").ap()
    Wk_in = nc.dram_tensor("Wk", [D, D], F32, kind="ExternalInput").ap()
    Wv_in = nc.dram_tensor("Wv", [D, D], F32, kind="ExternalInput").ap()
    Wf_in = nc.dram_tensor("Wf", [D, H], F32, kind="ExternalInput").ap()
    Wb_in = nc.dram_tensor("Wb", [D, H], F32, kind="ExternalInput").ap()
    gw1_in = nc.dram_tensor("gw1", [D, 64], F32, kind="ExternalInput").ap()
    gw2_in = nc.dram_tensor("gw2", [64, D], F32, kind="ExternalInput").ap()
    delta_in = nc.dram_tensor("delta", [1, H], F32, kind="ExternalInput").ap()
    wgqT_in = nc.dram_tensor("wgqT", [D, I_FFN], BF16, kind="ExternalInput").ap()
    wuqT_in = nc.dram_tensor("wuqT", [D, I_FFN], BF16, kind="ExternalInput").ap()
    wdqT_in = nc.dram_tensor("wdqT", [I_FFN, D], BF16, kind="ExternalInput").ap()
    ident_in = nc.dram_tensor("ident", [128, 128], BF16, kind="ExternalInput").ap()
    identf_in = nc.dram_tensor("identf", [128, 128], F32, kind="ExternalInput").ap()

    q_out = nc.dram_tensor("q", [TOK, D], F32, kind="ExternalOutput").ap()
    k_out = nc.dram_tensor("k", [TOK, D], BF16, kind="ExternalOutput").ap()
    v_out = nc.dram_tensor("v", [TOK, D], BF16, kind="ExternalOutput").ap()
    logf_out = nc.dram_tensor("logf", [TOK, H], F32, kind="ExternalOutput").ap()
    logb_out = nc.dram_tensor("logb", [TOK, H], F32, kind="ExternalOutput").ap()
    gate_out = nc.dram_tensor("gate", [TOK, D], F32, kind="ExternalOutput").ap()
    g_out = nc.dram_tensor("g", [TOK, D], F32, kind="ExternalOutput").ap()

    with ExitStack() as ctx:
        E = ctx.enter_context
        ident = E(nc.sbuf_tensor("identt", [128, 128], BF16))
        identf = E(nc.sbuf_tensor("identft", [128, 128], F32))
        xn_col = E(nc.sbuf_tensor("xn_col", [128, 8 * TOK], BF16))
        xq_col = E(nc.sbuf_tensor("xq_col", [128, 8 * TOK], BF16))
        scr = mk_quant_scratch(nc, ctx, 128, 1024, "qs")
        eps_b = E(nc.sbuf_tensor("eps_b", [128, 1], F32))
        epsln = E(nc.sbuf_tensor("epsln", [128, 1], F32))
        one_b = E(nc.sbuf_tensor("one_b", [128, 1], F32))
        ones_row = E(nc.sbuf_tensor("ones_row", [1, 128], F32))
        delta_t = E(nc.sbuf_tensor("delta_t", [128, H], F32))
        ld = E(nc.semaphore("ld"))
        ldg = E(nc.semaphore("ldg"))
        cmp = E(nc.semaphore("cmp"))
        pe = E(nc.semaphore("pe"))
        ev = E(nc.semaphore("ev"))
        st = E(nc.semaphore("st"))
        c_ld, c_ldg, c_cmp, c_pe, c_ev, c_st = Ctr(), Ctr(), Ctr(), Ctr(), Ctr(), Ctr()
        block = E(nc.Block())

        with ExitStack() as ctxA:
            EA = ctxA.enter_context
            xt = EA(nc.sbuf_tensor("xt", [128, 1024], F32))
            xqt = EA(nc.sbuf_tensor("xqt", [128, 1024], BF16))
            wqkv = EA(nc.sbuf_tensor("wqkv", [128, 3 * 8 * 1024], BF16))
            wcat = EA(nc.sbuf_tensor("wcat", [128, 8 * 96], BF16))
            gw2b = EA(nc.sbuf_tensor("gw2b", [64, 1024], BF16))
            stg = EA(nc.sbuf_tensor("stg", [128, 1024], F32))
            stg2 = EA(nc.sbuf_tensor("stg2", [128, 1024], F32))
            sml = EA(nc.sbuf_tensor("sml", [128, 128], F32))
            smlb = EA(nc.sbuf_tensor("smlb", [128, 128], BF16))
            xg1c = EA(nc.sbuf_tensor("xg1c", [64, TOK], BF16))
            ps = [EA(nc.psum_tensor(f"pA{b}", [128, 512], F32)) for b in range(4)]

            @block.sync
            def _(sync):
                v, t, sc, g = nc.vector, nc.tensor, nc.scalar, nc.gpsimd

                def dma(out, in_, q=sync):
                    q.dma_start(out=out, in_=in_).then_inc(ld, 16)
                    return c_ld.inc(16)

                def dmac(out, in_):
                    g.dma_start(out=out, in_=in_).then_inc(ldg, 16)
                    return c_ldg.inc(16)

                dma(ident[:], ident_in)
                dma(identf[:], identf_in)
                dma(delta_t[0:1, :], delta_in)
                for w_i, (wsrc) in enumerate((Wq_in, Wk_in, Wv_in)):
                    for kc in range(8):
                        dmac(wqkv[:, (w_i * 8 + kc) * 1024:(w_i * 8 + kc) * 1024 + 1024],
                             wsrc[kc * 128:(kc + 1) * 128, :])
                for kc in range(8):
                    dmac(wcat[:, kc * 96:kc * 96 + 16], Wf_in[kc * 128:(kc + 1) * 128, :])
                    dmac(wcat[:, kc * 96 + 16:kc * 96 + 32], Wb_in[kc * 128:(kc + 1) * 128, :])
                    dmac(wcat[:, kc * 96 + 32:kc * 96 + 96], gw1_in[kc * 128:(kc + 1) * 128, :])
                v.memset(eps_b[:], 1e-6)
                v.memset(epsln[:], 1e-5)
                v.memset(one_b[:], 1.0)
                v.memset(ones_row[:], 1.0)
                v.sem_inc(cmp, 1)
                c_cmp.inc()
                dmac(gw2b[:, :], gw2_in)
                wld = c_ldg.v
                t.wait_ge(ld, 48)
                t.wait_ge(cmp, c_cmp.v)
                t.matmul(out=ps[0][0:128, 0:H], lhsT=ones_row[0:1, 0:128],
                         rhs=delta_t[0:1, :], start=True, stop=True).then_inc(pe, 1)
                c_pe.inc()
                v.wait_ge(pe, c_pe.v)
                v.tensor_copy(out=delta_t[:, :], in_=ps[0][0:128, 0:H]).then_inc(cmp, 1)
                c_cmp.inc()

                for i in range(NT):
                    ti = i * 1024
                    if i >= 1:
                        sync.wait_ge(pe, c_pe.v)
                    lv = dma(xt[:], x_in[i * 128:(i + 1) * 128, :])
                    v.wait_ge(ld, lv)
                    v.wait_ge(st, c_st.v)
                    sc.wait_ge(st, c_st.v)
                    if i == 0:
                        v.wait_ge(ldg, wld)
                        t.wait_ge(ldg, wld)
                    am = scr["amax"]
                    v.tensor_reduce(out=am[:, 0:1], in_=xt[:], axis=AX.X, op=OP.add).then_inc(cmp, 1)
                    v.wait_ge(cmp, c_cmp.inc())
                    v.tensor_scalar(out=am[:, 1:2], in0=am[:, 0:1],
                                    scalar1=1.0 / 1024.0, scalar2=None, op0=OP.mult).then_inc(cmp, 1)
                    v.wait_ge(cmp, c_cmp.inc())
                    v.tensor_tensor(out=scr["t2"][:, 0:1024], in0=xt[:],
                                    in1=am[:, 1:2].broadcast_to([128, 1024]),
                                    op=OP.subtract)
                    v.scalar_tensor_tensor(out=scr["u"][:, 0:1024], in0=scr["t2"][:, 0:1024],
                                           scalar=1.0, in1=scr["t2"][:, 0:1024],
                                           op0=OP.mult, op1=OP.mult,
                                           accum_out=am[:, 2:3]).then_inc(cmp, 1)
                    v.wait_ge(cmp, c_cmp.inc())
                    v.tensor_scalar(out=am[:, 3:4], in0=am[:, 2:3],
                                    scalar1=1.0 / 1024.0, scalar2=None, op0=OP.mult).then_inc(cmp, 1)
                    c_cmp.inc()
                    sc.wait_ge(cmp, c_cmp.v)
                    sc.activation(out=am[:, 4:5], in_=am[:, 3:4],
                                  func=AF.Sqrt, bias=epsln[:], scale=1.0).then_inc(cmp, 1)
                    c_cmp.inc()
                    v.wait_ge(cmp, c_cmp.v)
                    v.reciprocal(out=am[:, 5:6], in_=am[:, 4:5]).then_inc(cmp, 1)
                    v.wait_ge(cmp, c_cmp.inc())
                    v.tensor_tensor(out=xt[:], in0=scr["t2"][:, 0:1024],
                                    in1=am[:, 5:6].broadcast_to([128, 1024]),
                                    op=OP.mult).then_inc(cmp, 1)
                    c_cmp.inc()

                    emit_fp4_quant(nc, xt, xqt, scr, 128, 1024, cmp, c_cmp, done_sem=cmp)
                    c_cmp.inc()

                    t.wait_ge(cmp, c_cmp.v)
                    for cch in range(8):
                        t.matmul(out=ps[2][0:128, 0:128], lhsT=xt[:, cch * 128:(cch + 1) * 128],
                                 rhs=identf[:], start=True, stop=True).then_inc(pe, 1)
                        c_pe.inc()
                        sc.wait_ge(pe, c_pe.v)
                        sc.copy(out=xn_col[:, cch * TOK + i * 128: cch * TOK + (i + 1) * 128],
                                in_=ps[2][0:128, 0:128]).then_inc(ev, 1)
                        c_ev.inc()
                        t.wait_ge(ev, c_ev.v)
                        t.matmul(out=ps[3][0:128, 0:128], lhsT=xqt[:, cch * 128:(cch + 1) * 128],
                                 rhs=ident[:], start=True, stop=True).then_inc(pe, 1)
                        c_pe.inc()
                        sc.wait_ge(pe, c_pe.v)
                        sc.copy(out=xq_col[:, cch * TOK + i * 128: cch * TOK + (i + 1) * 128],
                                in_=ps[3][0:128, 0:128]).then_inc(ev, 1)
                        c_ev.inc()
                        t.wait_ge(ev, c_ev.v)

                    for w_i, (dst, is_q, is_norm) in enumerate(((q_out, True, True),
                                                                (k_out, False, True),
                                                                (v_out, False, False))):
                        t.wait_ge(cmp, c_cmp.v)
                        for half in range(2):
                            for kc in range(8):
                                t.matmul(out=ps[half][0:128, 0:512],
                                         lhsT=xn_col[:, kc * TOK + i * 128: kc * TOK + (i + 1) * 128],
                                         rhs=wqkv[:, (w_i * 8 + kc) * 1024 + half * 512:
                                                  (w_i * 8 + kc) * 1024 + half * 512 + 512],
                                         start=(kc == 0), stop=(kc == 7)).then_inc(pe, 1)
                                c_pe.inc()
                            sc.wait_ge(pe, c_pe.v)
                            sc.copy(out=stg[:, half * 512:half * 512 + 512],
                                    in_=ps[half][0:128, 0:512]).then_inc(cmp, 1)
                            c_cmp.inc()
                        v.wait_ge(cmp, c_cmp.v)
                        if is_norm:
                            v.scalar_tensor_tensor(out=stg2[:, 0:1024], in0=stg[:, 0:1024],
                                                   scalar=1.0, in1=stg[:, 0:1024],
                                                   op0=OP.mult, op1=OP.mult)
                            v.tensor_reduce(out=sml[:, 0:H],
                                            in_=stg2[:, 0:1024].rearrange("p (h d) -> p h d", d=64),
                                            axis=AX.X, op=OP.add).then_inc(cmp, 1)
                            v.wait_ge(cmp, c_cmp.inc())
                            v.sem_inc(cmp, 1)
                            c_cmp.inc()
                            sc.wait_ge(cmp, c_cmp.v)
                            sc.activation(out=sml[:, H:2 * H], in_=sml[:, 0:H],
                                          func=AF.Sqrt, bias=eps_b[:], scale=1.0).then_inc(cmp, 1)
                            c_cmp.inc()
                            v.wait_ge(cmp, c_cmp.v)
                            v.reciprocal(out=sml[:, 2 * H:3 * H], in_=sml[:, H:2 * H]).then_inc(cmp, 1)
                            v.wait_ge(cmp, c_cmp.inc())
                            rb = sml[:, 2 * H:3 * H].unsqueeze(2).broadcast_to([128, H, 64])
                            v.tensor_tensor(out=stg2[:, 0:1024].rearrange("p (h d) -> p h d", d=64),
                                            in0=stg[:, 0:1024].rearrange("p (h d) -> p h d", d=64),
                                            in1=rb, op=OP.mult).then_inc(cmp, 1)
                            c_cmp.inc()
                            src_t = stg2
                        else:
                            src_t = stg
                        if is_q:
                            g.wait_ge(cmp, c_cmp.v)
                            g.dma_start(out=dst[i * 128:(i + 1) * 128, :],
                                        in_=src_t[:, 0:1024]).then_inc(st, 16)
                            c_st.inc(16)
                        else:
                            g.wait_ge(cmp, c_cmp.v)
                            g.dma_start(out=dst[i * 128:(i + 1) * 128, :],
                                        in_=src_t[:, 0:1024]).then_inc(st, 16)
                            c_st.inc(16)
                        sc.wait_ge(st, c_st.v)
                        v.wait_ge(st, c_st.v)

                    for kc in range(8):
                        t.matmul(out=ps[2][0:128, 0:96],
                                 lhsT=xn_col[:, kc * TOK + i * 128: kc * TOK + (i + 1) * 128],
                                 rhs=wcat[:, kc * 96:(kc + 1) * 96],
                                 start=(kc == 0), stop=(kc == 7)).then_inc(pe, 1)
                        c_pe.inc()
                    sc.wait_ge(pe, c_pe.v)
                    sc.copy(out=sml[:, 0:96], in_=ps[2][0:128, 0:96]).then_inc(cmp, 1)
                    c_cmp.inc()
                    v.wait_ge(cmp, c_cmp.v)
                    v.tensor_tensor(out=sml[:, 96:96 + H], in0=sml[:, 0:H], in1=delta_t[:, 0:H],
                                    op=OP.add).then_inc(cmp, 1)
                    v.wait_ge(cmp, c_cmp.inc())
                    v.sem_inc(cmp, 1)
                    c_cmp.inc()
                    for (c0, dsttens) in ((96, logf_out), (16, logb_out)):
                        sc.wait_ge(cmp, c_cmp.v)
                        sc.activation(out=sml[:, 112:112 + H], in_=sml[:, c0:c0 + H],
                                      func=AF.Abs).then_inc(cmp, 1)
                        sc.wait_ge(cmp, c_cmp.inc())
                        sc.activation(out=sml[:, 112:112 + H], in_=sml[:, 112:112 + H],
                                      func=AF.Exp, scale=-1.0).then_inc(cmp, 1)
                        sc.wait_ge(cmp, c_cmp.inc())
                        sc.activation(out=sml[:, 112:112 + H], in_=sml[:, 112:112 + H],
                                      func=AF.Ln, bias=one_b[:], scale=1.0).then_inc(cmp, 1)
                        c_cmp.inc()
                        v.wait_ge(cmp, c_cmp.v)
                        v.wait_ge(st, c_st.v)
                        v.tensor_scalar(out=sml[:, 0:H], in0=sml[:, c0:c0 + H],
                                        scalar1=-1.0, scalar2=0.0, op0=OP.mult, op1=OP.max).then_inc(cmp, 1)
                        v.wait_ge(cmp, c_cmp.inc())
                        v.scalar_tensor_tensor(out=sml[:, 0:H], in0=sml[:, 0:H], scalar=-1.0,
                                               in1=sml[:, 112:112 + H], op0=OP.bypass, op1=OP.add).then_inc(cmp, 1)
                        v.wait_ge(cmp, c_cmp.inc())
                        v.tensor_scalar(out=sml[:, 0:H], in0=sml[:, 0:H], scalar1=-1.0,
                                        scalar2=None, op0=OP.mult).then_inc(cmp, 1)
                        c_cmp.inc()
                        g.wait_ge(cmp, c_cmp.v)
                        g.dma_start(out=dsttens[i * 128:(i + 1) * 128, :],
                                    in_=sml[:, 0:H]).then_inc(st, 16)
                        c_st.inc(16)
                    v.wait_ge(cmp, c_cmp.v)
                    v.tensor_copy(out=smlb[:, 0:64], in_=sml[:, 32:96]).then_inc(cmp, 1)
                    c_cmp.inc()
                    t.wait_ge(cmp, c_cmp.v)
                    t.matmul(out=ps[3][0:64, 0:128], lhsT=smlb[0:128, 0:64], rhs=ident[:],
                             start=True, stop=True).then_inc(pe, 1)
                    c_pe.inc()
                    sc.wait_ge(pe, c_pe.v)
                    sc.copy(out=xg1c[0:64, i * 128:(i + 1) * 128],
                            in_=ps[3][0:64, 0:128]).then_inc(ev, 1)
                    c_ev.inc()
                    t.wait_ge(ev, c_ev.v)

                    sc.wait_ge(st, c_st.v)
                    t.wait_ge(cmp, c_cmp.v)
                    for half in range(2):
                        t.matmul(out=ps[half][0:128, 0:512],
                                 lhsT=xg1c[0:64, i * 128:(i + 1) * 128],
                                 rhs=gw2b[0:64, half * 512:half * 512 + 512],
                                 start=True, stop=True).then_inc(pe, 1)
                        c_pe.inc()
                        sc.wait_ge(pe, c_pe.v)
                        sc.activation(out=stg[:, half * 512:half * 512 + 512],
                                      in_=ps[half][0:128, 0:512],
                                      func=AF.Sigmoid).then_inc(cmp, 1)
                        c_cmp.inc()
                    g.wait_ge(cmp, c_cmp.v)
                    g.dma_start(out=gate_out[i * 128:(i + 1) * 128, :],
                                in_=stg[:, 0:1024]).then_inc(st, 16)
                    c_st.inc(16)
                    t.wait_ge(st, c_st.v)

        with ExitStack() as ctxB:
            EB = ctxB.enter_context
            g_acc = EB(nc.sbuf_tensor("g_acc", [128, NT * 1024], F32))
            wgu = EB(nc.sbuf_tensor("wgu", [128, 8 * 1024], BF16))
            wdc = EB(nc.sbuf_tensor("wdc", [128, 4 * 1024], BF16))
            hq_t = EB(nc.sbuf_tensor("hq_t", [128, 512], BF16))
            hf_t = EB(nc.sbuf_tensor("hf_t", [128, 512], F32))
            hcol = EB(nc.sbuf_tensor("hcol", [128, 512], BF16))
            sil = EB(nc.sbuf_tensor("sil", [128, 512], F32))
            psB = [EB(nc.psum_tensor(f"pB{b}", [128, 512], F32)) for b in range(6)]

            @block.sync
            def _(sync):
                v, t, sc, g = nc.vector, nc.tensor, nc.scalar, nc.gpsimd
                sync.wait_ge(st, c_st.v)
                sync.wait_ge(pe, c_pe.v)
                v.wait_ge(st, c_st.v)
                v.wait_ge(pe, c_pe.v)
                for i in range(NT):
                    v.memset(g_acc[:, i * 1024:(i + 1) * 1024], 0.0)
                v.sem_inc(cmp, 1)
                c_cmp.inc()
                zdone = c_cmp.v
                nchunks = [(n * 512, 512) for n in range(5)] + [(2560, 256)]
                for nci, (n0, nw) in enumerate(nchunks):
                    if nci >= 1:
                        sync.wait_ge(pe, c_pe.v)
                    lv0 = c_ld.v
                    for kc in range(8):
                        sync.dma_start(out=wgu[:, kc * 1024:kc * 1024 + nw],
                                       in_=wgqT_in[kc * 128:(kc + 1) * 128, n0:n0 + nw]).then_inc(ld, 16)
                        c_ld.inc(16)
                        sync.dma_start(out=wgu[:, kc * 1024 + 512:kc * 1024 + 512 + nw],
                                       in_=wuqT_in[kc * 128:(kc + 1) * 128, n0:n0 + nw]).then_inc(ld, 16)
                        c_ld.inc(16)
                    nsl = nw // 128
                    for sl in range(nsl):
                        sync.dma_start(out=wdc[:, sl * 1024:(sl + 1) * 1024],
                                       in_=wdqT_in[n0 + sl * 128:n0 + (sl + 1) * 128, :]).then_inc(ld, 16)
                        c_ld.inc(16)
                    t.wait_ge(ld, c_ld.v)
                    if nci == 0:
                        v.wait_ge(cmp, zdone)
                    for i in range(NT):
                        v.wait_ge(pe, c_pe.v)
                        for kc in range(8):
                            t.matmul(out=psB[0][0:128, 0:nw],
                                     lhsT=xq_col[:, kc * TOK + i * 128: kc * TOK + (i + 1) * 128],
                                     rhs=wgu[:, kc * 1024:kc * 1024 + nw],
                                     start=(kc == 0), stop=(kc == 7)).then_inc(pe, 1)
                            c_pe.inc()
                        for kc in range(8):
                            t.matmul(out=psB[1][0:128, 0:nw],
                                     lhsT=xq_col[:, kc * TOK + i * 128: kc * TOK + (i + 1) * 128],
                                     rhs=wgu[:, kc * 1024 + 512:kc * 1024 + 512 + nw],
                                     start=(kc == 0), stop=(kc == 7)).then_inc(pe, 1)
                            c_pe.inc()
                        sc.wait_ge(pe, c_pe.v)
                        sc.activation(out=sil[:, 0:nw], in_=psB[0][0:128, 0:nw],
                                      func=AF.Silu).then_inc(cmp, 1)
                        c_cmp.inc()
                        v.wait_ge(cmp, c_cmp.v)
                        v.tensor_tensor(out=hf_t[:, 0:nw], in0=sil[:, 0:nw],
                                        in1=psB[1][0:128, 0:nw], op=OP.mult).then_inc(cmp, 1)
                        c_cmp.inc()
                        emit_fp4_quant(nc, hf_t, hq_t, scr, 128, nw, cmp, c_cmp, done_sem=cmp)
                        c_cmp.inc()
                        t.wait_ge(cmp, c_cmp.v)
                        for sl in range(nsl):
                            t.matmul(out=psB[2][0:128, 0:128],
                                     lhsT=hq_t[:, sl * 128:(sl + 1) * 128],
                                     rhs=ident[:], start=True, stop=True).then_inc(pe, 1)
                            c_pe.inc()
                            sc.wait_ge(pe, c_pe.v)
                            sc.copy(out=hcol[:, 0:128], in_=psB[2][0:128, 0:128]).then_inc(ev, 1)
                            c_ev.inc()
                            t.wait_ge(ev, c_ev.v)
                            for half in range(2):
                                t.matmul(out=psB[3 + half][0:128, 0:512],
                                         lhsT=hcol[:, 0:128],
                                         rhs=wdc[:, sl * 1024 + half * 512:sl * 1024 + half * 512 + 512],
                                         start=(sl == 0), stop=(sl == nsl - 1)).then_inc(pe, 1)
                                c_pe.inc()
                        v.wait_ge(pe, c_pe.v)
                        for half in range(2):
                            v.tensor_tensor(out=g_acc[:, i * 1024 + half * 512:i * 1024 + half * 512 + 512],
                                            in0=g_acc[:, i * 1024 + half * 512:i * 1024 + half * 512 + 512],
                                            in1=psB[3 + half][0:128, 0:512], op=OP.add)
                        v.sem_inc(cmp, 1)
                        c_cmp.inc()
                        t.wait_ge(cmp, c_cmp.v)
                for i in range(NT):
                    sc.wait_ge(cmp, c_cmp.v)
                    sc.activation(out=g_acc[:, i * 1024:(i + 1) * 1024],
                                  in_=g_acc[:, i * 1024:(i + 1) * 1024],
                                  func=AF.Silu).then_inc(cmp, 1)
                    c_cmp.inc()
                    g.wait_ge(cmp, c_cmp.v)
                    g.dma_start(out=g_out[i * 128:(i + 1) * 128, :],
                                in_=g_acc[:, i * 1024:(i + 1) * 1024]).then_inc(st, 16)
                    c_st.inc(16)
                g.wait_ge(st, c_st.v)

    return nc


def run_a(inputs, a0):
    ident = np_identity_bf16()
    identf = np.eye(128, dtype=np.float32)
    x_full = inputs["L0_attn_x"].reshape(B * N, D)
    in_maps = []
    for c in range(NC):
        in_maps.append({
            "x": np.ascontiguousarray(x_full[c * TOK:(c + 1) * TOK]),
            "Wq": inputs["L0_attn_Wq"], "Wk": inputs["L0_attn_Wk"], "Wv": inputs["L0_attn_Wv"],
            "Wf": inputs["L0_attn_Wf"], "Wb": inputs["L0_attn_Wbet"],
            "gw1": inputs["L0_attn_gate_w1"], "gw2": inputs["L0_attn_gate_w2"],
            "delta": inputs["L0_attn_delta"].reshape(1, H),
            "wgqT": a0["wgqT"], "wuqT": a0["wuqT"], "wdqT": a0["wdqT"],
            "ident": ident, "identf": identf,
        })
    nc = build_a()
    res = run_bass_kernel_spmd(nc, in_maps, core_ids=list(range(NC)))
    out = {}
    for key in ("q", "k", "v", "logf", "logb", "gate", "g"):
        out[key] = np.concatenate([r[key] for r in res.results], axis=0)
    return out


NBH = 4
NCH = 128
CGIT = 30
GRP = 4
NGRP = 16


def build_b(NBH=NBH, NCH=NCH, NGRP=NGRP, CGIT=CGIT, DBG=False):
    nc = bass.Bass("TRN2", target_bir_lowering=False)
    NTOK = NCH * 64
    q_in = nc.dram_tensor("q", [NBH, NTOK, DH], F32, kind="ExternalInput").ap()
    k_in = nc.dram_tensor("k", [NBH, 64, NCH * 64], BF16, kind="ExternalInput").ap()
    v_in = nc.dram_tensor("v", [NBH, 64, NCH * 64], BF16, kind="ExternalInput").ap()
    lf_in = nc.dram_tensor("lf", [NBH, NTOK], F32, kind="ExternalInput").ap()
    lb_in = nc.dram_tensor("lb", [NBH, NTOK], F32, kind="ExternalInput").ap()
    lp_in = nc.dram_tensor("lp", [DH, NBH], F32, kind="ExternalInput").ap()
    ident_in = nc.dram_tensor("ident", [128, 128], BF16, kind="ExternalInput").ap()
    identf_in = nc.dram_tensor("identf", [128, 128], F32, kind="ExternalInput").ap()
    tri_in = nc.dram_tensor("tri", [128, 64], BF16, kind="ExternalInput").ap()
    id64_in = nc.dram_tensor("id64", [128, 64], BF16, kind="ExternalInput").ap()
    o_out = nc.dram_tensor("o", [NBH, NTOK, DH], F32, kind="ExternalOutput").ap()
    cumscr = nc.dram_tensor("cumscr", [NCH, 64], F32, kind="Internal").ap()
    if DBG:
        dbg_ab = nc.dram_tensor("dbg_ab", [64, 128], F32, kind="ExternalOutput").ap()
        dbg_cum = nc.dram_tensor("dbg_cum", [NCH, 64], F32, kind="ExternalOutput").ap()
        dbg_hall = nc.dram_tensor("dbg_hall", [64, NCH * 128], F32, kind="ExternalOutput").ap()
        dbg_u = nc.dram_tensor("dbg_u", [64, NCH * 128], F32, kind="ExternalOutput").ap()
        dbg_ka = nc.dram_tensor("dbg_ka", [64, NCH * 64], F32, kind="ExternalOutput").ap()
        dbg_va = nc.dram_tensor("dbg_va", [64, NCH * 64], F32, kind="ExternalOutput").ap()
        dbg_m = nc.dram_tensor("dbg_m", [64, NCH * 64], F32, kind="ExternalOutput").ap()
        dbg_kc = nc.dram_tensor("dbg_kc", [64, NCH * 64], F32, kind="ExternalOutput").ap()

    with ExitStack() as ctx:
        E = ctx.enter_context
        ident = E(nc.sbuf_tensor("identt", [128, 128], BF16))
        identf = E(nc.sbuf_tensor("identft", [128, 128], F32))
        tri = E(nc.sbuf_tensor("trit", [128, 64], BF16))
        id64 = E(nc.sbuf_tensor("id64t", [128, 64], BF16))
        ones_row = E(nc.sbuf_tensor("ones_row", [1, 128], F32))
        kaug = E(nc.sbuf_tensor("kaug", [128, NCH * 64], BF16))
        vaug = E(nc.sbuf_tensor("vaug", [128, NCH * 64], BF16))
        kcol = E(nc.sbuf_tensor("kcol", [64, NCH * 64], BF16))
        mall = E(nc.sbuf_tensor("mall", [128, NCH * 64], BF16))
        uall = E(nc.sbuf_tensor("uall", [64, NCH * 128], BF16))
        hall = E(nc.sbuf_tensor("hall", [64, NCH * 128], F32))
        lf_pm = E(nc.sbuf_tensor("lf_pm", [NCH, 64], F32))
        lb_pm = E(nc.sbuf_tensor("lb_pm", [NCH, 64], F32))
        cum_pm = E(nc.sbuf_tensor("cum_pm", [NCH, 64], F32))
        w_pm = E(nc.sbuf_tensor("w_pm", [NCH, 64], F32))
        w_cm = E(nc.sbuf_tensor("w_cm", [128, 128], F32))
        zer = E(nc.sbuf_tensor("zer", [NCH, 64], F32))
        cum_lin = E(nc.sbuf_tensor("cum_lin", [1, NTOK], F32))
        ablin = E(nc.sbuf_tensor("ablin", [1, 128], F32))
        ab = E(nc.sbuf_tensor("ab", [64, 128], F32))
        khat = E(nc.sbuf_tensor("khat", [128, 64], BF16))
        zdiag = E(nc.sbuf_tensor("zdiag", [64, 64], BF16))
        lam_cm = E(nc.sbuf_tensor("lam_cm", [64, 16], F32))
        diag_l = E(nc.sbuf_tensor("diag_l", [64, NBH * 64], BF16))
        Xs = [E(nc.sbuf_tensor(f"X{b}", [128, GRP * 64], F32)) for b in range(2)]
        Rs = [E(nc.sbuf_tensor(f"R{b}", [128, GRP * 64], F32)) for b in range(2)]
        Ps = [E(nc.sbuf_tensor(f"P{b}", [128, GRP * 64], F32)) for b in range(2)]
        pb16 = E(nc.sbuf_tensor("pb16", [128, GRP * 64], BF16))
        pcol = E(nc.sbuf_tensor("pcol", [64, GRP * 128], BF16))
        daug = E(nc.sbuf_tensor("daug", [128, GRP * 128], BF16))
        vball = E(nc.sbuf_tensor("vball", [64, GRP * 128], BF16))
        sm = E(nc.sbuf_tensor("smB", [128, 64], F32))
        o_sb = E(nc.sbuf_tensor("o_sb", [128, GRP * 64], F32))
        ps_T = E(nc.psum_tensor("ps_T", [128, 512], F32))
        ps_C = E(nc.psum_tensor("ps_C", [128, 512], F32))
        ps_AP = E(nc.psum_tensor("ps_AP", [128, 512], F32))
        ps_M = E(nc.psum_tensor("ps_M", [128, 512], F32))
        ps_U = E(nc.psum_tensor("ps_U", [128, 512], F32))
        ps_W = E(nc.psum_tensor("ps_W", [128, 512], F32))
        ld = E(nc.semaphore("ld"))
        ldg = E(nc.semaphore("ldg"))
        cmp = E(nc.semaphore("cmp"))
        pe = E(nc.semaphore("pe"))
        st = E(nc.semaphore("st"))
        c_ld, c_ldg, c_cmp, c_pe, c_st = Ctr(), Ctr(), Ctr(), Ctr(), Ctr()
        block = E(nc.Block())

        @block.sync
        def _(sync):
            v, t, sc, g = nc.vector, nc.tensor, nc.scalar, nc.gpsimd

            def vw():
                v.wait_ge(cmp, c_cmp.v)

            sync.dma_start(out=ident[:], in_=ident_in).then_inc(ld, 16)
            sync.dma_start(out=identf[:], in_=identf_in).then_inc(ld, 16)
            sync.dma_start(out=tri[:], in_=tri_in).then_inc(ld, 16)
            sync.dma_start(out=id64[:], in_=id64_in).then_inc(ld, 16)
            sync.dma_start(out=lam_cm[:, 0:NBH], in_=lp_in).then_inc(ld, 16)
            c_ld.inc(80)
            v.wait_ge(ld, c_ld.v)
            v.memset(ones_row[:], 1.0)
            v.memset(zer[:], 0.0)
            v.memset(zdiag[:], 0.0)
            v.memset(sm[:], 0.0)
            v.sem_inc(cmp, 1); c_cmp.inc()
            sc.wait_ge(ld, c_ld.v)
            sc.activation(out=lam_cm[:, 4:4 + NBH], in_=lam_cm[:, 0:NBH], func=AF.Abs).then_inc(cmp, 1)
            sc.wait_ge(cmp, c_cmp.inc())
            sc.activation(out=lam_cm[:, 8:8 + NBH], in_=lam_cm[:, 4:4 + NBH], func=AF.Exp, scale=-1.0).then_inc(cmp, 1)
            sc.wait_ge(cmp, c_cmp.inc())
            v.memset(sm[:, 63:64], 1.0)
            v.sem_inc(cmp, 1); c_cmp.inc()
            sc.wait_ge(cmp, c_cmp.v)
            sc.activation(out=lam_cm[:, 8:8 + NBH], in_=lam_cm[:, 8:8 + NBH], func=AF.Ln,
                          bias=sm[0:64, 63:64], scale=1.0).then_inc(cmp, 1)
            c_cmp.inc()
            vw()
            v.tensor_scalar(out=lam_cm[:, 12:12 + NBH], in0=lam_cm[:, 0:NBH], scalar1=0.0,
                            scalar2=None, op0=OP.max).then_inc(cmp, 1)
            v.wait_ge(cmp, c_cmp.inc())
            v.tensor_tensor(out=lam_cm[:, 12:12 + NBH], in0=lam_cm[:, 12:12 + NBH],
                            in1=lam_cm[:, 8:8 + NBH], op=OP.add).then_inc(cmp, 1)
            v.wait_ge(cmp, c_cmp.inc())
            v.tensor_scalar(out=lam_cm[:, 12:12 + NBH], in0=lam_cm[:, 12:12 + NBH],
                            scalar1=0.25, scalar2=None, op0=OP.add).then_inc(cmp, 1)
            v.wait_ge(cmp, c_cmp.inc())
            for h in range(NBH):
                v.tensor_scalar(out=diag_l[:, h * 64:(h + 1) * 64],
                                in0=identf[0:64, 0:64],
                                scalar1=lam_cm[:, 12 + h:13 + h], scalar2=None,
                                op0=OP.mult).then_inc(cmp, 1)
                c_cmp.inc()

            for bh in range(NBH):
                base_pe0 = c_pe.v
                lv0 = c_ld.v
                sync.wait_ge(st, c_st.v)
                sync.wait_ge(pe, c_pe.v)
                sync.dma_start(out=kaug[64:128, :], in_=k_in[bh]).then_inc(ld, 16)
                sync.dma_start(out=vaug[64:128, :], in_=v_in[bh]).then_inc(ld, 16)
                sync.dma_start(out=lf_pm[:, :],
                               in_=lf_in[bh].rearrange("(c t) -> c t", t=64)).then_inc(ld, 16)
                sync.dma_start(out=lb_pm[:, :],
                               in_=lb_in[bh].rearrange("(c t) -> c t", t=64)).then_inc(ld, 16)
                c_ld.inc(64)
                v.wait_ge(ld, c_ld.v)
                v.tensor_tensor_scan(out=cum_pm[:], data0=lf_pm[:], data1=zer[:],
                                     initial=0.0, op0=OP.add, op1=OP.add).then_inc(cmp, 1)
                v.wait_ge(cmp, c_cmp.inc())
                v.tensor_tensor(out=w_pm[:], in0=lb_pm[:], in1=cum_pm[:],
                                op=OP.subtract).then_inc(cmp, 1)
                c_cmp.inc()
                t.wait_ge(cmp, c_cmp.v)
                t.matmul(out=ps_W[64:128, 0:NCH], lhsT=w_pm[:, 0:64], rhs=identf[0:NCH, 0:NCH],
                         start=True, stop=True, tile_position=(0, 64)).then_inc(pe, 1)
                c_pe.inc()
                sc.wait_ge(pe, c_pe.v)
                sc.copy(out=w_cm[64:128, 0:NCH], in_=ps_W[64:128, 0:NCH]).then_inc(cmp, 1)
                c_cmp.inc()
                g.wait_ge(cmp, c_cmp.v)
                g.dma_start(out=cumscr, in_=cum_pm[:]).then_inc(ldg, 16)
                g.wait_ge(ldg, c_ldg.inc(16))
                g.dma_start(out=cum_lin[0:1, :],
                            in_=cumscr.rearrange("c t -> (c t)").unsqueeze(0)).then_inc(ldg, 16)
                c_ldg.inc(16)
                g.dma_start(out=ablin[0:1, 0:NCH],
                            in_=cumscr[:, 63:64].rearrange("c o -> (c o)").unsqueeze(0),
                            allow_slow_non_contiguous=True).then_inc(ldg, 16)
                c_ldg.inc(16)
                t.wait_ge(ldg, c_ldg.v)
                t.matmul(out=ps_W[0:64, 0:NCH], lhsT=ones_row[0:1, 0:64],
                         rhs=ablin[0:1, 0:NCH],
                         start=True, stop=True).then_inc(pe, 1)
                c_pe.inc()
                sc.wait_ge(pe, c_pe.v)
                sc.activation(out=ab[:, 0:NCH], in_=ps_W[0:64, 0:NCH], func=AF.Exp).then_inc(cmp, 1)
                c_cmp.inc()
                for mb in range(NCH // 8):
                    t.wait_ge(cmp, c_cmp.v)
                    for j in range(8):
                        ch = mb * 8 + j
                        t.matmul(out=ps_M[64:128, j * 64:(j + 1) * 64],
                                 lhsT=ones_row[0:1, 0:64],
                                 rhs=cum_lin[0:1, ch * 64:(ch + 1) * 64],
                                 start=(j == 0), stop=(j == 7), tile_position=(0, 64)).then_inc(pe, 1)
                        c_pe.inc()
                    sc.wait_ge(pe, c_pe.v)
                    for j in range(8):
                        ch = mb * 8 + j
                        sc.activation(out=mall[64:128, ch * 64:(ch + 1) * 64],
                                      in_=ps_M[64:128, j * 64:(j + 1) * 64], func=AF.Exp,
                                      bias=w_cm[64:128, ch:ch + 1], scale=1.0).then_inc(cmp, 1)
                        c_cmp.inc()
                    v.wait_ge(cmp, c_cmp.v)
                    trib = tri[64:128, :].unsqueeze(1).broadcast_to([64, 8, 64])
                    v.tensor_tensor(out=mall[64:128, mb * 512:(mb + 1) * 512].rearrange(
                                        "p (c t) -> p c t", t=64),
                                    in0=mall[64:128, mb * 512:(mb + 1) * 512].rearrange(
                                        "p (c t) -> p c t", t=64),
                                    in1=trib, op=OP.mult).then_inc(cmp, 1)
                    c_cmp.inc()
                for ch in range(NCH):
                    v.wait_ge(cmp, c_cmp.v)
                    v.wait_ge(pe, c_pe.v)
                    v.tensor_tensor(out=khat[64:128, :], in0=kaug[64:128, ch * 64:(ch + 1) * 64],
                                    in1=mall[64:128, ch * 64 + 63:ch * 64 + 64].broadcast_to([64, 64]),
                                    op=OP.mult).then_inc(cmp, 1)
                    c_cmp.inc()
                    t.wait_ge(cmp, c_cmp.v)
                    if ch >= 1:
                        pass
                    t.matmul(out=ps_U[0:64, 0:64], lhsT=khat[64:128, :],
                             rhs=kaug[64:128, ch * 64:(ch + 1) * 64],
                             start=True, stop=False, tile_position=(64, 0)).then_inc(pe, 1)
                    t.matmul(out=ps_U[0:64, 64:128], lhsT=khat[64:128, :],
                             rhs=vaug[64:128, ch * 64:(ch + 1) * 64],
                             start=False, stop=True, tile_position=(64, 0)).then_inc(pe, 1)
                    c_pe.inc(2)
                    sc.wait_ge(pe, c_pe.v)
                    sc.copy(out=uall[:, ch * 128:(ch + 1) * 128],
                            in_=ps_U[0:64, 0:128]).then_inc(cmp, 1)
                    c_cmp.inc()
                vw()
                v.tensor_copy(out=hall[:, 0:128], in_=uall[:, 0:128]).then_inc(cmp, 1)
                v.wait_ge(cmp, c_cmp.inc())
                for ch in range(1, NCH):
                    v.scalar_tensor_tensor(out=hall[:, ch * 128:ch * 128 + 64],
                                           in0=hall[:, (ch - 1) * 128:(ch - 1) * 128 + 64],
                                           scalar=ab[:, ch:ch + 1],
                                           in1=uall[:, ch * 128:ch * 128 + 64],
                                           op0=OP.mult, op1=OP.add).then_inc(cmp, 1)
                    v.wait_ge(cmp, c_cmp.inc())
                    v.scalar_tensor_tensor(out=hall[:, ch * 128 + 64:(ch + 1) * 128],
                                           in0=hall[:, (ch - 1) * 128 + 64:ch * 128],
                                           scalar=ab[:, ch:ch + 1],
                                           in1=uall[:, ch * 128 + 64:(ch + 1) * 128],
                                           op0=OP.mult, op1=OP.add).then_inc(cmp, 1)
                    v.wait_ge(cmp, c_cmp.inc())
                if DBG and bh == 0:
                    g.wait_ge(cmp, c_cmp.v)
                    g.dma_start(out=dbg_ab, in_=ab[:, :]).then_inc(st, 16)
                    g.dma_start(out=dbg_cum, in_=cum_pm[:, :]).then_inc(st, 16)
                    g.dma_start(out=dbg_hall, in_=hall[:, :]).then_inc(st, 16)
                    g.dma_start(out=dbg_u, in_=uall[:, :]).then_inc(st, 16)
                    c_st.inc(64)
                v.memset(kaug[0:64, 0:64], 0.0)
                v.memset(vaug[0:64, 0:64], 0.0)
                v.sem_inc(cmp, 1); c_cmp.inc()
                g.wait_ge(cmp, c_cmp.v)
                g.dma_start(out=kaug[0:64, 64:NCH * 64],
                            in_=hall[:, 0:(NCH - 1) * 128].rearrange(
                                "p (c z) -> p c z", z=128)[:, :, 0:64]).then_inc(ldg, 16)
                c_ldg.inc(16)
                g.dma_start(out=vaug[0:64, 64:NCH * 64],
                            in_=hall[:, 0:(NCH - 1) * 128].rearrange(
                                "p (c z) -> p c z", z=128)[:, :, 64:128]).then_inc(ldg, 16)
                c_ldg.inc(16)
                if DBG and bh == 0:
                    g.wait_ge(ldg, c_ldg.v)
                    g.dma_start(out=dbg_ka, in_=kaug[0:64, :]).then_inc(st, 16)
                    g.dma_start(out=dbg_va, in_=vaug[0:64, :]).then_inc(st, 16)
                    g.dma_start(out=dbg_m, in_=mall[64:128, :]).then_inc(st, 16)
                    c_st.inc(48)
                t.wait_ge(ldg, c_ldg.v)
                for kb in range(NCH // 8):
                    t.wait_ge(cmp, c_cmp.v)
                    for j in range(8):
                        ch = kb * 8 + j
                        t.matmul(out=ps_T[0:64, j * 64:(j + 1) * 64],
                                 lhsT=kaug[64:128, ch * 64:(ch + 1) * 64],
                                 rhs=id64[64:128, :], start=(j == 0), stop=(j == 7),
                                 tile_position=(64, 0)).then_inc(pe, 1)
                        c_pe.inc()
                    sc.wait_ge(pe, c_pe.v)
                    sc.copy(out=kcol[:, kb * 512:(kb + 1) * 512],
                            in_=ps_T[0:64, 0:512]).then_inc(cmp, 1)
                    c_cmp.inc()
                if DBG and bh == 0:
                    g.wait_ge(cmp, c_cmp.v)
                    g.dma_start(out=dbg_kc, in_=kcol[:, :]).then_inc(st, 16)
                    c_st.inc(16)
                for gi in range(NGRP):
                    g0 = gi * GRP
                    ch0 = g0 * 2
                    tok0 = ch0 * 64
                    t.wait_ge(cmp, c_cmp.v)
                    t.matmul(out=ps_T[0:64, 0:512], lhsT=ones_row[0:1, 0:64],
                             rhs=cum_lin[0:1, tok0:tok0 + 512],
                             start=True, stop=True).then_inc(pe, 1)
                    c_pe.inc()
                    sc.wait_ge(pe, c_pe.v)
                    sc.activation(out=vball[:, :], in_=ps_T[0:64, 0:512], func=AF.Exp).then_inc(cmp, 1)
                    c_cmp.inc()
                    sync.wait_ge(pe, c_pe.v)
                    sync.dma_start(out=Ps[0][:, :].rearrange("p (j d) -> p j d", d=64),
                                   in_=q_in[bh, tok0:tok0 + 512, :].rearrange(
                                       "(j p) d -> p j d", p=128)).then_inc(ld, 16)
                    c_ld.inc(16)
                    v.wait_ge(ld, c_ld.v)
                    v.tensor_copy(out=Rs[0][:, :], in_=Ps[0][:, :])
                    v.memset(Xs[0][:, :], 0.0)
                    for j in range(GRP):
                        inst = v.scalar_tensor_tensor(out=o_sb[:, j * 64:(j + 1) * 64],
                                               in0=Ps[0][:, j * 64:(j + 1) * 64], scalar=1.0,
                                               in1=Ps[0][:, j * 64:(j + 1) * 64],
                                               op0=OP.mult, op1=OP.mult,
                                               accum_out=sm[:, j:j + 1])
                    inst.then_inc(cmp, 1)
                    v.wait_ge(cmp, c_cmp.inc())
                    cur = 0
                    for it in range(CGIT):
                        Pc, Rc, Xc = Ps[cur], Rs[cur], Xs[cur]
                        Pn, Rn, Xn_ = Ps[1 - cur], Rs[1 - cur], Xs[1 - cur]
                        sc.wait_ge(cmp, c_cmp.v)
                        sc.copy(out=pb16[:, :], in_=Pc[:, :])
                        sc.activation(out=o_sb[:, 0:GRP * 64], in_=pb16[:, :], func=AF.Abs).then_inc(cmp, 1)
                        c_cmp.inc()
                        t.wait_ge(cmp, c_cmp.v)
                        for j in range(GRP):
                            t.matmul(out=ps_T[0:64, j * 128:(j + 1) * 128],
                                     lhsT=pb16[:, j * 64:(j + 1) * 64], rhs=ident[:],
                                     start=(j == 0), stop=(j == GRP - 1)).then_inc(pe, 1)
                            c_pe.inc()
                        sc.wait_ge(pe, c_pe.v)
                        sc.copy(out=pcol[:, :], in_=ps_T[0:64, 0:512])
                        sc.activation(out=daug[0:64, 0:GRP * 128], in_=pcol[:, :], func=AF.Abs).then_inc(cmp, 1)
                        c_cmp.inc()
                        v.wait_ge(cmp, c_cmp.v)
                        v.tensor_tensor(out=daug[0:64, :], in0=pcol[:, :], in1=vball[:, :],
                                        op=OP.mult).then_inc(cmp, 1)
                        v.wait_ge(cmp, c_cmp.inc())
                        v.tensor_reduce(out=sm[0:64, 34:35], in_=daug[0:64, :], axis=AX.X,
                                        op=OP.max).then_inc(cmp, 1)
                        c_cmp.inc()
                        t.wait_ge(cmp, c_cmp.v)
                        for cj in range(2 * GRP):
                            ch = ch0 + cj
                            t.matmul(out=ps_C[64:128, cj * 64:(cj + 1) * 64],
                                     lhsT=kcol[:, ch * 64:(ch + 1) * 64],
                                     rhs=pcol[:, cj * 64:(cj + 1) * 64],
                                     start=(cj == 0), stop=(cj == 2 * GRP - 1),
                                     tile_position=(0, 64)).then_inc(pe, 1)
                            c_pe.inc()
                        v.wait_ge(pe, c_pe.v)
                        v.tensor_tensor(out=daug[64:128, :], in0=ps_C[64:128, 0:512],
                                        in1=mall[64:128, tok0:tok0 + 512],
                                        op=OP.mult).then_inc(cmp, 1)
                        v.wait_ge(cmp, c_cmp.inc())
                        v.tensor_reduce(out=sm[64:128, 34:35], in_=daug[64:128, :], axis=AX.X,
                                        op=OP.max).then_inc(cmp, 1)
                        c_cmp.inc()
                        t.wait_ge(cmp, c_cmp.v)
                        t.matmul(out=ps_AP[0:128, 0:64],
                                 lhsT=pcol[:, 0:128],
                                 rhs=diag_l[:, bh * 64:(bh + 1) * 64],
                                 start=True, stop=False).then_inc(pe, 1)
                        c_pe.inc()
                        for cj in range(2 * GRP):
                            ch = ch0 + cj
                            pos = (0, 0) if cj % 2 == 0 else (0, 64)
                            t.matmul(out=ps_AP[(cj % 2) * 64:(cj % 2) * 64 + 64,
                                               (cj // 2) * 64:(cj // 2) * 64 + 64],
                                     lhsT=daug[:, cj * 64:(cj + 1) * 64],
                                     rhs=kaug[:, ch * 64:(ch + 1) * 64],
                                     start=False, stop=False,
                                     tile_position=pos).then_inc(pe, 1)
                            c_pe.inc()
                        for j in range(1, GRP):
                            t.matmul(out=ps_AP[0:128, j * 64:(j + 1) * 64],
                                     lhsT=pcol[:, j * 128:(j + 1) * 128],
                                     rhs=diag_l[:, bh * 64:(bh + 1) * 64],
                                     start=False, stop=(j == GRP - 1)).then_inc(pe, 1)
                            c_pe.inc()
                        v.wait_ge(pe, c_pe.v)
                        for j in range(GRP):
                            inst = v.scalar_tensor_tensor(out=o_sb[:, j * 64:(j + 1) * 64],
                                                   in0=ps_AP[0:128, j * 64:(j + 1) * 64],
                                                   scalar=1.0,
                                                   in1=Pc[:, j * 64:(j + 1) * 64],
                                                   op0=OP.mult, op1=OP.mult,
                                                   accum_out=sm[:, 4 + j:5 + j])
                        inst.then_inc(cmp, 1)
                        v.wait_ge(cmp, c_cmp.inc())
                        v.tensor_scalar(out=sm[:, 16:20], in0=sm[:, 4:8], scalar1=1e-12,
                                        scalar2=None, op0=OP.add).then_inc(cmp, 1)
                        v.wait_ge(cmp, c_cmp.inc())
                        v.reciprocal(out=sm[:, 16:20], in_=sm[:, 16:20]).then_inc(cmp, 1)
                        v.wait_ge(cmp, c_cmp.inc())
                        v.tensor_tensor(out=sm[:, 8:12], in0=sm[:, 0:4], in1=sm[:, 16:20],
                                        op=OP.mult).then_inc(cmp, 1)
                        v.wait_ge(cmp, c_cmp.inc())
                        v.tensor_scalar(out=sm[:, 20:24], in0=sm[:, 8:12], scalar1=-1.0,
                                        scalar2=None, op0=OP.mult).then_inc(cmp, 1)
                        v.wait_ge(cmp, c_cmp.inc())
                        for j in range(GRP):
                            v.scalar_tensor_tensor(out=Xn_[:, j * 64:(j + 1) * 64],
                                                   in0=Pc[:, j * 64:(j + 1) * 64],
                                                   scalar=sm[:, 8 + j:9 + j],
                                                   in1=Xc[:, j * 64:(j + 1) * 64],
                                                   op0=OP.mult, op1=OP.add)
                        for j in range(GRP):
                            inst = v.scalar_tensor_tensor(out=Rn[:, j * 64:(j + 1) * 64],
                                                   in0=ps_AP[0:128, j * 64:(j + 1) * 64],
                                                   scalar=sm[:, 20 + j:21 + j],
                                                   in1=Rc[:, j * 64:(j + 1) * 64],
                                                   op0=OP.mult, op1=OP.add)
                        inst.then_inc(cmp, 1)
                        v.wait_ge(cmp, c_cmp.inc())
                        for j in range(GRP):
                            inst = v.scalar_tensor_tensor(out=o_sb[:, j * 64:(j + 1) * 64],
                                                   in0=Rn[:, j * 64:(j + 1) * 64], scalar=1.0,
                                                   in1=Rn[:, j * 64:(j + 1) * 64],
                                                   op0=OP.mult, op1=OP.mult,
                                                   accum_out=sm[:, 24 + j:25 + j])
                        inst.then_inc(cmp, 1)
                        v.wait_ge(cmp, c_cmp.inc())
                        v.tensor_scalar(out=sm[:, 16:20], in0=sm[:, 0:4], scalar1=1e-12,
                                        scalar2=None, op0=OP.add).then_inc(cmp, 1)
                        v.wait_ge(cmp, c_cmp.inc())
                        v.reciprocal(out=sm[:, 16:20], in_=sm[:, 16:20]).then_inc(cmp, 1)
                        v.wait_ge(cmp, c_cmp.inc())
                        v.tensor_tensor(out=sm[:, 12:16], in0=sm[:, 24:28], in1=sm[:, 16:20],
                                        op=OP.mult).then_inc(cmp, 1)
                        v.wait_ge(cmp, c_cmp.inc())
                        for j in range(GRP):
                            v.scalar_tensor_tensor(out=Pn[:, j * 64:(j + 1) * 64],
                                                   in0=Pc[:, j * 64:(j + 1) * 64],
                                                   scalar=sm[:, 12 + j:13 + j],
                                                   in1=Rn[:, j * 64:(j + 1) * 64],
                                                   op0=OP.mult, op1=OP.add)
                        v.tensor_copy(out=sm[:, 0:4], in_=sm[:, 24:28]).then_inc(cmp, 1)
                        v.wait_ge(cmp, c_cmp.inc())
                        cur = 1 - cur
                    Xc = Xs[cur]
                    sc.wait_ge(cmp, c_cmp.v)
                    sc.copy(out=pb16[:, :], in_=Xc[:, :])
                    sc.activation(out=o_sb[:, 0:GRP * 64], in_=pb16[:, :], func=AF.Abs).then_inc(cmp, 1)
                    c_cmp.inc()
                    t.wait_ge(cmp, c_cmp.v)
                    for j in range(GRP):
                        t.matmul(out=ps_T[0:64, j * 128:(j + 1) * 128],
                                 lhsT=pb16[:, j * 64:(j + 1) * 64], rhs=ident[:],
                                 start=(j == 0), stop=(j == GRP - 1)).then_inc(pe, 1)
                        c_pe.inc()
                    sc.wait_ge(pe, c_pe.v)
                    sc.copy(out=pcol[:, :], in_=ps_T[0:64, 0:512])
                    sc.activation(out=daug[0:64, 0:GRP * 128], in_=pcol[:, :], func=AF.Abs).then_inc(cmp, 1)
                    c_cmp.inc()
                    v.wait_ge(cmp, c_cmp.v)
                    v.tensor_tensor(out=daug[0:64, :], in0=pcol[:, :], in1=vball[:, :],
                                    op=OP.mult).then_inc(cmp, 1)
                    v.wait_ge(cmp, c_cmp.inc())
                    v.tensor_reduce(out=sm[0:64, 34:35], in_=daug[0:64, :], axis=AX.X,
                                    op=OP.max).then_inc(cmp, 1)
                    c_cmp.inc()
                    t.wait_ge(cmp, c_cmp.v)
                    for cj in range(2 * GRP):
                        ch = ch0 + cj
                        t.matmul(out=ps_C[64:128, cj * 64:(cj + 1) * 64],
                                 lhsT=kcol[:, ch * 64:(ch + 1) * 64],
                                 rhs=pcol[:, cj * 64:(cj + 1) * 64],
                                 start=(cj == 0), stop=(cj == 2 * GRP - 1),
                                 tile_position=(0, 64)).then_inc(pe, 1)
                        c_pe.inc()
                    v.wait_ge(pe, c_pe.v)
                    v.tensor_tensor(out=daug[64:128, :], in0=ps_C[64:128, 0:512],
                                    in1=mall[64:128, tok0:tok0 + 512],
                                    op=OP.mult).then_inc(cmp, 1)
                    v.wait_ge(cmp, c_cmp.inc())
                    v.tensor_reduce(out=sm[64:128, 34:35], in_=daug[64:128, :], axis=AX.X,
                                    op=OP.max).then_inc(cmp, 1)
                    c_cmp.inc()
                    t.wait_ge(cmp, c_cmp.v)
                    t.matmul(out=ps_AP[0:128, 0:64], lhsT=pcol[:, 0:128],
                             rhs=zdiag[:], start=True, stop=False).then_inc(pe, 1)
                    c_pe.inc()
                    for cj in range(2 * GRP):
                        ch = ch0 + cj
                        pos = (0, 0) if cj % 2 == 0 else (0, 64)
                        t.matmul(out=ps_AP[(cj % 2) * 64:(cj % 2) * 64 + 64,
                                           (cj // 2) * 64:(cj // 2) * 64 + 64],
                                 lhsT=daug[:, cj * 64:(cj + 1) * 64],
                                 rhs=vaug[:, ch * 64:(ch + 1) * 64],
                                 start=False, stop=False,
                                 tile_position=pos).then_inc(pe, 1)
                        c_pe.inc()
                    for j in range(1, GRP):
                        t.matmul(out=ps_AP[0:128, j * 64:(j + 1) * 64],
                                 lhsT=pcol[:, j * 128:(j + 1) * 128],
                                 rhs=zdiag[:], start=False,
                                 stop=(j == GRP - 1)).then_inc(pe, 1)
                        c_pe.inc()
                    sc.wait_ge(pe, c_pe.v)
                    sc.copy(out=o_sb[:, :], in_=ps_AP[0:128, 0:256]).then_inc(cmp, 1)
                    c_cmp.inc()
                    g.wait_ge(cmp, c_cmp.v)
                    g.dma_start(out=o_out[bh, tok0:tok0 + 512, :].rearrange(
                                    "(j p) d -> p j d", p=128),
                                in_=o_sb[:, :].rearrange("p (j d) -> p j d", d=64)).then_inc(st, 16)
                    c_st.inc(16)
                    v.wait_ge(st, c_st.v)
                    sc.wait_ge(st, c_st.v)
            g.wait_ge(st, c_st.v)

    return nc


def run_b(inputs, a):
    qf = a["q"].reshape(B, N, H, DH)
    kf = a["k"].reshape(B, N, H, DH)
    vf = a["v"].reshape(B, N, H, DH)
    lff = a["logf"].reshape(B, N, H)
    lbf = a["logb"].reshape(B, N, H)
    lp = inputs["L0_attn_lamb_params"].reshape(H, DH)
    ident = np_identity_bf16()
    identf = np.eye(128, dtype=np.float32)
    tri = np.zeros((128, 64), dtype=BF)
    tri[64:128] = np.triu(np.ones((64, 64))).astype(BF)
    id64c = np.concatenate([np.eye(64), np.eye(64)], axis=0).astype(BF)
    in_maps = []
    for c in range(NC):
        b = c // 4
        hs = (c % 4) * 4
        kb = kf[b, :, hs:hs + 4].transpose(1, 0, 2)
        vb = vf[b, :, hs:hs + 4].transpose(1, 0, 2)
        kb = np.ascontiguousarray(kb.reshape(4, 128, 64, 64).transpose(0, 2, 1, 3).reshape(4, 64, 128 * 64))
        vb = np.ascontiguousarray(vb.reshape(4, 128, 64, 64).transpose(0, 2, 1, 3).reshape(4, 64, 128 * 64))
        in_maps.append({
            "q": np.ascontiguousarray(qf[b, :, hs:hs + 4].transpose(1, 0, 2)),
            "k": kb,
            "v": vb,
            "lf": np.ascontiguousarray(lff[b, :, hs:hs + 4].T),
            "lb": np.ascontiguousarray(lbf[b, :, hs:hs + 4].T),
            "lp": np.ascontiguousarray(lp[hs:hs + 4].T),
            "ident": ident, "identf": identf, "tri": tri, "id64": id64c,
        })
    nc = build_b()
    res = run_bass_kernel_spmd(nc, in_maps, core_ids=list(range(NC)))
    o = np.zeros((B, N, H, DH), np.float32)
    for c in range(NC):
        b = c // 4
        hs = (c % 4) * 4
        o[b, :, hs:hs + 4] = res.results[c]["o"].transpose(1, 0, 2)
    return o


def build_c():
    nc = bass.Bass("TRN2", target_bir_lowering=False)
    x_in = nc.dram_tensor("x", [TOK, D], F32, kind="ExternalInput").ap()
    o_in = nc.dram_tensor("o", [TOK, D], F32, kind="ExternalInput").ap()
    gate_in = nc.dram_tensor("gate", [TOK, D], F32, kind="ExternalInput").ap()
    g_in = nc.dram_tensor("g", [TOK, D], F32, kind="ExternalInput").ap()
    nw_in = nc.dram_tensor("nw", [1, D], F32, kind="ExternalInput").ap()
    Wo_in = nc.dram_tensor("Wo", [D, D], F32, kind="ExternalInput").ap()
    Wq1_in = nc.dram_tensor("Wq1", [D, D], F32, kind="ExternalInput").ap()
    Wdkv_in = nc.dram_tensor("Wdkv", [D, 256], F32, kind="ExternalInput").ap()
    Wuk_in = nc.dram_tensor("Wuk", [256, D], F32, kind="ExternalInput").ap()
    Wuv_in = nc.dram_tensor("Wuv", [256, D], F32, kind="ExternalInput").ap()
    lng_in = nc.dram_tensor("lng", [1, 256], F32, kind="ExternalInput").ap()
    lnb_in = nc.dram_tensor("lnb", [1, 256], F32, kind="ExternalInput").ap()
    cosb_in = nc.dram_tensor("cosb", [TOK, 32], F32, kind="ExternalInput").ap()
    sinb_in = nc.dram_tensor("sinb", [TOK, 32], F32, kind="ExternalInput").ap()
    identf_in = nc.dram_tensor("identf", [128, 128], F32, kind="ExternalInput").ap()
    x1_out = nc.dram_tensor("x1", [TOK, D], F32, kind="ExternalOutput").ap()
    xn1_out = nc.dram_tensor("xn1", [TOK, D], F32, kind="ExternalOutput").ap()
    q1_out = nc.dram_tensor("q1", [TOK, D], BF16, kind="ExternalOutput").ap()
    k1_out = nc.dram_tensor("k1", [TOK, D], BF16, kind="ExternalOutput").ap()
    v1_out = nc.dram_tensor("v1", [TOK, D], BF16, kind="ExternalOutput").ap()

    with ExitStack() as ctx:
        E = ctx.enter_context
        identf = E(nc.sbuf_tensor("identft", [128, 128], F32))
        ones_row = E(nc.sbuf_tensor("ones_row", [1, 128], F32))
        w_all = E(nc.sbuf_tensor("w_all", [128, (8 + 8 + 2) * 1024 + 2 * 2048], BF16))
        OFF_WO, OFF_WQ, OFF_DKV, OFF_UK, OFF_UV = 0, 8 * 1024, 16 * 1024, 16 * 1024 + 2 * 1024, 16 * 1024 + 4 * 1024
        nw_t = E(nc.sbuf_tensor("nw_t", [128, 1024], F32))
        lng_t = E(nc.sbuf_tensor("lng_t", [128, 256], F32))
        lnb_t = E(nc.sbuf_tensor("lnb_t", [128, 256], F32))
        xt = E(nc.sbuf_tensor("xtC", [128, 1024], F32))
        ot = E(nc.sbuf_tensor("otC", [128, 1024], F32))
        gt = E(nc.sbuf_tensor("gtC", [128, 1024], F32))
        st1 = E(nc.sbuf_tensor("st1C", [128, 1024], F32))
        st2 = E(nc.sbuf_tensor("st2C", [128, 1024], F32))
        stb = E(nc.sbuf_tensor("stbC", [128, 1024], BF16))
        colb = E(nc.sbuf_tensor("colbC", [128, 1024], BF16))
        lat = E(nc.sbuf_tensor("latC", [128, 256], F32))
        latc = E(nc.sbuf_tensor("latcC", [128, 256], BF16))
        cost = E(nc.sbuf_tensor("cosT", [128, 32], F32))
        sint = E(nc.sbuf_tensor("sinT", [128, 32], F32))
        sm = E(nc.sbuf_tensor("smC", [128, 80], F32))
        ps = [E(nc.psum_tensor(f"pC{b}", [128, 512], F32)) for b in range(4)]
        ld = E(nc.semaphore("ld"))
        ldg = E(nc.semaphore("ldg"))
        cmp = E(nc.semaphore("cmp"))
        pe = E(nc.semaphore("pe"))
        st = E(nc.semaphore("st"))
        c_ld, c_ldg, c_cmp, c_pe, c_st = Ctr(), Ctr(), Ctr(), Ctr(), Ctr()
        block = E(nc.Block())

        @block.sync
        def _(sync):
            v, t, sc, g = nc.vector, nc.tensor, nc.scalar, nc.gpsimd
            sync.dma_start(out=identf[:], in_=identf_in).then_inc(ld, 16)
            sync.dma_start(out=nw_t[0:1, :], in_=nw_in).then_inc(ld, 16)
            sync.dma_start(out=lng_t[0:1, :], in_=lng_in).then_inc(ld, 16)
            sync.dma_start(out=lnb_t[0:1, :], in_=lnb_in).then_inc(ld, 16)
            c_ld.inc(64)
            for kc in range(8):
                g.dma_start(out=w_all[:, OFF_WO + kc * 1024:OFF_WO + (kc + 1) * 1024],
                            in_=Wo_in[kc * 128:(kc + 1) * 128, :]).then_inc(ldg, 16)
                g.dma_start(out=w_all[:, OFF_WQ + kc * 1024:OFF_WQ + (kc + 1) * 1024],
                            in_=Wq1_in[kc * 128:(kc + 1) * 128, :]).then_inc(ldg, 16)
                g.dma_start(out=w_all[:, OFF_DKV + kc * 256:OFF_DKV + (kc + 1) * 256],
                            in_=Wdkv_in[kc * 128:(kc + 1) * 128, :]).then_inc(ldg, 16)
                c_ldg.inc(48)
            for kc in range(2):
                g.dma_start(out=w_all[:, OFF_UK + kc * 1024:OFF_UK + (kc + 1) * 1024],
                            in_=Wuk_in[kc * 128:(kc + 1) * 128, :]).then_inc(ldg, 16)
                g.dma_start(out=w_all[:, OFF_UV + kc * 1024:OFF_UV + (kc + 1) * 1024],
                            in_=Wuv_in[kc * 128:(kc + 1) * 128, :]).then_inc(ldg, 16)
                c_ldg.inc(32)
            v.wait_ge(ld, c_ld.v)
            v.memset(ones_row[:], 1.0)
            v.memset(sm[:, 64:65], 1e-5)
            v.memset(sm[:, 65:66], 1e-5).then_inc(cmp, 1)
            c_cmp.inc()
            v.wait_ge(cmp, c_cmp.v)
            t.wait_ge(ld, c_ld.v)
            t.wait_ge(cmp, c_cmp.v)
            for half in range(2):
                t.matmul(out=ps[0][0:128, 0:512], lhsT=ones_row[0:1, 0:128],
                         rhs=nw_t[0:1, half * 512:(half + 1) * 512],
                         start=True, stop=True).then_inc(pe, 1)
                c_pe.inc()
                sc.wait_ge(pe, c_pe.v)
                sc.copy(out=nw_t[:, half * 512:(half + 1) * 512],
                        in_=ps[0][0:128, 0:512]).then_inc(cmp, 1)
                c_cmp.inc()
                t.wait_ge(cmp, c_cmp.v)
            t.matmul(out=ps[0][0:128, 0:256], lhsT=ones_row[0:1, 0:128],
                     rhs=lng_t[0:1, :], start=True, stop=True).then_inc(pe, 1)
            c_pe.inc()
            sc.wait_ge(pe, c_pe.v)
            sc.copy(out=lng_t[:, :], in_=ps[0][0:128, 0:256]).then_inc(cmp, 1)
            c_cmp.inc()
            t.wait_ge(cmp, c_cmp.v)
            t.matmul(out=ps[0][0:128, 0:256], lhsT=ones_row[0:1, 0:128],
                     rhs=lnb_t[0:1, :], start=True, stop=True).then_inc(pe, 1)
            c_pe.inc()
            sc.wait_ge(pe, c_pe.v)
            sc.copy(out=lnb_t[:, :], in_=ps[0][0:128, 0:256]).then_inc(cmp, 1)
            c_cmp.inc()

            def transpose_to_colb(src_bf, nch=8):
                for cch in range(nch):
                    t.matmul(out=ps[3][0:128, 0:128], lhsT=src_bf[:, cch * 128:(cch + 1) * 128],
                             rhs=identf[:].bitcast(F32) if False else identb[:],
                             start=True, stop=True).then_inc(pe, 1)
                    c_pe.inc()
                    sc.wait_ge(pe, c_pe.v)
                    sc.copy(out=colb[:, cch * 128:(cch + 1) * 128],
                            in_=ps[3][0:128, 0:128])
                    sc.activation(out=stb[:, cch * 128:(cch + 1) * 128],
                                  in_=colb[:, cch * 128:(cch + 1) * 128],
                                  func=AF.Abs).then_inc(cmp, 1)
                    c_cmp.inc()
                    t.wait_ge(cmp, c_cmp.v)

            identb = E(nc.sbuf_tensor("identbC", [128, 128], BF16))
            v.wait_ge(cmp, c_cmp.v)
            v.tensor_copy(out=identb[:], in_=identf[:]).then_inc(cmp, 1)
            c_cmp.inc()

            for i in range(NT):
                r0 = i * 128
                if i >= 1:
                    sync.wait_ge(pe, c_pe.v)
                    sync.wait_ge(cmp, c_cmp.v)
                sync.dma_start(out=ot[:], in_=o_in[r0:r0 + 128, :]).then_inc(ld, 16)
                sync.dma_start(out=gt[:], in_=gate_in[r0:r0 + 128, :]).then_inc(ld, 16)
                c_ld.inc(32)
                v.wait_ge(ld, c_ld.v)
                v.wait_ge(st, c_st.v)
                sc.wait_ge(st, c_st.v)
                if i == 0:
                    v.wait_ge(ldg, c_ldg.v)
                    t.wait_ge(ldg, c_ldg.v)
                v.tensor_tensor(out=st1[:], in0=ot[:], in1=gt[:], op=OP.mult).then_inc(cmp, 1)
                c_cmp.inc()
                c_og = c_cmp.v
                v.scalar_tensor_tensor(out=st2[:], in0=st1[:], scalar=1.0, in1=st1[:],
                                       op0=OP.mult, op1=OP.mult)
                v.tensor_reduce(out=sm[:, 0:H], in_=st2[:].rearrange("p (h d) -> p h d", d=64),
                                axis=AX.X, op=OP.add).then_inc(cmp, 1)
                v.wait_ge(cmp, c_cmp.inc())
                v.tensor_scalar(out=sm[:, 16:16 + H], in0=sm[:, 0:H], scalar1=1.0 / 64.0,
                                scalar2=None, op0=OP.mult).then_inc(cmp, 1)
                c_cmp.inc()
                sc.wait_ge(cmp, c_cmp.v)
                sc.activation(out=sm[:, 32:32 + H], in_=sm[:, 16:16 + H], func=AF.Sqrt,
                              bias=sm[:, 65:66], scale=1.0).then_inc(cmp, 1)
                c_cmp.inc()
                v.wait_ge(cmp, c_cmp.v)
                v.reciprocal(out=sm[:, 48:48 + H], in_=sm[:, 32:32 + H]).then_inc(cmp, 1)
                v.wait_ge(cmp, c_cmp.inc())
                rb = sm[:, 48:48 + H].unsqueeze(2).broadcast_to([128, H, 64])
                v.tensor_tensor(out=st2[:].rearrange("p (h d) -> p h d", d=64),
                                in0=st1[:].rearrange("p (h d) -> p h d", d=64),
                                in1=rb, op=OP.mult).then_inc(cmp, 1)
                v.wait_ge(cmp, c_cmp.inc())
                v.tensor_tensor(out=st1[:], in0=st2[:], in1=nw_t[:, 0:1024], op=OP.mult).then_inc(cmp, 1)
                v.wait_ge(cmp, c_cmp.inc())
                v.tensor_copy(out=stb[:, 0:1024], in_=st1[:]).then_inc(cmp, 1)
                v.wait_ge(cmp, c_cmp.inc())
                v.tensor_reduce(out=sm[:, 66:67], in_=stb[:, 0:1024], axis=AX.X, op=OP.max).then_inc(cmp, 1)
                c_cmp.inc()
                t.wait_ge(cmp, c_cmp.v)
                transpose_to_colb(stb)
                for half in range(2):
                    for kc in range(8):
                        t.matmul(out=ps[half][0:128, 0:512],
                                 lhsT=colb[:, kc * 128:(kc + 1) * 128],
                                 rhs=w_all[:, OFF_WO + kc * 1024 + half * 512:
                                           OFF_WO + kc * 1024 + half * 512 + 512],
                                 start=(kc == 0), stop=(kc == 7)).then_inc(pe, 1)
                        c_pe.inc()
                sync.wait_ge(cmp, c_og)
                sync.dma_start(out=xt[:], in_=x_in[r0:r0 + 128, :]).then_inc(ld, 16)
                sync.dma_start(out=gt[:], in_=g_in[r0:r0 + 128, :]).then_inc(ld, 16)
                c_ld.inc(32)
                v.wait_ge(ld, c_ld.v)
                v.wait_ge(pe, c_pe.v)
                for half in range(2):
                    v.tensor_tensor(out=st1[:, half * 512:half * 512 + 512],
                                    in0=ps[half][0:128, 0:512],
                                    in1=gt[:, half * 512:half * 512 + 512], op=OP.mult)
                v.tensor_tensor(out=xt[:], in0=xt[:], in1=st1[:], op=OP.add).then_inc(cmp, 1)
                v.wait_ge(cmp, c_cmp.inc())
                g.wait_ge(cmp, c_cmp.v)
                g.dma_start(out=x1_out[r0:r0 + 128, :], in_=xt[:]).then_inc(st, 16)
                c_st.inc(16)
                v.tensor_reduce(out=sm[:, 0:1], in_=xt[:], axis=AX.X, op=OP.add).then_inc(cmp, 1)
                v.wait_ge(cmp, c_cmp.inc())
                v.tensor_scalar(out=sm[:, 1:2], in0=sm[:, 0:1], scalar1=1.0 / 1024.0,
                                scalar2=None, op0=OP.mult).then_inc(cmp, 1)
                v.wait_ge(cmp, c_cmp.inc())
                v.tensor_tensor(out=st2[:], in0=xt[:], in1=sm[:, 1:2].broadcast_to([128, 1024]),
                                op=OP.subtract)
                v.scalar_tensor_tensor(out=st1[:], in0=st2[:], scalar=1.0, in1=st2[:],
                                       op0=OP.mult, op1=OP.mult,
                                       accum_out=sm[:, 2:3]).then_inc(cmp, 1)
                v.wait_ge(cmp, c_cmp.inc())
                v.tensor_scalar(out=sm[:, 3:4], in0=sm[:, 2:3], scalar1=1.0 / 1024.0,
                                scalar2=None, op0=OP.mult).then_inc(cmp, 1)
                c_cmp.inc()
                sc.wait_ge(cmp, c_cmp.v)
                sc.activation(out=sm[:, 4:5], in_=sm[:, 3:4], func=AF.Sqrt,
                              bias=sm[:, 64:65], scale=1.0).then_inc(cmp, 1)
                c_cmp.inc()
                v.wait_ge(cmp, c_cmp.v)
                v.reciprocal(out=sm[:, 5:6], in_=sm[:, 4:5]).then_inc(cmp, 1)
                v.wait_ge(cmp, c_cmp.inc())
                v.wait_ge(st, c_st.v)
                v.tensor_tensor(out=xt[:], in0=st2[:], in1=sm[:, 5:6].broadcast_to([128, 1024]),
                                op=OP.mult).then_inc(cmp, 1)
                v.wait_ge(cmp, c_cmp.inc())
                g.wait_ge(cmp, c_cmp.v)
                g.dma_start(out=xn1_out[r0:r0 + 128, :], in_=xt[:]).then_inc(st, 16)
                c_st.inc(16)
                v.tensor_copy(out=stb[:, 0:1024], in_=xt[:]).then_inc(cmp, 1)
                v.wait_ge(cmp, c_cmp.inc())
                v.tensor_reduce(out=sm[:, 66:67], in_=stb[:, 0:1024], axis=AX.X, op=OP.max).then_inc(cmp, 1)
                c_cmp.inc()
                t.wait_ge(cmp, c_cmp.v)
                transpose_to_colb(stb)
                sync.dma_start(out=cost[:], in_=cosb_in[r0:r0 + 128, :]).then_inc(ld, 16)
                sync.dma_start(out=sint[:], in_=sinb_in[r0:r0 + 128, :]).then_inc(ld, 16)
                c_ld.inc(32)

                def rope_store(src_f32, dst):
                    a = src_f32[:].rearrange("p (h z) -> p h z", z=64)
                    cb = cost[:].unsqueeze(1).broadcast_to([128, H, 32])
                    sb = sint[:].unsqueeze(1).broadcast_to([128, H, 32])
                    x1v = src_f32[:].rearrange("p (h z) -> p h z", z=64)[:, :, 0:32]
                    x2v = src_f32[:].rearrange("p (h z) -> p h z", z=64)[:, :, 32:64]
                    o1 = st2[:].rearrange("p (h z) -> p h z", z=64)[:, :, 0:32]
                    o2 = st2[:].rearrange("p (h z) -> p h z", z=64)[:, :, 32:64]
                    tt = gt[:].rearrange("p (h z) -> p h z", z=64)
                    v.tensor_tensor(out=tt[:, :, 0:32], in0=x1v, in1=cb, op=OP.mult)
                    v.tensor_tensor(out=tt[:, :, 32:64], in0=x2v, in1=sb, op=OP.mult).then_inc(cmp, 1)
                    v.wait_ge(cmp, c_cmp.inc())
                    v.tensor_tensor(out=o1, in0=tt[:, :, 0:32], in1=tt[:, :, 32:64], op=OP.subtract)
                    v.tensor_tensor(out=tt[:, :, 0:32], in0=x1v, in1=sb, op=OP.mult)
                    v.tensor_tensor(out=tt[:, :, 32:64], in0=x2v, in1=cb, op=OP.mult).then_inc(cmp, 1)
                    v.wait_ge(cmp, c_cmp.inc())
                    v.tensor_tensor(out=o2, in0=tt[:, :, 0:32], in1=tt[:, :, 32:64], op=OP.add).then_inc(cmp, 1)
                    v.wait_ge(cmp, c_cmp.inc())
                    g.wait_ge(cmp, c_cmp.v)
                    g.dma_start(out=dst[r0:r0 + 128, :], in_=st2[:]).then_inc(st, 16)
                    c_st.inc(16)

                v.wait_ge(ld, c_ld.v)
                t.wait_ge(cmp, c_cmp.v)
                for half in range(2):
                    for kc in range(8):
                        t.matmul(out=ps[half][0:128, 0:512],
                                 lhsT=colb[:, kc * 128:(kc + 1) * 128],
                                 rhs=w_all[:, OFF_WQ + kc * 1024 + half * 512:
                                           OFF_WQ + kc * 1024 + half * 512 + 512],
                                 start=(kc == 0), stop=(kc == 7)).then_inc(pe, 1)
                        c_pe.inc()
                v.wait_ge(pe, c_pe.v)
                for half in range(2):
                    v.tensor_copy(out=st1[:, half * 512:half * 512 + 512],
                                  in_=ps[half][0:128, 0:512])
                v.sem_inc(cmp, 0) if False else None
                v.tensor_copy(out=sm[:, 67:68], in_=sm[:, 66:67]).then_inc(cmp, 1)
                v.wait_ge(cmp, c_cmp.inc())
                rope_store(st1, q1_out)
                v.wait_ge(st, c_st.v)
                t.wait_ge(cmp, c_cmp.v)
                for kc in range(8):
                    t.matmul(out=ps[2][0:128, 0:256],
                             lhsT=colb[:, kc * 128:(kc + 1) * 128],
                             rhs=w_all[:, OFF_DKV + kc * 256:OFF_DKV + (kc + 1) * 256],
                             start=(kc == 0), stop=(kc == 7)).then_inc(pe, 1)
                    c_pe.inc()
                v.wait_ge(pe, c_pe.v)
                v.tensor_reduce(out=sm[:, 8:9], in_=ps[2][0:128, 0:256], axis=AX.X,
                                op=OP.add).then_inc(cmp, 1)
                v.wait_ge(cmp, c_cmp.inc())
                v.tensor_scalar(out=sm[:, 9:10], in0=sm[:, 8:9], scalar1=1.0 / 256.0,
                                scalar2=None, op0=OP.mult).then_inc(cmp, 1)
                v.wait_ge(cmp, c_cmp.inc())
                v.tensor_tensor(out=lat[:], in0=ps[2][0:128, 0:256],
                                in1=sm[:, 9:10].broadcast_to([128, 256]), op=OP.subtract)
                v.scalar_tensor_tensor(out=st2[:, 0:256], in0=lat[:], scalar=1.0, in1=lat[:],
                                       op0=OP.mult, op1=OP.mult,
                                       accum_out=sm[:, 10:11]).then_inc(cmp, 1)
                v.wait_ge(cmp, c_cmp.inc())
                v.tensor_scalar(out=sm[:, 11:12], in0=sm[:, 10:11], scalar1=1.0 / 256.0,
                                scalar2=None, op0=OP.mult).then_inc(cmp, 1)
                c_cmp.inc()
                sc.wait_ge(cmp, c_cmp.v)
                sc.activation(out=sm[:, 12:13], in_=sm[:, 11:12], func=AF.Sqrt,
                              bias=sm[:, 64:65], scale=1.0).then_inc(cmp, 1)
                c_cmp.inc()
                v.wait_ge(cmp, c_cmp.v)
                v.reciprocal(out=sm[:, 13:14], in_=sm[:, 12:13]).then_inc(cmp, 1)
                v.wait_ge(cmp, c_cmp.inc())
                v.tensor_tensor(out=lat[:], in0=lat[:], in1=sm[:, 13:14].broadcast_to([128, 256]),
                                op=OP.mult).then_inc(cmp, 1)
                v.wait_ge(cmp, c_cmp.inc())
                v.tensor_tensor(out=lat[:], in0=lat[:], in1=lng_t[:], op=OP.mult).then_inc(cmp, 1)
                v.wait_ge(cmp, c_cmp.inc())
                v.tensor_tensor(out=lat[:], in0=lat[:], in1=lnb_t[:], op=OP.add).then_inc(cmp, 1)
                v.wait_ge(cmp, c_cmp.inc())
                v.tensor_copy(out=latc[:], in_=lat[:]).then_inc(cmp, 1)
                v.wait_ge(cmp, c_cmp.inc())
                v.tensor_reduce(out=sm[:, 66:67], in_=latc[:], axis=AX.X, op=OP.max).then_inc(cmp, 1)
                c_cmp.inc()
                t.wait_ge(cmp, c_cmp.v)
                for cch in range(2):
                    t.matmul(out=ps[3][0:128, 0:128], lhsT=latc[:, cch * 128:(cch + 1) * 128],
                             rhs=identb[:], start=True, stop=True).then_inc(pe, 1)
                    c_pe.inc()
                    sc.wait_ge(pe, c_pe.v)
                    sc.copy(out=colb[:, cch * 128:(cch + 1) * 128], in_=ps[3][0:128, 0:128])
                    sc.activation(out=stb[:, cch * 128:(cch + 1) * 128],
                                  in_=colb[:, cch * 128:(cch + 1) * 128],
                                  func=AF.Abs).then_inc(cmp, 1)
                    c_cmp.inc()
                    t.wait_ge(cmp, c_cmp.v)
                for w_i, dst in ((0, k1_out), (1, v1_out)):
                    off = OFF_UK if w_i == 0 else OFF_UV
                    t.wait_ge(cmp, c_cmp.v)
                    for half in range(2):
                        for kc in range(2):
                            t.matmul(out=ps[half][0:128, 0:512],
                                     lhsT=colb[:, kc * 128:(kc + 1) * 128],
                                     rhs=w_all[:, off + kc * 1024 + half * 512:
                                               off + kc * 1024 + half * 512 + 512],
                                     start=(kc == 0), stop=(kc == 1)).then_inc(pe, 1)
                            c_pe.inc()
                    v.wait_ge(pe, c_pe.v)
                    for half in range(2):
                        v.tensor_copy(out=st1[:, half * 512:half * 512 + 512],
                                      in_=ps[half][0:128, 0:512])
                    v.tensor_copy(out=sm[:, 67:68], in_=sm[:, 66:67]).then_inc(cmp, 1)
                    v.wait_ge(cmp, c_cmp.inc())
                    if w_i == 0:
                        rope_store(st1, dst)
                    else:
                        g.wait_ge(cmp, c_cmp.v)
                        g.dma_start(out=dst[r0:r0 + 128, :], in_=st1[:]).then_inc(st, 16)
                        c_st.inc(16)
                t.wait_ge(st, c_st.v)
            g.wait_ge(st, c_st.v)
    return nc


def run_c(inputs, a, o):
    x_full = inputs["L0_attn_x"].reshape(B * N, D)
    o_tm = o.reshape(B * N, D)
    identf = np.eye(128, dtype=np.float32)
    pos = np.arange(N, dtype=np.float64)[:, None]
    inv = 1.0 / 10000.0 ** (np.arange(32, dtype=np.float64) / 32.0)
    ang = (pos * inv[None, :]).astype(np.float32)
    cosf = np.cos(ang).astype(np.float32)
    sinf = np.sin(ang).astype(np.float32)
    in_maps = []
    for c in range(NC):
        r0 = c * TOK
        seq0 = (c % 4) * TOK
        in_maps.append({
            "x": np.ascontiguousarray(x_full[r0:r0 + TOK]),
            "o": np.ascontiguousarray(o_tm[r0:r0 + TOK]),
            "gate": np.ascontiguousarray(a["gate"][r0:r0 + TOK]),
            "g": np.ascontiguousarray(a["g"][r0:r0 + TOK]),
            "nw": inputs["L0_attn_norm_weight"].reshape(1, D),
            "Wo": inputs["L0_attn_Wo"], "Wq1": inputs["L1_attn_W_q"],
            "Wdkv": inputs["L1_attn_W_dkv"], "Wuk": inputs["L1_attn_W_uk"],
            "Wuv": inputs["L1_attn_W_uv"],
            "lng": inputs["L1_attn_ln_gamma"].reshape(1, 256),
            "lnb": inputs["L1_attn_ln_beta"].reshape(1, 256),
            "cosb": np.ascontiguousarray(cosf[seq0:seq0 + TOK]),
            "sinb": np.ascontiguousarray(sinf[seq0:seq0 + TOK]),
            "identf": identf,
        })
    nc = build_c()
    res = run_bass_kernel_spmd(nc, in_maps, core_ids=list(range(NC)))
    out = {}
    for key in ("x1", "xn1", "q1", "k1", "v1"):
        out[key] = np.concatenate([r[key] for r in res.results], axis=0)
    return out


DSH = 12.0


def build_d():
    nc = bass.Bass("TRN2", target_bir_lowering=False)
    kcol_in = nc.dram_tensor("kcol", [2, 128, N], BF16, kind="ExternalInput").ap()
    qcol_in = nc.dram_tensor("qcol", [2, 128, N], BF16, kind="ExternalInput").ap()
    vaug_in = nc.dram_tensor("vaug", [2, 128, 2 * 64 * 65], BF16, kind="ExternalInput").ap()
    umask_in = nc.dram_tensor("umask", [128, 896], BF16, kind="ExternalInput").ap()
    identf_in = nc.dram_tensor("identf", [128, 128], F32, kind="ExternalInput").ap()
    oat_out = nc.dram_tensor("oat", [2, 2, N, 64], BF16, kind="ExternalOutput").ap()

    with ExitStack() as ctx:
        E = ctx.enter_context
        kd = [E(nc.sbuf_tensor(f"kdD{d}", [128, N], BF16)) for d in range(2)]
        qd = [E(nc.sbuf_tensor(f"qdD{d}", [128, N], BF16)) for d in range(2)]
        va = [E(nc.sbuf_tensor(f"vaD{d}", [128, 2 * 64 * 65], BF16)) for d in range(2)]
        umaskt = E(nc.sbuf_tensor("umaskD", [128, 896], BF16))
        identft = E(nc.sbuf_tensor("identfD", [128, 128], F32))
        smd = E(nc.sbuf_tensor("smD", [128, 4], F32))
        pt = [[E(nc.sbuf_tensor(f"ptD{p}{b}", [128, 512], BF16)) for b in range(2)]
              for p in range(2)]
        oTs = [E(nc.sbuf_tensor(f"oTsD{p}", [128, 512], F32)) for p in range(2)]
        junkD = E(nc.sbuf_tensor("junkD", [128, 512], F32))
        orw = E(nc.sbuf_tensor("orwD", [128, 68], F32))
        obf = [E(nc.sbuf_tensor(f"obfD{j}", [128, 64], BF16)) for j in range(2)]
        psS = [[E(nc.psum_tensor(f"psSD{p}{b}", [128, 512], F32)) for b in range(2)]
               for p in range(2)]
        psO = [E(nc.psum_tensor(f"psOD{p}", [128, 512], F32)) for p in range(2)]
        pstr = E(nc.psum_tensor("pstrD", [128, 512], F32))
        ld = E(nc.semaphore("ld"))
        cmp = E(nc.semaphore("cmp"))
        pe = E(nc.semaphore("pe"))
        st = E(nc.semaphore("st"))
        c_ld, c_cmp, c_pe, c_st = Ctr(), Ctr(), Ctr(), Ctr()
        block = E(nc.Block())

        @block.sync
        def _(sync):
            v, t, sc, g = nc.vector, nc.tensor, nc.scalar, nc.gpsimd
            sync.dma_start(out=umaskt[:], in_=umask_in).then_inc(ld, 16)
            sync.dma_start(out=identft[:], in_=identf_in).then_inc(ld, 16)
            c_ld.inc(32)
            duo_ld = []
            for dd in range(2):
                sync.dma_start(out=kd[dd][:], in_=kcol_in[dd]).then_inc(ld, 16)
                sync.dma_start(out=qd[dd][:], in_=qcol_in[dd]).then_inc(ld, 16)
                sync.dma_start(out=va[dd][:], in_=vaug_in[dd]).then_inc(ld, 16)
                c_ld.inc(48)
                duo_ld.append(c_ld.v)
            v.wait_ge(ld, 32)
            v.memset(smd[:, 0:1], -DSH).then_inc(cmp, 1)
            c_cmp.inc()
            sc.wait_ge(cmp, c_cmp.v)
            sc_bias_ready = c_cmp.v

            psS_evac = [[0, 0], [0, 0]]
            pt_used = [[0, 0], [0, 0]]
            psO_evac = [0, 0]
            oTs_used = [0, 0]
            obf_st = [0, 0]

            for dd in range(2):
                t.wait_ge(ld, duo_ld[dd])
                sc.wait_ge(ld, duo_ld[dd])
                for qb in range(16):
                    nkv = 4 * (qb + 1)
                    for kc in range(nkv):
                        sbb = kc % 2
                        for pi in range(2):
                            base = pi * 64
                            t.wait_ge(cmp, psS_evac[pi][sbb])
                            t.matmul(out=psS[pi][sbb][0:128, 0:512],
                                     lhsT=kd[dd][base:base + 64, kc * 128:(kc + 1) * 128],
                                     rhs=qd[dd][base:base + 64, qb * 512:(qb + 1) * 512],
                                     start=True, stop=True,
                                     tile_position=(base, 0)).then_inc(pe, 1)
                            c_pe.inc()
                            sc.wait_ge(pe, c_pe.v)
                            sc.wait_ge(pe, pt_used[pi][sbb])
                            m = kc - 4 * qb
                            sc.activation(out=pt[pi][sbb][:], in_=psS[pi][sbb][0:128, 0:512],
                                          func=AF.Exp, scale=0.125,
                                          bias=smd[:, 0:1]).then_inc(cmp, 1)
                            c_cmp.inc()
                            if m >= 0:
                                v.wait_ge(cmp, c_cmp.v)
                                v.tensor_tensor(out=pt[pi][sbb][:], in0=pt[pi][sbb][:],
                                                in1=umaskt[:, 384 - 128 * m:896 - 128 * m],
                                                op=OP.mult).then_inc(cmp, 1)
                                c_cmp.inc()
                            psS_evac[pi][sbb] = c_cmp.v
                            t.wait_ge(cmp, c_cmp.v)
                            if kc == 0:
                                t.wait_ge(cmp, psO_evac[pi])
                            t.matmul(out=psO[pi][0:65, 0:512],
                                     lhsT=va[dd][:, pi * 64 * 65 + kc * 65:
                                                 pi * 64 * 65 + (kc + 1) * 65],
                                     rhs=pt[pi][sbb][:],
                                     start=(kc == 0), stop=(kc == nkv - 1)).then_inc(pe, 1)
                            c_pe.inc()
                            pt_used[pi][sbb] = c_pe.v
                    for pi in range(2):
                        sc.wait_ge(pe, c_pe.v)
                        sc.copy(out=oTs[pi][0:65, 0:512],
                                in_=psO[pi][0:65, 0:512])
                        sc.activation(out=junkD[0:65, 0:512], in_=oTs[pi][0:65, 0:512],
                                      func=AF.Abs).then_inc(cmp, 1)
                        c_cmp.inc()
                        psO_evac[pi] = c_cmp.v
                        t.wait_ge(cmp, c_cmp.v)
                        for j in range(4):
                            t.matmul(out=pstr[0:128, 0:65],
                                     lhsT=oTs[pi][0:65, j * 128:(j + 1) * 128],
                                     rhs=identft[0:65, 0:65],
                                     start=True, stop=True).then_inc(pe, 1)
                            c_pe.inc()
                            v.wait_ge(pe, c_pe.v)
                            v.tensor_copy(out=orw[0:128, 0:65],
                                          in_=pstr[0:128, 0:65]).then_inc(cmp, 1)
                            c_cmp.inc()
                            t.wait_ge(cmp, c_cmp.v)
                            v.wait_ge(cmp, c_cmp.v)
                            v.reciprocal(out=orw[:, 66:67], in_=orw[:, 64:65]).then_inc(cmp, 1)
                            v.wait_ge(cmp, c_cmp.inc())
                            jj = j % 2
                            v.wait_ge(st, obf_st[jj])
                            v.tensor_scalar(out=obf[jj][0:128, 0:64], in0=orw[:, 0:64],
                                            scalar1=orw[:, 66:67], scalar2=None,
                                            op0=OP.mult).then_inc(cmp, 1)
                            c_cmp.inc()
                            g.wait_ge(cmp, c_cmp.v)
                            g.dma_start(out=oat_out[dd, pi, qb * 512 + j * 128:
                                                    qb * 512 + (j + 1) * 128, :],
                                        in_=obf[jj][:]).then_inc(st, 16)
                            c_st.inc(16)
                            obf_st[jj] = c_st.v
                        oTs_used[pi] = c_pe.v
            v.wait_ge(st, c_st.v)
            v.wait_ge(ld, c_ld.v)
    return nc


def run_d(inputs, c_out):
    q1 = c_out["q1"].reshape(B, N, H, DH)
    k1 = c_out["k1"].reshape(B, N, H, DH)
    v1 = c_out["v1"].reshape(B, N, H, DH)
    qcol = np.ascontiguousarray(q1.transpose(0, 2, 3, 1))
    kcol = np.ascontiguousarray(k1.transpose(0, 2, 3, 1))
    vr = v1.transpose(0, 2, 1, 3)
    ones = np.ones((B, H, N, 1), dtype=BF)
    vaug = np.concatenate([vr, ones], axis=-1)
    vap = vaug.reshape(B, H, 64, 128, 65).transpose(0, 1, 3, 2, 4).reshape(B, H, 128, 64 * 65)
    umask = (np.arange(896)[None, :] - 384 >= np.arange(128)[:, None]).astype(BF)
    identf = np.eye(128, dtype=np.float32)
    in_maps = []
    for c in range(NC):
        kcolD = np.zeros((2, 128, N), dtype=BF)
        qcolD = np.zeros((2, 128, N), dtype=BF)
        vaugD = np.zeros((2, 128, 2 * 64 * 65), dtype=BF)
        for ddp in range(4):
            pair = 4 * c + ddp
            b, h = pair // H, pair % H
            dd, pi = ddp // 2, ddp % 2
            kcolD[dd, pi * 64:(pi + 1) * 64, :] = kcol[b, h]
            qcolD[dd, pi * 64:(pi + 1) * 64, :] = qcol[b, h]
            vaugD[dd, :, pi * 64 * 65:(pi + 1) * 64 * 65] = vap[b, h]
        in_maps.append({"kcol": kcolD, "qcol": qcolD, "vaug": vaugD,
                        "umask": umask, "identf": identf})
    nc = build_d()
    res = run_bass_kernel_spmd(nc, in_maps, core_ids=list(range(NC)))
    attn = np.empty((B, N, H, DH), dtype=BF)
    for c in range(NC):
        oat = res.results[c]["oat"]
        for ddp in range(4):
            pair = 4 * c + ddp
            b, h = pair // H, pair % H
            attn[b, :, h, :] = oat[ddp // 2, ddp % 2]
    return attn.reshape(B * N, D)


GRPE = 256
NTILE_E = (B * N) // 128


def build_e1():
    nc = bass.Bass("TRN2", target_bir_lowering=False)
    xn1_in = nc.dram_tensor("xn1", [B * N, D], F32, kind="ExternalInput").ap()
    w1_in = nc.dram_tensor("w1", [128, 8 * 4096], BF16, kind="ExternalInput").ap()
    w2_in = nc.dram_tensor("w2", [128, 32 * 1024], BF16, kind="ExternalInput").ap()
    b1t_in = nc.dram_tensor("b1t", [128, 32], F32, kind="ExternalInput").ap()
    b2_in = nc.dram_tensor("b2", [1, 1024], F32, kind="ExternalInput").ap()
    wr_in = nc.dram_tensor("wr", [128, 64], F32, kind="ExternalInput").ap()
    br_in = nc.dram_tensor("br", [1, 8], F32, kind="ExternalInput").ap()
    gam_in = nc.dram_tensor("gam", [1, 1024], F32, kind="ExternalInput").ap()
    bet_in = nc.dram_tensor("bet", [1, 1024], F32, kind="ExternalInput").ap()
    esel_in = nc.dram_tensor("esel", [128, 8], F32, kind="ExternalInput").ap()
    iora_in = nc.dram_tensor("iora", [128, 8], F32, kind="ExternalInput").ap()
    ci99_in = nc.dram_tensor("ci99", [128, 8], F32, kind="ExternalInput").ap()
    identf_in = nc.dram_tensor("identf", [128, 128], F32, kind="ExternalInput").ap()
    part_out = nc.dram_tensor("part", [B * N, D], F32, kind="ExternalOutput").ap()
    wsl_out = nc.dram_tensor("wsl", [128, 128], F32, kind="ExternalOutput").ap()
    cvec_out = nc.dram_tensor("cvec", [1, 1024], F32, kind="ExternalOutput").ap()

    with ExitStack() as ctx:
        E = ctx.enter_context
        w1sb = E(nc.sbuf_tensor("w1sbE", [128, 8 * 4096], BF16))
        w2sb = E(nc.sbuf_tensor("w2sbE", [128, 32 * 1024], BF16))
        hcol = E(nc.sbuf_tensor("hcolE", [128, 32 * GRPE], BF16))
        colbE = E(nc.sbuf_tensor("colbEE", [128, 8 * GRPE], BF16))
        colbF = E(nc.sbuf_tensor("colbFE", [128, 1024], F32))
        gb = E(nc.sbuf_tensor("gbE", [128, 1024], F32))
        bbt = E(nc.sbuf_tensor("bbtE", [128, 1024], F32))
        rqb = E(nc.sbuf_tensor("rqbE", [128, 1024], F32))
        xt = E(nc.sbuf_tensor("xtE", [128, 1024], F32))
        st1 = E(nc.sbuf_tensor("st1E", [128, 1024], F32))
        st2 = E(nc.sbuf_tensor("st2E", [128, 1024], F32))
        pstage = E(nc.sbuf_tensor("pstageE", [128, 1024], F32))
        wbuf = E(nc.sbuf_tensor("wbufE", [128, 128], F32))
        sm = E(nc.sbuf_tensor("smE", [128, 64], F32))
        wrsb = E(nc.sbuf_tensor("wrsbE", [128, 64], F32))
        brsb = E(nc.sbuf_tensor("brsbE", [128, 8], F32))
        iora = E(nc.sbuf_tensor("ioraE", [128, 8], F32))
        ci99 = E(nc.sbuf_tensor("ci99E", [128, 8], F32))
        esel = E(nc.sbuf_tensor("eselE", [128, 8], F32))
        b1sb = E(nc.sbuf_tensor("b1sbE", [128, 32], F32))
        rb1 = E(nc.sbuf_tensor("rb1E", [128, 32], BF16))
        b2sb = E(nc.sbuf_tensor("b2sbE", [1, 1024], F32))
        rqv = E(nc.sbuf_tensor("rqvE", [1, 1024], F32))
        cvst = E(nc.sbuf_tensor("cvstE", [1, 1024], F32))
        ones_row = E(nc.sbuf_tensor("onesE", [1, 128], F32))
        identft = E(nc.sbuf_tensor("identfE", [128, 128], F32))
        junk = E(nc.sbuf_tensor("junkE", [128, 512], F32))
        pstr = E(nc.psum_tensor("pstrE", [128, 512], F32))
        psR = E(nc.psum_tensor("psRE", [128, 512], F32))
        psH = [E(nc.psum_tensor(f"psHE{i}", [128, 512], F32)) for i in range(2)]
        psO = [E(nc.psum_tensor(f"psOE{i}", [128, 512], F32)) for i in range(2)]
        ld = E(nc.semaphore("ld"))
        cmp = E(nc.semaphore("cmp"))
        pe = E(nc.semaphore("pe"))
        st = E(nc.semaphore("st"))
        c_ld, c_cmp, c_pe, c_st = Ctr(), Ctr(), Ctr(), Ctr()
        block = E(nc.Block())

        @block.sync
        def _(sync):
            v, t, sc, g = nc.vector, nc.tensor, nc.scalar, nc.gpsimd
            for buf, src in ((identft, identf_in), (wrsb, wr_in), (iora, iora_in),
                             (ci99, ci99_in), (esel, esel_in), (b1sb, b1t_in)):
                sync.dma_start(out=buf[:], in_=src).then_inc(ld, 16)
                c_ld.inc(16)
            sync.dma_start(out=b2sb[:], in_=b2_in).then_inc(ld, 16)
            sync.dma_start(out=gb[0:1, :], in_=gam_in).then_inc(ld, 16)
            sync.dma_start(out=bbt[0:1, :], in_=bet_in).then_inc(ld, 16)
            sync.dma_start(out=brsb[0:1, :], in_=br_in).then_inc(ld, 16)
            c_ld.inc(64)
            sync.dma_start(out=w1sb[:], in_=w1_in).then_inc(ld, 16)
            sync.dma_start(out=w2sb[:], in_=w2_in).then_inc(ld, 16)
            c_ld.inc(32)
            v.wait_ge(ld, c_ld.v)
            v.memset(ones_row[:], 1.0)
            v.memset(sm[:, 54:55], 1e-5)
            v.tensor_scalar(out=rb1[:], in0=b1sb[:], scalar1=0.0, scalar2=None,
                            op0=OP.max).then_inc(cmp, 1)
            c_cmp.inc()
            sc.wait_ge(cmp, c_cmp.v)
            sc.activation(out=junk[:, 0:32], in_=rb1[:], func=AF.Abs).then_inc(cmp, 1)
            c_cmp.inc()
            t.wait_ge(ld, c_ld.v)
            t.wait_ge(cmp, c_cmp.v)
            for buf in (gb, bbt):
                for half in range(2):
                    t.matmul(out=pstr[0:128, 0:512], lhsT=ones_row[0:1, 0:128],
                             rhs=buf[0:1, half * 512:(half + 1) * 512],
                             start=True, stop=True).then_inc(pe, 1)
                    c_pe.inc()
                    sc.wait_ge(pe, c_pe.v)
                    sc.copy(out=buf[:, half * 512:(half + 1) * 512],
                            in_=pstr[0:128, 0:512]).then_inc(cmp, 1)
                    c_cmp.inc()
                    t.wait_ge(cmp, c_cmp.v)
            t.matmul(out=pstr[0:128, 0:8], lhsT=ones_row[0:1, 0:128],
                     rhs=brsb[0:1, 0:8], start=True, stop=True).then_inc(pe, 1)
            c_pe.inc()
            sc.wait_ge(pe, c_pe.v)
            sc.copy(out=brsb[:, :], in_=pstr[0:128, 0:8]).then_inc(cmp, 1)
            c_cmp.inc()
            t.wait_ge(cmp, c_cmp.v)
            for half in range(2):
                for fc in range(32):
                    t.matmul(out=psR[0:1, 0:512], lhsT=rb1[:, fc:fc + 1],
                             rhs=w2sb[:, fc * 1024 + half * 512:fc * 1024 + half * 512 + 512],
                             start=(fc == 0), stop=(fc == 31)).then_inc(pe, 1)
                    c_pe.inc()
                sc.wait_ge(pe, c_pe.v)
                sc.copy(out=rqv[0:1, half * 512:(half + 1) * 512],
                        in_=psR[0:1, 0:512]).then_inc(cmp, 1)
                c_cmp.inc()
                t.wait_ge(cmp, c_cmp.v)
            v.wait_ge(cmp, c_cmp.v)
            v.tensor_tensor(out=cvst[0:1, :], in0=rqv[0:1, :], in1=b2sb[0:1, :],
                            op=OP.add).then_inc(cmp, 1)
            c_cmp.inc()
            g.wait_ge(cmp, c_cmp.v)
            g.dma_start(out=cvec_out, in_=cvst[0:1, :]).then_inc(st, 16)
            c_st.inc(16)
            for half in range(2):
                t.matmul(out=pstr[0:128, 0:512], lhsT=ones_row[0:1, 0:128],
                         rhs=rqv[0:1, half * 512:(half + 1) * 512],
                         start=True, stop=True).then_inc(pe, 1)
                c_pe.inc()
                sc.wait_ge(pe, c_pe.v)
                sc.copy(out=rqb[:, half * 512:(half + 1) * 512],
                        in_=pstr[0:128, 0:512]).then_inc(cmp, 1)
                c_cmp.inc()
                t.wait_ge(cmp, c_cmp.v)

            xt_free = 0
            colbE_free = 0
            hcol_free = 0
            psH_evac = [0, 0]
            psO_evac = 0
            pstage_st = 0

            def wv(n=1):
                v.wait_ge(cmp, c_cmp.inc(n))

            for grp in range(NTILE_E // 2):
                for t01 in range(2):
                    ti = grp * 2 + t01
                    row0 = ti * 128
                    sync.wait_ge(cmp, xt_free)
                    sync.dma_start(out=xt[:], in_=xn1_in[row0:row0 + 128, :]).then_inc(ld, 16)
                    c_ld.inc(16)
                    v.wait_ge(ld, c_ld.v)
                    v.wait_ge(pe, colbE_free)
                    v.tensor_reduce(out=sm[:, 48:49], in_=xt[:], axis=AX.X,
                                    op=OP.add).then_inc(cmp, 1)
                    wv()
                    v.tensor_scalar(out=sm[:, 49:50], in0=sm[:, 48:49],
                                    scalar1=1.0 / 1024.0, scalar2=None,
                                    op0=OP.mult).then_inc(cmp, 1)
                    wv()
                    v.tensor_tensor(out=st2[:], in0=xt[:],
                                    in1=sm[:, 49:50].broadcast_to([128, 1024]),
                                    op=OP.subtract).then_inc(cmp, 1)
                    c_cmp.inc()
                    xt_free = c_cmp.v
                    v.scalar_tensor_tensor(out=st1[:], in0=st2[:], scalar=1.0, in1=st2[:],
                                           op0=OP.mult, op1=OP.mult,
                                           accum_out=sm[:, 50:51]).then_inc(cmp, 1)
                    wv()
                    v.tensor_scalar(out=sm[:, 51:52], in0=sm[:, 50:51],
                                    scalar1=1.0 / 1024.0, scalar2=None,
                                    op0=OP.mult).then_inc(cmp, 1)
                    c_cmp.inc()
                    sc.wait_ge(cmp, c_cmp.v)
                    sc.activation(out=sm[:, 52:53], in_=sm[:, 51:52], func=AF.Sqrt,
                                  bias=sm[:, 54:55], scale=1.0).then_inc(cmp, 1)
                    wv()
                    v.reciprocal(out=sm[:, 53:54], in_=sm[:, 52:53]).then_inc(cmp, 1)
                    wv()
                    v.tensor_tensor(out=st1[:], in0=st2[:],
                                    in1=sm[:, 53:54].broadcast_to([128, 1024]),
                                    op=OP.mult).then_inc(cmp, 1)
                    wv()
                    v.tensor_tensor(out=st1[:], in0=st1[:], in1=gb[:], op=OP.mult)
                    v.tensor_tensor(out=st1[:], in0=st1[:], in1=bbt[:],
                                    op=OP.add).then_inc(cmp, 1)
                    c_cmp.inc()
                    t.wait_ge(cmp, c_cmp.v)
                    for cch in range(8):
                        t.matmul(out=pstr[0:128, 0:128], lhsT=st1[:, cch * 128:(cch + 1) * 128],
                                 rhs=identft[0:128, 0:128], start=True,
                                 stop=True).then_inc(pe, 1)
                        c_pe.inc()
                        sc.wait_ge(pe, c_pe.v)
                        sc.copy(out=colbF[:, cch * 128:(cch + 1) * 128],
                                in_=pstr[0:128, 0:128])
                        sc.activation(out=junk[:, 0:128],
                                      in_=colbF[:, cch * 128:(cch + 1) * 128],
                                      func=AF.Abs).then_inc(cmp, 1)
                        c_cmp.inc()
                        t.wait_ge(cmp, c_cmp.v)
                        v.wait_ge(cmp, c_cmp.v)
                        v.tensor_copy(out=colbE[:, cch * GRPE + t01 * 128:
                                               cch * GRPE + t01 * 128 + 128],
                                      in_=colbF[:, cch * 128:(cch + 1) * 128]).then_inc(cmp, 1)
                        c_cmp.inc()
                    for kc in range(8):
                        t.matmul(out=psR[0:128, 0:8], lhsT=colbF[:, kc * 128:(kc + 1) * 128],
                                 rhs=wrsb[:, kc * 8:(kc + 1) * 8], start=(kc == 0),
                                 stop=(kc == 7)).then_inc(pe, 1)
                        c_pe.inc()
                    v.wait_ge(pe, c_pe.v)
                    v.tensor_tensor(out=sm[:, 0:8], in0=psR[0:128, 0:8], in1=brsb[:],
                                    op=OP.add).then_inc(cmp, 1)
                    wv()
                    v.tensor_reduce(out=sm[:, 8:9], in_=sm[:, 0:8], axis=AX.X,
                                    op=OP.max).then_inc(cmp, 1)
                    wv()
                    v.tensor_tensor(out=sm[:, 0:8], in0=sm[:, 0:8],
                                    in1=sm[:, 8:9].broadcast_to([128, 8]),
                                    op=OP.subtract).then_inc(cmp, 1)
                    c_cmp.inc()
                    sc.wait_ge(cmp, c_cmp.v)
                    sc.activation(out=sm[:, 0:8], in_=sm[:, 0:8], func=AF.Exp).then_inc(cmp, 1)
                    wv()
                    v.tensor_reduce(out=sm[:, 9:10], in_=sm[:, 0:8], axis=AX.X,
                                    op=OP.add).then_inc(cmp, 1)
                    wv()
                    v.reciprocal(out=sm[:, 10:11], in_=sm[:, 9:10]).then_inc(cmp, 1)
                    wv()
                    v.tensor_tensor(out=sm[:, 0:8], in0=sm[:, 0:8],
                                    in1=sm[:, 10:11].broadcast_to([128, 8]),
                                    op=OP.mult).then_inc(cmp, 1)
                    wv()
                    v.tensor_reduce(out=sm[:, 11:12], in_=sm[:, 0:8], axis=AX.X,
                                    op=OP.max).then_inc(cmp, 1)
                    wv()
                    v.tensor_tensor(out=sm[:, 16:24], in0=sm[:, 0:8],
                                    in1=sm[:, 11:12].broadcast_to([128, 8]),
                                    op=OP.is_equal).then_inc(cmp, 1)
                    wv()
                    v.tensor_tensor(out=sm[:, 24:32], in0=sm[:, 16:24], in1=ci99[:],
                                    op=OP.mult).then_inc(cmp, 1)
                    wv()
                    v.tensor_scalar(out=sm[:, 24:32], in0=sm[:, 24:32], scalar1=-1.0,
                                    scalar2=99.0, op0=OP.mult, op1=OP.add).then_inc(cmp, 1)
                    wv()
                    v.tensor_reduce(out=sm[:, 12:13], in_=sm[:, 24:32], axis=AX.X,
                                    op=OP.min).then_inc(cmp, 1)
                    wv()
                    v.tensor_tensor(out=sm[:, 16:24], in0=iora[:],
                                    in1=sm[:, 12:13].broadcast_to([128, 8]),
                                    op=OP.is_equal).then_inc(cmp, 1)
                    wv()
                    v.tensor_scalar(out=sm[:, 24:32], in0=sm[:, 16:24], scalar1=1e9,
                                    scalar2=None, op0=OP.mult).then_inc(cmp, 1)
                    wv()
                    v.tensor_tensor(out=sm[:, 32:40], in0=sm[:, 0:8], in1=sm[:, 24:32],
                                    op=OP.subtract).then_inc(cmp, 1)
                    wv()
                    v.tensor_reduce(out=sm[:, 13:14], in_=sm[:, 32:40], axis=AX.X,
                                    op=OP.max).then_inc(cmp, 1)
                    wv()
                    v.tensor_tensor(out=sm[:, 40:48], in0=sm[:, 32:40],
                                    in1=sm[:, 13:14].broadcast_to([128, 8]),
                                    op=OP.is_equal).then_inc(cmp, 1)
                    wv()
                    v.tensor_tensor(out=sm[:, 24:32], in0=sm[:, 40:48], in1=ci99[:],
                                    op=OP.mult).then_inc(cmp, 1)
                    wv()
                    v.tensor_scalar(out=sm[:, 24:32], in0=sm[:, 24:32], scalar1=-1.0,
                                    scalar2=99.0, op0=OP.mult, op1=OP.add).then_inc(cmp, 1)
                    wv()
                    v.tensor_reduce(out=sm[:, 14:15], in_=sm[:, 24:32], axis=AX.X,
                                    op=OP.min).then_inc(cmp, 1)
                    wv()
                    v.tensor_tensor(out=sm[:, 40:48], in0=iora[:],
                                    in1=sm[:, 14:15].broadcast_to([128, 8]),
                                    op=OP.is_equal).then_inc(cmp, 1)
                    wv()
                    v.tensor_tensor(out=sm[:, 16:24], in0=sm[:, 16:24],
                                    in1=sm[:, 11:12].broadcast_to([128, 8]),
                                    op=OP.mult)
                    v.tensor_tensor(out=sm[:, 40:48], in0=sm[:, 40:48],
                                    in1=sm[:, 13:14].broadcast_to([128, 8]),
                                    op=OP.mult).then_inc(cmp, 1)
                    wv()
                    v.tensor_tensor(out=sm[:, 16:24], in0=sm[:, 16:24], in1=sm[:, 40:48],
                                    op=OP.add).then_inc(cmp, 1)
                    wv()
                    v.tensor_tensor(out=sm[:, 24:32], in0=sm[:, 16:24], in1=esel[:],
                                    op=OP.mult).then_inc(cmp, 1)
                    wv()
                    v.tensor_reduce(out=wbuf[:, ti:ti + 1], in_=sm[:, 24:32], axis=AX.X,
                                    op=OP.add).then_inc(cmp, 1)
                    wv()
                sc.wait_ge(pe, hcol_free)
                t.wait_ge(cmp, c_cmp.v)
                for fc in range(32):
                    bb = fc % 2
                    t.wait_ge(cmp, psH_evac[bb])
                    for kc in range(8):
                        t.matmul(out=psH[bb][0:128, 0:GRPE],
                                 lhsT=w1sb[:, kc * 4096 + fc * 128:kc * 4096 + fc * 128 + 128],
                                 rhs=colbE[:, kc * GRPE:(kc + 1) * GRPE],
                                 start=(kc == 0), stop=(kc == 7)).then_inc(pe, 1)
                        c_pe.inc()
                    sc.wait_ge(pe, c_pe.v)
                    sc.activation(out=hcol[:, fc * GRPE:(fc + 1) * GRPE],
                                  in_=psH[bb][0:128, 0:GRPE], func=AF.Relu,
                                  bias=b1sb[:, fc:fc + 1], scale=1.0)
                    sc.activation(out=junk[:, 0:GRPE],
                                  in_=hcol[:, fc * GRPE:(fc + 1) * GRPE],
                                  func=AF.Abs).then_inc(cmp, 1)
                    c_cmp.inc()
                    psH_evac[bb] = c_cmp.v
                colbE_free = c_pe.v
                t.wait_ge(cmp, c_cmp.v)
                for t01 in range(2):
                    ti = grp * 2 + t01
                    row0 = ti * 128
                    t.wait_ge(cmp, psO_evac)
                    for fc in range(32):
                        for half in range(2):
                            t.matmul(out=psO[half][0:128, 0:512],
                                     lhsT=hcol[:, fc * GRPE + t01 * 128:
                                               fc * GRPE + t01 * 128 + 128],
                                     rhs=w2sb[:, fc * 1024 + half * 512:
                                              fc * 1024 + half * 512 + 512],
                                     start=(fc == 0), stop=(fc == 31)).then_inc(pe, 1)
                            c_pe.inc()
                    v.wait_ge(pe, c_pe.v)
                    v.wait_ge(st, pstage_st)
                    for half in range(2):
                        v.tensor_tensor(out=pstage[:, half * 512:half * 512 + 512],
                                        in0=psO[half][0:128, 0:512],
                                        in1=rqb[:, half * 512:half * 512 + 512],
                                        op=OP.subtract)
                    v.tensor_scalar(out=pstage[:], in0=pstage[:],
                                    scalar1=wbuf[:, ti:ti + 1], scalar2=None,
                                    op0=OP.mult).then_inc(cmp, 1)
                    c_cmp.inc()
                    psO_evac = c_cmp.v
                    g.wait_ge(cmp, c_cmp.v)
                    g.dma_start(out=part_out[row0:row0 + 128, :],
                                in_=pstage[:]).then_inc(st, 16)
                    c_st.inc(16)
                    pstage_st = c_st.v
                hcol_free = c_pe.v
            g.dma_start(out=wsl_out, in_=wbuf[:]).then_inc(st, 16)
            c_st.inc(16)
            v.wait_ge(st, c_st.v)
            v.wait_ge(ld, c_ld.v)
    return nc


def run_e1(inputs, c_out, a0):
    xn1 = np.ascontiguousarray(c_out["xn1"])
    iora = np.tile(np.arange(8, dtype=np.float32), (128, 1))
    ci99 = 99.0 - iora
    identf = np.eye(128, dtype=np.float32)
    wr = np.ascontiguousarray(
        inputs["L1_ffn_Wr"].reshape(8, 128, 8).transpose(1, 0, 2).reshape(128, 64))
    in_maps = []
    for c in range(NC):
        w1p = np.ascontiguousarray(
            a0["w1b"][c].reshape(8, 128, 4096).transpose(1, 0, 2).reshape(128, 8 * 4096))
        w2p = np.ascontiguousarray(
            a0["w2b"][c].reshape(32, 128, 1024).transpose(1, 0, 2).reshape(128, 32 * 1024))
        esel = np.zeros((128, 8), dtype=np.float32)
        esel[:, c] = 1.0
        in_maps.append({
            "xn1": xn1, "w1": w1p, "w2": w2p,
            "b1t": np.ascontiguousarray(inputs["L1_ffn_b1"][c].reshape(32, 128).T),
            "b2": inputs["L1_ffn_b2"][c].reshape(1, 1024),
            "wr": wr, "br": inputs["L1_ffn_br"].reshape(1, 8),
            "gam": inputs["L1_ffn_gamma"].reshape(1, 1024),
            "bet": inputs["L1_ffn_beta"].reshape(1, 1024),
            "esel": esel, "iora": iora, "ci99": ci99, "identf": identf,
        })
    nc = build_e1()
    res = run_bass_kernel_spmd(nc, in_maps, core_ids=list(range(NC)))
    parts = [res.results[c]["part"] for c in range(NC)]
    wtok = np.stack([res.results[c]["wsl"].T.reshape(B * N) for c in range(NC)], axis=1)
    cvs = np.stack([res.results[c]["cvec"][0] for c in range(NC)], axis=0)
    return parts, wtok, cvs


def build_e2():
    nc = bass.Bass("TRN2", target_bir_lowering=False)
    x1_in = nc.dram_tensor("x1", [TOK, D], F32, kind="ExternalInput").ap()
    xn1_in = nc.dram_tensor("xn1", [TOK, D], F32, kind="ExternalInput").ap()
    attn_in = nc.dram_tensor("attn", [TOK, D], BF16, kind="ExternalInput").ap()
    wo_in = nc.dram_tensor("wo", [128, 8 * 1024], BF16, kind="ExternalInput").ap()
    p_in = [nc.dram_tensor(f"p{e}", [TOK, D], F32, kind="ExternalInput").ap()
            for e in range(8)]
    wtok_in = nc.dram_tensor("wtok", [TOK, 8], F32, kind="ExternalInput").ap()
    cvs_in = nc.dram_tensor("cvs", [8, 1024], F32, kind="ExternalInput").ap()
    identb_in = nc.dram_tensor("identb", [128, 128], BF16, kind="ExternalInput").ap()
    fin_out = nc.dram_tensor("fin", [TOK, D], F32, kind="ExternalOutput").ap()

    with ExitStack() as ctx:
        E = ctx.enter_context
        wosb = E(nc.sbuf_tensor("wosbF", [128, 8 * 1024], BF16))
        xt = E(nc.sbuf_tensor("xtF", [128, 1024], F32))
        xnt = E(nc.sbuf_tensor("xntF", [128, 1024], F32))
        att = E(nc.sbuf_tensor("attF", [128, 1024], BF16))
        colb = E(nc.sbuf_tensor("colbF2", [128, 1024], BF16))
        st1 = E(nc.sbuf_tensor("st1F", [128, 1024], F32))
        st2 = E(nc.sbuf_tensor("st2F", [128, 1024], F32))
        acc = E(nc.sbuf_tensor("accF", [128, 1024], F32))
        pt2 = [E(nc.sbuf_tensor(f"pt2F{i}", [128, 1024], F32)) for i in range(2)]
        wt = E(nc.sbuf_tensor("wtF", [128, 8], F32))
        ctb = E(nc.sbuf_tensor("ctbF", [128, 1024], F32))
        ctv = E(nc.sbuf_tensor("ctvF", [1, 1024], F32))
        cvsb = E(nc.sbuf_tensor("cvsbF", [8, 1024], F32))
        onescol = E(nc.sbuf_tensor("onescolF", [8, 1], F32))
        ones_row = E(nc.sbuf_tensor("onesF", [1, 128], F32))
        identbt = E(nc.sbuf_tensor("identbF", [128, 128], BF16))
        junk = E(nc.sbuf_tensor("junkF", [128, 512], F32))
        sm = E(nc.sbuf_tensor("smF", [128, 8], F32))
        pstr = E(nc.psum_tensor("pstrF", [128, 512], F32))
        psS = [E(nc.psum_tensor(f"psSF{i}", [128, 512], F32)) for i in range(2)]
        ld = E(nc.semaphore("ld"))
        cmp = E(nc.semaphore("cmp"))
        pe = E(nc.semaphore("pe"))
        st = E(nc.semaphore("st"))
        c_ld, c_cmp, c_pe, c_st = Ctr(), Ctr(), Ctr(), Ctr()
        block = E(nc.Block())

        @block.sync
        def _(sync):
            v, t, sc, g = nc.vector, nc.tensor, nc.scalar, nc.gpsimd
            sync.dma_start(out=identbt[:], in_=identb_in).then_inc(ld, 16)
            sync.dma_start(out=cvsb[:], in_=cvs_in).then_inc(ld, 16)
            sync.dma_start(out=wosb[:], in_=wo_in).then_inc(ld, 16)
            c_ld.inc(48)
            v.wait_ge(ld, c_ld.v)
            v.memset(ones_row[:], 1.0)
            v.memset(onescol[0:8, 0:1], 1.0).then_inc(cmp, 1)
            c_cmp.inc()
            t.wait_ge(ld, c_ld.v)
            t.wait_ge(cmp, c_cmp.v)
            for half in range(2):
                t.matmul(out=pstr[0:1, 0:512], lhsT=onescol[0:8, 0:1],
                         rhs=cvsb[0:8, half * 512:(half + 1) * 512],
                         start=True, stop=True).then_inc(pe, 1)
                c_pe.inc()
                sc.wait_ge(pe, c_pe.v)
                sc.copy(out=ctv[0:1, half * 512:(half + 1) * 512],
                        in_=pstr[0:1, 0:512]).then_inc(cmp, 1)
                c_cmp.inc()
                t.wait_ge(cmp, c_cmp.v)
            for half in range(2):
                t.matmul(out=pstr[0:128, 0:512], lhsT=ones_row[0:1, 0:128],
                         rhs=ctv[0:1, half * 512:(half + 1) * 512],
                         start=True, stop=True).then_inc(pe, 1)
                c_pe.inc()
                sc.wait_ge(pe, c_pe.v)
                sc.copy(out=ctb[:, half * 512:(half + 1) * 512],
                        in_=pstr[0:128, 0:512]).then_inc(cmp, 1)
                c_cmp.inc()
                t.wait_ge(cmp, c_cmp.v)

            in_free = 0
            st_done = 0
            for i in range(NT):
                r0 = i * 128
                sync.wait_ge(cmp, in_free)
                sync.dma_start(out=att[:], in_=attn_in[r0:r0 + 128, :]).then_inc(ld, 16)
                sync.dma_start(out=xt[:], in_=x1_in[r0:r0 + 128, :]).then_inc(ld, 16)
                sync.dma_start(out=xnt[:], in_=xn1_in[r0:r0 + 128, :]).then_inc(ld, 16)
                sync.dma_start(out=wt[:], in_=wtok_in[r0:r0 + 128, :]).then_inc(ld, 16)
                sync.dma_start(out=pt2[0][:], in_=p_in[0][r0:r0 + 128, :]).then_inc(ld, 16)
                c_ld.inc(80)
                t.wait_ge(ld, c_ld.v - 64)
                for cch in range(8):
                    t.matmul(out=pstr[0:128, 0:128], lhsT=att[:, cch * 128:(cch + 1) * 128],
                             rhs=identbt[0:128, 0:128], start=True, stop=True).then_inc(pe, 1)
                    c_pe.inc()
                    sc.wait_ge(pe, c_pe.v)
                    sc.copy(out=colb[:, cch * 128:(cch + 1) * 128], in_=pstr[0:128, 0:128])
                    sc.activation(out=junk[:, 0:128], in_=colb[:, cch * 128:(cch + 1) * 128],
                                  func=AF.Abs).then_inc(cmp, 1)
                    c_cmp.inc()
                    t.wait_ge(cmp, c_cmp.v)
                for half in range(2):
                    for kc in range(8):
                        t.matmul(out=psS[half][0:128, 0:512],
                                 lhsT=colb[:, kc * 128:(kc + 1) * 128],
                                 rhs=wosb[:, kc * 1024 + half * 512:kc * 1024 + half * 512 + 512],
                                 start=(kc == 0), stop=(kc == 7)).then_inc(pe, 1)
                        c_pe.inc()
                v.wait_ge(pe, c_pe.v)
                add_done = []
                for e in range(1, 8):
                    if e >= 2:
                        sync.wait_ge(cmp, add_done[e - 3] if e >= 3 else add_done[0])
                    sync.dma_start(out=pt2[e % 2][:],
                                   in_=p_in[e][r0:r0 + 128, :]).then_inc(ld, 16)
                    c_ld.inc(16)
                    v.wait_ge(ld, c_ld.v)
                    if e == 1:
                        v.tensor_tensor(out=acc[:], in0=pt2[0][:], in1=pt2[1][:],
                                        op=OP.add).then_inc(cmp, 1)
                    else:
                        v.tensor_tensor(out=acc[:], in0=acc[:], in1=pt2[e % 2][:],
                                        op=OP.add).then_inc(cmp, 1)
                    c_cmp.inc()
                    add_done.append(c_cmp.v)
                v.tensor_reduce(out=sm[:, 0:1], in_=wt[:], axis=AX.X, op=OP.add).then_inc(cmp, 1)
                v.wait_ge(cmp, c_cmp.inc())
                v.tensor_scalar(out=st2[:], in0=ctb[:], scalar1=sm[:, 0:1], scalar2=None,
                                op0=OP.mult).then_inc(cmp, 1)
                v.wait_ge(cmp, c_cmp.inc())
                v.tensor_tensor(out=st2[:], in0=st2[:], in1=acc[:], op=OP.add)
                v.tensor_tensor(out=st2[:], in0=st2[:], in1=xnt[:], op=OP.add).then_inc(cmp, 1)
                c_cmp.inc()
                sc.wait_ge(cmp, c_cmp.v)
                sc.activation(out=acc[:], in_=st2[:], func=AF.Silu).then_inc(cmp, 1)
                c_cmp.inc()
                v.wait_ge(pe, c_pe.v)
                for half in range(2):
                    v.tensor_copy(out=st1[:, half * 512:half * 512 + 512],
                                  in_=psS[half][0:128, 0:512])
                v.wait_ge(cmp, c_cmp.v)
                v.wait_ge(st, st_done)
                v.tensor_tensor(out=st1[:], in0=st1[:], in1=acc[:], op=OP.mult)
                v.tensor_tensor(out=st1[:], in0=st1[:], in1=xt[:], op=OP.add).then_inc(cmp, 1)
                c_cmp.inc()
                in_free = c_cmp.v
                g.wait_ge(cmp, c_cmp.v)
                g.dma_start(out=fin_out[r0:r0 + 128, :], in_=st1[:]).then_inc(st, 16)
                c_st.inc(16)
                st_done = c_st.v
            v.wait_ge(st, c_st.v)
            v.wait_ge(ld, c_ld.v)
    return nc


def run_e2(inputs, c_out, attn, parts, wtok, cvs):
    identb = np_identity_bf16()
    wob = np.ascontiguousarray(
        inputs["L1_attn_W_o"].astype(BF).reshape(8, 128, 1024).transpose(1, 0, 2)
        .reshape(128, 8 * 1024))
    in_maps = []
    for c in range(NC):
        r0 = c * TOK
        m = {
            "x1": np.ascontiguousarray(c_out["x1"][r0:r0 + TOK]),
            "xn1": np.ascontiguousarray(c_out["xn1"][r0:r0 + TOK]),
            "attn": np.ascontiguousarray(attn[r0:r0 + TOK]),
            "wo": wob, "wtok": np.ascontiguousarray(wtok[r0:r0 + TOK]),
            "cvs": cvs, "identb": identb,
        }
        for e in range(8):
            m[f"p{e}"] = np.ascontiguousarray(parts[e][r0:r0 + TOK])
        in_maps.append(m)
    nc = build_e2()
    res = run_bass_kernel_spmd(nc, in_maps, core_ids=list(range(NC)))
    fin = np.concatenate([res.results[c]["fin"] for c in range(NC)], axis=0)
    return fin.reshape(B, N, D)


# version 02
# speedup vs baseline: 1.0739x; 1.0739x over previous
import numpy as np
import ml_dtypes
from contextlib import ExitStack

import concourse.bass as bass
import concourse.mybir as mybir
from concourse.bass_utils import run_bass_kernel_spmd
from concourse.alu_op_type import AluOpType as OP

F32 = mybir.dt.float32
BF16 = mybir.dt.bfloat16
I32 = mybir.dt.int32
AF = mybir.ActivationFunctionType
AX = mybir.AxisListType
BF = ml_dtypes.bfloat16

NC = 8
B, N, D = 2, 8192, 1024
H, DH = 16, 64
I_FFN = 2816


def np_identity_bf16(n=128):
    return np.eye(n, dtype=BF)


class Ctr:
    def __init__(self):
        self.v = 0

    def inc(self, n=1):
        self.v += n
        return self.v


def emit_fp4_quant(nc, x_sb, out_sb, scr, P, W, cmp, c_cmp, done_sem=None):
    v, sc = nc.vector, nc.scalar
    nb = W // 16
    amax, s_t, r_t = scr["amax"], scr["s"], scr["r"]
    u, au, sg = scr["u"], scr["au"], scr["sg"]
    t1, t2, ti = scr["t1"], scr["t2"], scr["ti"]
    x3 = x_sb[0:P, 0:W].rearrange("p (a b) -> p a b", b=16)
    v.tensor_reduce(out=amax[0:P, 0:nb], in_=x3, axis=AX.X, op=OP.max,
                    apply_absolute_value=True).then_inc(cmp, 1)
    v.wait_ge(cmp, c_cmp.inc())
    v.tensor_scalar(out=s_t[0:P, 0:nb], in0=amax[0:P, 0:nb], scalar1=1.0 / 6.0,
                    scalar2=1e-12, op0=OP.mult, op1=OP.max).then_inc(cmp, 1)
    v.wait_ge(cmp, c_cmp.inc())
    v.reciprocal(out=r_t[0:P, 0:nb], in_=s_t[0:P, 0:nb]).then_inc(cmp, 1)
    v.wait_ge(cmp, c_cmp.inc())
    rb = r_t[0:P, 0:nb].unsqueeze(2).broadcast_to([P, nb, 16])
    v.tensor_tensor(out=u[0:P, 0:W], in0=x3, in1=rb, op=OP.mult).then_inc(cmp, 1)
    c_cmp.inc()
    sc.wait_ge(cmp, c_cmp.v)
    sc.activation(out=au[0:P, 0:W], in_=u[0:P, 0:W], func=AF.Abs)
    sc.activation(out=sg[0:P, 0:W], in_=u[0:P, 0:W], func=AF.Sign).then_inc(cmp, 1)
    c_cmp.inc()
    v.wait_ge(cmp, c_cmp.v)
    MAGIC = float(2 ** 22)
    v.tensor_scalar(out=t1[0:P, 0:W], in0=au[0:P, 0:W], scalar1=MAGIC, scalar2=-MAGIC,
                    op0=OP.add, op1=OP.add)
    v.tensor_scalar(out=t1[0:P, 0:W], in0=t1[0:P, 0:W], scalar1=2.0, scalar2=None,
                    op0=OP.min)
    v.scalar_tensor_tensor(out=t2[0:P, 0:W], in0=au[0:P, 0:W], scalar=2.5,
                           in1=t1[0:P, 0:W], op0=OP.is_gt, op1=OP.add)
    v.scalar_tensor_tensor(out=t1[0:P, 0:W], in0=au[0:P, 0:W], scalar=3.5,
                           in1=t2[0:P, 0:W], op0=OP.is_gt, op1=OP.add)
    v.scalar_tensor_tensor(out=t2[0:P, 0:W], in0=au[0:P, 0:W], scalar=5.0,
                           in1=t1[0:P, 0:W], op0=OP.is_gt, op1=OP.add)
    v.scalar_tensor_tensor(out=t1[0:P, 0:W], in0=au[0:P, 0:W], scalar=5.0,
                           in1=t2[0:P, 0:W], op0=OP.is_gt, op1=OP.add)
    v.tensor_tensor(out=t2[0:P, 0:W], in0=t1[0:P, 0:W], in1=sg[0:P, 0:W], op=OP.mult)
    sb = s_t[0:P, 0:nb].unsqueeze(2).broadcast_to([P, nb, 16])
    inst = v.tensor_tensor(out=out_sb[0:P, 0:W].rearrange("p (a b) -> p a b", b=16),
                           in0=t2[0:P, 0:W].rearrange("p (a b) -> p a b", b=16),
                           in1=sb, op=OP.mult)
    if done_sem is not None:
        inst.then_inc(done_sem, 1)


def mk_quant_scratch(nc, ctx, P, W, tag):
    E = ctx.enter_context
    nb = W // 16
    return {
        "amax": E(nc.sbuf_tensor(f"{tag}_amax", [P, nb], F32)),
        "s": E(nc.sbuf_tensor(f"{tag}_s", [P, nb], F32)),
        "r": E(nc.sbuf_tensor(f"{tag}_r", [P, nb], F32)),
        "u": E(nc.sbuf_tensor(f"{tag}_u", [P, W], F32)),
        "au": E(nc.sbuf_tensor(f"{tag}_au", [P, W], F32)),
        "sg": E(nc.sbuf_tensor(f"{tag}_sg", [P, W], F32)),
        "t1": E(nc.sbuf_tensor(f"{tag}_t1", [P, W], F32)),
        "t2": E(nc.sbuf_tensor(f"{tag}_t2", [P, W], F32)),
        "ti": E(nc.sbuf_tensor(f"{tag}_ti", [P, W], I32)),
    }


def build_a0():
    nc = bass.Bass("TRN2", target_bir_lowering=False)
    wg = nc.dram_tensor("wg", [352, 1024], F32, kind="ExternalInput").ap()
    wu = nc.dram_tensor("wu", [352, 1024], F32, kind="ExternalInput").ap()
    wd = nc.dram_tensor("wd", [128, I_FFN], F32, kind="ExternalInput").ap()
    w1 = nc.dram_tensor("w1", [1024, 4096], F32, kind="ExternalInput").ap()
    w2 = nc.dram_tensor("w2", [4096, 1024], F32, kind="ExternalInput").ap()
    ident_in = nc.dram_tensor("ident", [128, 128], BF16, kind="ExternalInput").ap()
    wgqT = nc.dram_tensor("wgqT", [1024, 352], BF16, kind="ExternalOutput").ap()
    wuqT = nc.dram_tensor("wuqT", [1024, 352], BF16, kind="ExternalOutput").ap()
    wdqT = nc.dram_tensor("wdqT", [I_FFN, 128], BF16, kind="ExternalOutput").ap()
    w1b = nc.dram_tensor("w1b", [1024, 4096], BF16, kind="ExternalOutput").ap()
    w2b = nc.dram_tensor("w2b", [4096, 1024], BF16, kind="ExternalOutput").ap()

    tiles = []
    for src, dst, rts, W in ((wg, wgqT, (128, 128, 96), 1024),
                             (wu, wuqT, (128, 128, 96), 1024),
                             (wd, wdqT, (128,), I_FFN)):
        r0 = 0
        for P in rts:
            tiles.append((src, r0, P, W, dst))
            r0 += P

    with ExitStack() as ctx:
        E = ctx.enter_context
        ident = E(nc.sbuf_tensor("identt", [128, 128], BF16))
        xin = [E(nc.sbuf_tensor(f"xin{b}", [128, I_FFN], F32)) for b in range(2)]
        qo = [E(nc.sbuf_tensor(f"qo{b}", [128, I_FFN], BF16)) for b in range(2)]
        scr = mk_quant_scratch(nc, ctx, 128, I_FFN, "qs")
        tpo = [E(nc.sbuf_tensor(f"tpo{b}", [128, 128], BF16)) for b in range(2)]
        cvt = [E(nc.sbuf_tensor(f"cvt{b}", [128, 4096], BF16)) for b in range(2)]
        ps = [E(nc.psum_tensor(f"ps{b}", [128, 512], F32)) for b in range(2)]
        ld = E(nc.semaphore("ld"))
        cmp = E(nc.semaphore("cmp"))
        qd = E(nc.semaphore("qd"))
        pe = E(nc.semaphore("pe"))
        ev = E(nc.semaphore("ev"))
        st = E(nc.semaphore("st"))
        ld2 = E(nc.semaphore("ld2"))
        st2 = E(nc.semaphore("st2"))
        block = E(nc.Block())

        nsl_of = [t[3] // 128 for t in tiles]
        slice_base = np.cumsum([0] + nsl_of).tolist()

        @block.sync
        def _(sync):
            v, t, sc, g = nc.vector, nc.tensor, nc.scalar, nc.gpsimd
            sync.dma_start(out=ident[:], in_=ident_in).then_inc(ld, 16)
            for i, (src, r0, P, W, dst) in enumerate(tiles):
                if i >= 2:
                    sync.wait_ge(qd, i - 1)
                sync.dma_start(out=xin[i % 2][0:P, 0:W],
                               in_=src[r0:r0 + P, :]).then_inc(ld, 16)
            c_cmp = Ctr()
            for i, (src, r0, P, W, dst) in enumerate(tiles):
                v.wait_ge(ld, 16 + 16 * (i + 1))
                if i >= 1:
                    v.wait_ge(pe, slice_base[i])
                emit_fp4_quant(nc, xin[i % 2], qo[i % 2], scr, P, W, cmp, c_cmp,
                               done_sem=qd)
                t.wait_ge(qd, i + 1)
                nsl = W // 128
                for sl in range(nsl):
                    gsl = slice_base[i] + sl
                    if gsl >= 2:
                        t.wait_ge(ev, gsl - 1)
                    t.matmul(out=ps[gsl % 2][0:128, 0:P],
                             lhsT=qo[i % 2][0:P, sl * 128:(sl + 1) * 128],
                             rhs=ident[0:P, 0:P],
                             start=True, stop=True).then_inc(pe, 1)
                    sc.wait_ge(pe, gsl + 1)
                    if gsl >= 2:
                        sc.wait_ge(st, 16 * (gsl - 1))
                    sc.copy(out=tpo[gsl % 2][0:128, 0:P],
                            in_=ps[gsl % 2][0:128, 0:P]).then_inc(ev, 1)
                    g.wait_ge(ev, gsl + 1)
                    g.dma_start(out=dst[sl * 128:(sl + 1) * 128, r0:r0 + P],
                                in_=tpo[gsl % 2][0:128, 0:P]).then_inc(st, 16)

            nconv = 0
            for src, dst, ntile, Wc in ((w1, w1b, 8, 4096), (w2, w2b, 32, 1024)):
                for i in range(ntile):
                    bidx = nconv % 2
                    if nconv >= 2:
                        g.wait_ge(st2, 16 * (nconv - 1))
                    g.dma_start(out=cvt[bidx][:, 0:Wc],
                                in_=src[i * 128:(i + 1) * 128, :]).then_inc(ld2, 16)
                    g.wait_ge(ld2, 16 * (nconv + 1))
                    g.dma_start(out=dst[i * 128:(i + 1) * 128, :],
                                in_=cvt[bidx][:, 0:Wc]).then_inc(st2, 16)
                    nconv += 1
            g.wait_ge(st2, 16 * nconv)
            g.wait_ge(st, 16 * slice_base[-1])

    return nc


def run_a0(inputs, core_ids=None):
    ident = np_identity_bf16()
    in_maps = []
    for c in range(NC):
        in_maps.append({
            "wg": np.ascontiguousarray(inputs["L0_ffn_w_gate"][c * 352:(c + 1) * 352]),
            "wu": np.ascontiguousarray(inputs["L0_ffn_w_up"][c * 352:(c + 1) * 352]),
            "wd": np.ascontiguousarray(inputs["L0_ffn_w_down"][c * 128:(c + 1) * 128]),
            "w1": np.ascontiguousarray(inputs["L1_ffn_W1"][c]),
            "w2": np.ascontiguousarray(inputs["L1_ffn_W2"][c]),
            "ident": ident,
        })
    nc = build_a0()
    res = run_bass_kernel_spmd(nc, in_maps, core_ids=list(range(NC)))
    wgqT = np.concatenate([r["wgqT"] for r in res.results], axis=1)
    wuqT = np.concatenate([r["wuqT"] for r in res.results], axis=1)
    wdqT = np.concatenate([r["wdqT"] for r in res.results], axis=1)
    w1b = np.stack([r["w1b"] for r in res.results], axis=0)
    w2b = np.stack([r["w2b"] for r in res.results], axis=0)
    return {"wgqT": wgqT, "wuqT": wuqT, "wdqT": wdqT, "w1b": w1b, "w2b": w2b}


def kernel(**inputs):
    a0 = run_a0(inputs)
    a = run_a(inputs, a0)
    o = run_b(inputs, a)
    cc = run_c(inputs, a, o)
    attn = run_d(inputs, cc)
    parts, wtok, cvs = run_e1(inputs, cc, a0)
    return run_e2(inputs, cc, attn, parts, wtok, cvs)


NT = 16
TOK = 2048


def build_a():
    nc = bass.Bass("TRN2", target_bir_lowering=False)
    x_in = nc.dram_tensor("x", [TOK, D], F32, kind="ExternalInput").ap()
    Wq_in = nc.dram_tensor("Wq", [D, D], F32, kind="ExternalInput").ap()
    Wk_in = nc.dram_tensor("Wk", [D, D], F32, kind="ExternalInput").ap()
    Wv_in = nc.dram_tensor("Wv", [D, D], F32, kind="ExternalInput").ap()
    Wf_in = nc.dram_tensor("Wf", [D, H], F32, kind="ExternalInput").ap()
    Wb_in = nc.dram_tensor("Wb", [D, H], F32, kind="ExternalInput").ap()
    gw1_in = nc.dram_tensor("gw1", [D, 64], F32, kind="ExternalInput").ap()
    gw2_in = nc.dram_tensor("gw2", [64, D], F32, kind="ExternalInput").ap()
    delta_in = nc.dram_tensor("delta", [1, H], F32, kind="ExternalInput").ap()
    wgqT_in = nc.dram_tensor("wgqT", [D, I_FFN], BF16, kind="ExternalInput").ap()
    wuqT_in = nc.dram_tensor("wuqT", [D, I_FFN], BF16, kind="ExternalInput").ap()
    wdqT_in = nc.dram_tensor("wdqT", [I_FFN, D], BF16, kind="ExternalInput").ap()
    ident_in = nc.dram_tensor("ident", [128, 128], BF16, kind="ExternalInput").ap()
    identf_in = nc.dram_tensor("identf", [128, 128], F32, kind="ExternalInput").ap()

    q_out = nc.dram_tensor("q", [TOK, D], F32, kind="ExternalOutput").ap()
    k_out = nc.dram_tensor("k", [TOK, D], BF16, kind="ExternalOutput").ap()
    v_out = nc.dram_tensor("v", [TOK, D], BF16, kind="ExternalOutput").ap()
    logf_out = nc.dram_tensor("logf", [TOK, H], F32, kind="ExternalOutput").ap()
    logb_out = nc.dram_tensor("logb", [TOK, H], F32, kind="ExternalOutput").ap()
    gate_out = nc.dram_tensor("gate", [TOK, D], F32, kind="ExternalOutput").ap()
    g_out = nc.dram_tensor("g", [TOK, D], F32, kind="ExternalOutput").ap()

    with ExitStack() as ctx:
        E = ctx.enter_context
        ident = E(nc.sbuf_tensor("identt", [128, 128], BF16))
        identf = E(nc.sbuf_tensor("identft", [128, 128], F32))
        xn_col = E(nc.sbuf_tensor("xn_col", [128, 8 * TOK], BF16))
        xq_col = E(nc.sbuf_tensor("xq_col", [128, 8 * TOK], BF16))
        scr = mk_quant_scratch(nc, ctx, 128, 1024, "qs")
        eps_b = E(nc.sbuf_tensor("eps_b", [128, 1], F32))
        epsln = E(nc.sbuf_tensor("epsln", [128, 1], F32))
        one_b = E(nc.sbuf_tensor("one_b", [128, 1], F32))
        ones_row = E(nc.sbuf_tensor("ones_row", [1, 128], F32))
        delta_t = E(nc.sbuf_tensor("delta_t", [128, H], F32))
        ld = E(nc.semaphore("ld"))
        ldg = E(nc.semaphore("ldg"))
        cmp = E(nc.semaphore("cmp"))
        pe = E(nc.semaphore("pe"))
        ev = E(nc.semaphore("ev"))
        st = E(nc.semaphore("st"))
        c_ld, c_ldg, c_cmp, c_pe, c_ev, c_st = Ctr(), Ctr(), Ctr(), Ctr(), Ctr(), Ctr()
        block = E(nc.Block())

        with ExitStack() as ctxA:
            EA = ctxA.enter_context
            xt = EA(nc.sbuf_tensor("xt", [128, 1024], F32))
            xqt = EA(nc.sbuf_tensor("xqt", [128, 1024], BF16))
            wqkv = EA(nc.sbuf_tensor("wqkv", [128, 3 * 8 * 1024], BF16))
            wcat = EA(nc.sbuf_tensor("wcat", [128, 8 * 96], BF16))
            gw2b = EA(nc.sbuf_tensor("gw2b", [64, 1024], BF16))
            stg = EA(nc.sbuf_tensor("stg", [128, 1024], F32))
            stg2 = EA(nc.sbuf_tensor("stg2", [128, 1024], F32))
            sml = EA(nc.sbuf_tensor("sml", [128, 128], F32))
            smlb = EA(nc.sbuf_tensor("smlb", [128, 128], BF16))
            xg1c = EA(nc.sbuf_tensor("xg1c", [64, TOK], BF16))
            ps = [EA(nc.psum_tensor(f"pA{b}", [128, 512], F32)) for b in range(4)]

            @block.sync
            def _(sync):
                v, t, sc, g = nc.vector, nc.tensor, nc.scalar, nc.gpsimd

                def dma(out, in_, q=sync):
                    q.dma_start(out=out, in_=in_).then_inc(ld, 16)
                    return c_ld.inc(16)

                def dmac(out, in_):
                    g.dma_start(out=out, in_=in_).then_inc(ldg, 16)
                    return c_ldg.inc(16)

                dma(ident[:], ident_in)
                dma(identf[:], identf_in)
                dma(delta_t[0:1, :], delta_in)
                for w_i, (wsrc) in enumerate((Wq_in, Wk_in, Wv_in)):
                    for kc in range(8):
                        dmac(wqkv[:, (w_i * 8 + kc) * 1024:(w_i * 8 + kc) * 1024 + 1024],
                             wsrc[kc * 128:(kc + 1) * 128, :])
                for kc in range(8):
                    dmac(wcat[:, kc * 96:kc * 96 + 16], Wf_in[kc * 128:(kc + 1) * 128, :])
                    dmac(wcat[:, kc * 96 + 16:kc * 96 + 32], Wb_in[kc * 128:(kc + 1) * 128, :])
                    dmac(wcat[:, kc * 96 + 32:kc * 96 + 96], gw1_in[kc * 128:(kc + 1) * 128, :])
                v.memset(eps_b[:], 1e-6)
                v.memset(epsln[:], 1e-5)
                v.memset(one_b[:], 1.0)
                v.memset(ones_row[:], 1.0)
                v.sem_inc(cmp, 1)
                c_cmp.inc()
                dmac(gw2b[:, :], gw2_in)
                wld = c_ldg.v
                t.wait_ge(ld, 48)
                t.wait_ge(cmp, c_cmp.v)
                t.matmul(out=ps[0][0:128, 0:H], lhsT=ones_row[0:1, 0:128],
                         rhs=delta_t[0:1, :], start=True, stop=True).then_inc(pe, 1)
                c_pe.inc()
                v.wait_ge(pe, c_pe.v)
                v.tensor_copy(out=delta_t[:, :], in_=ps[0][0:128, 0:H]).then_inc(cmp, 1)
                c_cmp.inc()

                for i in range(NT):
                    ti = i * 1024
                    if i >= 1:
                        sync.wait_ge(pe, c_pe.v)
                    lv = dma(xt[:], x_in[i * 128:(i + 1) * 128, :])
                    v.wait_ge(ld, lv)
                    v.wait_ge(st, c_st.v)
                    sc.wait_ge(st, c_st.v)
                    if i == 0:
                        v.wait_ge(ldg, wld)
                        t.wait_ge(ldg, wld)
                    am = scr["amax"]
                    v.tensor_reduce(out=am[:, 0:1], in_=xt[:], axis=AX.X, op=OP.add).then_inc(cmp, 1)
                    v.wait_ge(cmp, c_cmp.inc())
                    v.tensor_scalar(out=am[:, 1:2], in0=am[:, 0:1],
                                    scalar1=1.0 / 1024.0, scalar2=None, op0=OP.mult).then_inc(cmp, 1)
                    v.wait_ge(cmp, c_cmp.inc())
                    v.tensor_tensor(out=scr["t2"][:, 0:1024], in0=xt[:],
                                    in1=am[:, 1:2].broadcast_to([128, 1024]),
                                    op=OP.subtract)
                    v.scalar_tensor_tensor(out=scr["u"][:, 0:1024], in0=scr["t2"][:, 0:1024],
                                           scalar=1.0, in1=scr["t2"][:, 0:1024],
                                           op0=OP.mult, op1=OP.mult,
                                           accum_out=am[:, 2:3]).then_inc(cmp, 1)
                    v.wait_ge(cmp, c_cmp.inc())
                    v.tensor_scalar(out=am[:, 3:4], in0=am[:, 2:3],
                                    scalar1=1.0 / 1024.0, scalar2=None, op0=OP.mult).then_inc(cmp, 1)
                    c_cmp.inc()
                    sc.wait_ge(cmp, c_cmp.v)
                    sc.activation(out=am[:, 4:5], in_=am[:, 3:4],
                                  func=AF.Sqrt, bias=epsln[:], scale=1.0).then_inc(cmp, 1)
                    c_cmp.inc()
                    v.wait_ge(cmp, c_cmp.v)
                    v.reciprocal(out=am[:, 5:6], in_=am[:, 4:5]).then_inc(cmp, 1)
                    v.wait_ge(cmp, c_cmp.inc())
                    v.tensor_tensor(out=xt[:], in0=scr["t2"][:, 0:1024],
                                    in1=am[:, 5:6].broadcast_to([128, 1024]),
                                    op=OP.mult).then_inc(cmp, 1)
                    c_cmp.inc()

                    emit_fp4_quant(nc, xt, xqt, scr, 128, 1024, cmp, c_cmp, done_sem=cmp)
                    c_cmp.inc()

                    t.wait_ge(cmp, c_cmp.v)
                    for cch in range(8):
                        t.matmul(out=ps[2][0:128, 0:128], lhsT=xt[:, cch * 128:(cch + 1) * 128],
                                 rhs=identf[:], start=True, stop=True).then_inc(pe, 1)
                        c_pe.inc()
                        sc.wait_ge(pe, c_pe.v)
                        sc.copy(out=xn_col[:, cch * TOK + i * 128: cch * TOK + (i + 1) * 128],
                                in_=ps[2][0:128, 0:128]).then_inc(ev, 1)
                        c_ev.inc()
                        t.wait_ge(ev, c_ev.v)
                        t.matmul(out=ps[3][0:128, 0:128], lhsT=xqt[:, cch * 128:(cch + 1) * 128],
                                 rhs=ident[:], start=True, stop=True).then_inc(pe, 1)
                        c_pe.inc()
                        sc.wait_ge(pe, c_pe.v)
                        sc.copy(out=xq_col[:, cch * TOK + i * 128: cch * TOK + (i + 1) * 128],
                                in_=ps[3][0:128, 0:128]).then_inc(ev, 1)
                        c_ev.inc()
                        t.wait_ge(ev, c_ev.v)

                    for w_i, (dst, is_q, is_norm) in enumerate(((q_out, True, True),
                                                                (k_out, False, True),
                                                                (v_out, False, False))):
                        t.wait_ge(cmp, c_cmp.v)
                        for half in range(2):
                            for kc in range(8):
                                t.matmul(out=ps[half][0:128, 0:512],
                                         lhsT=xn_col[:, kc * TOK + i * 128: kc * TOK + (i + 1) * 128],
                                         rhs=wqkv[:, (w_i * 8 + kc) * 1024 + half * 512:
                                                  (w_i * 8 + kc) * 1024 + half * 512 + 512],
                                         start=(kc == 0), stop=(kc == 7)).then_inc(pe, 1)
                                c_pe.inc()
                            sc.wait_ge(pe, c_pe.v)
                            sc.copy(out=stg[:, half * 512:half * 512 + 512],
                                    in_=ps[half][0:128, 0:512]).then_inc(cmp, 1)
                            c_cmp.inc()
                        v.wait_ge(cmp, c_cmp.v)
                        if is_norm:
                            v.scalar_tensor_tensor(out=stg2[:, 0:1024], in0=stg[:, 0:1024],
                                                   scalar=1.0, in1=stg[:, 0:1024],
                                                   op0=OP.mult, op1=OP.mult)
                            v.tensor_reduce(out=sml[:, 0:H],
                                            in_=stg2[:, 0:1024].rearrange("p (h d) -> p h d", d=64),
                                            axis=AX.X, op=OP.add).then_inc(cmp, 1)
                            v.wait_ge(cmp, c_cmp.inc())
                            v.sem_inc(cmp, 1)
                            c_cmp.inc()
                            sc.wait_ge(cmp, c_cmp.v)
                            sc.activation(out=sml[:, H:2 * H], in_=sml[:, 0:H],
                                          func=AF.Sqrt, bias=eps_b[:], scale=1.0).then_inc(cmp, 1)
                            c_cmp.inc()
                            v.wait_ge(cmp, c_cmp.v)
                            v.reciprocal(out=sml[:, 2 * H:3 * H], in_=sml[:, H:2 * H]).then_inc(cmp, 1)
                            v.wait_ge(cmp, c_cmp.inc())
                            rb = sml[:, 2 * H:3 * H].unsqueeze(2).broadcast_to([128, H, 64])
                            v.tensor_tensor(out=stg2[:, 0:1024].rearrange("p (h d) -> p h d", d=64),
                                            in0=stg[:, 0:1024].rearrange("p (h d) -> p h d", d=64),
                                            in1=rb, op=OP.mult).then_inc(cmp, 1)
                            c_cmp.inc()
                            src_t = stg2
                        else:
                            src_t = stg
                        if is_q:
                            g.wait_ge(cmp, c_cmp.v)
                            g.dma_start(out=dst[i * 128:(i + 1) * 128, :],
                                        in_=src_t[:, 0:1024]).then_inc(st, 16)
                            c_st.inc(16)
                        else:
                            g.wait_ge(cmp, c_cmp.v)
                            g.dma_start(out=dst[i * 128:(i + 1) * 128, :],
                                        in_=src_t[:, 0:1024]).then_inc(st, 16)
                            c_st.inc(16)
                        sc.wait_ge(st, c_st.v)
                        v.wait_ge(st, c_st.v)

                    for kc in range(8):
                        t.matmul(out=ps[2][0:128, 0:96],
                                 lhsT=xn_col[:, kc * TOK + i * 128: kc * TOK + (i + 1) * 128],
                                 rhs=wcat[:, kc * 96:(kc + 1) * 96],
                                 start=(kc == 0), stop=(kc == 7)).then_inc(pe, 1)
                        c_pe.inc()
                    sc.wait_ge(pe, c_pe.v)
                    sc.copy(out=sml[:, 0:96], in_=ps[2][0:128, 0:96]).then_inc(cmp, 1)
                    c_cmp.inc()
                    v.wait_ge(cmp, c_cmp.v)
                    v.tensor_tensor(out=sml[:, 96:96 + H], in0=sml[:, 0:H], in1=delta_t[:, 0:H],
                                    op=OP.add).then_inc(cmp, 1)
                    v.wait_ge(cmp, c_cmp.inc())
                    v.sem_inc(cmp, 1)
                    c_cmp.inc()
                    for (c0, dsttens) in ((96, logf_out), (16, logb_out)):
                        sc.wait_ge(cmp, c_cmp.v)
                        sc.activation(out=sml[:, 112:112 + H], in_=sml[:, c0:c0 + H],
                                      func=AF.Abs).then_inc(cmp, 1)
                        sc.wait_ge(cmp, c_cmp.inc())
                        sc.activation(out=sml[:, 112:112 + H], in_=sml[:, 112:112 + H],
                                      func=AF.Exp, scale=-1.0).then_inc(cmp, 1)
                        sc.wait_ge(cmp, c_cmp.inc())
                        sc.activation(out=sml[:, 112:112 + H], in_=sml[:, 112:112 + H],
                                      func=AF.Ln, bias=one_b[:], scale=1.0).then_inc(cmp, 1)
                        c_cmp.inc()
                        v.wait_ge(cmp, c_cmp.v)
                        v.wait_ge(st, c_st.v)
                        v.tensor_scalar(out=sml[:, 0:H], in0=sml[:, c0:c0 + H],
                                        scalar1=-1.0, scalar2=0.0, op0=OP.mult, op1=OP.max).then_inc(cmp, 1)
                        v.wait_ge(cmp, c_cmp.inc())
                        v.scalar_tensor_tensor(out=sml[:, 0:H], in0=sml[:, 0:H], scalar=-1.0,
                                               in1=sml[:, 112:112 + H], op0=OP.bypass, op1=OP.add).then_inc(cmp, 1)
                        v.wait_ge(cmp, c_cmp.inc())
                        v.tensor_scalar(out=sml[:, 0:H], in0=sml[:, 0:H], scalar1=-1.0,
                                        scalar2=None, op0=OP.mult).then_inc(cmp, 1)
                        c_cmp.inc()
                        g.wait_ge(cmp, c_cmp.v)
                        g.dma_start(out=dsttens[i * 128:(i + 1) * 128, :],
                                    in_=sml[:, 0:H]).then_inc(st, 16)
                        c_st.inc(16)
                    v.wait_ge(cmp, c_cmp.v)
                    v.tensor_copy(out=smlb[:, 0:64], in_=sml[:, 32:96]).then_inc(cmp, 1)
                    c_cmp.inc()
                    t.wait_ge(cmp, c_cmp.v)
                    t.matmul(out=ps[3][0:64, 0:128], lhsT=smlb[0:128, 0:64], rhs=ident[:],
                             start=True, stop=True).then_inc(pe, 1)
                    c_pe.inc()
                    sc.wait_ge(pe, c_pe.v)
                    sc.copy(out=xg1c[0:64, i * 128:(i + 1) * 128],
                            in_=ps[3][0:64, 0:128]).then_inc(ev, 1)
                    c_ev.inc()
                    t.wait_ge(ev, c_ev.v)

                    sc.wait_ge(st, c_st.v)
                    t.wait_ge(cmp, c_cmp.v)
                    for half in range(2):
                        t.matmul(out=ps[half][0:128, 0:512],
                                 lhsT=xg1c[0:64, i * 128:(i + 1) * 128],
                                 rhs=gw2b[0:64, half * 512:half * 512 + 512],
                                 start=True, stop=True).then_inc(pe, 1)
                        c_pe.inc()
                        sc.wait_ge(pe, c_pe.v)
                        sc.activation(out=stg[:, half * 512:half * 512 + 512],
                                      in_=ps[half][0:128, 0:512],
                                      func=AF.Sigmoid).then_inc(cmp, 1)
                        c_cmp.inc()
                    g.wait_ge(cmp, c_cmp.v)
                    g.dma_start(out=gate_out[i * 128:(i + 1) * 128, :],
                                in_=stg[:, 0:1024]).then_inc(st, 16)
                    c_st.inc(16)
                    t.wait_ge(st, c_st.v)

        with ExitStack() as ctxB:
            EB = ctxB.enter_context
            g_acc = EB(nc.sbuf_tensor("g_acc", [128, NT * 1024], F32))
            wgu = EB(nc.sbuf_tensor("wgu", [128, 8 * 1024], BF16))
            wdc = EB(nc.sbuf_tensor("wdc", [128, 4 * 1024], BF16))
            hq_t = EB(nc.sbuf_tensor("hq_t", [128, 512], BF16))
            hf_t = EB(nc.sbuf_tensor("hf_t", [128, 512], F32))
            hcol = EB(nc.sbuf_tensor("hcol", [128, 512], BF16))
            sil = EB(nc.sbuf_tensor("sil", [128, 512], F32))
            psB = [EB(nc.psum_tensor(f"pB{b}", [128, 512], F32)) for b in range(6)]

            @block.sync
            def _(sync):
                v, t, sc, g = nc.vector, nc.tensor, nc.scalar, nc.gpsimd
                sync.wait_ge(st, c_st.v)
                sync.wait_ge(pe, c_pe.v)
                v.wait_ge(st, c_st.v)
                v.wait_ge(pe, c_pe.v)
                for i in range(NT):
                    v.memset(g_acc[:, i * 1024:(i + 1) * 1024], 0.0)
                v.sem_inc(cmp, 1)
                c_cmp.inc()
                zdone = c_cmp.v
                nchunks = [(n * 512, 512) for n in range(5)] + [(2560, 256)]
                for nci, (n0, nw) in enumerate(nchunks):
                    if nci >= 1:
                        sync.wait_ge(pe, c_pe.v)
                    lv0 = c_ld.v
                    for kc in range(8):
                        sync.dma_start(out=wgu[:, kc * 1024:kc * 1024 + nw],
                                       in_=wgqT_in[kc * 128:(kc + 1) * 128, n0:n0 + nw]).then_inc(ld, 16)
                        c_ld.inc(16)
                        sync.dma_start(out=wgu[:, kc * 1024 + 512:kc * 1024 + 512 + nw],
                                       in_=wuqT_in[kc * 128:(kc + 1) * 128, n0:n0 + nw]).then_inc(ld, 16)
                        c_ld.inc(16)
                    nsl = nw // 128
                    for sl in range(nsl):
                        sync.dma_start(out=wdc[:, sl * 1024:(sl + 1) * 1024],
                                       in_=wdqT_in[n0 + sl * 128:n0 + (sl + 1) * 128, :]).then_inc(ld, 16)
                        c_ld.inc(16)
                    t.wait_ge(ld, c_ld.v)
                    if nci == 0:
                        v.wait_ge(cmp, zdone)
                    for i in range(NT):
                        v.wait_ge(pe, c_pe.v)
                        for kc in range(8):
                            t.matmul(out=psB[0][0:128, 0:nw],
                                     lhsT=xq_col[:, kc * TOK + i * 128: kc * TOK + (i + 1) * 128],
                                     rhs=wgu[:, kc * 1024:kc * 1024 + nw],
                                     start=(kc == 0), stop=(kc == 7)).then_inc(pe, 1)
                            c_pe.inc()
                        for kc in range(8):
                            t.matmul(out=psB[1][0:128, 0:nw],
                                     lhsT=xq_col[:, kc * TOK + i * 128: kc * TOK + (i + 1) * 128],
                                     rhs=wgu[:, kc * 1024 + 512:kc * 1024 + 512 + nw],
                                     start=(kc == 0), stop=(kc == 7)).then_inc(pe, 1)
                            c_pe.inc()
                        sc.wait_ge(pe, c_pe.v)
                        sc.activation(out=sil[:, 0:nw], in_=psB[0][0:128, 0:nw],
                                      func=AF.Silu).then_inc(cmp, 1)
                        c_cmp.inc()
                        v.wait_ge(cmp, c_cmp.v)
                        v.tensor_tensor(out=hf_t[:, 0:nw], in0=sil[:, 0:nw],
                                        in1=psB[1][0:128, 0:nw], op=OP.mult).then_inc(cmp, 1)
                        c_cmp.inc()
                        emit_fp4_quant(nc, hf_t, hq_t, scr, 128, nw, cmp, c_cmp, done_sem=cmp)
                        c_cmp.inc()
                        t.wait_ge(cmp, c_cmp.v)
                        for sl in range(nsl):
                            t.matmul(out=psB[2][0:128, 0:128],
                                     lhsT=hq_t[:, sl * 128:(sl + 1) * 128],
                                     rhs=ident[:], start=True, stop=True).then_inc(pe, 1)
                            c_pe.inc()
                            sc.wait_ge(pe, c_pe.v)
                            sc.copy(out=hcol[:, 0:128], in_=psB[2][0:128, 0:128]).then_inc(ev, 1)
                            c_ev.inc()
                            t.wait_ge(ev, c_ev.v)
                            for half in range(2):
                                t.matmul(out=psB[3 + half][0:128, 0:512],
                                         lhsT=hcol[:, 0:128],
                                         rhs=wdc[:, sl * 1024 + half * 512:sl * 1024 + half * 512 + 512],
                                         start=(sl == 0), stop=(sl == nsl - 1)).then_inc(pe, 1)
                                c_pe.inc()
                        v.wait_ge(pe, c_pe.v)
                        for half in range(2):
                            v.tensor_tensor(out=g_acc[:, i * 1024 + half * 512:i * 1024 + half * 512 + 512],
                                            in0=g_acc[:, i * 1024 + half * 512:i * 1024 + half * 512 + 512],
                                            in1=psB[3 + half][0:128, 0:512], op=OP.add)
                        v.sem_inc(cmp, 1)
                        c_cmp.inc()
                        t.wait_ge(cmp, c_cmp.v)
                for i in range(NT):
                    sc.wait_ge(cmp, c_cmp.v)
                    sc.activation(out=g_acc[:, i * 1024:(i + 1) * 1024],
                                  in_=g_acc[:, i * 1024:(i + 1) * 1024],
                                  func=AF.Silu).then_inc(cmp, 1)
                    c_cmp.inc()
                    g.wait_ge(cmp, c_cmp.v)
                    g.dma_start(out=g_out[i * 128:(i + 1) * 128, :],
                                in_=g_acc[:, i * 1024:(i + 1) * 1024]).then_inc(st, 16)
                    c_st.inc(16)
                g.wait_ge(st, c_st.v)

    return nc


def run_a(inputs, a0):
    ident = np_identity_bf16()
    identf = np.eye(128, dtype=np.float32)
    x_full = inputs["L0_attn_x"].reshape(B * N, D)
    in_maps = []
    for c in range(NC):
        in_maps.append({
            "x": np.ascontiguousarray(x_full[c * TOK:(c + 1) * TOK]),
            "Wq": inputs["L0_attn_Wq"], "Wk": inputs["L0_attn_Wk"], "Wv": inputs["L0_attn_Wv"],
            "Wf": inputs["L0_attn_Wf"], "Wb": inputs["L0_attn_Wbet"],
            "gw1": inputs["L0_attn_gate_w1"], "gw2": inputs["L0_attn_gate_w2"],
            "delta": inputs["L0_attn_delta"].reshape(1, H),
            "wgqT": a0["wgqT"], "wuqT": a0["wuqT"], "wdqT": a0["wdqT"],
            "ident": ident, "identf": identf,
        })
    nc = build_a()
    res = run_bass_kernel_spmd(nc, in_maps, core_ids=list(range(NC)))
    out = {}
    for key in ("q", "k", "v", "logf", "logb", "gate", "g"):
        out[key] = np.concatenate([r[key] for r in res.results], axis=0)
    return out


NBH = 4
NCH = 128
CGIT = 30
GRP = 4
NGRP = 16


def build_b(NBH=NBH, NCH=NCH, NGRP=NGRP, CGIT=CGIT, DBG=False):
    nc = bass.Bass("TRN2", target_bir_lowering=False)
    NTOK = NCH * 64
    q_in = nc.dram_tensor("q", [NBH, NTOK, DH], F32, kind="ExternalInput").ap()
    k_in = nc.dram_tensor("k", [NBH, 64, NCH * 64], BF16, kind="ExternalInput").ap()
    v_in = nc.dram_tensor("v", [NBH, 64, NCH * 64], BF16, kind="ExternalInput").ap()
    lf_in = nc.dram_tensor("lf", [NBH, NTOK], F32, kind="ExternalInput").ap()
    lb_in = nc.dram_tensor("lb", [NBH, NTOK], F32, kind="ExternalInput").ap()
    lp_in = nc.dram_tensor("lp", [DH, NBH], F32, kind="ExternalInput").ap()
    ident_in = nc.dram_tensor("ident", [128, 128], BF16, kind="ExternalInput").ap()
    identf_in = nc.dram_tensor("identf", [128, 128], F32, kind="ExternalInput").ap()
    tri_in = nc.dram_tensor("tri", [128, 64], BF16, kind="ExternalInput").ap()
    id64_in = nc.dram_tensor("id64", [128, 64], BF16, kind="ExternalInput").ap()
    o_out = nc.dram_tensor("o", [NBH, NTOK, DH], F32, kind="ExternalOutput").ap()
    cumscr = nc.dram_tensor("cumscr", [NCH, 64], F32, kind="Internal").ap()
    if DBG:
        dbg_ab = nc.dram_tensor("dbg_ab", [64, 128], F32, kind="ExternalOutput").ap()
        dbg_cum = nc.dram_tensor("dbg_cum", [NCH, 64], F32, kind="ExternalOutput").ap()
        dbg_hall = nc.dram_tensor("dbg_hall", [64, NCH * 128], F32, kind="ExternalOutput").ap()
        dbg_u = nc.dram_tensor("dbg_u", [64, NCH * 128], F32, kind="ExternalOutput").ap()
        dbg_ka = nc.dram_tensor("dbg_ka", [64, NCH * 64], F32, kind="ExternalOutput").ap()
        dbg_va = nc.dram_tensor("dbg_va", [64, NCH * 64], F32, kind="ExternalOutput").ap()
        dbg_m = nc.dram_tensor("dbg_m", [64, NCH * 64], F32, kind="ExternalOutput").ap()
        dbg_kc = nc.dram_tensor("dbg_kc", [64, NCH * 64], F32, kind="ExternalOutput").ap()

    with ExitStack() as ctx:
        E = ctx.enter_context
        ident = E(nc.sbuf_tensor("identt", [128, 128], BF16))
        identf = E(nc.sbuf_tensor("identft", [128, 128], F32))
        tri = E(nc.sbuf_tensor("trit", [128, 64], BF16))
        id64 = E(nc.sbuf_tensor("id64t", [128, 64], BF16))
        ones_row = E(nc.sbuf_tensor("ones_row", [1, 128], F32))
        kaug = E(nc.sbuf_tensor("kaug", [128, NCH * 64], BF16))
        vaug = E(nc.sbuf_tensor("vaug", [128, NCH * 64], BF16))
        kcol = E(nc.sbuf_tensor("kcol", [64, NCH * 64], BF16))
        mall = E(nc.sbuf_tensor("mall", [128, NCH * 64], BF16))
        uall = E(nc.sbuf_tensor("uall", [64, NCH * 128], BF16))
        hall = E(nc.sbuf_tensor("hall", [64, NCH * 128], F32))
        lf_pm = E(nc.sbuf_tensor("lf_pm", [NCH, 64], F32))
        lb_pm = E(nc.sbuf_tensor("lb_pm", [NCH, 64], F32))
        cum_pm = E(nc.sbuf_tensor("cum_pm", [NCH, 64], F32))
        w_pm = E(nc.sbuf_tensor("w_pm", [NCH, 64], F32))
        w_cm = E(nc.sbuf_tensor("w_cm", [128, 128], F32))
        zer = E(nc.sbuf_tensor("zer", [NCH, 64], F32))
        cum_lin = E(nc.sbuf_tensor("cum_lin", [1, NTOK], F32))
        ablin = E(nc.sbuf_tensor("ablin", [1, 128], F32))
        ab = E(nc.sbuf_tensor("ab", [64, 128], F32))
        khat = E(nc.sbuf_tensor("khat", [128, 64], BF16))
        zdiag = E(nc.sbuf_tensor("zdiag", [64, 64], BF16))
        lam_cm = E(nc.sbuf_tensor("lam_cm", [64, 16], F32))
        diag_l = E(nc.sbuf_tensor("diag_l", [64, NBH * 64], BF16))
        Xs = [E(nc.sbuf_tensor(f"X{b}", [128, GRP * 64], F32)) for b in range(2)]
        Rs = [E(nc.sbuf_tensor(f"R{b}", [128, GRP * 64], F32)) for b in range(2)]
        Ps = [E(nc.sbuf_tensor(f"P{b}", [128, GRP * 64], F32)) for b in range(2)]
        pb16 = E(nc.sbuf_tensor("pb16", [128, GRP * 64], BF16))
        pcol = E(nc.sbuf_tensor("pcol", [64, GRP * 128], BF16))
        daug = E(nc.sbuf_tensor("daug", [128, GRP * 128], BF16))
        vball = E(nc.sbuf_tensor("vball", [64, GRP * 128], BF16))
        sm = E(nc.sbuf_tensor("smB", [128, 64], F32))
        o_sb = E(nc.sbuf_tensor("o_sb", [128, GRP * 64], F32))
        ps_T = E(nc.psum_tensor("ps_T", [128, 512], F32))
        ps_C = E(nc.psum_tensor("ps_C", [128, 512], F32))
        ps_AP = E(nc.psum_tensor("ps_AP", [128, 512], F32))
        ps_M = E(nc.psum_tensor("ps_M", [128, 512], F32))
        ps_U = E(nc.psum_tensor("ps_U", [128, 512], F32))
        ps_W = E(nc.psum_tensor("ps_W", [128, 512], F32))
        ld = E(nc.semaphore("ld"))
        ldg = E(nc.semaphore("ldg"))
        cmp = E(nc.semaphore("cmp"))
        pe = E(nc.semaphore("pe"))
        st = E(nc.semaphore("st"))
        c_ld, c_ldg, c_cmp, c_pe, c_st = Ctr(), Ctr(), Ctr(), Ctr(), Ctr()
        block = E(nc.Block())

        @block.sync
        def _(sync):
            v, t, sc, g = nc.vector, nc.tensor, nc.scalar, nc.gpsimd

            def vw():
                v.wait_ge(cmp, c_cmp.v)

            sync.dma_start(out=ident[:], in_=ident_in).then_inc(ld, 16)
            sync.dma_start(out=identf[:], in_=identf_in).then_inc(ld, 16)
            sync.dma_start(out=tri[:], in_=tri_in).then_inc(ld, 16)
            sync.dma_start(out=id64[:], in_=id64_in).then_inc(ld, 16)
            sync.dma_start(out=lam_cm[:, 0:NBH], in_=lp_in).then_inc(ld, 16)
            c_ld.inc(80)
            v.wait_ge(ld, c_ld.v)
            v.memset(ones_row[:], 1.0)
            v.memset(zer[:], 0.0)
            v.memset(zdiag[:], 0.0)
            v.memset(sm[:], 0.0)
            v.sem_inc(cmp, 1); c_cmp.inc()
            sc.wait_ge(ld, c_ld.v)
            sc.activation(out=lam_cm[:, 4:4 + NBH], in_=lam_cm[:, 0:NBH], func=AF.Abs).then_inc(cmp, 1)
            sc.wait_ge(cmp, c_cmp.inc())
            sc.activation(out=lam_cm[:, 8:8 + NBH], in_=lam_cm[:, 4:4 + NBH], func=AF.Exp, scale=-1.0).then_inc(cmp, 1)
            sc.wait_ge(cmp, c_cmp.inc())
            v.memset(sm[:, 63:64], 1.0)
            v.sem_inc(cmp, 1); c_cmp.inc()
            sc.wait_ge(cmp, c_cmp.v)
            sc.activation(out=lam_cm[:, 8:8 + NBH], in_=lam_cm[:, 8:8 + NBH], func=AF.Ln,
                          bias=sm[0:64, 63:64], scale=1.0).then_inc(cmp, 1)
            c_cmp.inc()
            vw()
            v.tensor_scalar(out=lam_cm[:, 12:12 + NBH], in0=lam_cm[:, 0:NBH], scalar1=0.0,
                            scalar2=None, op0=OP.max).then_inc(cmp, 1)
            v.wait_ge(cmp, c_cmp.inc())
            v.tensor_tensor(out=lam_cm[:, 12:12 + NBH], in0=lam_cm[:, 12:12 + NBH],
                            in1=lam_cm[:, 8:8 + NBH], op=OP.add).then_inc(cmp, 1)
            v.wait_ge(cmp, c_cmp.inc())
            v.tensor_scalar(out=lam_cm[:, 12:12 + NBH], in0=lam_cm[:, 12:12 + NBH],
                            scalar1=0.25, scalar2=None, op0=OP.add).then_inc(cmp, 1)
            v.wait_ge(cmp, c_cmp.inc())
            for h in range(NBH):
                v.tensor_scalar(out=diag_l[:, h * 64:(h + 1) * 64],
                                in0=identf[0:64, 0:64],
                                scalar1=lam_cm[:, 12 + h:13 + h], scalar2=None,
                                op0=OP.mult).then_inc(cmp, 1)
                c_cmp.inc()

            for bh in range(NBH):
                base_pe0 = c_pe.v
                lv0 = c_ld.v
                sync.wait_ge(st, c_st.v)
                sync.wait_ge(pe, c_pe.v)
                sync.dma_start(out=kaug[64:128, :], in_=k_in[bh]).then_inc(ld, 16)
                sync.dma_start(out=vaug[64:128, :], in_=v_in[bh]).then_inc(ld, 16)
                sync.dma_start(out=lf_pm[:, :],
                               in_=lf_in[bh].rearrange("(c t) -> c t", t=64)).then_inc(ld, 16)
                sync.dma_start(out=lb_pm[:, :],
                               in_=lb_in[bh].rearrange("(c t) -> c t", t=64)).then_inc(ld, 16)
                c_ld.inc(64)
                v.wait_ge(ld, c_ld.v)
                v.tensor_tensor_scan(out=cum_pm[:], data0=lf_pm[:], data1=zer[:],
                                     initial=0.0, op0=OP.add, op1=OP.add).then_inc(cmp, 1)
                v.wait_ge(cmp, c_cmp.inc())
                v.tensor_tensor(out=w_pm[:], in0=lb_pm[:], in1=cum_pm[:],
                                op=OP.subtract).then_inc(cmp, 1)
                c_cmp.inc()
                t.wait_ge(cmp, c_cmp.v)
                t.matmul(out=ps_W[64:128, 0:NCH], lhsT=w_pm[:, 0:64], rhs=identf[0:NCH, 0:NCH],
                         start=True, stop=True, tile_position=(0, 64)).then_inc(pe, 1)
                c_pe.inc()
                sc.wait_ge(pe, c_pe.v)
                sc.copy(out=w_cm[64:128, 0:NCH], in_=ps_W[64:128, 0:NCH]).then_inc(cmp, 1)
                c_cmp.inc()
                g.wait_ge(cmp, c_cmp.v)
                g.dma_start(out=cumscr, in_=cum_pm[:]).then_inc(ldg, 16)
                g.wait_ge(ldg, c_ldg.inc(16))
                g.dma_start(out=cum_lin[0:1, :],
                            in_=cumscr.rearrange("c t -> (c t)").unsqueeze(0)).then_inc(ldg, 16)
                c_ldg.inc(16)
                g.dma_start(out=ablin[0:1, 0:NCH],
                            in_=cumscr[:, 63:64].rearrange("c o -> (c o)").unsqueeze(0),
                            allow_slow_non_contiguous=True).then_inc(ldg, 16)
                c_ldg.inc(16)
                t.wait_ge(ldg, c_ldg.v)
                t.matmul(out=ps_W[0:64, 0:NCH], lhsT=ones_row[0:1, 0:64],
                         rhs=ablin[0:1, 0:NCH],
                         start=True, stop=True).then_inc(pe, 1)
                c_pe.inc()
                sc.wait_ge(pe, c_pe.v)
                sc.activation(out=ab[:, 0:NCH], in_=ps_W[0:64, 0:NCH], func=AF.Exp).then_inc(cmp, 1)
                c_cmp.inc()
                for mb in range(NCH // 8):
                    t.wait_ge(cmp, c_cmp.v)
                    for j in range(8):
                        ch = mb * 8 + j
                        t.matmul(out=ps_M[64:128, j * 64:(j + 1) * 64],
                                 lhsT=ones_row[0:1, 0:64],
                                 rhs=cum_lin[0:1, ch * 64:(ch + 1) * 64],
                                 start=(j == 0), stop=(j == 7), tile_position=(0, 64)).then_inc(pe, 1)
                        c_pe.inc()
                    sc.wait_ge(pe, c_pe.v)
                    for j in range(8):
                        ch = mb * 8 + j
                        sc.activation(out=mall[64:128, ch * 64:(ch + 1) * 64],
                                      in_=ps_M[64:128, j * 64:(j + 1) * 64], func=AF.Exp,
                                      bias=w_cm[64:128, ch:ch + 1], scale=1.0).then_inc(cmp, 1)
                        c_cmp.inc()
                    v.wait_ge(cmp, c_cmp.v)
                    trib = tri[64:128, :].unsqueeze(1).broadcast_to([64, 8, 64])
                    v.tensor_tensor(out=mall[64:128, mb * 512:(mb + 1) * 512].rearrange(
                                        "p (c t) -> p c t", t=64),
                                    in0=mall[64:128, mb * 512:(mb + 1) * 512].rearrange(
                                        "p (c t) -> p c t", t=64),
                                    in1=trib, op=OP.mult).then_inc(cmp, 1)
                    c_cmp.inc()
                for ch in range(NCH):
                    v.wait_ge(cmp, c_cmp.v)
                    v.wait_ge(pe, c_pe.v)
                    v.tensor_tensor(out=khat[64:128, :], in0=kaug[64:128, ch * 64:(ch + 1) * 64],
                                    in1=mall[64:128, ch * 64 + 63:ch * 64 + 64].broadcast_to([64, 64]),
                                    op=OP.mult).then_inc(cmp, 1)
                    c_cmp.inc()
                    t.wait_ge(cmp, c_cmp.v)
                    if ch >= 1:
                        pass
                    t.matmul(out=ps_U[0:64, 0:64], lhsT=khat[64:128, :],
                             rhs=kaug[64:128, ch * 64:(ch + 1) * 64],
                             start=True, stop=False, tile_position=(64, 0)).then_inc(pe, 1)
                    t.matmul(out=ps_U[0:64, 64:128], lhsT=khat[64:128, :],
                             rhs=vaug[64:128, ch * 64:(ch + 1) * 64],
                             start=False, stop=True, tile_position=(64, 0)).then_inc(pe, 1)
                    c_pe.inc(2)
                    sc.wait_ge(pe, c_pe.v)
                    sc.copy(out=uall[:, ch * 128:(ch + 1) * 128],
                            in_=ps_U[0:64, 0:128]).then_inc(cmp, 1)
                    c_cmp.inc()
                vw()
                v.tensor_copy(out=hall[:, 0:128], in_=uall[:, 0:128]).then_inc(cmp, 1)
                v.wait_ge(cmp, c_cmp.inc())
                for ch in range(1, NCH):
                    v.scalar_tensor_tensor(out=hall[:, ch * 128:ch * 128 + 64],
                                           in0=hall[:, (ch - 1) * 128:(ch - 1) * 128 + 64],
                                           scalar=ab[:, ch:ch + 1],
                                           in1=uall[:, ch * 128:ch * 128 + 64],
                                           op0=OP.mult, op1=OP.add).then_inc(cmp, 1)
                    v.wait_ge(cmp, c_cmp.inc())
                    v.scalar_tensor_tensor(out=hall[:, ch * 128 + 64:(ch + 1) * 128],
                                           in0=hall[:, (ch - 1) * 128 + 64:ch * 128],
                                           scalar=ab[:, ch:ch + 1],
                                           in1=uall[:, ch * 128 + 64:(ch + 1) * 128],
                                           op0=OP.mult, op1=OP.add).then_inc(cmp, 1)
                    v.wait_ge(cmp, c_cmp.inc())
                if DBG and bh == 0:
                    g.wait_ge(cmp, c_cmp.v)
                    g.dma_start(out=dbg_ab, in_=ab[:, :]).then_inc(st, 16)
                    g.dma_start(out=dbg_cum, in_=cum_pm[:, :]).then_inc(st, 16)
                    g.dma_start(out=dbg_hall, in_=hall[:, :]).then_inc(st, 16)
                    g.dma_start(out=dbg_u, in_=uall[:, :]).then_inc(st, 16)
                    c_st.inc(64)
                v.memset(kaug[0:64, 0:64], 0.0)
                v.memset(vaug[0:64, 0:64], 0.0)
                v.sem_inc(cmp, 1); c_cmp.inc()
                g.wait_ge(cmp, c_cmp.v)
                g.dma_start(out=kaug[0:64, 64:NCH * 64],
                            in_=hall[:, 0:(NCH - 1) * 128].rearrange(
                                "p (c z) -> p c z", z=128)[:, :, 0:64]).then_inc(ldg, 16)
                c_ldg.inc(16)
                g.dma_start(out=vaug[0:64, 64:NCH * 64],
                            in_=hall[:, 0:(NCH - 1) * 128].rearrange(
                                "p (c z) -> p c z", z=128)[:, :, 64:128]).then_inc(ldg, 16)
                c_ldg.inc(16)
                if DBG and bh == 0:
                    g.wait_ge(ldg, c_ldg.v)
                    g.dma_start(out=dbg_ka, in_=kaug[0:64, :]).then_inc(st, 16)
                    g.dma_start(out=dbg_va, in_=vaug[0:64, :]).then_inc(st, 16)
                    g.dma_start(out=dbg_m, in_=mall[64:128, :]).then_inc(st, 16)
                    c_st.inc(48)
                t.wait_ge(ldg, c_ldg.v)
                for kb in range(NCH // 8):
                    t.wait_ge(cmp, c_cmp.v)
                    for j in range(8):
                        ch = kb * 8 + j
                        t.matmul(out=ps_T[0:64, j * 64:(j + 1) * 64],
                                 lhsT=kaug[64:128, ch * 64:(ch + 1) * 64],
                                 rhs=id64[64:128, :], start=(j == 0), stop=(j == 7),
                                 tile_position=(64, 0)).then_inc(pe, 1)
                        c_pe.inc()
                    sc.wait_ge(pe, c_pe.v)
                    sc.copy(out=kcol[:, kb * 512:(kb + 1) * 512],
                            in_=ps_T[0:64, 0:512]).then_inc(cmp, 1)
                    c_cmp.inc()
                if DBG and bh == 0:
                    g.wait_ge(cmp, c_cmp.v)
                    g.dma_start(out=dbg_kc, in_=kcol[:, :]).then_inc(st, 16)
                    c_st.inc(16)
                for gi in range(NGRP):
                    g0 = gi * GRP
                    ch0 = g0 * 2
                    tok0 = ch0 * 64
                    t.wait_ge(cmp, c_cmp.v)
                    t.matmul(out=ps_T[0:64, 0:512], lhsT=ones_row[0:1, 0:64],
                             rhs=cum_lin[0:1, tok0:tok0 + 512],
                             start=True, stop=True).then_inc(pe, 1)
                    c_pe.inc()
                    sc.wait_ge(pe, c_pe.v)
                    sc.activation(out=vball[:, :], in_=ps_T[0:64, 0:512], func=AF.Exp).then_inc(cmp, 1)
                    c_cmp.inc()
                    sync.wait_ge(pe, c_pe.v)
                    sync.dma_start(out=Ps[0][:, :].rearrange("p (j d) -> p j d", d=64),
                                   in_=q_in[bh, tok0:tok0 + 512, :].rearrange(
                                       "(j p) d -> p j d", p=128)).then_inc(ld, 16)
                    c_ld.inc(16)
                    v.wait_ge(ld, c_ld.v)
                    v.tensor_copy(out=Rs[0][:, :], in_=Ps[0][:, :])
                    v.memset(Xs[0][:, :], 0.0)
                    for j in range(GRP):
                        inst = v.scalar_tensor_tensor(out=o_sb[:, j * 64:(j + 1) * 64],
                                               in0=Ps[0][:, j * 64:(j + 1) * 64], scalar=1.0,
                                               in1=Ps[0][:, j * 64:(j + 1) * 64],
                                               op0=OP.mult, op1=OP.mult,
                                               accum_out=sm[:, j:j + 1])
                    inst.then_inc(cmp, 1)
                    v.wait_ge(cmp, c_cmp.inc())
                    cur = 0
                    for it in range(CGIT):
                        Pc, Rc, Xc = Ps[cur], Rs[cur], Xs[cur]
                        Pn, Rn, Xn_ = Ps[1 - cur], Rs[1 - cur], Xs[1 - cur]
                        sc.wait_ge(cmp, c_cmp.v)
                        sc.copy(out=pb16[:, :], in_=Pc[:, :])
                        sc.activation(out=o_sb[:, 0:GRP * 64], in_=pb16[:, :], func=AF.Abs).then_inc(cmp, 1)
                        c_cmp.inc()
                        t.wait_ge(cmp, c_cmp.v)
                        for j in range(GRP):
                            t.matmul(out=ps_T[0:64, j * 128:(j + 1) * 128],
                                     lhsT=pb16[:, j * 64:(j + 1) * 64], rhs=ident[:],
                                     start=(j == 0), stop=(j == GRP - 1)).then_inc(pe, 1)
                            c_pe.inc()
                        sc.wait_ge(pe, c_pe.v)
                        sc.copy(out=pcol[:, :], in_=ps_T[0:64, 0:512])
                        sc.activation(out=daug[0:64, 0:GRP * 128], in_=pcol[:, :], func=AF.Abs).then_inc(cmp, 1)
                        c_cmp.inc()
                        v.wait_ge(cmp, c_cmp.v)
                        v.tensor_tensor(out=daug[0:64, :], in0=pcol[:, :], in1=vball[:, :],
                                        op=OP.mult).then_inc(cmp, 1)
                        v.wait_ge(cmp, c_cmp.inc())
                        v.tensor_reduce(out=sm[0:64, 34:35], in_=daug[0:64, :], axis=AX.X,
                                        op=OP.max).then_inc(cmp, 1)
                        c_cmp.inc()
                        t.wait_ge(cmp, c_cmp.v)
                        for cj in range(2 * GRP):
                            ch = ch0 + cj
                            t.matmul(out=ps_C[64:128, cj * 64:(cj + 1) * 64],
                                     lhsT=kcol[:, ch * 64:(ch + 1) * 64],
                                     rhs=pcol[:, cj * 64:(cj + 1) * 64],
                                     start=(cj == 0), stop=(cj == 2 * GRP - 1),
                                     tile_position=(0, 64)).then_inc(pe, 1)
                            c_pe.inc()
                        v.wait_ge(pe, c_pe.v)
                        v.tensor_tensor(out=daug[64:128, :], in0=ps_C[64:128, 0:512],
                                        in1=mall[64:128, tok0:tok0 + 512],
                                        op=OP.mult).then_inc(cmp, 1)
                        v.wait_ge(cmp, c_cmp.inc())
                        v.tensor_reduce(out=sm[64:128, 34:35], in_=daug[64:128, :], axis=AX.X,
                                        op=OP.max).then_inc(cmp, 1)
                        c_cmp.inc()
                        t.wait_ge(cmp, c_cmp.v)
                        t.matmul(out=ps_AP[0:128, 0:64],
                                 lhsT=pcol[:, 0:128],
                                 rhs=diag_l[:, bh * 64:(bh + 1) * 64],
                                 start=True, stop=False).then_inc(pe, 1)
                        c_pe.inc()
                        for cj in range(2 * GRP):
                            ch = ch0 + cj
                            pos = (0, 0) if cj % 2 == 0 else (0, 64)
                            t.matmul(out=ps_AP[(cj % 2) * 64:(cj % 2) * 64 + 64,
                                               (cj // 2) * 64:(cj // 2) * 64 + 64],
                                     lhsT=daug[:, cj * 64:(cj + 1) * 64],
                                     rhs=kaug[:, ch * 64:(ch + 1) * 64],
                                     start=False, stop=False,
                                     tile_position=pos).then_inc(pe, 1)
                            c_pe.inc()
                        for j in range(1, GRP):
                            t.matmul(out=ps_AP[0:128, j * 64:(j + 1) * 64],
                                     lhsT=pcol[:, j * 128:(j + 1) * 128],
                                     rhs=diag_l[:, bh * 64:(bh + 1) * 64],
                                     start=False, stop=(j == GRP - 1)).then_inc(pe, 1)
                            c_pe.inc()
                        v.wait_ge(pe, c_pe.v)
                        for j in range(GRP):
                            inst = v.scalar_tensor_tensor(out=o_sb[:, j * 64:(j + 1) * 64],
                                                   in0=ps_AP[0:128, j * 64:(j + 1) * 64],
                                                   scalar=1.0,
                                                   in1=Pc[:, j * 64:(j + 1) * 64],
                                                   op0=OP.mult, op1=OP.mult,
                                                   accum_out=sm[:, 4 + j:5 + j])
                        inst.then_inc(cmp, 1)
                        v.wait_ge(cmp, c_cmp.inc())
                        v.tensor_scalar(out=sm[:, 16:20], in0=sm[:, 4:8], scalar1=1e-12,
                                        scalar2=None, op0=OP.add).then_inc(cmp, 1)
                        v.wait_ge(cmp, c_cmp.inc())
                        v.reciprocal(out=sm[:, 16:20], in_=sm[:, 16:20]).then_inc(cmp, 1)
                        v.wait_ge(cmp, c_cmp.inc())
                        v.tensor_tensor(out=sm[:, 8:12], in0=sm[:, 0:4], in1=sm[:, 16:20],
                                        op=OP.mult).then_inc(cmp, 1)
                        v.wait_ge(cmp, c_cmp.inc())
                        v.tensor_scalar(out=sm[:, 20:24], in0=sm[:, 8:12], scalar1=-1.0,
                                        scalar2=None, op0=OP.mult).then_inc(cmp, 1)
                        v.wait_ge(cmp, c_cmp.inc())
                        for j in range(GRP):
                            v.scalar_tensor_tensor(out=Xn_[:, j * 64:(j + 1) * 64],
                                                   in0=Pc[:, j * 64:(j + 1) * 64],
                                                   scalar=sm[:, 8 + j:9 + j],
                                                   in1=Xc[:, j * 64:(j + 1) * 64],
                                                   op0=OP.mult, op1=OP.add)
                        for j in range(GRP):
                            inst = v.scalar_tensor_tensor(out=Rn[:, j * 64:(j + 1) * 64],
                                                   in0=ps_AP[0:128, j * 64:(j + 1) * 64],
                                                   scalar=sm[:, 20 + j:21 + j],
                                                   in1=Rc[:, j * 64:(j + 1) * 64],
                                                   op0=OP.mult, op1=OP.add)
                        inst.then_inc(cmp, 1)
                        v.wait_ge(cmp, c_cmp.inc())
                        for j in range(GRP):
                            inst = v.scalar_tensor_tensor(out=o_sb[:, j * 64:(j + 1) * 64],
                                                   in0=Rn[:, j * 64:(j + 1) * 64], scalar=1.0,
                                                   in1=Rn[:, j * 64:(j + 1) * 64],
                                                   op0=OP.mult, op1=OP.mult,
                                                   accum_out=sm[:, 24 + j:25 + j])
                        inst.then_inc(cmp, 1)
                        v.wait_ge(cmp, c_cmp.inc())
                        v.tensor_scalar(out=sm[:, 16:20], in0=sm[:, 0:4], scalar1=1e-12,
                                        scalar2=None, op0=OP.add).then_inc(cmp, 1)
                        v.wait_ge(cmp, c_cmp.inc())
                        v.reciprocal(out=sm[:, 16:20], in_=sm[:, 16:20]).then_inc(cmp, 1)
                        v.wait_ge(cmp, c_cmp.inc())
                        v.tensor_tensor(out=sm[:, 12:16], in0=sm[:, 24:28], in1=sm[:, 16:20],
                                        op=OP.mult).then_inc(cmp, 1)
                        v.wait_ge(cmp, c_cmp.inc())
                        for j in range(GRP):
                            v.scalar_tensor_tensor(out=Pn[:, j * 64:(j + 1) * 64],
                                                   in0=Pc[:, j * 64:(j + 1) * 64],
                                                   scalar=sm[:, 12 + j:13 + j],
                                                   in1=Rn[:, j * 64:(j + 1) * 64],
                                                   op0=OP.mult, op1=OP.add)
                        v.tensor_copy(out=sm[:, 0:4], in_=sm[:, 24:28]).then_inc(cmp, 1)
                        v.wait_ge(cmp, c_cmp.inc())
                        cur = 1 - cur
                    Xc = Xs[cur]
                    sc.wait_ge(cmp, c_cmp.v)
                    sc.copy(out=pb16[:, :], in_=Xc[:, :])
                    sc.activation(out=o_sb[:, 0:GRP * 64], in_=pb16[:, :], func=AF.Abs).then_inc(cmp, 1)
                    c_cmp.inc()
                    t.wait_ge(cmp, c_cmp.v)
                    for j in range(GRP):
                        t.matmul(out=ps_T[0:64, j * 128:(j + 1) * 128],
                                 lhsT=pb16[:, j * 64:(j + 1) * 64], rhs=ident[:],
                                 start=(j == 0), stop=(j == GRP - 1)).then_inc(pe, 1)
                        c_pe.inc()
                    sc.wait_ge(pe, c_pe.v)
                    sc.copy(out=pcol[:, :], in_=ps_T[0:64, 0:512])
                    sc.activation(out=daug[0:64, 0:GRP * 128], in_=pcol[:, :], func=AF.Abs).then_inc(cmp, 1)
                    c_cmp.inc()
                    v.wait_ge(cmp, c_cmp.v)
                    v.tensor_tensor(out=daug[0:64, :], in0=pcol[:, :], in1=vball[:, :],
                                    op=OP.mult).then_inc(cmp, 1)
                    v.wait_ge(cmp, c_cmp.inc())
                    v.tensor_reduce(out=sm[0:64, 34:35], in_=daug[0:64, :], axis=AX.X,
                                    op=OP.max).then_inc(cmp, 1)
                    c_cmp.inc()
                    t.wait_ge(cmp, c_cmp.v)
                    for cj in range(2 * GRP):
                        ch = ch0 + cj
                        t.matmul(out=ps_C[64:128, cj * 64:(cj + 1) * 64],
                                 lhsT=kcol[:, ch * 64:(ch + 1) * 64],
                                 rhs=pcol[:, cj * 64:(cj + 1) * 64],
                                 start=(cj == 0), stop=(cj == 2 * GRP - 1),
                                 tile_position=(0, 64)).then_inc(pe, 1)
                        c_pe.inc()
                    v.wait_ge(pe, c_pe.v)
                    v.tensor_tensor(out=daug[64:128, :], in0=ps_C[64:128, 0:512],
                                    in1=mall[64:128, tok0:tok0 + 512],
                                    op=OP.mult).then_inc(cmp, 1)
                    v.wait_ge(cmp, c_cmp.inc())
                    v.tensor_reduce(out=sm[64:128, 34:35], in_=daug[64:128, :], axis=AX.X,
                                    op=OP.max).then_inc(cmp, 1)
                    c_cmp.inc()
                    t.wait_ge(cmp, c_cmp.v)
                    t.matmul(out=ps_AP[0:128, 0:64], lhsT=pcol[:, 0:128],
                             rhs=zdiag[:], start=True, stop=False).then_inc(pe, 1)
                    c_pe.inc()
                    for cj in range(2 * GRP):
                        ch = ch0 + cj
                        pos = (0, 0) if cj % 2 == 0 else (0, 64)
                        t.matmul(out=ps_AP[(cj % 2) * 64:(cj % 2) * 64 + 64,
                                           (cj // 2) * 64:(cj // 2) * 64 + 64],
                                 lhsT=daug[:, cj * 64:(cj + 1) * 64],
                                 rhs=vaug[:, ch * 64:(ch + 1) * 64],
                                 start=False, stop=False,
                                 tile_position=pos).then_inc(pe, 1)
                        c_pe.inc()
                    for j in range(1, GRP):
                        t.matmul(out=ps_AP[0:128, j * 64:(j + 1) * 64],
                                 lhsT=pcol[:, j * 128:(j + 1) * 128],
                                 rhs=zdiag[:], start=False,
                                 stop=(j == GRP - 1)).then_inc(pe, 1)
                        c_pe.inc()
                    sc.wait_ge(pe, c_pe.v)
                    sc.copy(out=o_sb[:, :], in_=ps_AP[0:128, 0:256]).then_inc(cmp, 1)
                    c_cmp.inc()
                    g.wait_ge(cmp, c_cmp.v)
                    g.dma_start(out=o_out[bh, tok0:tok0 + 512, :].rearrange(
                                    "(j p) d -> p j d", p=128),
                                in_=o_sb[:, :].rearrange("p (j d) -> p j d", d=64)).then_inc(st, 16)
                    c_st.inc(16)
                    v.wait_ge(st, c_st.v)
                    sc.wait_ge(st, c_st.v)
            g.wait_ge(st, c_st.v)

    return nc


def run_b(inputs, a):
    qf = a["q"].reshape(B, N, H, DH)
    kf = a["k"].reshape(B, N, H, DH)
    vf = a["v"].reshape(B, N, H, DH)
    lff = a["logf"].reshape(B, N, H)
    lbf = a["logb"].reshape(B, N, H)
    lp = inputs["L0_attn_lamb_params"].reshape(H, DH)
    ident = np_identity_bf16()
    identf = np.eye(128, dtype=np.float32)
    tri = np.zeros((128, 64), dtype=BF)
    tri[64:128] = np.triu(np.ones((64, 64))).astype(BF)
    id64c = np.concatenate([np.eye(64), np.eye(64)], axis=0).astype(BF)
    in_maps = []
    for c in range(NC):
        b = c // 4
        hs = (c % 4) * 4
        kb = kf[b, :, hs:hs + 4].transpose(1, 0, 2)
        vb = vf[b, :, hs:hs + 4].transpose(1, 0, 2)
        kb = np.ascontiguousarray(kb.reshape(4, 128, 64, 64).transpose(0, 2, 1, 3).reshape(4, 64, 128 * 64))
        vb = np.ascontiguousarray(vb.reshape(4, 128, 64, 64).transpose(0, 2, 1, 3).reshape(4, 64, 128 * 64))
        in_maps.append({
            "q": np.ascontiguousarray(qf[b, :, hs:hs + 4].transpose(1, 0, 2)),
            "k": kb,
            "v": vb,
            "lf": np.ascontiguousarray(lff[b, :, hs:hs + 4].T),
            "lb": np.ascontiguousarray(lbf[b, :, hs:hs + 4].T),
            "lp": np.ascontiguousarray(lp[hs:hs + 4].T),
            "ident": ident, "identf": identf, "tri": tri, "id64": id64c,
        })
    nc = build_b()
    res = run_bass_kernel_spmd(nc, in_maps, core_ids=list(range(NC)))
    o = np.zeros((B, N, H, DH), np.float32)
    for c in range(NC):
        b = c // 4
        hs = (c % 4) * 4
        o[b, :, hs:hs + 4] = res.results[c]["o"].transpose(1, 0, 2)
    return o


def build_c():
    nc = bass.Bass("TRN2", target_bir_lowering=False)
    x_in = nc.dram_tensor("x", [TOK, D], F32, kind="ExternalInput").ap()
    o_in = nc.dram_tensor("o", [TOK, D], F32, kind="ExternalInput").ap()
    gate_in = nc.dram_tensor("gate", [TOK, D], F32, kind="ExternalInput").ap()
    g_in = nc.dram_tensor("g", [TOK, D], F32, kind="ExternalInput").ap()
    nw_in = nc.dram_tensor("nw", [1, D], F32, kind="ExternalInput").ap()
    Wo_in = nc.dram_tensor("Wo", [D, D], F32, kind="ExternalInput").ap()
    Wq1_in = nc.dram_tensor("Wq1", [D, D], F32, kind="ExternalInput").ap()
    Wdkv_in = nc.dram_tensor("Wdkv", [D, 256], F32, kind="ExternalInput").ap()
    Wuk_in = nc.dram_tensor("Wuk", [256, D], F32, kind="ExternalInput").ap()
    Wuv_in = nc.dram_tensor("Wuv", [256, D], F32, kind="ExternalInput").ap()
    lng_in = nc.dram_tensor("lng", [1, 256], F32, kind="ExternalInput").ap()
    lnb_in = nc.dram_tensor("lnb", [1, 256], F32, kind="ExternalInput").ap()
    cosb_in = nc.dram_tensor("cosb", [TOK, 32], F32, kind="ExternalInput").ap()
    sinb_in = nc.dram_tensor("sinb", [TOK, 32], F32, kind="ExternalInput").ap()
    identf_in = nc.dram_tensor("identf", [128, 128], F32, kind="ExternalInput").ap()
    x1_out = nc.dram_tensor("x1", [TOK, D], F32, kind="ExternalOutput").ap()
    xn1_out = nc.dram_tensor("xn1", [TOK, D], F32, kind="ExternalOutput").ap()
    q1_out = nc.dram_tensor("q1", [TOK, D], BF16, kind="ExternalOutput").ap()
    k1_out = nc.dram_tensor("k1", [TOK, D], BF16, kind="ExternalOutput").ap()
    v1_out = nc.dram_tensor("v1", [TOK, D], BF16, kind="ExternalOutput").ap()

    with ExitStack() as ctx:
        E = ctx.enter_context
        identf = E(nc.sbuf_tensor("identft", [128, 128], F32))
        ones_row = E(nc.sbuf_tensor("ones_row", [1, 128], F32))
        w_all = E(nc.sbuf_tensor("w_all", [128, (8 + 8 + 2) * 1024 + 2 * 2048], BF16))
        OFF_WO, OFF_WQ, OFF_DKV, OFF_UK, OFF_UV = 0, 8 * 1024, 16 * 1024, 16 * 1024 + 2 * 1024, 16 * 1024 + 4 * 1024
        nw_t = E(nc.sbuf_tensor("nw_t", [128, 1024], F32))
        lng_t = E(nc.sbuf_tensor("lng_t", [128, 256], F32))
        lnb_t = E(nc.sbuf_tensor("lnb_t", [128, 256], F32))
        xt = E(nc.sbuf_tensor("xtC", [128, 1024], F32))
        ot = E(nc.sbuf_tensor("otC", [128, 1024], F32))
        gt = E(nc.sbuf_tensor("gtC", [128, 1024], F32))
        st1 = E(nc.sbuf_tensor("st1C", [128, 1024], F32))
        st2 = E(nc.sbuf_tensor("st2C", [128, 1024], F32))
        stb = E(nc.sbuf_tensor("stbC", [128, 1024], BF16))
        colb = E(nc.sbuf_tensor("colbC", [128, 1024], BF16))
        lat = E(nc.sbuf_tensor("latC", [128, 256], F32))
        latc = E(nc.sbuf_tensor("latcC", [128, 256], BF16))
        cost = E(nc.sbuf_tensor("cosT", [128, 32], F32))
        sint = E(nc.sbuf_tensor("sinT", [128, 32], F32))
        sm = E(nc.sbuf_tensor("smC", [128, 80], F32))
        ps = [E(nc.psum_tensor(f"pC{b}", [128, 512], F32)) for b in range(4)]
        ld = E(nc.semaphore("ld"))
        ldg = E(nc.semaphore("ldg"))
        cmp = E(nc.semaphore("cmp"))
        pe = E(nc.semaphore("pe"))
        st = E(nc.semaphore("st"))
        c_ld, c_ldg, c_cmp, c_pe, c_st = Ctr(), Ctr(), Ctr(), Ctr(), Ctr()
        block = E(nc.Block())

        @block.sync
        def _(sync):
            v, t, sc, g = nc.vector, nc.tensor, nc.scalar, nc.gpsimd
            sync.dma_start(out=identf[:], in_=identf_in).then_inc(ld, 16)
            sync.dma_start(out=nw_t[0:1, :], in_=nw_in).then_inc(ld, 16)
            sync.dma_start(out=lng_t[0:1, :], in_=lng_in).then_inc(ld, 16)
            sync.dma_start(out=lnb_t[0:1, :], in_=lnb_in).then_inc(ld, 16)
            c_ld.inc(64)
            for kc in range(8):
                g.dma_start(out=w_all[:, OFF_WO + kc * 1024:OFF_WO + (kc + 1) * 1024],
                            in_=Wo_in[kc * 128:(kc + 1) * 128, :]).then_inc(ldg, 16)
                g.dma_start(out=w_all[:, OFF_WQ + kc * 1024:OFF_WQ + (kc + 1) * 1024],
                            in_=Wq1_in[kc * 128:(kc + 1) * 128, :]).then_inc(ldg, 16)
                g.dma_start(out=w_all[:, OFF_DKV + kc * 256:OFF_DKV + (kc + 1) * 256],
                            in_=Wdkv_in[kc * 128:(kc + 1) * 128, :]).then_inc(ldg, 16)
                c_ldg.inc(48)
            for kc in range(2):
                g.dma_start(out=w_all[:, OFF_UK + kc * 1024:OFF_UK + (kc + 1) * 1024],
                            in_=Wuk_in[kc * 128:(kc + 1) * 128, :]).then_inc(ldg, 16)
                g.dma_start(out=w_all[:, OFF_UV + kc * 1024:OFF_UV + (kc + 1) * 1024],
                            in_=Wuv_in[kc * 128:(kc + 1) * 128, :]).then_inc(ldg, 16)
                c_ldg.inc(32)
            v.wait_ge(ld, c_ld.v)
            v.memset(ones_row[:], 1.0)
            v.memset(sm[:, 64:65], 1e-5)
            v.memset(sm[:, 65:66], 1e-5).then_inc(cmp, 1)
            c_cmp.inc()
            v.wait_ge(cmp, c_cmp.v)
            t.wait_ge(ld, c_ld.v)
            t.wait_ge(cmp, c_cmp.v)
            for half in range(2):
                t.matmul(out=ps[0][0:128, 0:512], lhsT=ones_row[0:1, 0:128],
                         rhs=nw_t[0:1, half * 512:(half + 1) * 512],
                         start=True, stop=True).then_inc(pe, 1)
                c_pe.inc()
                sc.wait_ge(pe, c_pe.v)
                sc.copy(out=nw_t[:, half * 512:(half + 1) * 512],
                        in_=ps[0][0:128, 0:512]).then_inc(cmp, 1)
                c_cmp.inc()
                t.wait_ge(cmp, c_cmp.v)
            t.matmul(out=ps[0][0:128, 0:256], lhsT=ones_row[0:1, 0:128],
                     rhs=lng_t[0:1, :], start=True, stop=True).then_inc(pe, 1)
            c_pe.inc()
            sc.wait_ge(pe, c_pe.v)
            sc.copy(out=lng_t[:, :], in_=ps[0][0:128, 0:256]).then_inc(cmp, 1)
            c_cmp.inc()
            t.wait_ge(cmp, c_cmp.v)
            t.matmul(out=ps[0][0:128, 0:256], lhsT=ones_row[0:1, 0:128],
                     rhs=lnb_t[0:1, :], start=True, stop=True).then_inc(pe, 1)
            c_pe.inc()
            sc.wait_ge(pe, c_pe.v)
            sc.copy(out=lnb_t[:, :], in_=ps[0][0:128, 0:256]).then_inc(cmp, 1)
            c_cmp.inc()

            def transpose_to_colb(src_bf, nch=8):
                for cch in range(nch):
                    t.matmul(out=ps[3][0:128, 0:128], lhsT=src_bf[:, cch * 128:(cch + 1) * 128],
                             rhs=identf[:].bitcast(F32) if False else identb[:],
                             start=True, stop=True).then_inc(pe, 1)
                    c_pe.inc()
                    sc.wait_ge(pe, c_pe.v)
                    sc.copy(out=colb[:, cch * 128:(cch + 1) * 128],
                            in_=ps[3][0:128, 0:128])
                    sc.activation(out=stb[:, cch * 128:(cch + 1) * 128],
                                  in_=colb[:, cch * 128:(cch + 1) * 128],
                                  func=AF.Abs).then_inc(cmp, 1)
                    c_cmp.inc()
                    t.wait_ge(cmp, c_cmp.v)

            identb = E(nc.sbuf_tensor("identbC", [128, 128], BF16))
            v.wait_ge(cmp, c_cmp.v)
            v.tensor_copy(out=identb[:], in_=identf[:]).then_inc(cmp, 1)
            c_cmp.inc()

            for i in range(NT):
                r0 = i * 128
                if i >= 1:
                    sync.wait_ge(pe, c_pe.v)
                    sync.wait_ge(cmp, c_cmp.v)
                sync.dma_start(out=ot[:], in_=o_in[r0:r0 + 128, :]).then_inc(ld, 16)
                sync.dma_start(out=gt[:], in_=gate_in[r0:r0 + 128, :]).then_inc(ld, 16)
                c_ld.inc(32)
                v.wait_ge(ld, c_ld.v)
                v.wait_ge(st, c_st.v)
                sc.wait_ge(st, c_st.v)
                if i == 0:
                    v.wait_ge(ldg, c_ldg.v)
                    t.wait_ge(ldg, c_ldg.v)
                v.tensor_tensor(out=st1[:], in0=ot[:], in1=gt[:], op=OP.mult).then_inc(cmp, 1)
                c_cmp.inc()
                c_og = c_cmp.v
                v.scalar_tensor_tensor(out=st2[:], in0=st1[:], scalar=1.0, in1=st1[:],
                                       op0=OP.mult, op1=OP.mult)
                v.tensor_reduce(out=sm[:, 0:H], in_=st2[:].rearrange("p (h d) -> p h d", d=64),
                                axis=AX.X, op=OP.add).then_inc(cmp, 1)
                v.wait_ge(cmp, c_cmp.inc())
                v.tensor_scalar(out=sm[:, 16:16 + H], in0=sm[:, 0:H], scalar1=1.0 / 64.0,
                                scalar2=None, op0=OP.mult).then_inc(cmp, 1)
                c_cmp.inc()
                sc.wait_ge(cmp, c_cmp.v)
                sc.activation(out=sm[:, 32:32 + H], in_=sm[:, 16:16 + H], func=AF.Sqrt,
                              bias=sm[:, 65:66], scale=1.0).then_inc(cmp, 1)
                c_cmp.inc()
                v.wait_ge(cmp, c_cmp.v)
                v.reciprocal(out=sm[:, 48:48 + H], in_=sm[:, 32:32 + H]).then_inc(cmp, 1)
                v.wait_ge(cmp, c_cmp.inc())
                rb = sm[:, 48:48 + H].unsqueeze(2).broadcast_to([128, H, 64])
                v.tensor_tensor(out=st2[:].rearrange("p (h d) -> p h d", d=64),
                                in0=st1[:].rearrange("p (h d) -> p h d", d=64),
                                in1=rb, op=OP.mult).then_inc(cmp, 1)
                v.wait_ge(cmp, c_cmp.inc())
                v.tensor_tensor(out=st1[:], in0=st2[:], in1=nw_t[:, 0:1024], op=OP.mult).then_inc(cmp, 1)
                v.wait_ge(cmp, c_cmp.inc())
                v.tensor_copy(out=stb[:, 0:1024], in_=st1[:]).then_inc(cmp, 1)
                v.wait_ge(cmp, c_cmp.inc())
                v.tensor_reduce(out=sm[:, 66:67], in_=stb[:, 0:1024], axis=AX.X, op=OP.max).then_inc(cmp, 1)
                c_cmp.inc()
                t.wait_ge(cmp, c_cmp.v)
                transpose_to_colb(stb)
                for half in range(2):
                    for kc in range(8):
                        t.matmul(out=ps[half][0:128, 0:512],
                                 lhsT=colb[:, kc * 128:(kc + 1) * 128],
                                 rhs=w_all[:, OFF_WO + kc * 1024 + half * 512:
                                           OFF_WO + kc * 1024 + half * 512 + 512],
                                 start=(kc == 0), stop=(kc == 7)).then_inc(pe, 1)
                        c_pe.inc()
                sync.wait_ge(cmp, c_og)
                sync.dma_start(out=xt[:], in_=x_in[r0:r0 + 128, :]).then_inc(ld, 16)
                sync.dma_start(out=gt[:], in_=g_in[r0:r0 + 128, :]).then_inc(ld, 16)
                c_ld.inc(32)
                v.wait_ge(ld, c_ld.v)
                v.wait_ge(pe, c_pe.v)
                for half in range(2):
                    v.tensor_tensor(out=st1[:, half * 512:half * 512 + 512],
                                    in0=ps[half][0:128, 0:512],
                                    in1=gt[:, half * 512:half * 512 + 512], op=OP.mult)
                v.tensor_tensor(out=xt[:], in0=xt[:], in1=st1[:], op=OP.add).then_inc(cmp, 1)
                v.wait_ge(cmp, c_cmp.inc())
                g.wait_ge(cmp, c_cmp.v)
                g.dma_start(out=x1_out[r0:r0 + 128, :], in_=xt[:]).then_inc(st, 16)
                c_st.inc(16)
                v.tensor_reduce(out=sm[:, 0:1], in_=xt[:], axis=AX.X, op=OP.add).then_inc(cmp, 1)
                v.wait_ge(cmp, c_cmp.inc())
                v.tensor_scalar(out=sm[:, 1:2], in0=sm[:, 0:1], scalar1=1.0 / 1024.0,
                                scalar2=None, op0=OP.mult).then_inc(cmp, 1)
                v.wait_ge(cmp, c_cmp.inc())
                v.tensor_tensor(out=st2[:], in0=xt[:], in1=sm[:, 1:2].broadcast_to([128, 1024]),
                                op=OP.subtract)
                v.scalar_tensor_tensor(out=st1[:], in0=st2[:], scalar=1.0, in1=st2[:],
                                       op0=OP.mult, op1=OP.mult,
                                       accum_out=sm[:, 2:3]).then_inc(cmp, 1)
                v.wait_ge(cmp, c_cmp.inc())
                v.tensor_scalar(out=sm[:, 3:4], in0=sm[:, 2:3], scalar1=1.0 / 1024.0,
                                scalar2=None, op0=OP.mult).then_inc(cmp, 1)
                c_cmp.inc()
                sc.wait_ge(cmp, c_cmp.v)
                sc.activation(out=sm[:, 4:5], in_=sm[:, 3:4], func=AF.Sqrt,
                              bias=sm[:, 64:65], scale=1.0).then_inc(cmp, 1)
                c_cmp.inc()
                v.wait_ge(cmp, c_cmp.v)
                v.reciprocal(out=sm[:, 5:6], in_=sm[:, 4:5]).then_inc(cmp, 1)
                v.wait_ge(cmp, c_cmp.inc())
                v.wait_ge(st, c_st.v)
                v.tensor_tensor(out=xt[:], in0=st2[:], in1=sm[:, 5:6].broadcast_to([128, 1024]),
                                op=OP.mult).then_inc(cmp, 1)
                v.wait_ge(cmp, c_cmp.inc())
                g.wait_ge(cmp, c_cmp.v)
                g.dma_start(out=xn1_out[r0:r0 + 128, :], in_=xt[:]).then_inc(st, 16)
                c_st.inc(16)
                v.tensor_copy(out=stb[:, 0:1024], in_=xt[:]).then_inc(cmp, 1)
                v.wait_ge(cmp, c_cmp.inc())
                v.tensor_reduce(out=sm[:, 66:67], in_=stb[:, 0:1024], axis=AX.X, op=OP.max).then_inc(cmp, 1)
                c_cmp.inc()
                t.wait_ge(cmp, c_cmp.v)
                transpose_to_colb(stb)
                sync.dma_start(out=cost[:], in_=cosb_in[r0:r0 + 128, :]).then_inc(ld, 16)
                sync.dma_start(out=sint[:], in_=sinb_in[r0:r0 + 128, :]).then_inc(ld, 16)
                c_ld.inc(32)

                def rope_store(src_f32, dst):
                    a = src_f32[:].rearrange("p (h z) -> p h z", z=64)
                    cb = cost[:].unsqueeze(1).broadcast_to([128, H, 32])
                    sb = sint[:].unsqueeze(1).broadcast_to([128, H, 32])
                    x1v = src_f32[:].rearrange("p (h z) -> p h z", z=64)[:, :, 0:32]
                    x2v = src_f32[:].rearrange("p (h z) -> p h z", z=64)[:, :, 32:64]
                    o1 = st2[:].rearrange("p (h z) -> p h z", z=64)[:, :, 0:32]
                    o2 = st2[:].rearrange("p (h z) -> p h z", z=64)[:, :, 32:64]
                    tt = gt[:].rearrange("p (h z) -> p h z", z=64)
                    v.tensor_tensor(out=tt[:, :, 0:32], in0=x1v, in1=cb, op=OP.mult)
                    v.tensor_tensor(out=tt[:, :, 32:64], in0=x2v, in1=sb, op=OP.mult).then_inc(cmp, 1)
                    v.wait_ge(cmp, c_cmp.inc())
                    v.tensor_tensor(out=o1, in0=tt[:, :, 0:32], in1=tt[:, :, 32:64], op=OP.subtract)
                    v.tensor_tensor(out=tt[:, :, 0:32], in0=x1v, in1=sb, op=OP.mult)
                    v.tensor_tensor(out=tt[:, :, 32:64], in0=x2v, in1=cb, op=OP.mult).then_inc(cmp, 1)
                    v.wait_ge(cmp, c_cmp.inc())
                    v.tensor_tensor(out=o2, in0=tt[:, :, 0:32], in1=tt[:, :, 32:64], op=OP.add).then_inc(cmp, 1)
                    v.wait_ge(cmp, c_cmp.inc())
                    g.wait_ge(cmp, c_cmp.v)
                    g.dma_start(out=dst[r0:r0 + 128, :], in_=st2[:]).then_inc(st, 16)
                    c_st.inc(16)

                v.wait_ge(ld, c_ld.v)
                t.wait_ge(cmp, c_cmp.v)
                for half in range(2):
                    for kc in range(8):
                        t.matmul(out=ps[half][0:128, 0:512],
                                 lhsT=colb[:, kc * 128:(kc + 1) * 128],
                                 rhs=w_all[:, OFF_WQ + kc * 1024 + half * 512:
                                           OFF_WQ + kc * 1024 + half * 512 + 512],
                                 start=(kc == 0), stop=(kc == 7)).then_inc(pe, 1)
                        c_pe.inc()
                v.wait_ge(pe, c_pe.v)
                for half in range(2):
                    v.tensor_copy(out=st1[:, half * 512:half * 512 + 512],
                                  in_=ps[half][0:128, 0:512])
                v.sem_inc(cmp, 0) if False else None
                v.tensor_copy(out=sm[:, 67:68], in_=sm[:, 66:67]).then_inc(cmp, 1)
                v.wait_ge(cmp, c_cmp.inc())
                rope_store(st1, q1_out)
                v.wait_ge(st, c_st.v)
                t.wait_ge(cmp, c_cmp.v)
                for kc in range(8):
                    t.matmul(out=ps[2][0:128, 0:256],
                             lhsT=colb[:, kc * 128:(kc + 1) * 128],
                             rhs=w_all[:, OFF_DKV + kc * 256:OFF_DKV + (kc + 1) * 256],
                             start=(kc == 0), stop=(kc == 7)).then_inc(pe, 1)
                    c_pe.inc()
                v.wait_ge(pe, c_pe.v)
                v.tensor_reduce(out=sm[:, 8:9], in_=ps[2][0:128, 0:256], axis=AX.X,
                                op=OP.add).then_inc(cmp, 1)
                v.wait_ge(cmp, c_cmp.inc())
                v.tensor_scalar(out=sm[:, 9:10], in0=sm[:, 8:9], scalar1=1.0 / 256.0,
                                scalar2=None, op0=OP.mult).then_inc(cmp, 1)
                v.wait_ge(cmp, c_cmp.inc())
                v.tensor_tensor(out=lat[:], in0=ps[2][0:128, 0:256],
                                in1=sm[:, 9:10].broadcast_to([128, 256]), op=OP.subtract)
                v.scalar_tensor_tensor(out=st2[:, 0:256], in0=lat[:], scalar=1.0, in1=lat[:],
                                       op0=OP.mult, op1=OP.mult,
                                       accum_out=sm[:, 10:11]).then_inc(cmp, 1)
                v.wait_ge(cmp, c_cmp.inc())
                v.tensor_scalar(out=sm[:, 11:12], in0=sm[:, 10:11], scalar1=1.0 / 256.0,
                                scalar2=None, op0=OP.mult).then_inc(cmp, 1)
                c_cmp.inc()
                sc.wait_ge(cmp, c_cmp.v)
                sc.activation(out=sm[:, 12:13], in_=sm[:, 11:12], func=AF.Sqrt,
                              bias=sm[:, 64:65], scale=1.0).then_inc(cmp, 1)
                c_cmp.inc()
                v.wait_ge(cmp, c_cmp.v)
                v.reciprocal(out=sm[:, 13:14], in_=sm[:, 12:13]).then_inc(cmp, 1)
                v.wait_ge(cmp, c_cmp.inc())
                v.tensor_tensor(out=lat[:], in0=lat[:], in1=sm[:, 13:14].broadcast_to([128, 256]),
                                op=OP.mult).then_inc(cmp, 1)
                v.wait_ge(cmp, c_cmp.inc())
                v.tensor_tensor(out=lat[:], in0=lat[:], in1=lng_t[:], op=OP.mult).then_inc(cmp, 1)
                v.wait_ge(cmp, c_cmp.inc())
                v.tensor_tensor(out=lat[:], in0=lat[:], in1=lnb_t[:], op=OP.add).then_inc(cmp, 1)
                v.wait_ge(cmp, c_cmp.inc())
                v.tensor_copy(out=latc[:], in_=lat[:]).then_inc(cmp, 1)
                v.wait_ge(cmp, c_cmp.inc())
                v.tensor_reduce(out=sm[:, 66:67], in_=latc[:], axis=AX.X, op=OP.max).then_inc(cmp, 1)
                c_cmp.inc()
                t.wait_ge(cmp, c_cmp.v)
                for cch in range(2):
                    t.matmul(out=ps[3][0:128, 0:128], lhsT=latc[:, cch * 128:(cch + 1) * 128],
                             rhs=identb[:], start=True, stop=True).then_inc(pe, 1)
                    c_pe.inc()
                    sc.wait_ge(pe, c_pe.v)
                    sc.copy(out=colb[:, cch * 128:(cch + 1) * 128], in_=ps[3][0:128, 0:128])
                    sc.activation(out=stb[:, cch * 128:(cch + 1) * 128],
                                  in_=colb[:, cch * 128:(cch + 1) * 128],
                                  func=AF.Abs).then_inc(cmp, 1)
                    c_cmp.inc()
                    t.wait_ge(cmp, c_cmp.v)
                for w_i, dst in ((0, k1_out), (1, v1_out)):
                    off = OFF_UK if w_i == 0 else OFF_UV
                    t.wait_ge(cmp, c_cmp.v)
                    for half in range(2):
                        for kc in range(2):
                            t.matmul(out=ps[half][0:128, 0:512],
                                     lhsT=colb[:, kc * 128:(kc + 1) * 128],
                                     rhs=w_all[:, off + kc * 1024 + half * 512:
                                               off + kc * 1024 + half * 512 + 512],
                                     start=(kc == 0), stop=(kc == 1)).then_inc(pe, 1)
                            c_pe.inc()
                    v.wait_ge(pe, c_pe.v)
                    for half in range(2):
                        v.tensor_copy(out=st1[:, half * 512:half * 512 + 512],
                                      in_=ps[half][0:128, 0:512])
                    v.tensor_copy(out=sm[:, 67:68], in_=sm[:, 66:67]).then_inc(cmp, 1)
                    v.wait_ge(cmp, c_cmp.inc())
                    if w_i == 0:
                        rope_store(st1, dst)
                    else:
                        g.wait_ge(cmp, c_cmp.v)
                        g.dma_start(out=dst[r0:r0 + 128, :], in_=st1[:]).then_inc(st, 16)
                        c_st.inc(16)
                t.wait_ge(st, c_st.v)
            g.wait_ge(st, c_st.v)
    return nc


def run_c(inputs, a, o):
    x_full = inputs["L0_attn_x"].reshape(B * N, D)
    o_tm = o.reshape(B * N, D)
    identf = np.eye(128, dtype=np.float32)
    pos = np.arange(N, dtype=np.float64)[:, None]
    inv = 1.0 / 10000.0 ** (np.arange(32, dtype=np.float64) / 32.0)
    ang = (pos * inv[None, :]).astype(np.float32)
    cosf = np.cos(ang).astype(np.float32)
    sinf = np.sin(ang).astype(np.float32)
    in_maps = []
    for c in range(NC):
        r0 = c * TOK
        seq0 = (c % 4) * TOK
        in_maps.append({
            "x": np.ascontiguousarray(x_full[r0:r0 + TOK]),
            "o": np.ascontiguousarray(o_tm[r0:r0 + TOK]),
            "gate": np.ascontiguousarray(a["gate"][r0:r0 + TOK]),
            "g": np.ascontiguousarray(a["g"][r0:r0 + TOK]),
            "nw": inputs["L0_attn_norm_weight"].reshape(1, D),
            "Wo": inputs["L0_attn_Wo"], "Wq1": inputs["L1_attn_W_q"],
            "Wdkv": inputs["L1_attn_W_dkv"], "Wuk": inputs["L1_attn_W_uk"],
            "Wuv": inputs["L1_attn_W_uv"],
            "lng": inputs["L1_attn_ln_gamma"].reshape(1, 256),
            "lnb": inputs["L1_attn_ln_beta"].reshape(1, 256),
            "cosb": np.ascontiguousarray(cosf[seq0:seq0 + TOK]),
            "sinb": np.ascontiguousarray(sinf[seq0:seq0 + TOK]),
            "identf": identf,
        })
    nc = build_c()
    res = run_bass_kernel_spmd(nc, in_maps, core_ids=list(range(NC)))
    out = {}
    for key in ("x1", "xn1", "q1", "k1", "v1"):
        out[key] = np.concatenate([r[key] for r in res.results], axis=0)
    return out


DSH = 12.0


def build_d():
    nc = bass.Bass("TRN2", target_bir_lowering=False)
    kcol_in = nc.dram_tensor("kcol", [2, 128, N], BF16, kind="ExternalInput").ap()
    qcol_in = nc.dram_tensor("qcol", [2, 128, N], BF16, kind="ExternalInput").ap()
    vaug_in = nc.dram_tensor("vaug", [2, 128, 2 * 64 * 65], BF16, kind="ExternalInput").ap()
    umask_in = nc.dram_tensor("umask", [128, 896], BF16, kind="ExternalInput").ap()
    identf_in = nc.dram_tensor("identf", [128, 128], F32, kind="ExternalInput").ap()
    oat_out = nc.dram_tensor("oat", [2, 2, N, 64], BF16, kind="ExternalOutput").ap()

    with ExitStack() as ctx:
        E = ctx.enter_context
        kd = [E(nc.sbuf_tensor(f"kdD{d}", [128, N], BF16)) for d in range(2)]
        qd = [E(nc.sbuf_tensor(f"qdD{d}", [128, N], BF16)) for d in range(2)]
        va = [E(nc.sbuf_tensor(f"vaD{d}", [128, 2 * 64 * 65], BF16)) for d in range(2)]
        umaskt = E(nc.sbuf_tensor("umaskD", [128, 896], BF16))
        identft = E(nc.sbuf_tensor("identfD", [128, 128], F32))
        smd = E(nc.sbuf_tensor("smD", [128, 4], F32))
        pt = [[E(nc.sbuf_tensor(f"ptD{p}{b}", [128, 512], BF16)) for b in range(2)]
              for p in range(2)]
        oTs = [E(nc.sbuf_tensor(f"oTsD{p}", [128, 512], F32)) for p in range(2)]
        junkD = E(nc.sbuf_tensor("junkD", [128, 512], F32))
        orw = E(nc.sbuf_tensor("orwD", [128, 68], F32))
        obuf = [E(nc.sbuf_tensor(f"obufD{p}", [128, 256], BF16)) for p in range(2)]
        psS = [[E(nc.psum_tensor(f"psSD{p}{b}", [128, 512], F32)) for b in range(2)]
               for p in range(2)]
        psO = [E(nc.psum_tensor(f"psOD{p}", [128, 512], F32)) for p in range(2)]
        pstr = E(nc.psum_tensor("pstrD", [128, 512], F32))
        ld = E(nc.semaphore("ld"))
        cmp = E(nc.semaphore("cmp"))
        pe = E(nc.semaphore("pe"))
        st = E(nc.semaphore("st"))
        c_ld, c_cmp, c_pe, c_st = Ctr(), Ctr(), Ctr(), Ctr()
        block = E(nc.Block())

        @block.sync
        def _(sync):
            v, t, sc, g = nc.vector, nc.tensor, nc.scalar, nc.gpsimd
            sync.dma_start(out=umaskt[:], in_=umask_in).then_inc(ld, 16)
            sync.dma_start(out=identft[:], in_=identf_in).then_inc(ld, 16)
            c_ld.inc(32)
            duo_ld = []
            for dd in range(2):
                sync.dma_start(out=kd[dd][:], in_=kcol_in[dd]).then_inc(ld, 16)
                sync.dma_start(out=qd[dd][:], in_=qcol_in[dd]).then_inc(ld, 16)
                sync.dma_start(out=va[dd][:], in_=vaug_in[dd]).then_inc(ld, 16)
                c_ld.inc(48)
                duo_ld.append(c_ld.v)
            v.wait_ge(ld, 32)
            v.memset(smd[:, 0:1], -DSH).then_inc(cmp, 1)
            c_cmp.inc()
            sc.wait_ge(cmp, c_cmp.v)
            sc_bias_ready = c_cmp.v

            psS_evac = [[0, 0], [0, 0]]
            pt_used = [[0, 0], [0, 0]]
            psO_evac = [0, 0]
            obf_st = [0, 0]

            for dd in range(2):
                t.wait_ge(ld, duo_ld[dd])
                sc.wait_ge(ld, duo_ld[dd])
                for qb in range(16):
                    nkv = 4 * (qb + 1)
                    for kcx in range(nkv + 1):
                        if kcx < nkv:
                            kc = kcx
                            sbb = kc % 2
                            mm_pe = []
                            for pi in range(2):
                                base = pi * 64
                                t.wait_ge(cmp, psS_evac[pi][sbb])
                                t.matmul(out=psS[pi][sbb][0:128, 0:512],
                                         lhsT=kd[dd][base:base + 64, kc * 128:(kc + 1) * 128],
                                         rhs=qd[dd][base:base + 64, qb * 512:(qb + 1) * 512],
                                         start=True, stop=True,
                                         tile_position=(base, 0)).then_inc(pe, 1)
                                mm_pe.append(c_pe.inc())
                            m = kc - 4 * qb
                            for pi in range(2):
                                sc.wait_ge(pe, mm_pe[pi])
                                sc.wait_ge(pe, pt_used[pi][sbb])
                                sc.activation(out=pt[pi][sbb][:],
                                              in_=psS[pi][sbb][0:128, 0:512],
                                              func=AF.Exp, scale=0.125,
                                              bias=smd[:, 0:1]).then_inc(cmp, 1)
                                c_cmp.inc()
                                if m >= 0:
                                    v.wait_ge(cmp, c_cmp.v)
                                    v.tensor_tensor(out=pt[pi][sbb][:], in0=pt[pi][sbb][:],
                                                    in1=umaskt[:, 384 - 128 * m:896 - 128 * m],
                                                    op=OP.mult).then_inc(cmp, 1)
                                    c_cmp.inc()
                                psS_evac[pi][sbb] = c_cmp.v
                        if kcx >= 1:
                            kc = kcx - 1
                            sbb = kc % 2
                            for pi in range(2):
                                t.wait_ge(cmp, psS_evac[pi][sbb])
                                if kc == 0:
                                    t.wait_ge(cmp, psO_evac[pi])
                                t.matmul(out=psO[pi][0:65, 0:512],
                                         lhsT=va[dd][:, pi * 64 * 65 + kc * 65:
                                                     pi * 64 * 65 + (kc + 1) * 65],
                                         rhs=pt[pi][sbb][:],
                                         start=(kc == 0), stop=(kc == nkv - 1)).then_inc(pe, 1)
                                c_pe.inc()
                                pt_used[pi][sbb] = c_pe.v
                    for pi in range(2):
                        sc.wait_ge(pe, c_pe.v)
                        sc.copy(out=oTs[pi][0:65, 0:512],
                                in_=psO[pi][0:65, 0:512])
                        sc.activation(out=junkD[0:65, 0:512], in_=oTs[pi][0:65, 0:512],
                                      func=AF.Abs).then_inc(cmp, 1)
                        c_cmp.inc()
                        psO_evac[pi] = c_cmp.v
                        t.wait_ge(cmp, c_cmp.v)
                        for j in range(4):
                            t.matmul(out=pstr[0:128, 0:65],
                                     lhsT=oTs[pi][0:65, j * 128:(j + 1) * 128],
                                     rhs=identft[0:65, 0:65],
                                     start=True, stop=True).then_inc(pe, 1)
                            c_pe.inc()
                            v.wait_ge(pe, c_pe.v)
                            v.tensor_copy(out=orw[0:128, 0:65],
                                          in_=pstr[0:128, 0:65]).then_inc(cmp, 1)
                            c_cmp.inc()
                            t.wait_ge(cmp, c_cmp.v)
                            v.wait_ge(cmp, c_cmp.v)
                            v.reciprocal(out=orw[:, 66:67], in_=orw[:, 64:65]).then_inc(cmp, 1)
                            v.wait_ge(cmp, c_cmp.inc())
                            if j == 0:
                                v.wait_ge(st, obf_st[pi])
                            v.tensor_scalar(out=obuf[pi][0:128, j * 64:(j + 1) * 64],
                                            in0=orw[:, 0:64],
                                            scalar1=orw[:, 66:67], scalar2=None,
                                            op0=OP.mult).then_inc(cmp, 1)
                            c_cmp.inc()
                        sync.wait_ge(cmp, c_cmp.v)
                        dst = oat_out[dd, pi, qb * 512:(qb + 1) * 512, :]
                        sync.dma_start(out=dst.rearrange("(j p) w -> p j w", p=128),
                                       in_=obuf[pi][:].rearrange("p (j w) -> p j w", w=64)
                                       ).then_inc(st, 16)
                        c_st.inc(16)
                        obf_st[pi] = c_st.v
            v.wait_ge(st, c_st.v)
            v.wait_ge(ld, c_ld.v)
    return nc


def run_d(inputs, c_out):
    q1 = c_out["q1"].reshape(B, N, H, DH)
    k1 = c_out["k1"].reshape(B, N, H, DH)
    v1 = c_out["v1"].reshape(B, N, H, DH)
    qcol = np.ascontiguousarray(q1.transpose(0, 2, 3, 1))
    kcol = np.ascontiguousarray(k1.transpose(0, 2, 3, 1))
    vr = v1.transpose(0, 2, 1, 3)
    ones = np.ones((B, H, N, 1), dtype=BF)
    vaug = np.concatenate([vr, ones], axis=-1)
    vap = vaug.reshape(B, H, 64, 128, 65).transpose(0, 1, 3, 2, 4).reshape(B, H, 128, 64 * 65)
    umask = (np.arange(896)[None, :] - 384 >= np.arange(128)[:, None]).astype(BF)
    identf = np.eye(128, dtype=np.float32)
    in_maps = []
    for c in range(NC):
        kcolD = np.zeros((2, 128, N), dtype=BF)
        qcolD = np.zeros((2, 128, N), dtype=BF)
        vaugD = np.zeros((2, 128, 2 * 64 * 65), dtype=BF)
        for ddp in range(4):
            pair = 4 * c + ddp
            b, h = pair // H, pair % H
            dd, pi = ddp // 2, ddp % 2
            kcolD[dd, pi * 64:(pi + 1) * 64, :] = kcol[b, h]
            qcolD[dd, pi * 64:(pi + 1) * 64, :] = qcol[b, h]
            vaugD[dd, :, pi * 64 * 65:(pi + 1) * 64 * 65] = vap[b, h]
        in_maps.append({"kcol": kcolD, "qcol": qcolD, "vaug": vaugD,
                        "umask": umask, "identf": identf})
    nc = build_d()
    res = run_bass_kernel_spmd(nc, in_maps, core_ids=list(range(NC)))
    attn = np.empty((B, N, H, DH), dtype=BF)
    for c in range(NC):
        oat = res.results[c]["oat"]
        for ddp in range(4):
            pair = 4 * c + ddp
            b, h = pair // H, pair % H
            attn[b, :, h, :] = oat[ddp // 2, ddp % 2]
    return attn.reshape(B * N, D)


GRPE = 256
NTILE_E = (B * N) // 128


def build_e1():
    nc = bass.Bass("TRN2", target_bir_lowering=False)
    xn1_in = nc.dram_tensor("xn1", [B * N, D], F32, kind="ExternalInput").ap()
    w1_in = nc.dram_tensor("w1", [128, 8 * 4096], BF16, kind="ExternalInput").ap()
    w2_in = nc.dram_tensor("w2", [128, 32 * 1024], BF16, kind="ExternalInput").ap()
    b1t_in = nc.dram_tensor("b1t", [128, 32], F32, kind="ExternalInput").ap()
    b2_in = nc.dram_tensor("b2", [1, 1024], F32, kind="ExternalInput").ap()
    wr_in = nc.dram_tensor("wr", [128, 64], F32, kind="ExternalInput").ap()
    br_in = nc.dram_tensor("br", [1, 8], F32, kind="ExternalInput").ap()
    gam_in = nc.dram_tensor("gam", [1, 1024], F32, kind="ExternalInput").ap()
    bet_in = nc.dram_tensor("bet", [1, 1024], F32, kind="ExternalInput").ap()
    esel_in = nc.dram_tensor("esel", [128, 8], F32, kind="ExternalInput").ap()
    iora_in = nc.dram_tensor("iora", [128, 8], F32, kind="ExternalInput").ap()
    ci99_in = nc.dram_tensor("ci99", [128, 8], F32, kind="ExternalInput").ap()
    identf_in = nc.dram_tensor("identf", [128, 128], F32, kind="ExternalInput").ap()
    part_out = nc.dram_tensor("part", [B * N, D], F32, kind="ExternalOutput").ap()
    wsl_out = nc.dram_tensor("wsl", [128, 128], F32, kind="ExternalOutput").ap()
    cvec_out = nc.dram_tensor("cvec", [1, 1024], F32, kind="ExternalOutput").ap()

    with ExitStack() as ctx:
        E = ctx.enter_context
        w1sb = E(nc.sbuf_tensor("w1sbE", [128, 8 * 4096], BF16))
        w2sb = E(nc.sbuf_tensor("w2sbE", [128, 32 * 1024], BF16))
        hcol = E(nc.sbuf_tensor("hcolE", [128, 32 * GRPE], BF16))
        colbE = E(nc.sbuf_tensor("colbEE", [128, 8 * GRPE], BF16))
        colbF = E(nc.sbuf_tensor("colbFE", [128, 1024], F32))
        gb = E(nc.sbuf_tensor("gbE", [128, 1024], F32))
        bbt = E(nc.sbuf_tensor("bbtE", [128, 1024], F32))
        rqb = E(nc.sbuf_tensor("rqbE", [128, 1024], F32))
        xt = E(nc.sbuf_tensor("xtE", [128, 1024], F32))
        st1 = E(nc.sbuf_tensor("st1E", [128, 1024], F32))
        st2 = E(nc.sbuf_tensor("st2E", [128, 1024], F32))
        pstage = E(nc.sbuf_tensor("pstageE", [128, 1024], F32))
        wbuf = E(nc.sbuf_tensor("wbufE", [128, 128], F32))
        sm = E(nc.sbuf_tensor("smE", [128, 64], F32))
        wrsb = E(nc.sbuf_tensor("wrsbE", [128, 64], F32))
        brsb = E(nc.sbuf_tensor("brsbE", [128, 8], F32))
        iora = E(nc.sbuf_tensor("ioraE", [128, 8], F32))
        ci99 = E(nc.sbuf_tensor("ci99E", [128, 8], F32))
        esel = E(nc.sbuf_tensor("eselE", [128, 8], F32))
        b1sb = E(nc.sbuf_tensor("b1sbE", [128, 32], F32))
        rb1 = E(nc.sbuf_tensor("rb1E", [128, 32], BF16))
        b2sb = E(nc.sbuf_tensor("b2sbE", [1, 1024], F32))
        rqv = E(nc.sbuf_tensor("rqvE", [1, 1024], F32))
        cvst = E(nc.sbuf_tensor("cvstE", [1, 1024], F32))
        ones_row = E(nc.sbuf_tensor("onesE", [1, 128], F32))
        identft = E(nc.sbuf_tensor("identfE", [128, 128], F32))
        junk = E(nc.sbuf_tensor("junkE", [128, 512], F32))
        pstr = E(nc.psum_tensor("pstrE", [128, 512], F32))
        psR = E(nc.psum_tensor("psRE", [128, 512], F32))
        psH = [E(nc.psum_tensor(f"psHE{i}", [128, 512], F32)) for i in range(2)]
        psO = [E(nc.psum_tensor(f"psOE{i}", [128, 512], F32)) for i in range(2)]
        ld = E(nc.semaphore("ld"))
        cmp = E(nc.semaphore("cmp"))
        pe = E(nc.semaphore("pe"))
        st = E(nc.semaphore("st"))
        c_ld, c_cmp, c_pe, c_st = Ctr(), Ctr(), Ctr(), Ctr()
        block = E(nc.Block())

        @block.sync
        def _(sync):
            v, t, sc, g = nc.vector, nc.tensor, nc.scalar, nc.gpsimd
            for buf, src in ((identft, identf_in), (wrsb, wr_in), (iora, iora_in),
                             (ci99, ci99_in), (esel, esel_in), (b1sb, b1t_in)):
                sync.dma_start(out=buf[:], in_=src).then_inc(ld, 16)
                c_ld.inc(16)
            sync.dma_start(out=b2sb[:], in_=b2_in).then_inc(ld, 16)
            sync.dma_start(out=gb[0:1, :], in_=gam_in).then_inc(ld, 16)
            sync.dma_start(out=bbt[0:1, :], in_=bet_in).then_inc(ld, 16)
            sync.dma_start(out=brsb[0:1, :], in_=br_in).then_inc(ld, 16)
            c_ld.inc(64)
            sync.dma_start(out=w1sb[:], in_=w1_in).then_inc(ld, 16)
            sync.dma_start(out=w2sb[:], in_=w2_in).then_inc(ld, 16)
            c_ld.inc(32)
            v.wait_ge(ld, c_ld.v)
            v.memset(ones_row[:], 1.0)
            v.memset(sm[:, 54:55], 1e-5)
            v.tensor_scalar(out=rb1[:], in0=b1sb[:], scalar1=0.0, scalar2=None,
                            op0=OP.max).then_inc(cmp, 1)
            c_cmp.inc()
            sc.wait_ge(cmp, c_cmp.v)
            sc.activation(out=junk[:, 0:32], in_=rb1[:], func=AF.Abs).then_inc(cmp, 1)
            c_cmp.inc()
            t.wait_ge(ld, c_ld.v)
            t.wait_ge(cmp, c_cmp.v)
            for buf in (gb, bbt):
                for half in range(2):
                    t.matmul(out=pstr[0:128, 0:512], lhsT=ones_row[0:1, 0:128],
                             rhs=buf[0:1, half * 512:(half + 1) * 512],
                             start=True, stop=True).then_inc(pe, 1)
                    c_pe.inc()
                    sc.wait_ge(pe, c_pe.v)
                    sc.copy(out=buf[:, half * 512:(half + 1) * 512],
                            in_=pstr[0:128, 0:512]).then_inc(cmp, 1)
                    c_cmp.inc()
                    t.wait_ge(cmp, c_cmp.v)
            t.matmul(out=pstr[0:128, 0:8], lhsT=ones_row[0:1, 0:128],
                     rhs=brsb[0:1, 0:8], start=True, stop=True).then_inc(pe, 1)
            c_pe.inc()
            sc.wait_ge(pe, c_pe.v)
            sc.copy(out=brsb[:, :], in_=pstr[0:128, 0:8]).then_inc(cmp, 1)
            c_cmp.inc()
            t.wait_ge(cmp, c_cmp.v)
            for half in range(2):
                for fc in range(32):
                    t.matmul(out=psR[0:1, 0:512], lhsT=rb1[:, fc:fc + 1],
                             rhs=w2sb[:, fc * 1024 + half * 512:fc * 1024 + half * 512 + 512],
                             start=(fc == 0), stop=(fc == 31)).then_inc(pe, 1)
                    c_pe.inc()
                sc.wait_ge(pe, c_pe.v)
                sc.copy(out=rqv[0:1, half * 512:(half + 1) * 512],
                        in_=psR[0:1, 0:512]).then_inc(cmp, 1)
                c_cmp.inc()
                t.wait_ge(cmp, c_cmp.v)
            v.wait_ge(cmp, c_cmp.v)
            v.tensor_tensor(out=cvst[0:1, :], in0=rqv[0:1, :], in1=b2sb[0:1, :],
                            op=OP.add).then_inc(cmp, 1)
            c_cmp.inc()
            g.wait_ge(cmp, c_cmp.v)
            g.dma_start(out=cvec_out, in_=cvst[0:1, :]).then_inc(st, 16)
            c_st.inc(16)
            for half in range(2):
                t.matmul(out=pstr[0:128, 0:512], lhsT=ones_row[0:1, 0:128],
                         rhs=rqv[0:1, half * 512:(half + 1) * 512],
                         start=True, stop=True).then_inc(pe, 1)
                c_pe.inc()
                sc.wait_ge(pe, c_pe.v)
                sc.copy(out=rqb[:, half * 512:(half + 1) * 512],
                        in_=pstr[0:128, 0:512]).then_inc(cmp, 1)
                c_cmp.inc()
                t.wait_ge(cmp, c_cmp.v)

            xt_free = 0
            colbE_free = 0
            hcol_free = 0
            psH_evac = [0, 0]
            psO_evac = 0
            pstage_st = 0

            def wv(n=1):
                v.wait_ge(cmp, c_cmp.inc(n))

            for grp in range(NTILE_E // 2):
                for t01 in range(2):
                    ti = grp * 2 + t01
                    row0 = ti * 128
                    sync.wait_ge(cmp, xt_free)
                    sync.dma_start(out=xt[:], in_=xn1_in[row0:row0 + 128, :]).then_inc(ld, 16)
                    c_ld.inc(16)
                    v.wait_ge(ld, c_ld.v)
                    v.wait_ge(pe, colbE_free)
                    v.tensor_reduce(out=sm[:, 48:49], in_=xt[:], axis=AX.X,
                                    op=OP.add).then_inc(cmp, 1)
                    wv()
                    v.tensor_scalar(out=sm[:, 49:50], in0=sm[:, 48:49],
                                    scalar1=1.0 / 1024.0, scalar2=None,
                                    op0=OP.mult).then_inc(cmp, 1)
                    wv()
                    v.tensor_tensor(out=st2[:], in0=xt[:],
                                    in1=sm[:, 49:50].broadcast_to([128, 1024]),
                                    op=OP.subtract).then_inc(cmp, 1)
                    c_cmp.inc()
                    xt_free = c_cmp.v
                    v.scalar_tensor_tensor(out=st1[:], in0=st2[:], scalar=1.0, in1=st2[:],
                                           op0=OP.mult, op1=OP.mult,
                                           accum_out=sm[:, 50:51]).then_inc(cmp, 1)
                    wv()
                    v.tensor_scalar(out=sm[:, 51:52], in0=sm[:, 50:51],
                                    scalar1=1.0 / 1024.0, scalar2=None,
                                    op0=OP.mult).then_inc(cmp, 1)
                    c_cmp.inc()
                    sc.wait_ge(cmp, c_cmp.v)
                    sc.activation(out=sm[:, 52:53], in_=sm[:, 51:52], func=AF.Sqrt,
                                  bias=sm[:, 54:55], scale=1.0).then_inc(cmp, 1)
                    wv()
                    v.reciprocal(out=sm[:, 53:54], in_=sm[:, 52:53]).then_inc(cmp, 1)
                    wv()
                    v.tensor_tensor(out=st1[:], in0=st2[:],
                                    in1=sm[:, 53:54].broadcast_to([128, 1024]),
                                    op=OP.mult).then_inc(cmp, 1)
                    wv()
                    v.tensor_tensor(out=st1[:], in0=st1[:], in1=gb[:], op=OP.mult)
                    v.tensor_tensor(out=st1[:], in0=st1[:], in1=bbt[:],
                                    op=OP.add).then_inc(cmp, 1)
                    c_cmp.inc()
                    t.wait_ge(cmp, c_cmp.v)
                    for cch in range(8):
                        t.matmul(out=pstr[0:128, 0:128], lhsT=st1[:, cch * 128:(cch + 1) * 128],
                                 rhs=identft[0:128, 0:128], start=True,
                                 stop=True).then_inc(pe, 1)
                        c_pe.inc()
                        sc.wait_ge(pe, c_pe.v)
                        sc.copy(out=colbF[:, cch * 128:(cch + 1) * 128],
                                in_=pstr[0:128, 0:128])
                        sc.activation(out=junk[:, 0:128],
                                      in_=colbF[:, cch * 128:(cch + 1) * 128],
                                      func=AF.Abs).then_inc(cmp, 1)
                        c_cmp.inc()
                        t.wait_ge(cmp, c_cmp.v)
                        v.wait_ge(cmp, c_cmp.v)
                        v.tensor_copy(out=colbE[:, cch * GRPE + t01 * 128:
                                               cch * GRPE + t01 * 128 + 128],
                                      in_=colbF[:, cch * 128:(cch + 1) * 128]).then_inc(cmp, 1)
                        c_cmp.inc()
                    for kc in range(8):
                        t.matmul(out=psR[0:128, 0:8], lhsT=colbF[:, kc * 128:(kc + 1) * 128],
                                 rhs=wrsb[:, kc * 8:(kc + 1) * 8], start=(kc == 0),
                                 stop=(kc == 7)).then_inc(pe, 1)
                        c_pe.inc()
                    v.wait_ge(pe, c_pe.v)
                    v.tensor_tensor(out=sm[:, 0:8], in0=psR[0:128, 0:8], in1=brsb[:],
                                    op=OP.add).then_inc(cmp, 1)
                    wv()
                    v.tensor_reduce(out=sm[:, 8:9], in_=sm[:, 0:8], axis=AX.X,
                                    op=OP.max).then_inc(cmp, 1)
                    wv()
                    v.tensor_tensor(out=sm[:, 0:8], in0=sm[:, 0:8],
                                    in1=sm[:, 8:9].broadcast_to([128, 8]),
                                    op=OP.subtract).then_inc(cmp, 1)
                    c_cmp.inc()
                    sc.wait_ge(cmp, c_cmp.v)
                    sc.activation(out=sm[:, 0:8], in_=sm[:, 0:8], func=AF.Exp).then_inc(cmp, 1)
                    wv()
                    v.tensor_reduce(out=sm[:, 9:10], in_=sm[:, 0:8], axis=AX.X,
                                    op=OP.add).then_inc(cmp, 1)
                    wv()
                    v.reciprocal(out=sm[:, 10:11], in_=sm[:, 9:10]).then_inc(cmp, 1)
                    wv()
                    v.tensor_tensor(out=sm[:, 0:8], in0=sm[:, 0:8],
                                    in1=sm[:, 10:11].broadcast_to([128, 8]),
                                    op=OP.mult).then_inc(cmp, 1)
                    wv()
                    v.tensor_reduce(out=sm[:, 11:12], in_=sm[:, 0:8], axis=AX.X,
                                    op=OP.max).then_inc(cmp, 1)
                    wv()
                    v.tensor_tensor(out=sm[:, 16:24], in0=sm[:, 0:8],
                                    in1=sm[:, 11:12].broadcast_to([128, 8]),
                                    op=OP.is_equal).then_inc(cmp, 1)
                    wv()
                    v.tensor_tensor(out=sm[:, 24:32], in0=sm[:, 16:24], in1=ci99[:],
                                    op=OP.mult).then_inc(cmp, 1)
                    wv()
                    v.tensor_scalar(out=sm[:, 24:32], in0=sm[:, 24:32], scalar1=-1.0,
                                    scalar2=99.0, op0=OP.mult, op1=OP.add).then_inc(cmp, 1)
                    wv()
                    v.tensor_reduce(out=sm[:, 12:13], in_=sm[:, 24:32], axis=AX.X,
                                    op=OP.min).then_inc(cmp, 1)
                    wv()
                    v.tensor_tensor(out=sm[:, 16:24], in0=iora[:],
                                    in1=sm[:, 12:13].broadcast_to([128, 8]),
                                    op=OP.is_equal).then_inc(cmp, 1)
                    wv()
                    v.tensor_scalar(out=sm[:, 24:32], in0=sm[:, 16:24], scalar1=1e9,
                                    scalar2=None, op0=OP.mult).then_inc(cmp, 1)
                    wv()
                    v.tensor_tensor(out=sm[:, 32:40], in0=sm[:, 0:8], in1=sm[:, 24:32],
                                    op=OP.subtract).then_inc(cmp, 1)
                    wv()
                    v.tensor_reduce(out=sm[:, 13:14], in_=sm[:, 32:40], axis=AX.X,
                                    op=OP.max).then_inc(cmp, 1)
                    wv()
                    v.tensor_tensor(out=sm[:, 40:48], in0=sm[:, 32:40],
                                    in1=sm[:, 13:14].broadcast_to([128, 8]),
                                    op=OP.is_equal).then_inc(cmp, 1)
                    wv()
                    v.tensor_tensor(out=sm[:, 24:32], in0=sm[:, 40:48], in1=ci99[:],
                                    op=OP.mult).then_inc(cmp, 1)
                    wv()
                    v.tensor_scalar(out=sm[:, 24:32], in0=sm[:, 24:32], scalar1=-1.0,
                                    scalar2=99.0, op0=OP.mult, op1=OP.add).then_inc(cmp, 1)
                    wv()
                    v.tensor_reduce(out=sm[:, 14:15], in_=sm[:, 24:32], axis=AX.X,
                                    op=OP.min).then_inc(cmp, 1)
                    wv()
                    v.tensor_tensor(out=sm[:, 40:48], in0=iora[:],
                                    in1=sm[:, 14:15].broadcast_to([128, 8]),
                                    op=OP.is_equal).then_inc(cmp, 1)
                    wv()
                    v.tensor_tensor(out=sm[:, 16:24], in0=sm[:, 16:24],
                                    in1=sm[:, 11:12].broadcast_to([128, 8]),
                                    op=OP.mult)
                    v.tensor_tensor(out=sm[:, 40:48], in0=sm[:, 40:48],
                                    in1=sm[:, 13:14].broadcast_to([128, 8]),
                                    op=OP.mult).then_inc(cmp, 1)
                    wv()
                    v.tensor_tensor(out=sm[:, 16:24], in0=sm[:, 16:24], in1=sm[:, 40:48],
                                    op=OP.add).then_inc(cmp, 1)
                    wv()
                    v.tensor_tensor(out=sm[:, 24:32], in0=sm[:, 16:24], in1=esel[:],
                                    op=OP.mult).then_inc(cmp, 1)
                    wv()
                    v.tensor_reduce(out=wbuf[:, ti:ti + 1], in_=sm[:, 24:32], axis=AX.X,
                                    op=OP.add).then_inc(cmp, 1)
                    wv()
                sc.wait_ge(pe, hcol_free)
                t.wait_ge(cmp, c_cmp.v)
                for fc in range(32):
                    bb = fc % 2
                    t.wait_ge(cmp, psH_evac[bb])
                    for kc in range(8):
                        t.matmul(out=psH[bb][0:128, 0:GRPE],
                                 lhsT=w1sb[:, kc * 4096 + fc * 128:kc * 4096 + fc * 128 + 128],
                                 rhs=colbE[:, kc * GRPE:(kc + 1) * GRPE],
                                 start=(kc == 0), stop=(kc == 7)).then_inc(pe, 1)
                        c_pe.inc()
                    sc.wait_ge(pe, c_pe.v)
                    sc.activation(out=hcol[:, fc * GRPE:(fc + 1) * GRPE],
                                  in_=psH[bb][0:128, 0:GRPE], func=AF.Relu,
                                  bias=b1sb[:, fc:fc + 1], scale=1.0)
                    sc.activation(out=junk[:, 0:GRPE],
                                  in_=hcol[:, fc * GRPE:(fc + 1) * GRPE],
                                  func=AF.Abs).then_inc(cmp, 1)
                    c_cmp.inc()
                    psH_evac[bb] = c_cmp.v
                colbE_free = c_pe.v
                t.wait_ge(cmp, c_cmp.v)
                for t01 in range(2):
                    ti = grp * 2 + t01
                    row0 = ti * 128
                    t.wait_ge(cmp, psO_evac)
                    for fc in range(32):
                        for half in range(2):
                            t.matmul(out=psO[half][0:128, 0:512],
                                     lhsT=hcol[:, fc * GRPE + t01 * 128:
                                               fc * GRPE + t01 * 128 + 128],
                                     rhs=w2sb[:, fc * 1024 + half * 512:
                                              fc * 1024 + half * 512 + 512],
                                     start=(fc == 0), stop=(fc == 31)).then_inc(pe, 1)
                            c_pe.inc()
                    v.wait_ge(pe, c_pe.v)
                    v.wait_ge(st, pstage_st)
                    for half in range(2):
                        v.tensor_tensor(out=pstage[:, half * 512:half * 512 + 512],
                                        in0=psO[half][0:128, 0:512],
                                        in1=rqb[:, half * 512:half * 512 + 512],
                                        op=OP.subtract)
                    v.tensor_scalar(out=pstage[:], in0=pstage[:],
                                    scalar1=wbuf[:, ti:ti + 1], scalar2=None,
                                    op0=OP.mult).then_inc(cmp, 1)
                    c_cmp.inc()
                    psO_evac = c_cmp.v
                    g.wait_ge(cmp, c_cmp.v)
                    g.dma_start(out=part_out[row0:row0 + 128, :],
                                in_=pstage[:]).then_inc(st, 16)
                    c_st.inc(16)
                    pstage_st = c_st.v
                hcol_free = c_pe.v
            g.dma_start(out=wsl_out, in_=wbuf[:]).then_inc(st, 16)
            c_st.inc(16)
            v.wait_ge(st, c_st.v)
            v.wait_ge(ld, c_ld.v)
    return nc


def run_e1(inputs, c_out, a0):
    xn1 = np.ascontiguousarray(c_out["xn1"])
    iora = np.tile(np.arange(8, dtype=np.float32), (128, 1))
    ci99 = 99.0 - iora
    identf = np.eye(128, dtype=np.float32)
    wr = np.ascontiguousarray(
        inputs["L1_ffn_Wr"].reshape(8, 128, 8).transpose(1, 0, 2).reshape(128, 64))
    in_maps = []
    for c in range(NC):
        w1p = np.ascontiguousarray(
            a0["w1b"][c].reshape(8, 128, 4096).transpose(1, 0, 2).reshape(128, 8 * 4096))
        w2p = np.ascontiguousarray(
            a0["w2b"][c].reshape(32, 128, 1024).transpose(1, 0, 2).reshape(128, 32 * 1024))
        esel = np.zeros((128, 8), dtype=np.float32)
        esel[:, c] = 1.0
        in_maps.append({
            "xn1": xn1, "w1": w1p, "w2": w2p,
            "b1t": np.ascontiguousarray(inputs["L1_ffn_b1"][c].reshape(32, 128).T),
            "b2": inputs["L1_ffn_b2"][c].reshape(1, 1024),
            "wr": wr, "br": inputs["L1_ffn_br"].reshape(1, 8),
            "gam": inputs["L1_ffn_gamma"].reshape(1, 1024),
            "bet": inputs["L1_ffn_beta"].reshape(1, 1024),
            "esel": esel, "iora": iora, "ci99": ci99, "identf": identf,
        })
    nc = build_e1()
    res = run_bass_kernel_spmd(nc, in_maps, core_ids=list(range(NC)))
    parts = [res.results[c]["part"] for c in range(NC)]
    wtok = np.stack([res.results[c]["wsl"].T.reshape(B * N) for c in range(NC)], axis=1)
    cvs = np.stack([res.results[c]["cvec"][0] for c in range(NC)], axis=0)
    return parts, wtok, cvs


def build_e2():
    nc = bass.Bass("TRN2", target_bir_lowering=False)
    x1_in = nc.dram_tensor("x1", [TOK, D], F32, kind="ExternalInput").ap()
    xn1_in = nc.dram_tensor("xn1", [TOK, D], F32, kind="ExternalInput").ap()
    attn_in = nc.dram_tensor("attn", [TOK, D], BF16, kind="ExternalInput").ap()
    wo_in = nc.dram_tensor("wo", [128, 8 * 1024], BF16, kind="ExternalInput").ap()
    p_in = [nc.dram_tensor(f"p{e}", [TOK, D], F32, kind="ExternalInput").ap()
            for e in range(8)]
    wtok_in = nc.dram_tensor("wtok", [TOK, 8], F32, kind="ExternalInput").ap()
    cvs_in = nc.dram_tensor("cvs", [8, 1024], F32, kind="ExternalInput").ap()
    identb_in = nc.dram_tensor("identb", [128, 128], BF16, kind="ExternalInput").ap()
    fin_out = nc.dram_tensor("fin", [TOK, D], F32, kind="ExternalOutput").ap()

    with ExitStack() as ctx:
        E = ctx.enter_context
        wosb = E(nc.sbuf_tensor("wosbF", [128, 8 * 1024], BF16))
        xt = E(nc.sbuf_tensor("xtF", [128, 1024], F32))
        xnt = E(nc.sbuf_tensor("xntF", [128, 1024], F32))
        att = E(nc.sbuf_tensor("attF", [128, 1024], BF16))
        colb = E(nc.sbuf_tensor("colbF2", [128, 1024], BF16))
        st1 = E(nc.sbuf_tensor("st1F", [128, 1024], F32))
        st2 = E(nc.sbuf_tensor("st2F", [128, 1024], F32))
        acc = E(nc.sbuf_tensor("accF", [128, 1024], F32))
        pt2 = [E(nc.sbuf_tensor(f"pt2F{i}", [128, 1024], F32)) for i in range(2)]
        wt = E(nc.sbuf_tensor("wtF", [128, 8], F32))
        ctb = E(nc.sbuf_tensor("ctbF", [128, 1024], F32))
        ctv = E(nc.sbuf_tensor("ctvF", [1, 1024], F32))
        cvsb = E(nc.sbuf_tensor("cvsbF", [8, 1024], F32))
        onescol = E(nc.sbuf_tensor("onescolF", [8, 1], F32))
        ones_row = E(nc.sbuf_tensor("onesF", [1, 128], F32))
        identbt = E(nc.sbuf_tensor("identbF", [128, 128], BF16))
        junk = E(nc.sbuf_tensor("junkF", [128, 512], F32))
        sm = E(nc.sbuf_tensor("smF", [128, 8], F32))
        pstr = E(nc.psum_tensor("pstrF", [128, 512], F32))
        psS = [E(nc.psum_tensor(f"psSF{i}", [128, 512], F32)) for i in range(2)]
        ld = E(nc.semaphore("ld"))
        cmp = E(nc.semaphore("cmp"))
        pe = E(nc.semaphore("pe"))
        st = E(nc.semaphore("st"))
        c_ld, c_cmp, c_pe, c_st = Ctr(), Ctr(), Ctr(), Ctr()
        block = E(nc.Block())

        @block.sync
        def _(sync):
            v, t, sc, g = nc.vector, nc.tensor, nc.scalar, nc.gpsimd
            sync.dma_start(out=identbt[:], in_=identb_in).then_inc(ld, 16)
            sync.dma_start(out=cvsb[:], in_=cvs_in).then_inc(ld, 16)
            sync.dma_start(out=wosb[:], in_=wo_in).then_inc(ld, 16)
            c_ld.inc(48)
            v.wait_ge(ld, c_ld.v)
            v.memset(ones_row[:], 1.0)
            v.memset(onescol[0:8, 0:1], 1.0).then_inc(cmp, 1)
            c_cmp.inc()
            t.wait_ge(ld, c_ld.v)
            t.wait_ge(cmp, c_cmp.v)
            for half in range(2):
                t.matmul(out=pstr[0:1, 0:512], lhsT=onescol[0:8, 0:1],
                         rhs=cvsb[0:8, half * 512:(half + 1) * 512],
                         start=True, stop=True).then_inc(pe, 1)
                c_pe.inc()
                sc.wait_ge(pe, c_pe.v)
                sc.copy(out=ctv[0:1, half * 512:(half + 1) * 512],
                        in_=pstr[0:1, 0:512]).then_inc(cmp, 1)
                c_cmp.inc()
                t.wait_ge(cmp, c_cmp.v)
            for half in range(2):
                t.matmul(out=pstr[0:128, 0:512], lhsT=ones_row[0:1, 0:128],
                         rhs=ctv[0:1, half * 512:(half + 1) * 512],
                         start=True, stop=True).then_inc(pe, 1)
                c_pe.inc()
                sc.wait_ge(pe, c_pe.v)
                sc.copy(out=ctb[:, half * 512:(half + 1) * 512],
                        in_=pstr[0:128, 0:512]).then_inc(cmp, 1)
                c_cmp.inc()
                t.wait_ge(cmp, c_cmp.v)

            in_free = 0
            st_done = 0
            for i in range(NT):
                r0 = i * 128
                sync.wait_ge(cmp, in_free)
                sync.dma_start(out=att[:], in_=attn_in[r0:r0 + 128, :]).then_inc(ld, 16)
                sync.dma_start(out=xt[:], in_=x1_in[r0:r0 + 128, :]).then_inc(ld, 16)
                sync.dma_start(out=xnt[:], in_=xn1_in[r0:r0 + 128, :]).then_inc(ld, 16)
                sync.dma_start(out=wt[:], in_=wtok_in[r0:r0 + 128, :]).then_inc(ld, 16)
                sync.dma_start(out=pt2[0][:], in_=p_in[0][r0:r0 + 128, :]).then_inc(ld, 16)
                c_ld.inc(80)
                t.wait_ge(ld, c_ld.v - 64)
                for cch in range(8):
                    t.matmul(out=pstr[0:128, 0:128], lhsT=att[:, cch * 128:(cch + 1) * 128],
                             rhs=identbt[0:128, 0:128], start=True, stop=True).then_inc(pe, 1)
                    c_pe.inc()
                    sc.wait_ge(pe, c_pe.v)
                    sc.copy(out=colb[:, cch * 128:(cch + 1) * 128], in_=pstr[0:128, 0:128])
                    sc.activation(out=junk[:, 0:128], in_=colb[:, cch * 128:(cch + 1) * 128],
                                  func=AF.Abs).then_inc(cmp, 1)
                    c_cmp.inc()
                    t.wait_ge(cmp, c_cmp.v)
                for half in range(2):
                    for kc in range(8):
                        t.matmul(out=psS[half][0:128, 0:512],
                                 lhsT=colb[:, kc * 128:(kc + 1) * 128],
                                 rhs=wosb[:, kc * 1024 + half * 512:kc * 1024 + half * 512 + 512],
                                 start=(kc == 0), stop=(kc == 7)).then_inc(pe, 1)
                        c_pe.inc()
                v.wait_ge(pe, c_pe.v)
                add_done = []
                for e in range(1, 8):
                    if e >= 2:
                        sync.wait_ge(cmp, add_done[e - 3] if e >= 3 else add_done[0])
                    sync.dma_start(out=pt2[e % 2][:],
                                   in_=p_in[e][r0:r0 + 128, :]).then_inc(ld, 16)
                    c_ld.inc(16)
                    v.wait_ge(ld, c_ld.v)
                    if e == 1:
                        v.tensor_tensor(out=acc[:], in0=pt2[0][:], in1=pt2[1][:],
                                        op=OP.add).then_inc(cmp, 1)
                    else:
                        v.tensor_tensor(out=acc[:], in0=acc[:], in1=pt2[e % 2][:],
                                        op=OP.add).then_inc(cmp, 1)
                    c_cmp.inc()
                    add_done.append(c_cmp.v)
                v.tensor_reduce(out=sm[:, 0:1], in_=wt[:], axis=AX.X, op=OP.add).then_inc(cmp, 1)
                v.wait_ge(cmp, c_cmp.inc())
                v.tensor_scalar(out=st2[:], in0=ctb[:], scalar1=sm[:, 0:1], scalar2=None,
                                op0=OP.mult).then_inc(cmp, 1)
                v.wait_ge(cmp, c_cmp.inc())
                v.tensor_tensor(out=st2[:], in0=st2[:], in1=acc[:], op=OP.add)
                v.tensor_tensor(out=st2[:], in0=st2[:], in1=xnt[:], op=OP.add).then_inc(cmp, 1)
                c_cmp.inc()
                sc.wait_ge(cmp, c_cmp.v)
                sc.activation(out=acc[:], in_=st2[:], func=AF.Silu).then_inc(cmp, 1)
                c_cmp.inc()
                v.wait_ge(pe, c_pe.v)
                for half in range(2):
                    v.tensor_copy(out=st1[:, half * 512:half * 512 + 512],
                                  in_=psS[half][0:128, 0:512])
                v.wait_ge(cmp, c_cmp.v)
                v.wait_ge(st, st_done)
                v.tensor_tensor(out=st1[:], in0=st1[:], in1=acc[:], op=OP.mult)
                v.tensor_tensor(out=st1[:], in0=st1[:], in1=xt[:], op=OP.add).then_inc(cmp, 1)
                c_cmp.inc()
                in_free = c_cmp.v
                g.wait_ge(cmp, c_cmp.v)
                g.dma_start(out=fin_out[r0:r0 + 128, :], in_=st1[:]).then_inc(st, 16)
                c_st.inc(16)
                st_done = c_st.v
            v.wait_ge(st, c_st.v)
            v.wait_ge(ld, c_ld.v)
    return nc


def run_e2(inputs, c_out, attn, parts, wtok, cvs):
    identb = np_identity_bf16()
    wob = np.ascontiguousarray(
        inputs["L1_attn_W_o"].astype(BF).reshape(8, 128, 1024).transpose(1, 0, 2)
        .reshape(128, 8 * 1024))
    in_maps = []
    for c in range(NC):
        r0 = c * TOK
        m = {
            "x1": np.ascontiguousarray(c_out["x1"][r0:r0 + TOK]),
            "xn1": np.ascontiguousarray(c_out["xn1"][r0:r0 + TOK]),
            "attn": np.ascontiguousarray(attn[r0:r0 + TOK]),
            "wo": wob, "wtok": np.ascontiguousarray(wtok[r0:r0 + TOK]),
            "cvs": cvs, "identb": identb,
        }
        for e in range(8):
            m[f"p{e}"] = np.ascontiguousarray(parts[e][r0:r0 + TOK])
        in_maps.append(m)
    nc = build_e2()
    res = run_bass_kernel_spmd(nc, in_maps, core_ids=list(range(NC)))
    fin = np.concatenate([res.results[c]["fin"] for c in range(NC)], axis=0)
    return fin.reshape(B, N, D)


# version 03 (final)
# speedup vs baseline: 1.0906x; 1.0155x over previous
import numpy as np
import ml_dtypes
from contextlib import ExitStack

import concourse.bass as bass
import concourse.mybir as mybir
from concourse.bass_utils import run_bass_kernel_spmd
from concourse.alu_op_type import AluOpType as OP

F32 = mybir.dt.float32
BF16 = mybir.dt.bfloat16
I32 = mybir.dt.int32
AF = mybir.ActivationFunctionType
AX = mybir.AxisListType
BF = ml_dtypes.bfloat16

NC = 8
B, N, D = 2, 8192, 1024
H, DH = 16, 64
I_FFN = 2816


def np_identity_bf16(n=128):
    return np.eye(n, dtype=BF)


class Ctr:
    def __init__(self):
        self.v = 0

    def inc(self, n=1):
        self.v += n
        return self.v


def emit_fp4_quant(nc, x_sb, out_sb, scr, P, W, cmp, c_cmp, done_sem=None):
    v, sc = nc.vector, nc.scalar
    nb = W // 16
    amax, s_t, r_t = scr["amax"], scr["s"], scr["r"]
    u, au, sg = scr["u"], scr["au"], scr["sg"]
    t1, t2, ti = scr["t1"], scr["t2"], scr["ti"]
    x3 = x_sb[0:P, 0:W].rearrange("p (a b) -> p a b", b=16)
    v.tensor_reduce(out=amax[0:P, 0:nb], in_=x3, axis=AX.X, op=OP.max,
                    apply_absolute_value=True).then_inc(cmp, 1)
    v.wait_ge(cmp, c_cmp.inc())
    v.tensor_scalar(out=s_t[0:P, 0:nb], in0=amax[0:P, 0:nb], scalar1=1.0 / 6.0,
                    scalar2=1e-12, op0=OP.mult, op1=OP.max).then_inc(cmp, 1)
    v.wait_ge(cmp, c_cmp.inc())
    v.reciprocal(out=r_t[0:P, 0:nb], in_=s_t[0:P, 0:nb]).then_inc(cmp, 1)
    v.wait_ge(cmp, c_cmp.inc())
    rb = r_t[0:P, 0:nb].unsqueeze(2).broadcast_to([P, nb, 16])
    v.tensor_tensor(out=u[0:P, 0:W], in0=x3, in1=rb, op=OP.mult).then_inc(cmp, 1)
    c_cmp.inc()
    sc.wait_ge(cmp, c_cmp.v)
    sc.activation(out=au[0:P, 0:W], in_=u[0:P, 0:W], func=AF.Abs)
    sc.activation(out=sg[0:P, 0:W], in_=u[0:P, 0:W], func=AF.Sign).then_inc(cmp, 1)
    c_cmp.inc()
    v.wait_ge(cmp, c_cmp.v)
    MAGIC = float(2 ** 22)
    v.tensor_scalar(out=t1[0:P, 0:W], in0=au[0:P, 0:W], scalar1=MAGIC, scalar2=-MAGIC,
                    op0=OP.add, op1=OP.add)
    v.tensor_scalar(out=t1[0:P, 0:W], in0=t1[0:P, 0:W], scalar1=2.0, scalar2=None,
                    op0=OP.min)
    v.scalar_tensor_tensor(out=t2[0:P, 0:W], in0=au[0:P, 0:W], scalar=2.5,
                           in1=t1[0:P, 0:W], op0=OP.is_gt, op1=OP.add)
    v.scalar_tensor_tensor(out=t1[0:P, 0:W], in0=au[0:P, 0:W], scalar=3.5,
                           in1=t2[0:P, 0:W], op0=OP.is_gt, op1=OP.add)
    v.scalar_tensor_tensor(out=t2[0:P, 0:W], in0=au[0:P, 0:W], scalar=5.0,
                           in1=t1[0:P, 0:W], op0=OP.is_gt, op1=OP.add)
    v.scalar_tensor_tensor(out=t1[0:P, 0:W], in0=au[0:P, 0:W], scalar=5.0,
                           in1=t2[0:P, 0:W], op0=OP.is_gt, op1=OP.add)
    v.tensor_tensor(out=t2[0:P, 0:W], in0=t1[0:P, 0:W], in1=sg[0:P, 0:W], op=OP.mult)
    sb = s_t[0:P, 0:nb].unsqueeze(2).broadcast_to([P, nb, 16])
    inst = v.tensor_tensor(out=out_sb[0:P, 0:W].rearrange("p (a b) -> p a b", b=16),
                           in0=t2[0:P, 0:W].rearrange("p (a b) -> p a b", b=16),
                           in1=sb, op=OP.mult)
    if done_sem is not None:
        inst.then_inc(done_sem, 1)


def mk_quant_scratch(nc, ctx, P, W, tag):
    E = ctx.enter_context
    nb = W // 16
    return {
        "amax": E(nc.sbuf_tensor(f"{tag}_amax", [P, nb], F32)),
        "s": E(nc.sbuf_tensor(f"{tag}_s", [P, nb], F32)),
        "r": E(nc.sbuf_tensor(f"{tag}_r", [P, nb], F32)),
        "u": E(nc.sbuf_tensor(f"{tag}_u", [P, W], F32)),
        "au": E(nc.sbuf_tensor(f"{tag}_au", [P, W], F32)),
        "sg": E(nc.sbuf_tensor(f"{tag}_sg", [P, W], F32)),
        "t1": E(nc.sbuf_tensor(f"{tag}_t1", [P, W], F32)),
        "t2": E(nc.sbuf_tensor(f"{tag}_t2", [P, W], F32)),
        "ti": E(nc.sbuf_tensor(f"{tag}_ti", [P, W], I32)),
    }


def build_a0():
    nc = bass.Bass("TRN2", target_bir_lowering=False)
    wg = nc.dram_tensor("wg", [352, 1024], F32, kind="ExternalInput").ap()
    wu = nc.dram_tensor("wu", [352, 1024], F32, kind="ExternalInput").ap()
    wd = nc.dram_tensor("wd", [128, I_FFN], F32, kind="ExternalInput").ap()
    w1 = nc.dram_tensor("w1", [1024, 4096], F32, kind="ExternalInput").ap()
    w2 = nc.dram_tensor("w2", [4096, 1024], F32, kind="ExternalInput").ap()
    ident_in = nc.dram_tensor("ident", [128, 128], BF16, kind="ExternalInput").ap()
    wgqT = nc.dram_tensor("wgqT", [1024, 352], BF16, kind="ExternalOutput").ap()
    wuqT = nc.dram_tensor("wuqT", [1024, 352], BF16, kind="ExternalOutput").ap()
    wdqT = nc.dram_tensor("wdqT", [I_FFN, 128], BF16, kind="ExternalOutput").ap()
    w1b = nc.dram_tensor("w1b", [1024, 4096], BF16, kind="ExternalOutput").ap()
    w2b = nc.dram_tensor("w2b", [4096, 1024], BF16, kind="ExternalOutput").ap()

    tiles = []
    for src, dst, rts, W in ((wg, wgqT, (128, 128, 96), 1024),
                             (wu, wuqT, (128, 128, 96), 1024),
                             (wd, wdqT, (128,), I_FFN)):
        r0 = 0
        for P in rts:
            tiles.append((src, r0, P, W, dst))
            r0 += P

    with ExitStack() as ctx:
        E = ctx.enter_context
        ident = E(nc.sbuf_tensor("identt", [128, 128], BF16))
        xin = [E(nc.sbuf_tensor(f"xin{b}", [128, I_FFN], F32)) for b in range(2)]
        qo = [E(nc.sbuf_tensor(f"qo{b}", [128, I_FFN], BF16)) for b in range(2)]
        scr = mk_quant_scratch(nc, ctx, 128, I_FFN, "qs")
        tpo = [E(nc.sbuf_tensor(f"tpo{b}", [128, 128], BF16)) for b in range(2)]
        cvt = [E(nc.sbuf_tensor(f"cvt{b}", [128, 4096], BF16)) for b in range(2)]
        ps = [E(nc.psum_tensor(f"ps{b}", [128, 512], F32)) for b in range(2)]
        ld = E(nc.semaphore("ld"))
        cmp = E(nc.semaphore("cmp"))
        qd = E(nc.semaphore("qd"))
        pe = E(nc.semaphore("pe"))
        ev = E(nc.semaphore("ev"))
        st = E(nc.semaphore("st"))
        ld2 = E(nc.semaphore("ld2"))
        st2 = E(nc.semaphore("st2"))
        block = E(nc.Block())

        nsl_of = [t[3] // 128 for t in tiles]
        slice_base = np.cumsum([0] + nsl_of).tolist()

        @block.sync
        def _(sync):
            v, t, sc, g = nc.vector, nc.tensor, nc.scalar, nc.gpsimd
            sync.dma_start(out=ident[:], in_=ident_in).then_inc(ld, 16)
            for i, (src, r0, P, W, dst) in enumerate(tiles):
                if i >= 2:
                    sync.wait_ge(qd, i - 1)
                sync.dma_start(out=xin[i % 2][0:P, 0:W],
                               in_=src[r0:r0 + P, :]).then_inc(ld, 16)
            c_cmp = Ctr()
            for i, (src, r0, P, W, dst) in enumerate(tiles):
                v.wait_ge(ld, 16 + 16 * (i + 1))
                if i >= 1:
                    v.wait_ge(pe, slice_base[i])
                emit_fp4_quant(nc, xin[i % 2], qo[i % 2], scr, P, W, cmp, c_cmp,
                               done_sem=qd)
                t.wait_ge(qd, i + 1)
                nsl = W // 128
                for sl in range(nsl):
                    gsl = slice_base[i] + sl
                    if gsl >= 2:
                        t.wait_ge(ev, gsl - 1)
                    t.matmul(out=ps[gsl % 2][0:128, 0:P],
                             lhsT=qo[i % 2][0:P, sl * 128:(sl + 1) * 128],
                             rhs=ident[0:P, 0:P],
                             start=True, stop=True).then_inc(pe, 1)
                    sc.wait_ge(pe, gsl + 1)
                    if gsl >= 2:
                        sc.wait_ge(st, 16 * (gsl - 1))
                    sc.copy(out=tpo[gsl % 2][0:128, 0:P],
                            in_=ps[gsl % 2][0:128, 0:P]).then_inc(ev, 1)
                    g.wait_ge(ev, gsl + 1)
                    g.dma_start(out=dst[sl * 128:(sl + 1) * 128, r0:r0 + P],
                                in_=tpo[gsl % 2][0:128, 0:P]).then_inc(st, 16)

            nconv = 0
            for src, dst, ntile, Wc in ((w1, w1b, 8, 4096), (w2, w2b, 32, 1024)):
                for i in range(ntile):
                    bidx = nconv % 2
                    if nconv >= 2:
                        g.wait_ge(st2, 16 * (nconv - 1))
                    g.dma_start(out=cvt[bidx][:, 0:Wc],
                                in_=src[i * 128:(i + 1) * 128, :]).then_inc(ld2, 16)
                    g.wait_ge(ld2, 16 * (nconv + 1))
                    g.dma_start(out=dst[i * 128:(i + 1) * 128, :],
                                in_=cvt[bidx][:, 0:Wc]).then_inc(st2, 16)
                    nconv += 1
            g.wait_ge(st2, 16 * nconv)
            g.wait_ge(st, 16 * slice_base[-1])

    return nc


def run_a0(inputs, core_ids=None):
    ident = np_identity_bf16()
    in_maps = []
    for c in range(NC):
        in_maps.append({
            "wg": np.ascontiguousarray(inputs["L0_ffn_w_gate"][c * 352:(c + 1) * 352]),
            "wu": np.ascontiguousarray(inputs["L0_ffn_w_up"][c * 352:(c + 1) * 352]),
            "wd": np.ascontiguousarray(inputs["L0_ffn_w_down"][c * 128:(c + 1) * 128]),
            "w1": np.ascontiguousarray(inputs["L1_ffn_W1"][c]),
            "w2": np.ascontiguousarray(inputs["L1_ffn_W2"][c]),
            "ident": ident,
        })
    nc = build_a0()
    res = run_bass_kernel_spmd(nc, in_maps, core_ids=list(range(NC)))
    wgqT = np.concatenate([r["wgqT"] for r in res.results], axis=1)
    wuqT = np.concatenate([r["wuqT"] for r in res.results], axis=1)
    wdqT = np.concatenate([r["wdqT"] for r in res.results], axis=1)
    w1b = np.stack([r["w1b"] for r in res.results], axis=0)
    w2b = np.stack([r["w2b"] for r in res.results], axis=0)
    return {"wgqT": wgqT, "wuqT": wuqT, "wdqT": wdqT, "w1b": w1b, "w2b": w2b}


def kernel(**inputs):
    a0 = run_a0(inputs)
    a = run_a(inputs, a0)
    o = run_b(inputs, a)
    cc = run_c(inputs, a, o)
    attn = run_d(inputs, cc)
    parts, wtok, cvs = run_e1(inputs, cc, a0)
    return run_e2(inputs, cc, attn, parts, wtok, cvs)


NT = 16
TOK = 2048


def build_a():
    nc = bass.Bass("TRN2", target_bir_lowering=False)
    x_in = nc.dram_tensor("x", [TOK, D], F32, kind="ExternalInput").ap()
    Wq_in = nc.dram_tensor("Wq", [D, D], F32, kind="ExternalInput").ap()
    Wk_in = nc.dram_tensor("Wk", [D, D], F32, kind="ExternalInput").ap()
    Wv_in = nc.dram_tensor("Wv", [D, D], F32, kind="ExternalInput").ap()
    Wf_in = nc.dram_tensor("Wf", [D, H], F32, kind="ExternalInput").ap()
    Wb_in = nc.dram_tensor("Wb", [D, H], F32, kind="ExternalInput").ap()
    gw1_in = nc.dram_tensor("gw1", [D, 64], F32, kind="ExternalInput").ap()
    gw2_in = nc.dram_tensor("gw2", [64, D], F32, kind="ExternalInput").ap()
    delta_in = nc.dram_tensor("delta", [1, H], F32, kind="ExternalInput").ap()
    wgqT_in = nc.dram_tensor("wgqT", [D, I_FFN], BF16, kind="ExternalInput").ap()
    wuqT_in = nc.dram_tensor("wuqT", [D, I_FFN], BF16, kind="ExternalInput").ap()
    wdqT_in = nc.dram_tensor("wdqT", [I_FFN, D], BF16, kind="ExternalInput").ap()
    ident_in = nc.dram_tensor("ident", [128, 128], BF16, kind="ExternalInput").ap()
    identf_in = nc.dram_tensor("identf", [128, 128], F32, kind="ExternalInput").ap()

    q_out = nc.dram_tensor("q", [TOK, D], F32, kind="ExternalOutput").ap()
    k_out = nc.dram_tensor("k", [TOK, D], BF16, kind="ExternalOutput").ap()
    v_out = nc.dram_tensor("v", [TOK, D], BF16, kind="ExternalOutput").ap()
    logf_out = nc.dram_tensor("logf", [TOK, H], F32, kind="ExternalOutput").ap()
    logb_out = nc.dram_tensor("logb", [TOK, H], F32, kind="ExternalOutput").ap()
    gate_out = nc.dram_tensor("gate", [TOK, D], F32, kind="ExternalOutput").ap()
    g_out = nc.dram_tensor("g", [TOK, D], F32, kind="ExternalOutput").ap()

    with ExitStack() as ctx:
        E = ctx.enter_context
        ident = E(nc.sbuf_tensor("identt", [128, 128], BF16))
        identf = E(nc.sbuf_tensor("identft", [128, 128], F32))
        xn_col = E(nc.sbuf_tensor("xn_col", [128, 8 * TOK], BF16))
        xq_col = E(nc.sbuf_tensor("xq_col", [128, 8 * TOK], BF16))
        scr = mk_quant_scratch(nc, ctx, 128, 1024, "qs")
        eps_b = E(nc.sbuf_tensor("eps_b", [128, 1], F32))
        epsln = E(nc.sbuf_tensor("epsln", [128, 1], F32))
        one_b = E(nc.sbuf_tensor("one_b", [128, 1], F32))
        ones_row = E(nc.sbuf_tensor("ones_row", [1, 128], F32))
        delta_t = E(nc.sbuf_tensor("delta_t", [128, H], F32))
        ld = E(nc.semaphore("ld"))
        ldg = E(nc.semaphore("ldg"))
        cmp = E(nc.semaphore("cmp"))
        pe = E(nc.semaphore("pe"))
        ev = E(nc.semaphore("ev"))
        st = E(nc.semaphore("st"))
        c_ld, c_ldg, c_cmp, c_pe, c_ev, c_st = Ctr(), Ctr(), Ctr(), Ctr(), Ctr(), Ctr()
        block = E(nc.Block())

        with ExitStack() as ctxA:
            EA = ctxA.enter_context
            xt = EA(nc.sbuf_tensor("xt", [128, 1024], F32))
            xqt = EA(nc.sbuf_tensor("xqt", [128, 1024], BF16))
            wqkv = EA(nc.sbuf_tensor("wqkv", [128, 3 * 8 * 1024], BF16))
            wcat = EA(nc.sbuf_tensor("wcat", [128, 8 * 96], BF16))
            gw2b = EA(nc.sbuf_tensor("gw2b", [64, 1024], BF16))
            stg = EA(nc.sbuf_tensor("stg", [128, 1024], F32))
            stg2 = EA(nc.sbuf_tensor("stg2", [128, 1024], F32))
            sml = EA(nc.sbuf_tensor("sml", [128, 128], F32))
            smlb = EA(nc.sbuf_tensor("smlb", [128, 128], BF16))
            xg1c = EA(nc.sbuf_tensor("xg1c", [64, TOK], BF16))
            ps = [EA(nc.psum_tensor(f"pA{b}", [128, 512], F32)) for b in range(4)]

            @block.sync
            def _(sync):
                v, t, sc, g = nc.vector, nc.tensor, nc.scalar, nc.gpsimd

                def dma(out, in_, q=sync):
                    q.dma_start(out=out, in_=in_).then_inc(ld, 16)
                    return c_ld.inc(16)

                def dmac(out, in_):
                    g.dma_start(out=out, in_=in_).then_inc(ldg, 16)
                    return c_ldg.inc(16)

                dma(ident[:], ident_in)
                dma(identf[:], identf_in)
                dma(delta_t[0:1, :], delta_in)
                for w_i, (wsrc) in enumerate((Wq_in, Wk_in, Wv_in)):
                    for kc in range(8):
                        dmac(wqkv[:, (w_i * 8 + kc) * 1024:(w_i * 8 + kc) * 1024 + 1024],
                             wsrc[kc * 128:(kc + 1) * 128, :])
                for kc in range(8):
                    dmac(wcat[:, kc * 96:kc * 96 + 16], Wf_in[kc * 128:(kc + 1) * 128, :])
                    dmac(wcat[:, kc * 96 + 16:kc * 96 + 32], Wb_in[kc * 128:(kc + 1) * 128, :])
                    dmac(wcat[:, kc * 96 + 32:kc * 96 + 96], gw1_in[kc * 128:(kc + 1) * 128, :])
                v.memset(eps_b[:], 1e-6)
                v.memset(epsln[:], 1e-5)
                v.memset(one_b[:], 1.0)
                v.memset(ones_row[:], 1.0)
                v.sem_inc(cmp, 1)
                c_cmp.inc()
                dmac(gw2b[:, :], gw2_in)
                wld = c_ldg.v
                t.wait_ge(ld, 48)
                t.wait_ge(cmp, c_cmp.v)
                t.matmul(out=ps[0][0:128, 0:H], lhsT=ones_row[0:1, 0:128],
                         rhs=delta_t[0:1, :], start=True, stop=True).then_inc(pe, 1)
                c_pe.inc()
                v.wait_ge(pe, c_pe.v)
                v.tensor_copy(out=delta_t[:, :], in_=ps[0][0:128, 0:H]).then_inc(cmp, 1)
                c_cmp.inc()

                for i in range(NT):
                    ti = i * 1024
                    if i >= 1:
                        sync.wait_ge(pe, c_pe.v)
                    lv = dma(xt[:], x_in[i * 128:(i + 1) * 128, :])
                    v.wait_ge(ld, lv)
                    v.wait_ge(st, c_st.v)
                    sc.wait_ge(st, c_st.v)
                    if i == 0:
                        v.wait_ge(ldg, wld)
                        t.wait_ge(ldg, wld)
                    am = scr["amax"]
                    v.tensor_reduce(out=am[:, 0:1], in_=xt[:], axis=AX.X, op=OP.add).then_inc(cmp, 1)
                    v.wait_ge(cmp, c_cmp.inc())
                    v.tensor_scalar(out=am[:, 1:2], in0=am[:, 0:1],
                                    scalar1=1.0 / 1024.0, scalar2=None, op0=OP.mult).then_inc(cmp, 1)
                    v.wait_ge(cmp, c_cmp.inc())
                    v.tensor_tensor(out=scr["t2"][:, 0:1024], in0=xt[:],
                                    in1=am[:, 1:2].broadcast_to([128, 1024]),
                                    op=OP.subtract)
                    v.scalar_tensor_tensor(out=scr["u"][:, 0:1024], in0=scr["t2"][:, 0:1024],
                                           scalar=1.0, in1=scr["t2"][:, 0:1024],
                                           op0=OP.mult, op1=OP.mult,
                                           accum_out=am[:, 2:3]).then_inc(cmp, 1)
                    v.wait_ge(cmp, c_cmp.inc())
                    v.tensor_scalar(out=am[:, 3:4], in0=am[:, 2:3],
                                    scalar1=1.0 / 1024.0, scalar2=None, op0=OP.mult).then_inc(cmp, 1)
                    c_cmp.inc()
                    sc.wait_ge(cmp, c_cmp.v)
                    sc.activation(out=am[:, 4:5], in_=am[:, 3:4],
                                  func=AF.Sqrt, bias=epsln[:], scale=1.0).then_inc(cmp, 1)
                    c_cmp.inc()
                    v.wait_ge(cmp, c_cmp.v)
                    v.reciprocal(out=am[:, 5:6], in_=am[:, 4:5]).then_inc(cmp, 1)
                    v.wait_ge(cmp, c_cmp.inc())
                    v.tensor_tensor(out=xt[:], in0=scr["t2"][:, 0:1024],
                                    in1=am[:, 5:6].broadcast_to([128, 1024]),
                                    op=OP.mult).then_inc(cmp, 1)
                    c_cmp.inc()

                    emit_fp4_quant(nc, xt, xqt, scr, 128, 1024, cmp, c_cmp, done_sem=cmp)
                    c_cmp.inc()

                    t.wait_ge(cmp, c_cmp.v)
                    for cch in range(8):
                        t.matmul(out=ps[2][0:128, 0:128], lhsT=xt[:, cch * 128:(cch + 1) * 128],
                                 rhs=identf[:], start=True, stop=True).then_inc(pe, 1)
                        c_pe.inc()
                        sc.wait_ge(pe, c_pe.v)
                        sc.copy(out=xn_col[:, cch * TOK + i * 128: cch * TOK + (i + 1) * 128],
                                in_=ps[2][0:128, 0:128]).then_inc(ev, 1)
                        c_ev.inc()
                        t.wait_ge(ev, c_ev.v)
                        t.matmul(out=ps[3][0:128, 0:128], lhsT=xqt[:, cch * 128:(cch + 1) * 128],
                                 rhs=ident[:], start=True, stop=True).then_inc(pe, 1)
                        c_pe.inc()
                        sc.wait_ge(pe, c_pe.v)
                        sc.copy(out=xq_col[:, cch * TOK + i * 128: cch * TOK + (i + 1) * 128],
                                in_=ps[3][0:128, 0:128]).then_inc(ev, 1)
                        c_ev.inc()
                        t.wait_ge(ev, c_ev.v)

                    for w_i, (dst, is_q, is_norm) in enumerate(((q_out, True, True),
                                                                (k_out, False, True),
                                                                (v_out, False, False))):
                        t.wait_ge(cmp, c_cmp.v)
                        for half in range(2):
                            for kc in range(8):
                                t.matmul(out=ps[half][0:128, 0:512],
                                         lhsT=xn_col[:, kc * TOK + i * 128: kc * TOK + (i + 1) * 128],
                                         rhs=wqkv[:, (w_i * 8 + kc) * 1024 + half * 512:
                                                  (w_i * 8 + kc) * 1024 + half * 512 + 512],
                                         start=(kc == 0), stop=(kc == 7)).then_inc(pe, 1)
                                c_pe.inc()
                            sc.wait_ge(pe, c_pe.v)
                            sc.copy(out=stg[:, half * 512:half * 512 + 512],
                                    in_=ps[half][0:128, 0:512]).then_inc(cmp, 1)
                            c_cmp.inc()
                        v.wait_ge(cmp, c_cmp.v)
                        if is_norm:
                            v.scalar_tensor_tensor(out=stg2[:, 0:1024], in0=stg[:, 0:1024],
                                                   scalar=1.0, in1=stg[:, 0:1024],
                                                   op0=OP.mult, op1=OP.mult)
                            v.tensor_reduce(out=sml[:, 0:H],
                                            in_=stg2[:, 0:1024].rearrange("p (h d) -> p h d", d=64),
                                            axis=AX.X, op=OP.add).then_inc(cmp, 1)
                            v.wait_ge(cmp, c_cmp.inc())
                            v.sem_inc(cmp, 1)
                            c_cmp.inc()
                            sc.wait_ge(cmp, c_cmp.v)
                            sc.activation(out=sml[:, H:2 * H], in_=sml[:, 0:H],
                                          func=AF.Sqrt, bias=eps_b[:], scale=1.0).then_inc(cmp, 1)
                            c_cmp.inc()
                            v.wait_ge(cmp, c_cmp.v)
                            v.reciprocal(out=sml[:, 2 * H:3 * H], in_=sml[:, H:2 * H]).then_inc(cmp, 1)
                            v.wait_ge(cmp, c_cmp.inc())
                            rb = sml[:, 2 * H:3 * H].unsqueeze(2).broadcast_to([128, H, 64])
                            v.tensor_tensor(out=stg2[:, 0:1024].rearrange("p (h d) -> p h d", d=64),
                                            in0=stg[:, 0:1024].rearrange("p (h d) -> p h d", d=64),
                                            in1=rb, op=OP.mult).then_inc(cmp, 1)
                            c_cmp.inc()
                            src_t = stg2
                        else:
                            src_t = stg
                        if is_q:
                            g.wait_ge(cmp, c_cmp.v)
                            g.dma_start(out=dst[i * 128:(i + 1) * 128, :],
                                        in_=src_t[:, 0:1024]).then_inc(st, 16)
                            c_st.inc(16)
                        else:
                            g.wait_ge(cmp, c_cmp.v)
                            g.dma_start(out=dst[i * 128:(i + 1) * 128, :],
                                        in_=src_t[:, 0:1024]).then_inc(st, 16)
                            c_st.inc(16)
                        sc.wait_ge(st, c_st.v)
                        v.wait_ge(st, c_st.v)

                    for kc in range(8):
                        t.matmul(out=ps[2][0:128, 0:96],
                                 lhsT=xn_col[:, kc * TOK + i * 128: kc * TOK + (i + 1) * 128],
                                 rhs=wcat[:, kc * 96:(kc + 1) * 96],
                                 start=(kc == 0), stop=(kc == 7)).then_inc(pe, 1)
                        c_pe.inc()
                    sc.wait_ge(pe, c_pe.v)
                    sc.copy(out=sml[:, 0:96], in_=ps[2][0:128, 0:96]).then_inc(cmp, 1)
                    c_cmp.inc()
                    v.wait_ge(cmp, c_cmp.v)
                    v.tensor_tensor(out=sml[:, 96:96 + H], in0=sml[:, 0:H], in1=delta_t[:, 0:H],
                                    op=OP.add).then_inc(cmp, 1)
                    v.wait_ge(cmp, c_cmp.inc())
                    v.sem_inc(cmp, 1)
                    c_cmp.inc()
                    for (c0, dsttens) in ((96, logf_out), (16, logb_out)):
                        sc.wait_ge(cmp, c_cmp.v)
                        sc.activation(out=sml[:, 112:112 + H], in_=sml[:, c0:c0 + H],
                                      func=AF.Abs).then_inc(cmp, 1)
                        sc.wait_ge(cmp, c_cmp.inc())
                        sc.activation(out=sml[:, 112:112 + H], in_=sml[:, 112:112 + H],
                                      func=AF.Exp, scale=-1.0).then_inc(cmp, 1)
                        sc.wait_ge(cmp, c_cmp.inc())
                        sc.activation(out=sml[:, 112:112 + H], in_=sml[:, 112:112 + H],
                                      func=AF.Ln, bias=one_b[:], scale=1.0).then_inc(cmp, 1)
                        c_cmp.inc()
                        v.wait_ge(cmp, c_cmp.v)
                        v.wait_ge(st, c_st.v)
                        v.tensor_scalar(out=sml[:, 0:H], in0=sml[:, c0:c0 + H],
                                        scalar1=-1.0, scalar2=0.0, op0=OP.mult, op1=OP.max).then_inc(cmp, 1)
                        v.wait_ge(cmp, c_cmp.inc())
                        v.scalar_tensor_tensor(out=sml[:, 0:H], in0=sml[:, 0:H], scalar=-1.0,
                                               in1=sml[:, 112:112 + H], op0=OP.bypass, op1=OP.add).then_inc(cmp, 1)
                        v.wait_ge(cmp, c_cmp.inc())
                        v.tensor_scalar(out=sml[:, 0:H], in0=sml[:, 0:H], scalar1=-1.0,
                                        scalar2=None, op0=OP.mult).then_inc(cmp, 1)
                        c_cmp.inc()
                        g.wait_ge(cmp, c_cmp.v)
                        g.dma_start(out=dsttens[i * 128:(i + 1) * 128, :],
                                    in_=sml[:, 0:H]).then_inc(st, 16)
                        c_st.inc(16)
                    v.wait_ge(cmp, c_cmp.v)
                    v.tensor_copy(out=smlb[:, 0:64], in_=sml[:, 32:96]).then_inc(cmp, 1)
                    c_cmp.inc()
                    t.wait_ge(cmp, c_cmp.v)
                    t.matmul(out=ps[3][0:64, 0:128], lhsT=smlb[0:128, 0:64], rhs=ident[:],
                             start=True, stop=True).then_inc(pe, 1)
                    c_pe.inc()
                    sc.wait_ge(pe, c_pe.v)
                    sc.copy(out=xg1c[0:64, i * 128:(i + 1) * 128],
                            in_=ps[3][0:64, 0:128]).then_inc(ev, 1)
                    c_ev.inc()
                    t.wait_ge(ev, c_ev.v)

                    sc.wait_ge(st, c_st.v)
                    t.wait_ge(cmp, c_cmp.v)
                    for half in range(2):
                        t.matmul(out=ps[half][0:128, 0:512],
                                 lhsT=xg1c[0:64, i * 128:(i + 1) * 128],
                                 rhs=gw2b[0:64, half * 512:half * 512 + 512],
                                 start=True, stop=True).then_inc(pe, 1)
                        c_pe.inc()
                        sc.wait_ge(pe, c_pe.v)
                        sc.activation(out=stg[:, half * 512:half * 512 + 512],
                                      in_=ps[half][0:128, 0:512],
                                      func=AF.Sigmoid).then_inc(cmp, 1)
                        c_cmp.inc()
                    g.wait_ge(cmp, c_cmp.v)
                    g.dma_start(out=gate_out[i * 128:(i + 1) * 128, :],
                                in_=stg[:, 0:1024]).then_inc(st, 16)
                    c_st.inc(16)
                    t.wait_ge(st, c_st.v)

        with ExitStack() as ctxB:
            EB = ctxB.enter_context
            g_acc = EB(nc.sbuf_tensor("g_acc", [128, NT * 1024], F32))
            wgu = EB(nc.sbuf_tensor("wgu", [128, 8 * 1024], BF16))
            wdc = EB(nc.sbuf_tensor("wdc", [128, 4 * 1024], BF16))
            hq_t = EB(nc.sbuf_tensor("hq_t", [128, 512], BF16))
            hf_t = EB(nc.sbuf_tensor("hf_t", [128, 512], F32))
            hcol = EB(nc.sbuf_tensor("hcol", [128, 512], BF16))
            sil = EB(nc.sbuf_tensor("sil", [128, 512], F32))
            psB = [EB(nc.psum_tensor(f"pB{b}", [128, 512], F32)) for b in range(6)]

            @block.sync
            def _(sync):
                v, t, sc, g = nc.vector, nc.tensor, nc.scalar, nc.gpsimd
                sync.wait_ge(st, c_st.v)
                sync.wait_ge(pe, c_pe.v)
                v.wait_ge(st, c_st.v)
                v.wait_ge(pe, c_pe.v)
                for i in range(NT):
                    v.memset(g_acc[:, i * 1024:(i + 1) * 1024], 0.0)
                v.sem_inc(cmp, 1)
                c_cmp.inc()
                zdone = c_cmp.v
                nchunks = [(n * 512, 512) for n in range(5)] + [(2560, 256)]
                for nci, (n0, nw) in enumerate(nchunks):
                    if nci >= 1:
                        sync.wait_ge(pe, c_pe.v)
                    lv0 = c_ld.v
                    for kc in range(8):
                        sync.dma_start(out=wgu[:, kc * 1024:kc * 1024 + nw],
                                       in_=wgqT_in[kc * 128:(kc + 1) * 128, n0:n0 + nw]).then_inc(ld, 16)
                        c_ld.inc(16)
                        sync.dma_start(out=wgu[:, kc * 1024 + 512:kc * 1024 + 512 + nw],
                                       in_=wuqT_in[kc * 128:(kc + 1) * 128, n0:n0 + nw]).then_inc(ld, 16)
                        c_ld.inc(16)
                    nsl = nw // 128
                    for sl in range(nsl):
                        sync.dma_start(out=wdc[:, sl * 1024:(sl + 1) * 1024],
                                       in_=wdqT_in[n0 + sl * 128:n0 + (sl + 1) * 128, :]).then_inc(ld, 16)
                        c_ld.inc(16)
                    t.wait_ge(ld, c_ld.v)
                    if nci == 0:
                        v.wait_ge(cmp, zdone)
                    for i in range(NT):
                        v.wait_ge(pe, c_pe.v)
                        for kc in range(8):
                            t.matmul(out=psB[0][0:128, 0:nw],
                                     lhsT=xq_col[:, kc * TOK + i * 128: kc * TOK + (i + 1) * 128],
                                     rhs=wgu[:, kc * 1024:kc * 1024 + nw],
                                     start=(kc == 0), stop=(kc == 7)).then_inc(pe, 1)
                            c_pe.inc()
                        for kc in range(8):
                            t.matmul(out=psB[1][0:128, 0:nw],
                                     lhsT=xq_col[:, kc * TOK + i * 128: kc * TOK + (i + 1) * 128],
                                     rhs=wgu[:, kc * 1024 + 512:kc * 1024 + 512 + nw],
                                     start=(kc == 0), stop=(kc == 7)).then_inc(pe, 1)
                            c_pe.inc()
                        sc.wait_ge(pe, c_pe.v)
                        sc.activation(out=sil[:, 0:nw], in_=psB[0][0:128, 0:nw],
                                      func=AF.Silu).then_inc(cmp, 1)
                        c_cmp.inc()
                        v.wait_ge(cmp, c_cmp.v)
                        v.tensor_tensor(out=hf_t[:, 0:nw], in0=sil[:, 0:nw],
                                        in1=psB[1][0:128, 0:nw], op=OP.mult).then_inc(cmp, 1)
                        c_cmp.inc()
                        emit_fp4_quant(nc, hf_t, hq_t, scr, 128, nw, cmp, c_cmp, done_sem=cmp)
                        c_cmp.inc()
                        t.wait_ge(cmp, c_cmp.v)
                        for sl in range(nsl):
                            t.matmul(out=psB[2][0:128, 0:128],
                                     lhsT=hq_t[:, sl * 128:(sl + 1) * 128],
                                     rhs=ident[:], start=True, stop=True).then_inc(pe, 1)
                            c_pe.inc()
                            sc.wait_ge(pe, c_pe.v)
                            sc.copy(out=hcol[:, 0:128], in_=psB[2][0:128, 0:128]).then_inc(ev, 1)
                            c_ev.inc()
                            t.wait_ge(ev, c_ev.v)
                            for half in range(2):
                                t.matmul(out=psB[3 + half][0:128, 0:512],
                                         lhsT=hcol[:, 0:128],
                                         rhs=wdc[:, sl * 1024 + half * 512:sl * 1024 + half * 512 + 512],
                                         start=(sl == 0), stop=(sl == nsl - 1)).then_inc(pe, 1)
                                c_pe.inc()
                        v.wait_ge(pe, c_pe.v)
                        for half in range(2):
                            v.tensor_tensor(out=g_acc[:, i * 1024 + half * 512:i * 1024 + half * 512 + 512],
                                            in0=g_acc[:, i * 1024 + half * 512:i * 1024 + half * 512 + 512],
                                            in1=psB[3 + half][0:128, 0:512], op=OP.add)
                        v.sem_inc(cmp, 1)
                        c_cmp.inc()
                        t.wait_ge(cmp, c_cmp.v)
                for i in range(NT):
                    sc.wait_ge(cmp, c_cmp.v)
                    sc.activation(out=g_acc[:, i * 1024:(i + 1) * 1024],
                                  in_=g_acc[:, i * 1024:(i + 1) * 1024],
                                  func=AF.Silu).then_inc(cmp, 1)
                    c_cmp.inc()
                    g.wait_ge(cmp, c_cmp.v)
                    g.dma_start(out=g_out[i * 128:(i + 1) * 128, :],
                                in_=g_acc[:, i * 1024:(i + 1) * 1024]).then_inc(st, 16)
                    c_st.inc(16)
                g.wait_ge(st, c_st.v)

    return nc


def run_a(inputs, a0):
    ident = np_identity_bf16()
    identf = np.eye(128, dtype=np.float32)
    x_full = inputs["L0_attn_x"].reshape(B * N, D)
    in_maps = []
    for c in range(NC):
        in_maps.append({
            "x": np.ascontiguousarray(x_full[c * TOK:(c + 1) * TOK]),
            "Wq": inputs["L0_attn_Wq"], "Wk": inputs["L0_attn_Wk"], "Wv": inputs["L0_attn_Wv"],
            "Wf": inputs["L0_attn_Wf"], "Wb": inputs["L0_attn_Wbet"],
            "gw1": inputs["L0_attn_gate_w1"], "gw2": inputs["L0_attn_gate_w2"],
            "delta": inputs["L0_attn_delta"].reshape(1, H),
            "wgqT": a0["wgqT"], "wuqT": a0["wuqT"], "wdqT": a0["wdqT"],
            "ident": ident, "identf": identf,
        })
    nc = build_a()
    res = run_bass_kernel_spmd(nc, in_maps, core_ids=list(range(NC)))
    out = {}
    for key in ("q", "k", "v", "logf", "logb", "gate", "g"):
        out[key] = np.concatenate([r[key] for r in res.results], axis=0)
    return out


NBH = 4
NCH = 128
CGIT = 30
GRP = 4
NGRP = 16


def build_b(NBH=NBH, NCH=NCH, NGRP=NGRP, CGIT=CGIT, DBG=False):
    nc = bass.Bass("TRN2", target_bir_lowering=False)
    NTOK = NCH * 64
    q_in = nc.dram_tensor("q", [NBH, NTOK, DH], F32, kind="ExternalInput").ap()
    k_in = nc.dram_tensor("k", [NBH, 64, NCH * 64], BF16, kind="ExternalInput").ap()
    v_in = nc.dram_tensor("v", [NBH, 64, NCH * 64], BF16, kind="ExternalInput").ap()
    lf_in = nc.dram_tensor("lf", [NBH, NTOK], F32, kind="ExternalInput").ap()
    lb_in = nc.dram_tensor("lb", [NBH, NTOK], F32, kind="ExternalInput").ap()
    lp_in = nc.dram_tensor("lp", [DH, NBH], F32, kind="ExternalInput").ap()
    ident_in = nc.dram_tensor("ident", [128, 128], BF16, kind="ExternalInput").ap()
    identf_in = nc.dram_tensor("identf", [128, 128], F32, kind="ExternalInput").ap()
    tri_in = nc.dram_tensor("tri", [128, 64], BF16, kind="ExternalInput").ap()
    id64_in = nc.dram_tensor("id64", [128, 64], BF16, kind="ExternalInput").ap()
    o_out = nc.dram_tensor("o", [NBH, NTOK, DH], F32, kind="ExternalOutput").ap()
    cumscr = nc.dram_tensor("cumscr", [NCH, 64], F32, kind="Internal").ap()
    if DBG:
        dbg_ab = nc.dram_tensor("dbg_ab", [64, 128], F32, kind="ExternalOutput").ap()
        dbg_cum = nc.dram_tensor("dbg_cum", [NCH, 64], F32, kind="ExternalOutput").ap()
        dbg_hall = nc.dram_tensor("dbg_hall", [64, NCH * 128], F32, kind="ExternalOutput").ap()
        dbg_u = nc.dram_tensor("dbg_u", [64, NCH * 128], F32, kind="ExternalOutput").ap()
        dbg_ka = nc.dram_tensor("dbg_ka", [64, NCH * 64], F32, kind="ExternalOutput").ap()
        dbg_va = nc.dram_tensor("dbg_va", [64, NCH * 64], F32, kind="ExternalOutput").ap()
        dbg_m = nc.dram_tensor("dbg_m", [64, NCH * 64], F32, kind="ExternalOutput").ap()
        dbg_kc = nc.dram_tensor("dbg_kc", [64, NCH * 64], F32, kind="ExternalOutput").ap()

    with ExitStack() as ctx:
        E = ctx.enter_context
        ident = E(nc.sbuf_tensor("identt", [128, 128], BF16))
        identf = E(nc.sbuf_tensor("identft", [128, 128], F32))
        tri = E(nc.sbuf_tensor("trit", [128, 64], BF16))
        id64 = E(nc.sbuf_tensor("id64t", [128, 64], BF16))
        ones_row = E(nc.sbuf_tensor("ones_row", [1, 128], F32))
        kaug = E(nc.sbuf_tensor("kaug", [128, NCH * 64], BF16))
        vaug = E(nc.sbuf_tensor("vaug", [128, NCH * 64], BF16))
        kcol = E(nc.sbuf_tensor("kcol", [64, NCH * 64], BF16))
        mall = E(nc.sbuf_tensor("mall", [128, NCH * 64], BF16))
        uall = E(nc.sbuf_tensor("uall", [64, NCH * 128], BF16))
        hall = E(nc.sbuf_tensor("hall", [64, NCH * 128], F32))
        lf_pm = E(nc.sbuf_tensor("lf_pm", [NCH, 64], F32))
        lb_pm = E(nc.sbuf_tensor("lb_pm", [NCH, 64], F32))
        cum_pm = E(nc.sbuf_tensor("cum_pm", [NCH, 64], F32))
        w_pm = E(nc.sbuf_tensor("w_pm", [NCH, 64], F32))
        w_cm = E(nc.sbuf_tensor("w_cm", [128, 128], F32))
        zer = E(nc.sbuf_tensor("zer", [NCH, 64], F32))
        cum_lin = E(nc.sbuf_tensor("cum_lin", [1, NTOK], F32))
        ablin = E(nc.sbuf_tensor("ablin", [1, 128], F32))
        ab = E(nc.sbuf_tensor("ab", [64, 128], F32))
        khat = E(nc.sbuf_tensor("khat", [128, 64], BF16))
        zdiag = E(nc.sbuf_tensor("zdiag", [64, 64], BF16))
        lam_cm = E(nc.sbuf_tensor("lam_cm", [64, 16], F32))
        diag_l = E(nc.sbuf_tensor("diag_l", [64, NBH * 64], BF16))
        Xs = [E(nc.sbuf_tensor(f"X{b}", [128, GRP * 64], F32)) for b in range(2)]
        Rs = [E(nc.sbuf_tensor(f"R{b}", [128, GRP * 64], F32)) for b in range(2)]
        Ps = [E(nc.sbuf_tensor(f"P{b}", [128, GRP * 64], F32)) for b in range(2)]
        pb16 = E(nc.sbuf_tensor("pb16", [128, GRP * 64], BF16))
        pcol = E(nc.sbuf_tensor("pcol", [64, GRP * 128], BF16))
        daug = E(nc.sbuf_tensor("daug", [128, GRP * 128], BF16))
        vball = E(nc.sbuf_tensor("vball", [64, GRP * 128], BF16))
        sm = E(nc.sbuf_tensor("smB", [128, 64], F32))
        o_sb = E(nc.sbuf_tensor("o_sb", [128, GRP * 64], F32))
        ps_T = E(nc.psum_tensor("ps_T", [128, 512], F32))
        ps_C = E(nc.psum_tensor("ps_C", [128, 512], F32))
        ps_AP = E(nc.psum_tensor("ps_AP", [128, 512], F32))
        ps_M = E(nc.psum_tensor("ps_M", [128, 512], F32))
        ps_U = E(nc.psum_tensor("ps_U", [128, 512], F32))
        ps_W = E(nc.psum_tensor("ps_W", [128, 512], F32))
        ld = E(nc.semaphore("ld"))
        ldg = E(nc.semaphore("ldg"))
        cmp = E(nc.semaphore("cmp"))
        pe = E(nc.semaphore("pe"))
        st = E(nc.semaphore("st"))
        c_ld, c_ldg, c_cmp, c_pe, c_st = Ctr(), Ctr(), Ctr(), Ctr(), Ctr()
        block = E(nc.Block())

        @block.sync
        def _(sync):
            v, t, sc, g = nc.vector, nc.tensor, nc.scalar, nc.gpsimd

            def vw():
                v.wait_ge(cmp, c_cmp.v)

            sync.dma_start(out=ident[:], in_=ident_in).then_inc(ld, 16)
            sync.dma_start(out=identf[:], in_=identf_in).then_inc(ld, 16)
            sync.dma_start(out=tri[:], in_=tri_in).then_inc(ld, 16)
            sync.dma_start(out=id64[:], in_=id64_in).then_inc(ld, 16)
            sync.dma_start(out=lam_cm[:, 0:NBH], in_=lp_in).then_inc(ld, 16)
            c_ld.inc(80)
            v.wait_ge(ld, c_ld.v)
            v.memset(ones_row[:], 1.0)
            v.memset(zer[:], 0.0)
            v.memset(zdiag[:], 0.0)
            v.memset(sm[:], 0.0)
            v.sem_inc(cmp, 1); c_cmp.inc()
            sc.wait_ge(ld, c_ld.v)
            sc.activation(out=lam_cm[:, 4:4 + NBH], in_=lam_cm[:, 0:NBH], func=AF.Abs).then_inc(cmp, 1)
            sc.wait_ge(cmp, c_cmp.inc())
            sc.activation(out=lam_cm[:, 8:8 + NBH], in_=lam_cm[:, 4:4 + NBH], func=AF.Exp, scale=-1.0).then_inc(cmp, 1)
            sc.wait_ge(cmp, c_cmp.inc())
            v.memset(sm[:, 63:64], 1.0)
            v.sem_inc(cmp, 1); c_cmp.inc()
            sc.wait_ge(cmp, c_cmp.v)
            sc.activation(out=lam_cm[:, 8:8 + NBH], in_=lam_cm[:, 8:8 + NBH], func=AF.Ln,
                          bias=sm[0:64, 63:64], scale=1.0).then_inc(cmp, 1)
            c_cmp.inc()
            vw()
            v.tensor_scalar(out=lam_cm[:, 12:12 + NBH], in0=lam_cm[:, 0:NBH], scalar1=0.0,
                            scalar2=None, op0=OP.max).then_inc(cmp, 1)
            v.wait_ge(cmp, c_cmp.inc())
            v.tensor_tensor(out=lam_cm[:, 12:12 + NBH], in0=lam_cm[:, 12:12 + NBH],
                            in1=lam_cm[:, 8:8 + NBH], op=OP.add).then_inc(cmp, 1)
            v.wait_ge(cmp, c_cmp.inc())
            v.tensor_scalar(out=lam_cm[:, 12:12 + NBH], in0=lam_cm[:, 12:12 + NBH],
                            scalar1=0.25, scalar2=None, op0=OP.add).then_inc(cmp, 1)
            v.wait_ge(cmp, c_cmp.inc())
            for h in range(NBH):
                v.tensor_scalar(out=diag_l[:, h * 64:(h + 1) * 64],
                                in0=identf[0:64, 0:64],
                                scalar1=lam_cm[:, 12 + h:13 + h], scalar2=None,
                                op0=OP.mult).then_inc(cmp, 1)
                c_cmp.inc()

            for bh in range(NBH):
                base_pe0 = c_pe.v
                lv0 = c_ld.v
                sync.wait_ge(st, c_st.v)
                sync.wait_ge(pe, c_pe.v)
                sync.dma_start(out=kaug[64:128, :], in_=k_in[bh]).then_inc(ld, 16)
                sync.dma_start(out=vaug[64:128, :], in_=v_in[bh]).then_inc(ld, 16)
                sync.dma_start(out=lf_pm[:, :],
                               in_=lf_in[bh].rearrange("(c t) -> c t", t=64)).then_inc(ld, 16)
                sync.dma_start(out=lb_pm[:, :],
                               in_=lb_in[bh].rearrange("(c t) -> c t", t=64)).then_inc(ld, 16)
                c_ld.inc(64)
                v.wait_ge(ld, c_ld.v)
                v.tensor_tensor_scan(out=cum_pm[:], data0=lf_pm[:], data1=zer[:],
                                     initial=0.0, op0=OP.add, op1=OP.add).then_inc(cmp, 1)
                v.wait_ge(cmp, c_cmp.inc())
                v.tensor_tensor(out=w_pm[:], in0=lb_pm[:], in1=cum_pm[:],
                                op=OP.subtract).then_inc(cmp, 1)
                c_cmp.inc()
                t.wait_ge(cmp, c_cmp.v)
                t.matmul(out=ps_W[64:128, 0:NCH], lhsT=w_pm[:, 0:64], rhs=identf[0:NCH, 0:NCH],
                         start=True, stop=True, tile_position=(0, 64)).then_inc(pe, 1)
                c_pe.inc()
                sc.wait_ge(pe, c_pe.v)
                sc.copy(out=w_cm[64:128, 0:NCH], in_=ps_W[64:128, 0:NCH]).then_inc(cmp, 1)
                c_cmp.inc()
                g.wait_ge(cmp, c_cmp.v)
                g.dma_start(out=cumscr, in_=cum_pm[:]).then_inc(ldg, 16)
                g.wait_ge(ldg, c_ldg.inc(16))
                g.dma_start(out=cum_lin[0:1, :],
                            in_=cumscr.rearrange("c t -> (c t)").unsqueeze(0)).then_inc(ldg, 16)
                c_ldg.inc(16)
                g.dma_start(out=ablin[0:1, 0:NCH],
                            in_=cumscr[:, 63:64].rearrange("c o -> (c o)").unsqueeze(0),
                            allow_slow_non_contiguous=True).then_inc(ldg, 16)
                c_ldg.inc(16)
                t.wait_ge(ldg, c_ldg.v)
                t.matmul(out=ps_W[0:64, 0:NCH], lhsT=ones_row[0:1, 0:64],
                         rhs=ablin[0:1, 0:NCH],
                         start=True, stop=True).then_inc(pe, 1)
                c_pe.inc()
                sc.wait_ge(pe, c_pe.v)
                sc.activation(out=ab[:, 0:NCH], in_=ps_W[0:64, 0:NCH], func=AF.Exp).then_inc(cmp, 1)
                c_cmp.inc()
                for mb in range(NCH // 8):
                    t.wait_ge(cmp, c_cmp.v)
                    for j in range(8):
                        ch = mb * 8 + j
                        t.matmul(out=ps_M[64:128, j * 64:(j + 1) * 64],
                                 lhsT=ones_row[0:1, 0:64],
                                 rhs=cum_lin[0:1, ch * 64:(ch + 1) * 64],
                                 start=(j == 0), stop=(j == 7), tile_position=(0, 64)).then_inc(pe, 1)
                        c_pe.inc()
                    sc.wait_ge(pe, c_pe.v)
                    for j in range(8):
                        ch = mb * 8 + j
                        sc.activation(out=mall[64:128, ch * 64:(ch + 1) * 64],
                                      in_=ps_M[64:128, j * 64:(j + 1) * 64], func=AF.Exp,
                                      bias=w_cm[64:128, ch:ch + 1], scale=1.0).then_inc(cmp, 1)
                        c_cmp.inc()
                    v.wait_ge(cmp, c_cmp.v)
                    trib = tri[64:128, :].unsqueeze(1).broadcast_to([64, 8, 64])
                    v.tensor_tensor(out=mall[64:128, mb * 512:(mb + 1) * 512].rearrange(
                                        "p (c t) -> p c t", t=64),
                                    in0=mall[64:128, mb * 512:(mb + 1) * 512].rearrange(
                                        "p (c t) -> p c t", t=64),
                                    in1=trib, op=OP.mult).then_inc(cmp, 1)
                    c_cmp.inc()
                for ch in range(NCH):
                    v.wait_ge(cmp, c_cmp.v)
                    v.wait_ge(pe, c_pe.v)
                    v.tensor_tensor(out=khat[64:128, :], in0=kaug[64:128, ch * 64:(ch + 1) * 64],
                                    in1=mall[64:128, ch * 64 + 63:ch * 64 + 64].broadcast_to([64, 64]),
                                    op=OP.mult).then_inc(cmp, 1)
                    c_cmp.inc()
                    t.wait_ge(cmp, c_cmp.v)
                    if ch >= 1:
                        pass
                    t.matmul(out=ps_U[0:64, 0:64], lhsT=khat[64:128, :],
                             rhs=kaug[64:128, ch * 64:(ch + 1) * 64],
                             start=True, stop=False, tile_position=(64, 0)).then_inc(pe, 1)
                    t.matmul(out=ps_U[0:64, 64:128], lhsT=khat[64:128, :],
                             rhs=vaug[64:128, ch * 64:(ch + 1) * 64],
                             start=False, stop=True, tile_position=(64, 0)).then_inc(pe, 1)
                    c_pe.inc(2)
                    sc.wait_ge(pe, c_pe.v)
                    sc.copy(out=uall[:, ch * 128:(ch + 1) * 128],
                            in_=ps_U[0:64, 0:128]).then_inc(cmp, 1)
                    c_cmp.inc()
                vw()
                v.tensor_copy(out=hall[:, 0:128], in_=uall[:, 0:128]).then_inc(cmp, 1)
                v.wait_ge(cmp, c_cmp.inc())
                for ch in range(1, NCH):
                    v.scalar_tensor_tensor(out=hall[:, ch * 128:ch * 128 + 64],
                                           in0=hall[:, (ch - 1) * 128:(ch - 1) * 128 + 64],
                                           scalar=ab[:, ch:ch + 1],
                                           in1=uall[:, ch * 128:ch * 128 + 64],
                                           op0=OP.mult, op1=OP.add).then_inc(cmp, 1)
                    v.wait_ge(cmp, c_cmp.inc())
                    v.scalar_tensor_tensor(out=hall[:, ch * 128 + 64:(ch + 1) * 128],
                                           in0=hall[:, (ch - 1) * 128 + 64:ch * 128],
                                           scalar=ab[:, ch:ch + 1],
                                           in1=uall[:, ch * 128 + 64:(ch + 1) * 128],
                                           op0=OP.mult, op1=OP.add).then_inc(cmp, 1)
                    v.wait_ge(cmp, c_cmp.inc())
                if DBG and bh == 0:
                    g.wait_ge(cmp, c_cmp.v)
                    g.dma_start(out=dbg_ab, in_=ab[:, :]).then_inc(st, 16)
                    g.dma_start(out=dbg_cum, in_=cum_pm[:, :]).then_inc(st, 16)
                    g.dma_start(out=dbg_hall, in_=hall[:, :]).then_inc(st, 16)
                    g.dma_start(out=dbg_u, in_=uall[:, :]).then_inc(st, 16)
                    c_st.inc(64)
                v.memset(kaug[0:64, 0:64], 0.0)
                v.memset(vaug[0:64, 0:64], 0.0)
                v.sem_inc(cmp, 1); c_cmp.inc()
                g.wait_ge(cmp, c_cmp.v)
                g.dma_start(out=kaug[0:64, 64:NCH * 64],
                            in_=hall[:, 0:(NCH - 1) * 128].rearrange(
                                "p (c z) -> p c z", z=128)[:, :, 0:64]).then_inc(ldg, 16)
                c_ldg.inc(16)
                g.dma_start(out=vaug[0:64, 64:NCH * 64],
                            in_=hall[:, 0:(NCH - 1) * 128].rearrange(
                                "p (c z) -> p c z", z=128)[:, :, 64:128]).then_inc(ldg, 16)
                c_ldg.inc(16)
                if DBG and bh == 0:
                    g.wait_ge(ldg, c_ldg.v)
                    g.dma_start(out=dbg_ka, in_=kaug[0:64, :]).then_inc(st, 16)
                    g.dma_start(out=dbg_va, in_=vaug[0:64, :]).then_inc(st, 16)
                    g.dma_start(out=dbg_m, in_=mall[64:128, :]).then_inc(st, 16)
                    c_st.inc(48)
                t.wait_ge(ldg, c_ldg.v)
                for kb in range(NCH // 8):
                    t.wait_ge(cmp, c_cmp.v)
                    for j in range(8):
                        ch = kb * 8 + j
                        t.matmul(out=ps_T[0:64, j * 64:(j + 1) * 64],
                                 lhsT=kaug[64:128, ch * 64:(ch + 1) * 64],
                                 rhs=id64[64:128, :], start=(j == 0), stop=(j == 7),
                                 tile_position=(64, 0)).then_inc(pe, 1)
                        c_pe.inc()
                    sc.wait_ge(pe, c_pe.v)
                    sc.copy(out=kcol[:, kb * 512:(kb + 1) * 512],
                            in_=ps_T[0:64, 0:512]).then_inc(cmp, 1)
                    c_cmp.inc()
                if DBG and bh == 0:
                    g.wait_ge(cmp, c_cmp.v)
                    g.dma_start(out=dbg_kc, in_=kcol[:, :]).then_inc(st, 16)
                    c_st.inc(16)
                for gi in range(NGRP):
                    g0 = gi * GRP
                    ch0 = g0 * 2
                    tok0 = ch0 * 64
                    t.wait_ge(cmp, c_cmp.v)
                    t.matmul(out=ps_T[0:64, 0:512], lhsT=ones_row[0:1, 0:64],
                             rhs=cum_lin[0:1, tok0:tok0 + 512],
                             start=True, stop=True).then_inc(pe, 1)
                    c_pe.inc()
                    sc.wait_ge(pe, c_pe.v)
                    sc.activation(out=vball[:, :], in_=ps_T[0:64, 0:512], func=AF.Exp).then_inc(cmp, 1)
                    c_cmp.inc()
                    sync.wait_ge(pe, c_pe.v)
                    sync.dma_start(out=Ps[0][:, :].rearrange("p (j d) -> p j d", d=64),
                                   in_=q_in[bh, tok0:tok0 + 512, :].rearrange(
                                       "(j p) d -> p j d", p=128)).then_inc(ld, 16)
                    c_ld.inc(16)
                    v.wait_ge(ld, c_ld.v)
                    v.tensor_copy(out=Rs[0][:, :], in_=Ps[0][:, :])
                    v.memset(Xs[0][:, :], 0.0)
                    for j in range(GRP):
                        inst = v.scalar_tensor_tensor(out=o_sb[:, j * 64:(j + 1) * 64],
                                               in0=Ps[0][:, j * 64:(j + 1) * 64], scalar=1.0,
                                               in1=Ps[0][:, j * 64:(j + 1) * 64],
                                               op0=OP.mult, op1=OP.mult,
                                               accum_out=sm[:, j:j + 1])
                    inst.then_inc(cmp, 1)
                    v.wait_ge(cmp, c_cmp.inc())
                    cur = 0
                    for it in range(CGIT):
                        Pc, Rc, Xc = Ps[cur], Rs[cur], Xs[cur]
                        Pn, Rn, Xn_ = Ps[1 - cur], Rs[1 - cur], Xs[1 - cur]
                        sc.wait_ge(cmp, c_cmp.v)
                        sc.copy(out=pb16[:, :], in_=Pc[:, :])
                        sc.activation(out=o_sb[:, 0:GRP * 64], in_=pb16[:, :], func=AF.Abs).then_inc(cmp, 1)
                        c_cmp.inc()
                        t.wait_ge(cmp, c_cmp.v)
                        for j in range(GRP):
                            t.matmul(out=ps_T[0:64, j * 128:(j + 1) * 128],
                                     lhsT=pb16[:, j * 64:(j + 1) * 64], rhs=ident[:],
                                     start=(j == 0), stop=(j == GRP - 1)).then_inc(pe, 1)
                            c_pe.inc()
                        sc.wait_ge(pe, c_pe.v)
                        sc.copy(out=pcol[:, :], in_=ps_T[0:64, 0:512])
                        sc.activation(out=daug[0:64, 0:GRP * 128], in_=pcol[:, :], func=AF.Abs).then_inc(cmp, 1)
                        c_cmp.inc()
                        v.wait_ge(cmp, c_cmp.v)
                        v.tensor_tensor(out=daug[0:64, :], in0=pcol[:, :], in1=vball[:, :],
                                        op=OP.mult).then_inc(cmp, 1)
                        v.wait_ge(cmp, c_cmp.inc())
                        v.tensor_reduce(out=sm[0:64, 34:35], in_=daug[0:64, :], axis=AX.X,
                                        op=OP.max).then_inc(cmp, 1)
                        c_cmp.inc()
                        t.wait_ge(cmp, c_cmp.v)
                        for cj in range(2 * GRP):
                            ch = ch0 + cj
                            t.matmul(out=ps_C[64:128, cj * 64:(cj + 1) * 64],
                                     lhsT=kcol[:, ch * 64:(ch + 1) * 64],
                                     rhs=pcol[:, cj * 64:(cj + 1) * 64],
                                     start=(cj == 0), stop=(cj == 2 * GRP - 1),
                                     tile_position=(0, 64)).then_inc(pe, 1)
                            c_pe.inc()
                        v.wait_ge(pe, c_pe.v)
                        v.tensor_tensor(out=daug[64:128, :], in0=ps_C[64:128, 0:512],
                                        in1=mall[64:128, tok0:tok0 + 512],
                                        op=OP.mult).then_inc(cmp, 1)
                        v.wait_ge(cmp, c_cmp.inc())
                        v.tensor_reduce(out=sm[64:128, 34:35], in_=daug[64:128, :], axis=AX.X,
                                        op=OP.max).then_inc(cmp, 1)
                        c_cmp.inc()
                        t.wait_ge(cmp, c_cmp.v)
                        t.matmul(out=ps_AP[0:128, 0:64],
                                 lhsT=pcol[:, 0:128],
                                 rhs=diag_l[:, bh * 64:(bh + 1) * 64],
                                 start=True, stop=False).then_inc(pe, 1)
                        c_pe.inc()
                        for cj in range(2 * GRP):
                            ch = ch0 + cj
                            pos = (0, 0) if cj % 2 == 0 else (0, 64)
                            t.matmul(out=ps_AP[(cj % 2) * 64:(cj % 2) * 64 + 64,
                                               (cj // 2) * 64:(cj // 2) * 64 + 64],
                                     lhsT=daug[:, cj * 64:(cj + 1) * 64],
                                     rhs=kaug[:, ch * 64:(ch + 1) * 64],
                                     start=False, stop=False,
                                     tile_position=pos).then_inc(pe, 1)
                            c_pe.inc()
                        for j in range(1, GRP):
                            t.matmul(out=ps_AP[0:128, j * 64:(j + 1) * 64],
                                     lhsT=pcol[:, j * 128:(j + 1) * 128],
                                     rhs=diag_l[:, bh * 64:(bh + 1) * 64],
                                     start=False, stop=(j == GRP - 1)).then_inc(pe, 1)
                            c_pe.inc()
                        v.wait_ge(pe, c_pe.v)
                        for j in range(GRP):
                            inst = v.scalar_tensor_tensor(out=o_sb[:, j * 64:(j + 1) * 64],
                                                   in0=ps_AP[0:128, j * 64:(j + 1) * 64],
                                                   scalar=1.0,
                                                   in1=Pc[:, j * 64:(j + 1) * 64],
                                                   op0=OP.mult, op1=OP.mult,
                                                   accum_out=sm[:, 4 + j:5 + j])
                        inst.then_inc(cmp, 1)
                        v.wait_ge(cmp, c_cmp.inc())
                        v.tensor_scalar(out=sm[:, 16:20], in0=sm[:, 4:8], scalar1=1e-12,
                                        scalar2=None, op0=OP.add).then_inc(cmp, 1)
                        v.wait_ge(cmp, c_cmp.inc())
                        v.reciprocal(out=sm[:, 16:20], in_=sm[:, 16:20]).then_inc(cmp, 1)
                        v.wait_ge(cmp, c_cmp.inc())
                        v.tensor_tensor(out=sm[:, 8:12], in0=sm[:, 0:4], in1=sm[:, 16:20],
                                        op=OP.mult).then_inc(cmp, 1)
                        v.wait_ge(cmp, c_cmp.inc())
                        v.tensor_scalar(out=sm[:, 20:24], in0=sm[:, 8:12], scalar1=-1.0,
                                        scalar2=None, op0=OP.mult).then_inc(cmp, 1)
                        v.wait_ge(cmp, c_cmp.inc())
                        for j in range(GRP):
                            v.scalar_tensor_tensor(out=Xn_[:, j * 64:(j + 1) * 64],
                                                   in0=Pc[:, j * 64:(j + 1) * 64],
                                                   scalar=sm[:, 8 + j:9 + j],
                                                   in1=Xc[:, j * 64:(j + 1) * 64],
                                                   op0=OP.mult, op1=OP.add)
                        for j in range(GRP):
                            inst = v.scalar_tensor_tensor(out=Rn[:, j * 64:(j + 1) * 64],
                                                   in0=ps_AP[0:128, j * 64:(j + 1) * 64],
                                                   scalar=sm[:, 20 + j:21 + j],
                                                   in1=Rc[:, j * 64:(j + 1) * 64],
                                                   op0=OP.mult, op1=OP.add)
                        inst.then_inc(cmp, 1)
                        v.wait_ge(cmp, c_cmp.inc())
                        for j in range(GRP):
                            inst = v.scalar_tensor_tensor(out=o_sb[:, j * 64:(j + 1) * 64],
                                                   in0=Rn[:, j * 64:(j + 1) * 64], scalar=1.0,
                                                   in1=Rn[:, j * 64:(j + 1) * 64],
                                                   op0=OP.mult, op1=OP.mult,
                                                   accum_out=sm[:, 24 + j:25 + j])
                        inst.then_inc(cmp, 1)
                        v.wait_ge(cmp, c_cmp.inc())
                        v.tensor_scalar(out=sm[:, 16:20], in0=sm[:, 0:4], scalar1=1e-12,
                                        scalar2=None, op0=OP.add).then_inc(cmp, 1)
                        v.wait_ge(cmp, c_cmp.inc())
                        v.reciprocal(out=sm[:, 16:20], in_=sm[:, 16:20]).then_inc(cmp, 1)
                        v.wait_ge(cmp, c_cmp.inc())
                        v.tensor_tensor(out=sm[:, 12:16], in0=sm[:, 24:28], in1=sm[:, 16:20],
                                        op=OP.mult).then_inc(cmp, 1)
                        v.wait_ge(cmp, c_cmp.inc())
                        for j in range(GRP):
                            v.scalar_tensor_tensor(out=Pn[:, j * 64:(j + 1) * 64],
                                                   in0=Pc[:, j * 64:(j + 1) * 64],
                                                   scalar=sm[:, 12 + j:13 + j],
                                                   in1=Rn[:, j * 64:(j + 1) * 64],
                                                   op0=OP.mult, op1=OP.add)
                        v.tensor_copy(out=sm[:, 0:4], in_=sm[:, 24:28]).then_inc(cmp, 1)
                        v.wait_ge(cmp, c_cmp.inc())
                        cur = 1 - cur
                    Xc = Xs[cur]
                    sc.wait_ge(cmp, c_cmp.v)
                    sc.copy(out=pb16[:, :], in_=Xc[:, :])
                    sc.activation(out=o_sb[:, 0:GRP * 64], in_=pb16[:, :], func=AF.Abs).then_inc(cmp, 1)
                    c_cmp.inc()
                    t.wait_ge(cmp, c_cmp.v)
                    for j in range(GRP):
                        t.matmul(out=ps_T[0:64, j * 128:(j + 1) * 128],
                                 lhsT=pb16[:, j * 64:(j + 1) * 64], rhs=ident[:],
                                 start=(j == 0), stop=(j == GRP - 1)).then_inc(pe, 1)
                        c_pe.inc()
                    sc.wait_ge(pe, c_pe.v)
                    sc.copy(out=pcol[:, :], in_=ps_T[0:64, 0:512])
                    sc.activation(out=daug[0:64, 0:GRP * 128], in_=pcol[:, :], func=AF.Abs).then_inc(cmp, 1)
                    c_cmp.inc()
                    v.wait_ge(cmp, c_cmp.v)
                    v.tensor_tensor(out=daug[0:64, :], in0=pcol[:, :], in1=vball[:, :],
                                    op=OP.mult).then_inc(cmp, 1)
                    v.wait_ge(cmp, c_cmp.inc())
                    v.tensor_reduce(out=sm[0:64, 34:35], in_=daug[0:64, :], axis=AX.X,
                                    op=OP.max).then_inc(cmp, 1)
                    c_cmp.inc()
                    t.wait_ge(cmp, c_cmp.v)
                    for cj in range(2 * GRP):
                        ch = ch0 + cj
                        t.matmul(out=ps_C[64:128, cj * 64:(cj + 1) * 64],
                                 lhsT=kcol[:, ch * 64:(ch + 1) * 64],
                                 rhs=pcol[:, cj * 64:(cj + 1) * 64],
                                 start=(cj == 0), stop=(cj == 2 * GRP - 1),
                                 tile_position=(0, 64)).then_inc(pe, 1)
                        c_pe.inc()
                    v.wait_ge(pe, c_pe.v)
                    v.tensor_tensor(out=daug[64:128, :], in0=ps_C[64:128, 0:512],
                                    in1=mall[64:128, tok0:tok0 + 512],
                                    op=OP.mult).then_inc(cmp, 1)
                    v.wait_ge(cmp, c_cmp.inc())
                    v.tensor_reduce(out=sm[64:128, 34:35], in_=daug[64:128, :], axis=AX.X,
                                    op=OP.max).then_inc(cmp, 1)
                    c_cmp.inc()
                    t.wait_ge(cmp, c_cmp.v)
                    t.matmul(out=ps_AP[0:128, 0:64], lhsT=pcol[:, 0:128],
                             rhs=zdiag[:], start=True, stop=False).then_inc(pe, 1)
                    c_pe.inc()
                    for cj in range(2 * GRP):
                        ch = ch0 + cj
                        pos = (0, 0) if cj % 2 == 0 else (0, 64)
                        t.matmul(out=ps_AP[(cj % 2) * 64:(cj % 2) * 64 + 64,
                                           (cj // 2) * 64:(cj // 2) * 64 + 64],
                                 lhsT=daug[:, cj * 64:(cj + 1) * 64],
                                 rhs=vaug[:, ch * 64:(ch + 1) * 64],
                                 start=False, stop=False,
                                 tile_position=pos).then_inc(pe, 1)
                        c_pe.inc()
                    for j in range(1, GRP):
                        t.matmul(out=ps_AP[0:128, j * 64:(j + 1) * 64],
                                 lhsT=pcol[:, j * 128:(j + 1) * 128],
                                 rhs=zdiag[:], start=False,
                                 stop=(j == GRP - 1)).then_inc(pe, 1)
                        c_pe.inc()
                    sc.wait_ge(pe, c_pe.v)
                    sc.copy(out=o_sb[:, :], in_=ps_AP[0:128, 0:256]).then_inc(cmp, 1)
                    c_cmp.inc()
                    g.wait_ge(cmp, c_cmp.v)
                    g.dma_start(out=o_out[bh, tok0:tok0 + 512, :].rearrange(
                                    "(j p) d -> p j d", p=128),
                                in_=o_sb[:, :].rearrange("p (j d) -> p j d", d=64)).then_inc(st, 16)
                    c_st.inc(16)
                    v.wait_ge(st, c_st.v)
                    sc.wait_ge(st, c_st.v)
            g.wait_ge(st, c_st.v)

    return nc


def run_b(inputs, a):
    qf = a["q"].reshape(B, N, H, DH)
    kf = a["k"].reshape(B, N, H, DH)
    vf = a["v"].reshape(B, N, H, DH)
    lff = a["logf"].reshape(B, N, H)
    lbf = a["logb"].reshape(B, N, H)
    lp = inputs["L0_attn_lamb_params"].reshape(H, DH)
    ident = np_identity_bf16()
    identf = np.eye(128, dtype=np.float32)
    tri = np.zeros((128, 64), dtype=BF)
    tri[64:128] = np.triu(np.ones((64, 64))).astype(BF)
    id64c = np.concatenate([np.eye(64), np.eye(64)], axis=0).astype(BF)
    in_maps = []
    for c in range(NC):
        b = c // 4
        hs = (c % 4) * 4
        kb = kf[b, :, hs:hs + 4].transpose(1, 0, 2)
        vb = vf[b, :, hs:hs + 4].transpose(1, 0, 2)
        kb = np.ascontiguousarray(kb.reshape(4, 128, 64, 64).transpose(0, 2, 1, 3).reshape(4, 64, 128 * 64))
        vb = np.ascontiguousarray(vb.reshape(4, 128, 64, 64).transpose(0, 2, 1, 3).reshape(4, 64, 128 * 64))
        in_maps.append({
            "q": np.ascontiguousarray(qf[b, :, hs:hs + 4].transpose(1, 0, 2)),
            "k": kb,
            "v": vb,
            "lf": np.ascontiguousarray(lff[b, :, hs:hs + 4].T),
            "lb": np.ascontiguousarray(lbf[b, :, hs:hs + 4].T),
            "lp": np.ascontiguousarray(lp[hs:hs + 4].T),
            "ident": ident, "identf": identf, "tri": tri, "id64": id64c,
        })
    nc = build_b()
    res = run_bass_kernel_spmd(nc, in_maps, core_ids=list(range(NC)))
    o = np.zeros((B, N, H, DH), np.float32)
    for c in range(NC):
        b = c // 4
        hs = (c % 4) * 4
        o[b, :, hs:hs + 4] = res.results[c]["o"].transpose(1, 0, 2)
    return o


def build_c():
    nc = bass.Bass("TRN2", target_bir_lowering=False)
    x_in = nc.dram_tensor("x", [TOK, D], F32, kind="ExternalInput").ap()
    o_in = nc.dram_tensor("o", [TOK, D], F32, kind="ExternalInput").ap()
    gate_in = nc.dram_tensor("gate", [TOK, D], F32, kind="ExternalInput").ap()
    g_in = nc.dram_tensor("g", [TOK, D], F32, kind="ExternalInput").ap()
    nw_in = nc.dram_tensor("nw", [1, D], F32, kind="ExternalInput").ap()
    Wo_in = nc.dram_tensor("Wo", [D, D], F32, kind="ExternalInput").ap()
    Wq1_in = nc.dram_tensor("Wq1", [D, D], F32, kind="ExternalInput").ap()
    Wdkv_in = nc.dram_tensor("Wdkv", [D, 256], F32, kind="ExternalInput").ap()
    Wuk_in = nc.dram_tensor("Wuk", [256, D], F32, kind="ExternalInput").ap()
    Wuv_in = nc.dram_tensor("Wuv", [256, D], F32, kind="ExternalInput").ap()
    lng_in = nc.dram_tensor("lng", [1, 256], F32, kind="ExternalInput").ap()
    lnb_in = nc.dram_tensor("lnb", [1, 256], F32, kind="ExternalInput").ap()
    cosb_in = nc.dram_tensor("cosb", [TOK, 32], F32, kind="ExternalInput").ap()
    sinb_in = nc.dram_tensor("sinb", [TOK, 32], F32, kind="ExternalInput").ap()
    identf_in = nc.dram_tensor("identf", [128, 128], F32, kind="ExternalInput").ap()
    x1_out = nc.dram_tensor("x1", [TOK, D], F32, kind="ExternalOutput").ap()
    xn1_out = nc.dram_tensor("xn1", [TOK, D], F32, kind="ExternalOutput").ap()
    q1_out = nc.dram_tensor("q1", [TOK, D], BF16, kind="ExternalOutput").ap()
    k1_out = nc.dram_tensor("k1", [TOK, D], BF16, kind="ExternalOutput").ap()
    v1_out = nc.dram_tensor("v1", [TOK, D], BF16, kind="ExternalOutput").ap()

    with ExitStack() as ctx:
        E = ctx.enter_context
        identf = E(nc.sbuf_tensor("identft", [128, 128], F32))
        ones_row = E(nc.sbuf_tensor("ones_row", [1, 128], F32))
        w_all = E(nc.sbuf_tensor("w_all", [128, (8 + 8 + 2) * 1024 + 2 * 2048], BF16))
        OFF_WO, OFF_WQ, OFF_DKV, OFF_UK, OFF_UV = 0, 8 * 1024, 16 * 1024, 16 * 1024 + 2 * 1024, 16 * 1024 + 4 * 1024
        nw_t = E(nc.sbuf_tensor("nw_t", [128, 1024], F32))
        lng_t = E(nc.sbuf_tensor("lng_t", [128, 256], F32))
        lnb_t = E(nc.sbuf_tensor("lnb_t", [128, 256], F32))
        xt = E(nc.sbuf_tensor("xtC", [128, 1024], F32))
        ot = E(nc.sbuf_tensor("otC", [128, 1024], F32))
        gt = E(nc.sbuf_tensor("gtC", [128, 1024], F32))
        st1 = E(nc.sbuf_tensor("st1C", [128, 1024], F32))
        st2 = E(nc.sbuf_tensor("st2C", [128, 1024], F32))
        stb = E(nc.sbuf_tensor("stbC", [128, 1024], BF16))
        colb = E(nc.sbuf_tensor("colbC", [128, 1024], BF16))
        lat = E(nc.sbuf_tensor("latC", [128, 256], F32))
        latc = E(nc.sbuf_tensor("latcC", [128, 256], BF16))
        cost = E(nc.sbuf_tensor("cosT", [128, 32], F32))
        sint = E(nc.sbuf_tensor("sinT", [128, 32], F32))
        sm = E(nc.sbuf_tensor("smC", [128, 80], F32))
        ps = [E(nc.psum_tensor(f"pC{b}", [128, 512], F32)) for b in range(4)]
        ld = E(nc.semaphore("ld"))
        ldg = E(nc.semaphore("ldg"))
        cmp = E(nc.semaphore("cmp"))
        pe = E(nc.semaphore("pe"))
        st = E(nc.semaphore("st"))
        c_ld, c_ldg, c_cmp, c_pe, c_st = Ctr(), Ctr(), Ctr(), Ctr(), Ctr()
        block = E(nc.Block())

        @block.sync
        def _(sync):
            v, t, sc, g = nc.vector, nc.tensor, nc.scalar, nc.gpsimd
            sync.dma_start(out=identf[:], in_=identf_in).then_inc(ld, 16)
            sync.dma_start(out=nw_t[0:1, :], in_=nw_in).then_inc(ld, 16)
            sync.dma_start(out=lng_t[0:1, :], in_=lng_in).then_inc(ld, 16)
            sync.dma_start(out=lnb_t[0:1, :], in_=lnb_in).then_inc(ld, 16)
            c_ld.inc(64)
            for kc in range(8):
                g.dma_start(out=w_all[:, OFF_WO + kc * 1024:OFF_WO + (kc + 1) * 1024],
                            in_=Wo_in[kc * 128:(kc + 1) * 128, :]).then_inc(ldg, 16)
                g.dma_start(out=w_all[:, OFF_WQ + kc * 1024:OFF_WQ + (kc + 1) * 1024],
                            in_=Wq1_in[kc * 128:(kc + 1) * 128, :]).then_inc(ldg, 16)
                g.dma_start(out=w_all[:, OFF_DKV + kc * 256:OFF_DKV + (kc + 1) * 256],
                            in_=Wdkv_in[kc * 128:(kc + 1) * 128, :]).then_inc(ldg, 16)
                c_ldg.inc(48)
            for kc in range(2):
                g.dma_start(out=w_all[:, OFF_UK + kc * 1024:OFF_UK + (kc + 1) * 1024],
                            in_=Wuk_in[kc * 128:(kc + 1) * 128, :]).then_inc(ldg, 16)
                g.dma_start(out=w_all[:, OFF_UV + kc * 1024:OFF_UV + (kc + 1) * 1024],
                            in_=Wuv_in[kc * 128:(kc + 1) * 128, :]).then_inc(ldg, 16)
                c_ldg.inc(32)
            v.wait_ge(ld, c_ld.v)
            v.memset(ones_row[:], 1.0)
            v.memset(sm[:, 64:65], 1e-5)
            v.memset(sm[:, 65:66], 1e-5).then_inc(cmp, 1)
            c_cmp.inc()
            v.wait_ge(cmp, c_cmp.v)
            t.wait_ge(ld, c_ld.v)
            t.wait_ge(cmp, c_cmp.v)
            for half in range(2):
                t.matmul(out=ps[0][0:128, 0:512], lhsT=ones_row[0:1, 0:128],
                         rhs=nw_t[0:1, half * 512:(half + 1) * 512],
                         start=True, stop=True).then_inc(pe, 1)
                c_pe.inc()
                sc.wait_ge(pe, c_pe.v)
                sc.copy(out=nw_t[:, half * 512:(half + 1) * 512],
                        in_=ps[0][0:128, 0:512]).then_inc(cmp, 1)
                c_cmp.inc()
                t.wait_ge(cmp, c_cmp.v)
            t.matmul(out=ps[0][0:128, 0:256], lhsT=ones_row[0:1, 0:128],
                     rhs=lng_t[0:1, :], start=True, stop=True).then_inc(pe, 1)
            c_pe.inc()
            sc.wait_ge(pe, c_pe.v)
            sc.copy(out=lng_t[:, :], in_=ps[0][0:128, 0:256]).then_inc(cmp, 1)
            c_cmp.inc()
            t.wait_ge(cmp, c_cmp.v)
            t.matmul(out=ps[0][0:128, 0:256], lhsT=ones_row[0:1, 0:128],
                     rhs=lnb_t[0:1, :], start=True, stop=True).then_inc(pe, 1)
            c_pe.inc()
            sc.wait_ge(pe, c_pe.v)
            sc.copy(out=lnb_t[:, :], in_=ps[0][0:128, 0:256]).then_inc(cmp, 1)
            c_cmp.inc()

            def transpose_to_colb(src_bf, nch=8):
                for cch in range(nch):
                    t.matmul(out=ps[3][0:128, 0:128], lhsT=src_bf[:, cch * 128:(cch + 1) * 128],
                             rhs=identf[:].bitcast(F32) if False else identb[:],
                             start=True, stop=True).then_inc(pe, 1)
                    c_pe.inc()
                    sc.wait_ge(pe, c_pe.v)
                    sc.copy(out=colb[:, cch * 128:(cch + 1) * 128],
                            in_=ps[3][0:128, 0:128])
                    sc.activation(out=stb[:, cch * 128:(cch + 1) * 128],
                                  in_=colb[:, cch * 128:(cch + 1) * 128],
                                  func=AF.Abs).then_inc(cmp, 1)
                    c_cmp.inc()
                    t.wait_ge(cmp, c_cmp.v)

            identb = E(nc.sbuf_tensor("identbC", [128, 128], BF16))
            v.wait_ge(cmp, c_cmp.v)
            v.tensor_copy(out=identb[:], in_=identf[:]).then_inc(cmp, 1)
            c_cmp.inc()

            for i in range(NT):
                r0 = i * 128
                if i >= 1:
                    sync.wait_ge(pe, c_pe.v)
                    sync.wait_ge(cmp, c_cmp.v)
                sync.dma_start(out=ot[:], in_=o_in[r0:r0 + 128, :]).then_inc(ld, 16)
                sync.dma_start(out=gt[:], in_=gate_in[r0:r0 + 128, :]).then_inc(ld, 16)
                c_ld.inc(32)
                v.wait_ge(ld, c_ld.v)
                v.wait_ge(st, c_st.v)
                sc.wait_ge(st, c_st.v)
                if i == 0:
                    v.wait_ge(ldg, c_ldg.v)
                    t.wait_ge(ldg, c_ldg.v)
                v.tensor_tensor(out=st1[:], in0=ot[:], in1=gt[:], op=OP.mult).then_inc(cmp, 1)
                c_cmp.inc()
                c_og = c_cmp.v
                v.scalar_tensor_tensor(out=st2[:], in0=st1[:], scalar=1.0, in1=st1[:],
                                       op0=OP.mult, op1=OP.mult)
                v.tensor_reduce(out=sm[:, 0:H], in_=st2[:].rearrange("p (h d) -> p h d", d=64),
                                axis=AX.X, op=OP.add).then_inc(cmp, 1)
                v.wait_ge(cmp, c_cmp.inc())
                v.tensor_scalar(out=sm[:, 16:16 + H], in0=sm[:, 0:H], scalar1=1.0 / 64.0,
                                scalar2=None, op0=OP.mult).then_inc(cmp, 1)
                c_cmp.inc()
                sc.wait_ge(cmp, c_cmp.v)
                sc.activation(out=sm[:, 32:32 + H], in_=sm[:, 16:16 + H], func=AF.Sqrt,
                              bias=sm[:, 65:66], scale=1.0).then_inc(cmp, 1)
                c_cmp.inc()
                v.wait_ge(cmp, c_cmp.v)
                v.reciprocal(out=sm[:, 48:48 + H], in_=sm[:, 32:32 + H]).then_inc(cmp, 1)
                v.wait_ge(cmp, c_cmp.inc())
                rb = sm[:, 48:48 + H].unsqueeze(2).broadcast_to([128, H, 64])
                v.tensor_tensor(out=st2[:].rearrange("p (h d) -> p h d", d=64),
                                in0=st1[:].rearrange("p (h d) -> p h d", d=64),
                                in1=rb, op=OP.mult).then_inc(cmp, 1)
                v.wait_ge(cmp, c_cmp.inc())
                v.tensor_tensor(out=st1[:], in0=st2[:], in1=nw_t[:, 0:1024], op=OP.mult).then_inc(cmp, 1)
                v.wait_ge(cmp, c_cmp.inc())
                v.tensor_copy(out=stb[:, 0:1024], in_=st1[:]).then_inc(cmp, 1)
                v.wait_ge(cmp, c_cmp.inc())
                v.tensor_reduce(out=sm[:, 66:67], in_=stb[:, 0:1024], axis=AX.X, op=OP.max).then_inc(cmp, 1)
                c_cmp.inc()
                t.wait_ge(cmp, c_cmp.v)
                transpose_to_colb(stb)
                for half in range(2):
                    for kc in range(8):
                        t.matmul(out=ps[half][0:128, 0:512],
                                 lhsT=colb[:, kc * 128:(kc + 1) * 128],
                                 rhs=w_all[:, OFF_WO + kc * 1024 + half * 512:
                                           OFF_WO + kc * 1024 + half * 512 + 512],
                                 start=(kc == 0), stop=(kc == 7)).then_inc(pe, 1)
                        c_pe.inc()
                sync.wait_ge(cmp, c_og)
                sync.dma_start(out=xt[:], in_=x_in[r0:r0 + 128, :]).then_inc(ld, 16)
                sync.dma_start(out=gt[:], in_=g_in[r0:r0 + 128, :]).then_inc(ld, 16)
                c_ld.inc(32)
                v.wait_ge(ld, c_ld.v)
                v.wait_ge(pe, c_pe.v)
                for half in range(2):
                    v.tensor_tensor(out=st1[:, half * 512:half * 512 + 512],
                                    in0=ps[half][0:128, 0:512],
                                    in1=gt[:, half * 512:half * 512 + 512], op=OP.mult)
                v.tensor_tensor(out=xt[:], in0=xt[:], in1=st1[:], op=OP.add).then_inc(cmp, 1)
                v.wait_ge(cmp, c_cmp.inc())
                g.wait_ge(cmp, c_cmp.v)
                g.dma_start(out=x1_out[r0:r0 + 128, :], in_=xt[:]).then_inc(st, 16)
                c_st.inc(16)
                v.tensor_reduce(out=sm[:, 0:1], in_=xt[:], axis=AX.X, op=OP.add).then_inc(cmp, 1)
                v.wait_ge(cmp, c_cmp.inc())
                v.tensor_scalar(out=sm[:, 1:2], in0=sm[:, 0:1], scalar1=1.0 / 1024.0,
                                scalar2=None, op0=OP.mult).then_inc(cmp, 1)
                v.wait_ge(cmp, c_cmp.inc())
                v.tensor_tensor(out=st2[:], in0=xt[:], in1=sm[:, 1:2].broadcast_to([128, 1024]),
                                op=OP.subtract)
                v.scalar_tensor_tensor(out=st1[:], in0=st2[:], scalar=1.0, in1=st2[:],
                                       op0=OP.mult, op1=OP.mult,
                                       accum_out=sm[:, 2:3]).then_inc(cmp, 1)
                v.wait_ge(cmp, c_cmp.inc())
                v.tensor_scalar(out=sm[:, 3:4], in0=sm[:, 2:3], scalar1=1.0 / 1024.0,
                                scalar2=None, op0=OP.mult).then_inc(cmp, 1)
                c_cmp.inc()
                sc.wait_ge(cmp, c_cmp.v)
                sc.activation(out=sm[:, 4:5], in_=sm[:, 3:4], func=AF.Sqrt,
                              bias=sm[:, 64:65], scale=1.0).then_inc(cmp, 1)
                c_cmp.inc()
                v.wait_ge(cmp, c_cmp.v)
                v.reciprocal(out=sm[:, 5:6], in_=sm[:, 4:5]).then_inc(cmp, 1)
                v.wait_ge(cmp, c_cmp.inc())
                v.wait_ge(st, c_st.v)
                v.tensor_tensor(out=xt[:], in0=st2[:], in1=sm[:, 5:6].broadcast_to([128, 1024]),
                                op=OP.mult).then_inc(cmp, 1)
                v.wait_ge(cmp, c_cmp.inc())
                g.wait_ge(cmp, c_cmp.v)
                g.dma_start(out=xn1_out[r0:r0 + 128, :], in_=xt[:]).then_inc(st, 16)
                c_st.inc(16)
                v.tensor_copy(out=stb[:, 0:1024], in_=xt[:]).then_inc(cmp, 1)
                v.wait_ge(cmp, c_cmp.inc())
                v.tensor_reduce(out=sm[:, 66:67], in_=stb[:, 0:1024], axis=AX.X, op=OP.max).then_inc(cmp, 1)
                c_cmp.inc()
                t.wait_ge(cmp, c_cmp.v)
                transpose_to_colb(stb)
                sync.dma_start(out=cost[:], in_=cosb_in[r0:r0 + 128, :]).then_inc(ld, 16)
                sync.dma_start(out=sint[:], in_=sinb_in[r0:r0 + 128, :]).then_inc(ld, 16)
                c_ld.inc(32)

                def rope_store(src_f32, dst):
                    a = src_f32[:].rearrange("p (h z) -> p h z", z=64)
                    cb = cost[:].unsqueeze(1).broadcast_to([128, H, 32])
                    sb = sint[:].unsqueeze(1).broadcast_to([128, H, 32])
                    x1v = src_f32[:].rearrange("p (h z) -> p h z", z=64)[:, :, 0:32]
                    x2v = src_f32[:].rearrange("p (h z) -> p h z", z=64)[:, :, 32:64]
                    o1 = st2[:].rearrange("p (h z) -> p h z", z=64)[:, :, 0:32]
                    o2 = st2[:].rearrange("p (h z) -> p h z", z=64)[:, :, 32:64]
                    tt = gt[:].rearrange("p (h z) -> p h z", z=64)
                    v.tensor_tensor(out=tt[:, :, 0:32], in0=x1v, in1=cb, op=OP.mult)
                    v.tensor_tensor(out=tt[:, :, 32:64], in0=x2v, in1=sb, op=OP.mult).then_inc(cmp, 1)
                    v.wait_ge(cmp, c_cmp.inc())
                    v.tensor_tensor(out=o1, in0=tt[:, :, 0:32], in1=tt[:, :, 32:64], op=OP.subtract)
                    v.tensor_tensor(out=tt[:, :, 0:32], in0=x1v, in1=sb, op=OP.mult)
                    v.tensor_tensor(out=tt[:, :, 32:64], in0=x2v, in1=cb, op=OP.mult).then_inc(cmp, 1)
                    v.wait_ge(cmp, c_cmp.inc())
                    v.tensor_tensor(out=o2, in0=tt[:, :, 0:32], in1=tt[:, :, 32:64], op=OP.add).then_inc(cmp, 1)
                    v.wait_ge(cmp, c_cmp.inc())
                    g.wait_ge(cmp, c_cmp.v)
                    g.dma_start(out=dst[r0:r0 + 128, :], in_=st2[:]).then_inc(st, 16)
                    c_st.inc(16)

                v.wait_ge(ld, c_ld.v)
                t.wait_ge(cmp, c_cmp.v)
                for half in range(2):
                    for kc in range(8):
                        t.matmul(out=ps[half][0:128, 0:512],
                                 lhsT=colb[:, kc * 128:(kc + 1) * 128],
                                 rhs=w_all[:, OFF_WQ + kc * 1024 + half * 512:
                                           OFF_WQ + kc * 1024 + half * 512 + 512],
                                 start=(kc == 0), stop=(kc == 7)).then_inc(pe, 1)
                        c_pe.inc()
                v.wait_ge(pe, c_pe.v)
                for half in range(2):
                    v.tensor_copy(out=st1[:, half * 512:half * 512 + 512],
                                  in_=ps[half][0:128, 0:512])
                v.sem_inc(cmp, 0) if False else None
                v.tensor_copy(out=sm[:, 67:68], in_=sm[:, 66:67]).then_inc(cmp, 1)
                v.wait_ge(cmp, c_cmp.inc())
                rope_store(st1, q1_out)
                v.wait_ge(st, c_st.v)
                t.wait_ge(cmp, c_cmp.v)
                for kc in range(8):
                    t.matmul(out=ps[2][0:128, 0:256],
                             lhsT=colb[:, kc * 128:(kc + 1) * 128],
                             rhs=w_all[:, OFF_DKV + kc * 256:OFF_DKV + (kc + 1) * 256],
                             start=(kc == 0), stop=(kc == 7)).then_inc(pe, 1)
                    c_pe.inc()
                v.wait_ge(pe, c_pe.v)
                v.tensor_reduce(out=sm[:, 8:9], in_=ps[2][0:128, 0:256], axis=AX.X,
                                op=OP.add).then_inc(cmp, 1)
                v.wait_ge(cmp, c_cmp.inc())
                v.tensor_scalar(out=sm[:, 9:10], in0=sm[:, 8:9], scalar1=1.0 / 256.0,
                                scalar2=None, op0=OP.mult).then_inc(cmp, 1)
                v.wait_ge(cmp, c_cmp.inc())
                v.tensor_tensor(out=lat[:], in0=ps[2][0:128, 0:256],
                                in1=sm[:, 9:10].broadcast_to([128, 256]), op=OP.subtract)
                v.scalar_tensor_tensor(out=st2[:, 0:256], in0=lat[:], scalar=1.0, in1=lat[:],
                                       op0=OP.mult, op1=OP.mult,
                                       accum_out=sm[:, 10:11]).then_inc(cmp, 1)
                v.wait_ge(cmp, c_cmp.inc())
                v.tensor_scalar(out=sm[:, 11:12], in0=sm[:, 10:11], scalar1=1.0 / 256.0,
                                scalar2=None, op0=OP.mult).then_inc(cmp, 1)
                c_cmp.inc()
                sc.wait_ge(cmp, c_cmp.v)
                sc.activation(out=sm[:, 12:13], in_=sm[:, 11:12], func=AF.Sqrt,
                              bias=sm[:, 64:65], scale=1.0).then_inc(cmp, 1)
                c_cmp.inc()
                v.wait_ge(cmp, c_cmp.v)
                v.reciprocal(out=sm[:, 13:14], in_=sm[:, 12:13]).then_inc(cmp, 1)
                v.wait_ge(cmp, c_cmp.inc())
                v.tensor_tensor(out=lat[:], in0=lat[:], in1=sm[:, 13:14].broadcast_to([128, 256]),
                                op=OP.mult).then_inc(cmp, 1)
                v.wait_ge(cmp, c_cmp.inc())
                v.tensor_tensor(out=lat[:], in0=lat[:], in1=lng_t[:], op=OP.mult).then_inc(cmp, 1)
                v.wait_ge(cmp, c_cmp.inc())
                v.tensor_tensor(out=lat[:], in0=lat[:], in1=lnb_t[:], op=OP.add).then_inc(cmp, 1)
                v.wait_ge(cmp, c_cmp.inc())
                v.tensor_copy(out=latc[:], in_=lat[:]).then_inc(cmp, 1)
                v.wait_ge(cmp, c_cmp.inc())
                v.tensor_reduce(out=sm[:, 66:67], in_=latc[:], axis=AX.X, op=OP.max).then_inc(cmp, 1)
                c_cmp.inc()
                t.wait_ge(cmp, c_cmp.v)
                for cch in range(2):
                    t.matmul(out=ps[3][0:128, 0:128], lhsT=latc[:, cch * 128:(cch + 1) * 128],
                             rhs=identb[:], start=True, stop=True).then_inc(pe, 1)
                    c_pe.inc()
                    sc.wait_ge(pe, c_pe.v)
                    sc.copy(out=colb[:, cch * 128:(cch + 1) * 128], in_=ps[3][0:128, 0:128])
                    sc.activation(out=stb[:, cch * 128:(cch + 1) * 128],
                                  in_=colb[:, cch * 128:(cch + 1) * 128],
                                  func=AF.Abs).then_inc(cmp, 1)
                    c_cmp.inc()
                    t.wait_ge(cmp, c_cmp.v)
                for w_i, dst in ((0, k1_out), (1, v1_out)):
                    off = OFF_UK if w_i == 0 else OFF_UV
                    t.wait_ge(cmp, c_cmp.v)
                    for half in range(2):
                        for kc in range(2):
                            t.matmul(out=ps[half][0:128, 0:512],
                                     lhsT=colb[:, kc * 128:(kc + 1) * 128],
                                     rhs=w_all[:, off + kc * 1024 + half * 512:
                                               off + kc * 1024 + half * 512 + 512],
                                     start=(kc == 0), stop=(kc == 1)).then_inc(pe, 1)
                            c_pe.inc()
                    v.wait_ge(pe, c_pe.v)
                    for half in range(2):
                        v.tensor_copy(out=st1[:, half * 512:half * 512 + 512],
                                      in_=ps[half][0:128, 0:512])
                    v.tensor_copy(out=sm[:, 67:68], in_=sm[:, 66:67]).then_inc(cmp, 1)
                    v.wait_ge(cmp, c_cmp.inc())
                    if w_i == 0:
                        rope_store(st1, dst)
                    else:
                        g.wait_ge(cmp, c_cmp.v)
                        g.dma_start(out=dst[r0:r0 + 128, :], in_=st1[:]).then_inc(st, 16)
                        c_st.inc(16)
                t.wait_ge(st, c_st.v)
            g.wait_ge(st, c_st.v)
    return nc


def run_c(inputs, a, o):
    x_full = inputs["L0_attn_x"].reshape(B * N, D)
    o_tm = o.reshape(B * N, D)
    identf = np.eye(128, dtype=np.float32)
    pos = np.arange(N, dtype=np.float64)[:, None]
    inv = 1.0 / 10000.0 ** (np.arange(32, dtype=np.float64) / 32.0)
    ang = (pos * inv[None, :]).astype(np.float32)
    cosf = np.cos(ang).astype(np.float32)
    sinf = np.sin(ang).astype(np.float32)
    in_maps = []
    for c in range(NC):
        r0 = c * TOK
        seq0 = (c % 4) * TOK
        in_maps.append({
            "x": np.ascontiguousarray(x_full[r0:r0 + TOK]),
            "o": np.ascontiguousarray(o_tm[r0:r0 + TOK]),
            "gate": np.ascontiguousarray(a["gate"][r0:r0 + TOK]),
            "g": np.ascontiguousarray(a["g"][r0:r0 + TOK]),
            "nw": inputs["L0_attn_norm_weight"].reshape(1, D),
            "Wo": inputs["L0_attn_Wo"], "Wq1": inputs["L1_attn_W_q"],
            "Wdkv": inputs["L1_attn_W_dkv"], "Wuk": inputs["L1_attn_W_uk"],
            "Wuv": inputs["L1_attn_W_uv"],
            "lng": inputs["L1_attn_ln_gamma"].reshape(1, 256),
            "lnb": inputs["L1_attn_ln_beta"].reshape(1, 256),
            "cosb": np.ascontiguousarray(cosf[seq0:seq0 + TOK]),
            "sinb": np.ascontiguousarray(sinf[seq0:seq0 + TOK]),
            "identf": identf,
        })
    nc = build_c()
    res = run_bass_kernel_spmd(nc, in_maps, core_ids=list(range(NC)))
    out = {}
    for key in ("x1", "xn1", "q1", "k1", "v1"):
        out[key] = np.concatenate([r[key] for r in res.results], axis=0)
    return out


DSH = 12.0


def build_d():
    nc = bass.Bass("TRN2", target_bir_lowering=False)
    kcol_in = nc.dram_tensor("kcol", [2, 128, N], BF16, kind="ExternalInput").ap()
    qcol_in = nc.dram_tensor("qcol", [2, 128, N], BF16, kind="ExternalInput").ap()
    vaug_in = nc.dram_tensor("vaug", [2, 128, 2 * 64 * 65], BF16, kind="ExternalInput").ap()
    umask_in = nc.dram_tensor("umask", [128, 896], BF16, kind="ExternalInput").ap()
    identf_in = nc.dram_tensor("identf", [128, 128], F32, kind="ExternalInput").ap()
    oat_out = nc.dram_tensor("oat", [2, 2, N, 64], BF16, kind="ExternalOutput").ap()

    with ExitStack() as ctx:
        E = ctx.enter_context
        kd = [E(nc.sbuf_tensor(f"kdD{d}", [128, N], BF16)) for d in range(2)]
        qd = [E(nc.sbuf_tensor(f"qdD{d}", [128, N], BF16)) for d in range(2)]
        va = [E(nc.sbuf_tensor(f"vaD{d}", [128, 2 * 64 * 65], BF16)) for d in range(2)]
        umaskt = E(nc.sbuf_tensor("umaskD", [128, 896], BF16))
        identft = E(nc.sbuf_tensor("identfD", [128, 128], F32))
        smd = E(nc.sbuf_tensor("smD", [128, 4], F32))
        pt = [[E(nc.sbuf_tensor(f"ptD{p}{b}", [128, 512], BF16)) for b in range(2)]
              for p in range(2)]
        oTs = [E(nc.sbuf_tensor(f"oTsD{p}", [128, 512], F32)) for p in range(2)]
        junkD = E(nc.sbuf_tensor("junkD", [128, 512], F32))
        orw = E(nc.sbuf_tensor("orwD", [128, 68], F32))
        obuf = [E(nc.sbuf_tensor(f"obufD{p}", [128, 256], BF16)) for p in range(2)]
        psS = [[E(nc.psum_tensor(f"psSD{p}{b}", [128, 512], F32)) for b in range(2)]
               for p in range(2)]
        psO = [E(nc.psum_tensor(f"psOD{p}", [128, 512], F32)) for p in range(2)]
        pstr = E(nc.psum_tensor("pstrD", [128, 512], F32))
        ld = E(nc.semaphore("ld"))
        cmp = E(nc.semaphore("cmp"))
        pe = E(nc.semaphore("pe"))
        st = E(nc.semaphore("st"))
        c_ld, c_cmp, c_pe, c_st = Ctr(), Ctr(), Ctr(), Ctr()
        block = E(nc.Block())

        @block.sync
        def _(sync):
            v, t, sc, g = nc.vector, nc.tensor, nc.scalar, nc.gpsimd
            sync.dma_start(out=umaskt[:], in_=umask_in).then_inc(ld, 16)
            sync.dma_start(out=identft[:], in_=identf_in).then_inc(ld, 16)
            c_ld.inc(32)
            duo_ld = []
            for dd in range(2):
                sync.dma_start(out=kd[dd][:], in_=kcol_in[dd]).then_inc(ld, 16)
                sync.dma_start(out=qd[dd][:], in_=qcol_in[dd]).then_inc(ld, 16)
                sync.dma_start(out=va[dd][:], in_=vaug_in[dd]).then_inc(ld, 16)
                c_ld.inc(48)
                duo_ld.append(c_ld.v)
            v.wait_ge(ld, 32)
            v.memset(smd[:, 0:1], -DSH).then_inc(cmp, 1)
            c_cmp.inc()
            sc.wait_ge(cmp, c_cmp.v)
            sc_bias_ready = c_cmp.v

            psS_evac = [[0, 0], [0, 0]]
            pt_used = [[0, 0], [0, 0]]
            psO_evac = [0, 0]
            obf_st = [0, 0]

            for dd in range(2):
                t.wait_ge(ld, duo_ld[dd])
                sc.wait_ge(ld, duo_ld[dd])
                for qb in range(16):
                    nkv = 4 * (qb + 1)
                    for kcx in range(nkv + 1):
                        if kcx < nkv:
                            kc = kcx
                            sbb = kc % 2
                            mm_pe = []
                            for pi in range(2):
                                base = pi * 64
                                t.wait_ge(cmp, psS_evac[pi][sbb])
                                t.matmul(out=psS[pi][sbb][0:128, 0:512],
                                         lhsT=kd[dd][base:base + 64, kc * 128:(kc + 1) * 128],
                                         rhs=qd[dd][base:base + 64, qb * 512:(qb + 1) * 512],
                                         start=True, stop=True,
                                         tile_position=(base, 0)).then_inc(pe, 1)
                                mm_pe.append(c_pe.inc())
                            m = kc - 4 * qb
                            for pi in range(2):
                                sc.wait_ge(pe, mm_pe[pi])
                                sc.wait_ge(pe, pt_used[pi][sbb])
                                sc.activation(out=pt[pi][sbb][:],
                                              in_=psS[pi][sbb][0:128, 0:512],
                                              func=AF.Exp, scale=0.125,
                                              bias=smd[:, 0:1]).then_inc(cmp, 1)
                                c_cmp.inc()
                                if m >= 0:
                                    v.wait_ge(cmp, c_cmp.v)
                                    v.tensor_tensor(out=pt[pi][sbb][:], in0=pt[pi][sbb][:],
                                                    in1=umaskt[:, 384 - 128 * m:896 - 128 * m],
                                                    op=OP.mult).then_inc(cmp, 1)
                                    c_cmp.inc()
                                psS_evac[pi][sbb] = c_cmp.v
                        if kcx >= 1:
                            kc = kcx - 1
                            sbb = kc % 2
                            for pi in range(2):
                                t.wait_ge(cmp, psS_evac[pi][sbb])
                                if kc == 0:
                                    t.wait_ge(cmp, psO_evac[pi])
                                t.matmul(out=psO[pi][0:65, 0:512],
                                         lhsT=va[dd][:, pi * 64 * 65 + kc * 65:
                                                     pi * 64 * 65 + (kc + 1) * 65],
                                         rhs=pt[pi][sbb][:],
                                         start=(kc == 0), stop=(kc == nkv - 1)).then_inc(pe, 1)
                                c_pe.inc()
                                pt_used[pi][sbb] = c_pe.v
                    for pi in range(2):
                        sc.wait_ge(pe, c_pe.v)
                        sc.copy(out=oTs[pi][0:65, 0:512],
                                in_=psO[pi][0:65, 0:512])
                        sc.activation(out=junkD[0:65, 0:512], in_=oTs[pi][0:65, 0:512],
                                      func=AF.Abs).then_inc(cmp, 1)
                        c_cmp.inc()
                        psO_evac[pi] = c_cmp.v
                        t.wait_ge(cmp, c_cmp.v)
                        for j in range(4):
                            t.matmul(out=pstr[0:128, 0:65],
                                     lhsT=oTs[pi][0:65, j * 128:(j + 1) * 128],
                                     rhs=identft[0:65, 0:65],
                                     start=True, stop=True).then_inc(pe, 1)
                            c_pe.inc()
                            v.wait_ge(pe, c_pe.v)
                            v.tensor_copy(out=orw[0:128, 0:65],
                                          in_=pstr[0:128, 0:65]).then_inc(cmp, 1)
                            c_cmp.inc()
                            t.wait_ge(cmp, c_cmp.v)
                            v.wait_ge(cmp, c_cmp.v)
                            v.reciprocal(out=orw[:, 66:67], in_=orw[:, 64:65]).then_inc(cmp, 1)
                            v.wait_ge(cmp, c_cmp.inc())
                            if j == 0:
                                v.wait_ge(st, obf_st[pi])
                            v.tensor_scalar(out=obuf[pi][0:128, j * 64:(j + 1) * 64],
                                            in0=orw[:, 0:64],
                                            scalar1=orw[:, 66:67], scalar2=None,
                                            op0=OP.mult).then_inc(cmp, 1)
                            c_cmp.inc()
                        sync.wait_ge(cmp, c_cmp.v)
                        dst = oat_out[dd, pi, qb * 512:(qb + 1) * 512, :]
                        sync.dma_start(out=dst.rearrange("(j p) w -> p j w", p=128),
                                       in_=obuf[pi][:].rearrange("p (j w) -> p j w", w=64)
                                       ).then_inc(st, 16)
                        c_st.inc(16)
                        obf_st[pi] = c_st.v
            v.wait_ge(st, c_st.v)
            v.wait_ge(ld, c_ld.v)
    return nc


def run_d(inputs, c_out):
    q1 = c_out["q1"].reshape(B, N, H, DH)
    k1 = c_out["k1"].reshape(B, N, H, DH)
    v1 = c_out["v1"].reshape(B, N, H, DH)
    qcol = np.ascontiguousarray(q1.transpose(0, 2, 3, 1))
    kcol = np.ascontiguousarray(k1.transpose(0, 2, 3, 1))
    vr = v1.transpose(0, 2, 1, 3)
    ones = np.ones((B, H, N, 1), dtype=BF)
    vaug = np.concatenate([vr, ones], axis=-1)
    vap = vaug.reshape(B, H, 64, 128, 65).transpose(0, 1, 3, 2, 4).reshape(B, H, 128, 64 * 65)
    umask = (np.arange(896)[None, :] - 384 >= np.arange(128)[:, None]).astype(BF)
    identf = np.eye(128, dtype=np.float32)
    in_maps = []
    for c in range(NC):
        kcolD = np.zeros((2, 128, N), dtype=BF)
        qcolD = np.zeros((2, 128, N), dtype=BF)
        vaugD = np.zeros((2, 128, 2 * 64 * 65), dtype=BF)
        for ddp in range(4):
            pair = 4 * c + ddp
            b, h = pair // H, pair % H
            dd, pi = ddp // 2, ddp % 2
            kcolD[dd, pi * 64:(pi + 1) * 64, :] = kcol[b, h]
            qcolD[dd, pi * 64:(pi + 1) * 64, :] = qcol[b, h]
            vaugD[dd, :, pi * 64 * 65:(pi + 1) * 64 * 65] = vap[b, h]
        in_maps.append({"kcol": kcolD, "qcol": qcolD, "vaug": vaugD,
                        "umask": umask, "identf": identf})
    nc = build_d()
    res = run_bass_kernel_spmd(nc, in_maps, core_ids=list(range(NC)))
    attn = np.empty((B, N, H, DH), dtype=BF)
    for c in range(NC):
        oat = res.results[c]["oat"]
        for ddp in range(4):
            pair = 4 * c + ddp
            b, h = pair // H, pair % H
            attn[b, :, h, :] = oat[ddp // 2, ddp % 2]
    return attn.reshape(B * N, D)


GRPE = 256
NTILE_E = (B * N) // 128


def build_e1():
    nc = bass.Bass("TRN2", target_bir_lowering=False)
    xn1_in = nc.dram_tensor("xn1", [B * N, D], F32, kind="ExternalInput").ap()
    w1_in = nc.dram_tensor("w1", [128, 8 * 4096], BF16, kind="ExternalInput").ap()
    w2_in = nc.dram_tensor("w2", [128, 32 * 1024], BF16, kind="ExternalInput").ap()
    b1t_in = nc.dram_tensor("b1t", [128, 32], F32, kind="ExternalInput").ap()
    b2_in = nc.dram_tensor("b2", [1, 1024], F32, kind="ExternalInput").ap()
    wr_in = nc.dram_tensor("wr", [128, 64], F32, kind="ExternalInput").ap()
    br_in = nc.dram_tensor("br", [1, 8], F32, kind="ExternalInput").ap()
    gam_in = nc.dram_tensor("gam", [1, 1024], F32, kind="ExternalInput").ap()
    bet_in = nc.dram_tensor("bet", [1, 1024], F32, kind="ExternalInput").ap()
    esel_in = nc.dram_tensor("esel", [128, 8], F32, kind="ExternalInput").ap()
    iora_in = nc.dram_tensor("iora", [128, 8], F32, kind="ExternalInput").ap()
    ci99_in = nc.dram_tensor("ci99", [128, 8], F32, kind="ExternalInput").ap()
    identf_in = nc.dram_tensor("identf", [128, 128], F32, kind="ExternalInput").ap()
    part_out = nc.dram_tensor("part", [B * N, D], F32, kind="ExternalOutput").ap()
    wsl_out = nc.dram_tensor("wsl", [128, 128], F32, kind="ExternalOutput").ap()
    cvec_out = nc.dram_tensor("cvec", [1, 1024], F32, kind="ExternalOutput").ap()

    with ExitStack() as ctx:
        E = ctx.enter_context
        w1sb = E(nc.sbuf_tensor("w1sbE", [128, 8 * 4096], BF16))
        w2sb = E(nc.sbuf_tensor("w2sbE", [128, 32 * 1024], BF16))
        hcol = E(nc.sbuf_tensor("hcolE", [128, 32 * GRPE], BF16))
        colbE = E(nc.sbuf_tensor("colbEE", [128, 8 * GRPE], BF16))
        colbF = E(nc.sbuf_tensor("colbFE", [128, 1024], F32))
        gb = E(nc.sbuf_tensor("gbE", [128, 1024], F32))
        bbt = E(nc.sbuf_tensor("bbtE", [128, 1024], F32))
        rqb = E(nc.sbuf_tensor("rqbE", [128, 1024], F32))
        xt = E(nc.sbuf_tensor("xtE", [128, 1024], F32))
        st1 = E(nc.sbuf_tensor("st1E", [128, 1024], F32))
        st2 = E(nc.sbuf_tensor("st2E", [128, 1024], F32))
        pstage = E(nc.sbuf_tensor("pstageE", [128, 1024], F32))
        wbuf = E(nc.sbuf_tensor("wbufE", [128, 128], F32))
        sm = E(nc.sbuf_tensor("smE", [128, 64], F32))
        wrsb = E(nc.sbuf_tensor("wrsbE", [128, 64], F32))
        brsb = E(nc.sbuf_tensor("brsbE", [128, 8], F32))
        iora = E(nc.sbuf_tensor("ioraE", [128, 8], F32))
        ci99 = E(nc.sbuf_tensor("ci99E", [128, 8], F32))
        esel = E(nc.sbuf_tensor("eselE", [128, 8], F32))
        b1sb = E(nc.sbuf_tensor("b1sbE", [128, 32], F32))
        rb1 = E(nc.sbuf_tensor("rb1E", [128, 32], BF16))
        b2sb = E(nc.sbuf_tensor("b2sbE", [1, 1024], F32))
        rqv = E(nc.sbuf_tensor("rqvE", [1, 1024], F32))
        cvst = E(nc.sbuf_tensor("cvstE", [1, 1024], F32))
        ones_row = E(nc.sbuf_tensor("onesE", [1, 128], F32))
        identft = E(nc.sbuf_tensor("identfE", [128, 128], F32))
        junk = E(nc.sbuf_tensor("junkE", [128, 512], F32))
        pstr = E(nc.psum_tensor("pstrE", [128, 512], F32))
        psR = E(nc.psum_tensor("psRE", [128, 512], F32))
        psH = [E(nc.psum_tensor(f"psHE{i}", [128, 512], F32)) for i in range(2)]
        psO = [E(nc.psum_tensor(f"psOE{i}", [128, 512], F32)) for i in range(2)]
        ld = E(nc.semaphore("ld"))
        cmp = E(nc.semaphore("cmp"))
        pe = E(nc.semaphore("pe"))
        st = E(nc.semaphore("st"))
        hev = E(nc.semaphore("hev"))
        c_ld, c_cmp, c_pe, c_st, c_hev = Ctr(), Ctr(), Ctr(), Ctr(), Ctr()
        block = E(nc.Block())

        @block.sync
        def _(sync):
            v, t, sc, g = nc.vector, nc.tensor, nc.scalar, nc.gpsimd
            for buf, src in ((identft, identf_in), (wrsb, wr_in), (iora, iora_in),
                             (ci99, ci99_in), (esel, esel_in), (b1sb, b1t_in)):
                sync.dma_start(out=buf[:], in_=src).then_inc(ld, 16)
                c_ld.inc(16)
            sync.dma_start(out=b2sb[:], in_=b2_in).then_inc(ld, 16)
            sync.dma_start(out=gb[0:1, :], in_=gam_in).then_inc(ld, 16)
            sync.dma_start(out=bbt[0:1, :], in_=bet_in).then_inc(ld, 16)
            sync.dma_start(out=brsb[0:1, :], in_=br_in).then_inc(ld, 16)
            c_ld.inc(64)
            sync.dma_start(out=w1sb[:], in_=w1_in).then_inc(ld, 16)
            sync.dma_start(out=w2sb[:], in_=w2_in).then_inc(ld, 16)
            c_ld.inc(32)
            v.wait_ge(ld, c_ld.v)
            v.memset(ones_row[:], 1.0)
            v.memset(sm[:, 54:55], 1e-5)
            v.tensor_scalar(out=rb1[:], in0=b1sb[:], scalar1=0.0, scalar2=None,
                            op0=OP.max).then_inc(cmp, 1)
            c_cmp.inc()
            sc.wait_ge(cmp, c_cmp.v)
            sc.activation(out=junk[:, 0:32], in_=rb1[:], func=AF.Abs).then_inc(cmp, 1)
            c_cmp.inc()
            t.wait_ge(ld, c_ld.v)
            t.wait_ge(cmp, c_cmp.v)
            for buf in (gb, bbt):
                for half in range(2):
                    t.matmul(out=pstr[0:128, 0:512], lhsT=ones_row[0:1, 0:128],
                             rhs=buf[0:1, half * 512:(half + 1) * 512],
                             start=True, stop=True).then_inc(pe, 1)
                    c_pe.inc()
                    sc.wait_ge(pe, c_pe.v)
                    sc.copy(out=buf[:, half * 512:(half + 1) * 512],
                            in_=pstr[0:128, 0:512]).then_inc(cmp, 1)
                    c_cmp.inc()
                    t.wait_ge(cmp, c_cmp.v)
            t.matmul(out=pstr[0:128, 0:8], lhsT=ones_row[0:1, 0:128],
                     rhs=brsb[0:1, 0:8], start=True, stop=True).then_inc(pe, 1)
            c_pe.inc()
            sc.wait_ge(pe, c_pe.v)
            sc.copy(out=brsb[:, :], in_=pstr[0:128, 0:8]).then_inc(cmp, 1)
            c_cmp.inc()
            t.wait_ge(cmp, c_cmp.v)
            for half in range(2):
                for fc in range(32):
                    t.matmul(out=psR[0:1, 0:512], lhsT=rb1[:, fc:fc + 1],
                             rhs=w2sb[:, fc * 1024 + half * 512:fc * 1024 + half * 512 + 512],
                             start=(fc == 0), stop=(fc == 31)).then_inc(pe, 1)
                    c_pe.inc()
                sc.wait_ge(pe, c_pe.v)
                sc.copy(out=rqv[0:1, half * 512:(half + 1) * 512],
                        in_=psR[0:1, 0:512]).then_inc(cmp, 1)
                c_cmp.inc()
                t.wait_ge(cmp, c_cmp.v)
            v.wait_ge(cmp, c_cmp.v)
            v.tensor_tensor(out=cvst[0:1, :], in0=rqv[0:1, :], in1=b2sb[0:1, :],
                            op=OP.add).then_inc(cmp, 1)
            c_cmp.inc()
            g.wait_ge(cmp, c_cmp.v)
            g.dma_start(out=cvec_out, in_=cvst[0:1, :]).then_inc(st, 16)
            c_st.inc(16)
            for half in range(2):
                t.matmul(out=pstr[0:128, 0:512], lhsT=ones_row[0:1, 0:128],
                         rhs=rqv[0:1, half * 512:(half + 1) * 512],
                         start=True, stop=True).then_inc(pe, 1)
                c_pe.inc()
                sc.wait_ge(pe, c_pe.v)
                sc.copy(out=rqb[:, half * 512:(half + 1) * 512],
                        in_=pstr[0:128, 0:512]).then_inc(cmp, 1)
                c_cmp.inc()
                t.wait_ge(cmp, c_cmp.v)

            xt_free = 0
            colbE_free = 0
            hcol_free = 0
            psH_evac = [0, 0]
            psO_evac = 0
            pstage_st = 0

            def wv(n=1):
                v.wait_ge(cmp, c_cmp.inc(n))

            wbuf_ready = {}

            def emit_tiles(grp):
                nonlocal xt_free
                for t01 in range(2):
                    ti = grp * 2 + t01
                    row0 = ti * 128
                    sync.wait_ge(cmp, xt_free)
                    sync.dma_start(out=xt[:], in_=xn1_in[row0:row0 + 128, :]).then_inc(ld, 16)
                    c_ld.inc(16)
                    v.wait_ge(ld, c_ld.v)
                    v.tensor_reduce(out=sm[:, 48:49], in_=xt[:], axis=AX.X,
                                    op=OP.add).then_inc(cmp, 1)
                    wv()
                    v.tensor_scalar(out=sm[:, 49:50], in0=sm[:, 48:49],
                                    scalar1=1.0 / 1024.0, scalar2=None,
                                    op0=OP.mult).then_inc(cmp, 1)
                    wv()
                    v.tensor_tensor(out=st2[:], in0=xt[:],
                                    in1=sm[:, 49:50].broadcast_to([128, 1024]),
                                    op=OP.subtract).then_inc(cmp, 1)
                    c_cmp.inc()
                    xt_free = c_cmp.v
                    v.scalar_tensor_tensor(out=st1[:], in0=st2[:], scalar=1.0, in1=st2[:],
                                           op0=OP.mult, op1=OP.mult,
                                           accum_out=sm[:, 50:51]).then_inc(cmp, 1)
                    wv()
                    v.tensor_scalar(out=sm[:, 51:52], in0=sm[:, 50:51],
                                    scalar1=1.0 / 1024.0, scalar2=None,
                                    op0=OP.mult).then_inc(cmp, 1)
                    c_cmp.inc()
                    sc.wait_ge(cmp, c_cmp.v)
                    sc.activation(out=sm[:, 52:53], in_=sm[:, 51:52], func=AF.Sqrt,
                                  bias=sm[:, 54:55], scale=1.0).then_inc(cmp, 1)
                    wv()
                    v.reciprocal(out=sm[:, 53:54], in_=sm[:, 52:53]).then_inc(cmp, 1)
                    wv()
                    v.tensor_tensor(out=st1[:], in0=st2[:],
                                    in1=sm[:, 53:54].broadcast_to([128, 1024]),
                                    op=OP.mult).then_inc(cmp, 1)
                    wv()
                    v.tensor_tensor(out=st1[:], in0=st1[:], in1=gb[:], op=OP.mult)
                    v.tensor_tensor(out=st1[:], in0=st1[:], in1=bbt[:],
                                    op=OP.add).then_inc(cmp, 1)
                    c_cmp.inc()
                    t.wait_ge(cmp, c_cmp.v)
                    for cch in range(8):
                        t.matmul(out=pstr[0:128, 0:128], lhsT=st1[:, cch * 128:(cch + 1) * 128],
                                 rhs=identft[0:128, 0:128], start=True,
                                 stop=True).then_inc(pe, 1)
                        c_pe.inc()
                        sc.wait_ge(pe, c_pe.v)
                        sc.copy(out=colbF[:, cch * 128:(cch + 1) * 128],
                                in_=pstr[0:128, 0:128])
                        sc.activation(out=junk[:, 0:128],
                                      in_=colbF[:, cch * 128:(cch + 1) * 128],
                                      func=AF.Abs).then_inc(cmp, 1)
                        c_cmp.inc()
                        t.wait_ge(cmp, c_cmp.v)
                        v.wait_ge(cmp, c_cmp.v)
                        if cch == 0:
                            v.wait_ge(pe, colbE_free)
                        v.tensor_copy(out=colbE[:, cch * GRPE + t01 * 128:
                                               cch * GRPE + t01 * 128 + 128],
                                      in_=colbF[:, cch * 128:(cch + 1) * 128]).then_inc(cmp, 1)
                        c_cmp.inc()
                    for kc in range(8):
                        t.matmul(out=psR[0:128, 0:8], lhsT=colbF[:, kc * 128:(kc + 1) * 128],
                                 rhs=wrsb[:, kc * 8:(kc + 1) * 8], start=(kc == 0),
                                 stop=(kc == 7)).then_inc(pe, 1)
                        c_pe.inc()
                    v.wait_ge(pe, c_pe.v)
                    v.tensor_tensor(out=sm[:, 0:8], in0=psR[0:128, 0:8], in1=brsb[:],
                                    op=OP.add).then_inc(cmp, 1)
                    wv()
                    v.tensor_reduce(out=sm[:, 8:9], in_=sm[:, 0:8], axis=AX.X,
                                    op=OP.max).then_inc(cmp, 1)
                    wv()
                    v.tensor_tensor(out=sm[:, 0:8], in0=sm[:, 0:8],
                                    in1=sm[:, 8:9].broadcast_to([128, 8]),
                                    op=OP.subtract).then_inc(cmp, 1)
                    c_cmp.inc()
                    sc.wait_ge(cmp, c_cmp.v)
                    sc.activation(out=sm[:, 0:8], in_=sm[:, 0:8], func=AF.Exp).then_inc(cmp, 1)
                    wv()
                    v.tensor_reduce(out=sm[:, 9:10], in_=sm[:, 0:8], axis=AX.X,
                                    op=OP.add).then_inc(cmp, 1)
                    wv()
                    v.reciprocal(out=sm[:, 10:11], in_=sm[:, 9:10]).then_inc(cmp, 1)
                    wv()
                    v.tensor_tensor(out=sm[:, 0:8], in0=sm[:, 0:8],
                                    in1=sm[:, 10:11].broadcast_to([128, 8]),
                                    op=OP.mult).then_inc(cmp, 1)
                    wv()
                    v.tensor_reduce(out=sm[:, 11:12], in_=sm[:, 0:8], axis=AX.X,
                                    op=OP.max).then_inc(cmp, 1)
                    wv()
                    v.tensor_tensor(out=sm[:, 16:24], in0=sm[:, 0:8],
                                    in1=sm[:, 11:12].broadcast_to([128, 8]),
                                    op=OP.is_equal).then_inc(cmp, 1)
                    wv()
                    v.tensor_tensor(out=sm[:, 24:32], in0=sm[:, 16:24], in1=ci99[:],
                                    op=OP.mult).then_inc(cmp, 1)
                    wv()
                    v.tensor_scalar(out=sm[:, 24:32], in0=sm[:, 24:32], scalar1=-1.0,
                                    scalar2=99.0, op0=OP.mult, op1=OP.add).then_inc(cmp, 1)
                    wv()
                    v.tensor_reduce(out=sm[:, 12:13], in_=sm[:, 24:32], axis=AX.X,
                                    op=OP.min).then_inc(cmp, 1)
                    wv()
                    v.tensor_tensor(out=sm[:, 16:24], in0=iora[:],
                                    in1=sm[:, 12:13].broadcast_to([128, 8]),
                                    op=OP.is_equal).then_inc(cmp, 1)
                    wv()
                    v.tensor_scalar(out=sm[:, 24:32], in0=sm[:, 16:24], scalar1=1e9,
                                    scalar2=None, op0=OP.mult).then_inc(cmp, 1)
                    wv()
                    v.tensor_tensor(out=sm[:, 32:40], in0=sm[:, 0:8], in1=sm[:, 24:32],
                                    op=OP.subtract).then_inc(cmp, 1)
                    wv()
                    v.tensor_reduce(out=sm[:, 13:14], in_=sm[:, 32:40], axis=AX.X,
                                    op=OP.max).then_inc(cmp, 1)
                    wv()
                    v.tensor_tensor(out=sm[:, 40:48], in0=sm[:, 32:40],
                                    in1=sm[:, 13:14].broadcast_to([128, 8]),
                                    op=OP.is_equal).then_inc(cmp, 1)
                    wv()
                    v.tensor_tensor(out=sm[:, 24:32], in0=sm[:, 40:48], in1=ci99[:],
                                    op=OP.mult).then_inc(cmp, 1)
                    wv()
                    v.tensor_scalar(out=sm[:, 24:32], in0=sm[:, 24:32], scalar1=-1.0,
                                    scalar2=99.0, op0=OP.mult, op1=OP.add).then_inc(cmp, 1)
                    wv()
                    v.tensor_reduce(out=sm[:, 14:15], in_=sm[:, 24:32], axis=AX.X,
                                    op=OP.min).then_inc(cmp, 1)
                    wv()
                    v.tensor_tensor(out=sm[:, 40:48], in0=iora[:],
                                    in1=sm[:, 14:15].broadcast_to([128, 8]),
                                    op=OP.is_equal).then_inc(cmp, 1)
                    wv()
                    v.tensor_tensor(out=sm[:, 16:24], in0=sm[:, 16:24],
                                    in1=sm[:, 11:12].broadcast_to([128, 8]),
                                    op=OP.mult)
                    v.tensor_tensor(out=sm[:, 40:48], in0=sm[:, 40:48],
                                    in1=sm[:, 13:14].broadcast_to([128, 8]),
                                    op=OP.mult).then_inc(cmp, 1)
                    wv()
                    v.tensor_tensor(out=sm[:, 16:24], in0=sm[:, 16:24], in1=sm[:, 40:48],
                                    op=OP.add).then_inc(cmp, 1)
                    wv()
                    v.tensor_tensor(out=sm[:, 24:32], in0=sm[:, 16:24], in1=esel[:],
                                    op=OP.mult).then_inc(cmp, 1)
                    wv()
                    v.tensor_reduce(out=wbuf[:, ti:ti + 1], in_=sm[:, 24:32], axis=AX.X,
                                    op=OP.add).then_inc(cmp, 1)
                    wv()
                wbuf_ready[grp] = c_cmp.v

            def emit_h(grp):
                nonlocal colbE_free
                sc.wait_ge(pe, hcol_free)
                t.wait_ge(cmp, c_cmp.v)
                for fc in range(32):
                    bb = fc % 2
                    t.wait_ge(hev, psH_evac[bb])
                    for kc in range(8):
                        t.matmul(out=psH[bb][0:128, 0:GRPE],
                                 lhsT=w1sb[:, kc * 4096 + fc * 128:kc * 4096 + fc * 128 + 128],
                                 rhs=colbE[:, kc * GRPE:(kc + 1) * GRPE],
                                 start=(kc == 0), stop=(kc == 7)).then_inc(pe, 1)
                        c_pe.inc()
                    sc.wait_ge(pe, c_pe.v)
                    sc.activation(out=hcol[:, fc * GRPE:(fc + 1) * GRPE],
                                  in_=psH[bb][0:128, 0:GRPE], func=AF.Relu,
                                  bias=b1sb[:, fc:fc + 1], scale=1.0)
                    sc.activation(out=junk[:, 0:GRPE],
                                  in_=hcol[:, fc * GRPE:(fc + 1) * GRPE],
                                  func=AF.Abs).then_inc(hev, 1)
                    c_hev.inc()
                    psH_evac[bb] = c_hev.v
                colbE_free = c_pe.v

            def emit_down(grp):
                nonlocal psO_evac, pstage_st, hcol_free
                t.wait_ge(cmp, wbuf_ready[grp])
                t.wait_ge(hev, c_hev.v)
                for t01 in range(2):
                    ti = grp * 2 + t01
                    row0 = ti * 128
                    t.wait_ge(cmp, psO_evac)
                    for fc in range(32):
                        for half in range(2):
                            t.matmul(out=psO[half][0:128, 0:512],
                                     lhsT=hcol[:, fc * GRPE + t01 * 128:
                                               fc * GRPE + t01 * 128 + 128],
                                     rhs=w2sb[:, fc * 1024 + half * 512:
                                              fc * 1024 + half * 512 + 512],
                                     start=(fc == 0), stop=(fc == 31)).then_inc(pe, 1)
                            c_pe.inc()
                    v.wait_ge(pe, c_pe.v)
                    v.wait_ge(st, pstage_st)
                    for half in range(2):
                        v.tensor_tensor(out=pstage[:, half * 512:half * 512 + 512],
                                        in0=psO[half][0:128, 0:512],
                                        in1=rqb[:, half * 512:half * 512 + 512],
                                        op=OP.subtract)
                    v.tensor_scalar(out=pstage[:], in0=pstage[:],
                                    scalar1=wbuf[:, ti:ti + 1], scalar2=None,
                                    op0=OP.mult).then_inc(cmp, 1)
                    c_cmp.inc()
                    psO_evac = c_cmp.v
                    g.wait_ge(cmp, c_cmp.v)
                    g.dma_start(out=part_out[row0:row0 + 128, :],
                                in_=pstage[:]).then_inc(st, 16)
                    c_st.inc(16)
                    pstage_st = c_st.v
                hcol_free = c_pe.v

            emit_tiles(0)
            for grp in range(NTILE_E // 2):
                emit_h(grp)
                if grp + 1 < NTILE_E // 2:
                    emit_tiles(grp + 1)
                emit_down(grp)
            g.dma_start(out=wsl_out, in_=wbuf[:]).then_inc(st, 16)
            c_st.inc(16)
            v.wait_ge(st, c_st.v)
            v.wait_ge(ld, c_ld.v)
    return nc


def run_e1(inputs, c_out, a0):
    xn1 = np.ascontiguousarray(c_out["xn1"])
    iora = np.tile(np.arange(8, dtype=np.float32), (128, 1))
    ci99 = 99.0 - iora
    identf = np.eye(128, dtype=np.float32)
    wr = np.ascontiguousarray(
        inputs["L1_ffn_Wr"].reshape(8, 128, 8).transpose(1, 0, 2).reshape(128, 64))
    in_maps = []
    for c in range(NC):
        w1p = np.ascontiguousarray(
            a0["w1b"][c].reshape(8, 128, 4096).transpose(1, 0, 2).reshape(128, 8 * 4096))
        w2p = np.ascontiguousarray(
            a0["w2b"][c].reshape(32, 128, 1024).transpose(1, 0, 2).reshape(128, 32 * 1024))
        esel = np.zeros((128, 8), dtype=np.float32)
        esel[:, c] = 1.0
        in_maps.append({
            "xn1": xn1, "w1": w1p, "w2": w2p,
            "b1t": np.ascontiguousarray(inputs["L1_ffn_b1"][c].reshape(32, 128).T),
            "b2": inputs["L1_ffn_b2"][c].reshape(1, 1024),
            "wr": wr, "br": inputs["L1_ffn_br"].reshape(1, 8),
            "gam": inputs["L1_ffn_gamma"].reshape(1, 1024),
            "bet": inputs["L1_ffn_beta"].reshape(1, 1024),
            "esel": esel, "iora": iora, "ci99": ci99, "identf": identf,
        })
    nc = build_e1()
    res = run_bass_kernel_spmd(nc, in_maps, core_ids=list(range(NC)))
    parts = [res.results[c]["part"] for c in range(NC)]
    wtok = np.stack([res.results[c]["wsl"].T.reshape(B * N) for c in range(NC)], axis=1)
    cvs = np.stack([res.results[c]["cvec"][0] for c in range(NC)], axis=0)
    return parts, wtok, cvs


def build_e2():
    nc = bass.Bass("TRN2", target_bir_lowering=False)
    x1_in = nc.dram_tensor("x1", [TOK, D], F32, kind="ExternalInput").ap()
    xn1_in = nc.dram_tensor("xn1", [TOK, D], F32, kind="ExternalInput").ap()
    attn_in = nc.dram_tensor("attn", [TOK, D], BF16, kind="ExternalInput").ap()
    wo_in = nc.dram_tensor("wo", [128, 8 * 1024], BF16, kind="ExternalInput").ap()
    p_in = [nc.dram_tensor(f"p{e}", [TOK, D], F32, kind="ExternalInput").ap()
            for e in range(8)]
    wtok_in = nc.dram_tensor("wtok", [TOK, 8], F32, kind="ExternalInput").ap()
    cvs_in = nc.dram_tensor("cvs", [8, 1024], F32, kind="ExternalInput").ap()
    identb_in = nc.dram_tensor("identb", [128, 128], BF16, kind="ExternalInput").ap()
    fin_out = nc.dram_tensor("fin", [TOK, D], F32, kind="ExternalOutput").ap()

    with ExitStack() as ctx:
        E = ctx.enter_context
        wosb = E(nc.sbuf_tensor("wosbF", [128, 8 * 1024], BF16))
        xt = E(nc.sbuf_tensor("xtF", [128, 1024], F32))
        xnt = E(nc.sbuf_tensor("xntF", [128, 1024], F32))
        att = E(nc.sbuf_tensor("attF", [128, 1024], BF16))
        colb = E(nc.sbuf_tensor("colbF2", [128, 1024], BF16))
        st1 = E(nc.sbuf_tensor("st1F", [128, 1024], F32))
        st2 = E(nc.sbuf_tensor("st2F", [128, 1024], F32))
        acc = E(nc.sbuf_tensor("accF", [128, 1024], F32))
        pt2 = [E(nc.sbuf_tensor(f"pt2F{i}", [128, 1024], F32)) for i in range(2)]
        wt = E(nc.sbuf_tensor("wtF", [128, 8], F32))
        ctb = E(nc.sbuf_tensor("ctbF", [128, 1024], F32))
        ctv = E(nc.sbuf_tensor("ctvF", [1, 1024], F32))
        cvsb = E(nc.sbuf_tensor("cvsbF", [8, 1024], F32))
        onescol = E(nc.sbuf_tensor("onescolF", [8, 1], F32))
        ones_row = E(nc.sbuf_tensor("onesF", [1, 128], F32))
        identbt = E(nc.sbuf_tensor("identbF", [128, 128], BF16))
        junk = E(nc.sbuf_tensor("junkF", [128, 512], F32))
        sm = E(nc.sbuf_tensor("smF", [128, 8], F32))
        pstr = E(nc.psum_tensor("pstrF", [128, 512], F32))
        psS = [E(nc.psum_tensor(f"psSF{i}", [128, 512], F32)) for i in range(2)]
        ld = E(nc.semaphore("ld"))
        cmp = E(nc.semaphore("cmp"))
        pe = E(nc.semaphore("pe"))
        st = E(nc.semaphore("st"))
        c_ld, c_cmp, c_pe, c_st = Ctr(), Ctr(), Ctr(), Ctr()
        block = E(nc.Block())

        @block.sync
        def _(sync):
            v, t, sc, g = nc.vector, nc.tensor, nc.scalar, nc.gpsimd
            sync.dma_start(out=identbt[:], in_=identb_in).then_inc(ld, 16)
            sync.dma_start(out=cvsb[:], in_=cvs_in).then_inc(ld, 16)
            sync.dma_start(out=wosb[:], in_=wo_in).then_inc(ld, 16)
            c_ld.inc(48)
            v.wait_ge(ld, c_ld.v)
            v.memset(ones_row[:], 1.0)
            v.memset(onescol[0:8, 0:1], 1.0).then_inc(cmp, 1)
            c_cmp.inc()
            t.wait_ge(ld, c_ld.v)
            t.wait_ge(cmp, c_cmp.v)
            for half in range(2):
                t.matmul(out=pstr[0:1, 0:512], lhsT=onescol[0:8, 0:1],
                         rhs=cvsb[0:8, half * 512:(half + 1) * 512],
                         start=True, stop=True).then_inc(pe, 1)
                c_pe.inc()
                sc.wait_ge(pe, c_pe.v)
                sc.copy(out=ctv[0:1, half * 512:(half + 1) * 512],
                        in_=pstr[0:1, 0:512]).then_inc(cmp, 1)
                c_cmp.inc()
                t.wait_ge(cmp, c_cmp.v)
            for half in range(2):
                t.matmul(out=pstr[0:128, 0:512], lhsT=ones_row[0:1, 0:128],
                         rhs=ctv[0:1, half * 512:(half + 1) * 512],
                         start=True, stop=True).then_inc(pe, 1)
                c_pe.inc()
                sc.wait_ge(pe, c_pe.v)
                sc.copy(out=ctb[:, half * 512:(half + 1) * 512],
                        in_=pstr[0:128, 0:512]).then_inc(cmp, 1)
                c_cmp.inc()
                t.wait_ge(cmp, c_cmp.v)

            in_free = 0
            st_done = 0
            for i in range(NT):
                r0 = i * 128
                sync.wait_ge(cmp, in_free)
                sync.dma_start(out=att[:], in_=attn_in[r0:r0 + 128, :]).then_inc(ld, 16)
                sync.dma_start(out=xt[:], in_=x1_in[r0:r0 + 128, :]).then_inc(ld, 16)
                sync.dma_start(out=xnt[:], in_=xn1_in[r0:r0 + 128, :]).then_inc(ld, 16)
                sync.dma_start(out=wt[:], in_=wtok_in[r0:r0 + 128, :]).then_inc(ld, 16)
                sync.dma_start(out=pt2[0][:], in_=p_in[0][r0:r0 + 128, :]).then_inc(ld, 16)
                c_ld.inc(80)
                t.wait_ge(ld, c_ld.v - 64)
                for cch in range(8):
                    t.matmul(out=pstr[0:128, 0:128], lhsT=att[:, cch * 128:(cch + 1) * 128],
                             rhs=identbt[0:128, 0:128], start=True, stop=True).then_inc(pe, 1)
                    c_pe.inc()
                    sc.wait_ge(pe, c_pe.v)
                    sc.copy(out=colb[:, cch * 128:(cch + 1) * 128], in_=pstr[0:128, 0:128])
                    sc.activation(out=junk[:, 0:128], in_=colb[:, cch * 128:(cch + 1) * 128],
                                  func=AF.Abs).then_inc(cmp, 1)
                    c_cmp.inc()
                    t.wait_ge(cmp, c_cmp.v)
                for half in range(2):
                    for kc in range(8):
                        t.matmul(out=psS[half][0:128, 0:512],
                                 lhsT=colb[:, kc * 128:(kc + 1) * 128],
                                 rhs=wosb[:, kc * 1024 + half * 512:kc * 1024 + half * 512 + 512],
                                 start=(kc == 0), stop=(kc == 7)).then_inc(pe, 1)
                        c_pe.inc()
                v.wait_ge(pe, c_pe.v)
                add_done = []
                for e in range(1, 8):
                    if e >= 2:
                        sync.wait_ge(cmp, add_done[e - 3] if e >= 3 else add_done[0])
                    sync.dma_start(out=pt2[e % 2][:],
                                   in_=p_in[e][r0:r0 + 128, :]).then_inc(ld, 16)
                    c_ld.inc(16)
                    v.wait_ge(ld, c_ld.v)
                    if e == 1:
                        v.tensor_tensor(out=acc[:], in0=pt2[0][:], in1=pt2[1][:],
                                        op=OP.add).then_inc(cmp, 1)
                    else:
                        v.tensor_tensor(out=acc[:], in0=acc[:], in1=pt2[e % 2][:],
                                        op=OP.add).then_inc(cmp, 1)
                    c_cmp.inc()
                    add_done.append(c_cmp.v)
                v.tensor_reduce(out=sm[:, 0:1], in_=wt[:], axis=AX.X, op=OP.add).then_inc(cmp, 1)
                v.wait_ge(cmp, c_cmp.inc())
                v.tensor_scalar(out=st2[:], in0=ctb[:], scalar1=sm[:, 0:1], scalar2=None,
                                op0=OP.mult).then_inc(cmp, 1)
                v.wait_ge(cmp, c_cmp.inc())
                v.tensor_tensor(out=st2[:], in0=st2[:], in1=acc[:], op=OP.add)
                v.tensor_tensor(out=st2[:], in0=st2[:], in1=xnt[:], op=OP.add).then_inc(cmp, 1)
                c_cmp.inc()
                sc.wait_ge(cmp, c_cmp.v)
                sc.activation(out=acc[:], in_=st2[:], func=AF.Silu).then_inc(cmp, 1)
                c_cmp.inc()
                v.wait_ge(pe, c_pe.v)
                for half in range(2):
                    v.tensor_copy(out=st1[:, half * 512:half * 512 + 512],
                                  in_=psS[half][0:128, 0:512])
                v.wait_ge(cmp, c_cmp.v)
                v.wait_ge(st, st_done)
                v.tensor_tensor(out=st1[:], in0=st1[:], in1=acc[:], op=OP.mult)
                v.tensor_tensor(out=st1[:], in0=st1[:], in1=xt[:], op=OP.add).then_inc(cmp, 1)
                c_cmp.inc()
                in_free = c_cmp.v
                g.wait_ge(cmp, c_cmp.v)
                g.dma_start(out=fin_out[r0:r0 + 128, :], in_=st1[:]).then_inc(st, 16)
                c_st.inc(16)
                st_done = c_st.v
            v.wait_ge(st, c_st.v)
            v.wait_ge(ld, c_ld.v)
    return nc


def run_e2(inputs, c_out, attn, parts, wtok, cvs):
    identb = np_identity_bf16()
    wob = np.ascontiguousarray(
        inputs["L1_attn_W_o"].astype(BF).reshape(8, 128, 1024).transpose(1, 0, 2)
        .reshape(128, 8 * 1024))
    in_maps = []
    for c in range(NC):
        r0 = c * TOK
        m = {
            "x1": np.ascontiguousarray(c_out["x1"][r0:r0 + TOK]),
            "xn1": np.ascontiguousarray(c_out["xn1"][r0:r0 + TOK]),
            "attn": np.ascontiguousarray(attn[r0:r0 + TOK]),
            "wo": wob, "wtok": np.ascontiguousarray(wtok[r0:r0 + TOK]),
            "cvs": cvs, "identb": identb,
        }
        for e in range(8):
            m[f"p{e}"] = np.ascontiguousarray(parts[e][r0:r0 + TOK])
        in_maps.append(m)
    nc = build_e2()
    res = run_bass_kernel_spmd(nc, in_maps, core_ids=list(range(NC)))
    fin = np.concatenate([res.results[c]["fin"] for c in range(NC)], axis=0)
    return fin.reshape(B, N, D)
